# version 01
# baseline (speedup 1.0000x reference)
import contextlib
import math
import numpy as np

import concourse.bass as bass
import concourse.mybir as mybir
from concourse.bass_utils import run_bass_kernel_spmd

F32 = mybir.dt.float32
BF16 = mybir.dt.bfloat16
I32 = mybir.dt.int32
AF = mybir.ActivationFunctionType
ALU = mybir.AluOpType
AX = mybir.AxisListType

ENGINES = ("tensor", "vector", "scalar", "gpsimd", "sync")

T = 2048
D = 1024
NTILE = D // 128
TC = 512
NTC = T // TC


import os as _os
SEM_ROT = int(_os.environ.get("SEM_ROT", "30000"))


class Prog:
    def __init__(self, nc, n_dma_sems=24):
        self.nc = nc
        self.ops = {e: [] for e in ENGINES}
        self.cs = {e: 0 for e in ENGINES}
        self.cgen = {e: 0 for e in ENGINES}
        self.n_dma = n_dma_sems
        self.dmac = [0] * n_dma_sems
        self.acc = {}
        self.waited = {e: {} for e in ENGINES}
        self.meta = {}
        self.stream_sem = {}
        self.next_stream = 0
        self.nwaits = 0

    def reg(self, handle, space):
        ap = handle.ap() if hasattr(handle, "ap") else handle[:]
        shape = list(ap.shape)
        rowlen = 1
        for s in shape[1:]:
            rowlen *= s
        if space == "dram":
            rowlen = rowlen * max(1, shape[0] if shape else 1)
        isz = {F32: 4, BF16: 2, I32: 4}.get(ap.dtype, 4)
        self.meta[ap.name] = (rowlen, isz, space)
        return handle

    def _rect(self, ap):
        if callable(getattr(ap, "ap", None)):
            ap = ap.ap()
        name = ap.name
        rowlen, isz, space = self.meta[name]
        if space == "dram":
            lo = hi = ap.offset
            for st, ct in ap.ap:
                if st >= 0:
                    hi += st * (ct - 1)
                else:
                    lo += st * (ct - 1)
            return name, (0, 1, lo * isz, (hi + 1) * isz)
        bp = ap.base_partition
        p0 = bp() if callable(bp) else bp
        pn = ap.partition_size
        pn = pn() if callable(pn) else pn
        f0 = ap.offset - p0 * rowlen
        lo = hi = f0
        for st, ct in ap.ap[1:]:
            if st >= 0:
                hi += st * (ct - 1)
            else:
                lo += st * (ct - 1)
        b0, b1 = lo * isz, (hi + 1) * isz
        if space == "psum":
            b0 = (b0 // 2048) * 2048
            b1 = ((b1 + 2047) // 2048) * 2048
        return name, (p0, p0 + pn, b0, b1)

    @staticmethod
    def _overlap(r1, r2):
        return not (r1[1] <= r2[0] or r2[1] <= r1[0]
                    or r1[3] <= r2[2] or r2[3] <= r1[2])

    def _need(self, engine, aps_r, aps_w):
        need = {}
        for ap, as_write in [(a, False) for a in aps_r] + [(a, True) for a in aps_w]:
            name, rect = self._rect(ap)
            for (is_w, r, sem, val, eng2, isdma) in self.acc.get(name, ()):
                if not (is_w or as_write):
                    continue
                if not self._overlap(rect, r):
                    continue
                if not isdma and eng2 == engine:
                    if not is_w:
                        continue
                    if engine == "tensor":
                        continue
                    if val <= self.cs[engine] - 8:
                        continue
                if self.waited[engine].get(sem, 0) >= val:
                    continue
                if need.get(sem, 0) < val:
                    need[sem] = val
        return need

    def _record(self, aps_r, aps_w, sem, val, engine, isdma):
        for ap, is_w in [(a, False) for a in aps_r] + [(a, True) for a in aps_w]:
            name, rect = self._rect(ap)
            lst = self.acc.setdefault(name, [])
            lst.append((is_w, rect, sem, val, engine, isdma))
            if len(lst) > 192:
                old, keep = lst[:96], lst[96:]
                floors = {}
                for (w2, r2, s2, v2, e2, d2) in old:
                    k = (s2, e2, d2, w2)
                    if floors.get(k, 0) < v2:
                        floors[k] = v2
                whole = (0, 1 << 20, 0, 1 << 40)
                self.acc[name] = [(w2, whole, s2, v2, e2, d2)
                                  for (s2, e2, d2, w2), v2 in floors.items()] + keep

    def op(self, engine, fn, reads=(), writes=()):
        need = self._need(engine, reads, writes)
        for sem, val in need.items():
            self.waited[engine][sem] = val
        self.nwaits += len(need)
        self.cs[engine] += 1
        if self.cs[engine] > SEM_ROT:
            self.cgen[engine] += 1
            self.cs[engine] = 1
        sem = f"c_{engine}_g{self.cgen[engine]}"
        self._record(reads, writes, sem, self.cs[engine], engine, False)
        self.ops[engine].append((need, fn, (sem, 1)))

    def dma(self, engine, out, in_, stream="w", **kw):
        if stream not in self.stream_sem:
            self.stream_sem[stream] = self.next_stream % self.n_dma
            self.next_stream += 1
        si = self.stream_sem[stream]
        need = self._need(engine, [in_], [out])
        if self.dmac[si] > 0:
            prev = 16 * self.dmac[si]
            if self.waited[engine].get(f"d{si}", 0) < prev and need.get(f"d{si}", 0) < prev:
                need[f"d{si}"] = prev
        for sem, val in need.items():
            self.waited[engine][sem] = val
        self.nwaits += len(need)
        self.dmac[si] += 1
        sem = f"d{si}"
        self._record([in_], [out], sem, 16 * self.dmac[si], engine, True)
        fn = lambda e: e.dma_start(out=out, in_=in_, **kw)
        self.ops[engine].append((need, fn, (sem, 16)))

    def barrier_wait(self, engine):
        need = {}
        for e in ENGINES:
            sem = f"c_{e}_g{self.cgen[e]}"
            if e != engine and self.cs[e] > 0 and self.waited[engine].get(sem, 0) < self.cs[e]:
                need[sem] = self.cs[e]
        for i in range(self.n_dma):
            if self.dmac[i] > 0 and self.waited[engine].get(f"d{i}", 0) < 16 * self.dmac[i]:
                need[f"d{i}"] = 16 * self.dmac[i]
        for sem, val in need.items():
            self.waited[engine][sem] = val
        self.ops[engine].append((need, None, None))

    def emit(self, stack):
        nc = self.nc
        sems = {}
        for e in ENGINES:
            for g in range(self.cgen[e] + 1):
                sems[f"c_{e}_g{g}"] = stack.enter_context(nc.semaphore(f"cs_{e}_{g}"))
        for i in range(self.n_dma):
            sems[f"d{i}"] = stack.enter_context(nc.semaphore(f"ds_{i}"))
        block = stack.enter_context(nc.Block())
        for ename in ENGINES:
            steps = self.ops[ename]
            if not steps:
                continue

            def mk(steps):
                def body(eng):
                    for waits, fn, inc in steps:
                        for sem, val in waits.items():
                            eng.wait_ge(sems[sem], val)
                        if fn is not None:
                            ins = fn(eng)
                            if inc is not None:
                                ins.then_inc(sems[inc[0]], inc[1])
                return body
            getattr(block, ename)(mk(steps))


class Ctx:
    pass


NROW = 7
R_MU, R_VAR, R_R1, R_S, R_B, R_T0, R_T1 = range(7)
R_SP, R_BP = R_T0, R_T1
R_ACC = R_S


def row(c, i):
    return c.ROWS[0:1, i * TC:(i + 1) * TC]


def alloc_common(nc, p, stack):
    c = Ctx()
    c.nc = nc
    c.p = p
    sb = lambda name, shape, dt: p.reg(
        stack.enter_context(nc.sbuf_tensor(name, shape, dt)), "sbuf")
    ps = lambda name, shape, dt: p.reg(
        stack.enter_context(nc.psum_tensor(name, shape, dt)), "psum")

    c.X = sb("X", [128, NTILE * T], F32)
    c.A = sb("A", [128, NTILE * T], BF16)
    c.B = sb("B", [128, NTILE * T], BF16)
    c.C = sb("C", [128, NTILE * T], BF16)
    c.SBC = sb("SBC", [128, TC], F32)
    c.BBC = sb("BBC", [128, TC], F32)
    c.SC16 = sb("SC16", [128, T], BF16)
    c.SC16b = sb("SC16b", [128, T], BF16)
    c.LAT = sb("LAT", [128, 2048], BF16)
    c.ROWS = sb("ROWS", [1, NROW * TC], F32)
    c.WS = sb("WS", [128, 2048], F32)
    c.PVEC = sb("PVEC", [128, 112], F32)
    c.CONSTF = sb("CONSTF", [128, 24], F32)
    c.PVI = sb("PVI", [128, 16], I32)
    c.RNDI = sb("RNDI", [128, TC], I32)
    c.TMP32 = sb("TMP32", [128, 512], F32)
    c.QSC = sb("QSC", [128, 96], F32)
    c.SEL = sb("SEL", [8, 1024], BF16)
    c.ONES = sb("ONES", [128, 8], BF16)
    c.ONESROW = sb("ONESROW", [1, 128], BF16)
    c.ONESROWF = sb("ONESROWF", [1, 128], F32)
    c.IDENT = sb("IDENT", [128, 128], BF16)
    c.IDENTF = sb("IDENTF", [128, 128], F32)

    c.PS_BIG = ps("PS_BIG", [128, 2048], F32)
    c.PS_A = ps("PS_A", [128, 512], F32)
    c.PS_B = ps("PS_B", [128, 512], F32)
    c.PS_T = ps("PS_T", [128, 1024], BF16)
    c.PS_O = ps("PS_O", [128, 512], F32)

    p.op("gpsimd", lambda e: e.memset(c.CONSTF[:, 0:1], 1e-5), writes=[c.CONSTF[:, 0:1]])
    p.op("gpsimd", lambda e: e.memset(c.CONSTF[:, 1:2], 1e-6), writes=[c.CONSTF[:, 1:2]])
    p.op("gpsimd", lambda e: e.memset(c.CONSTF[:, 2:3], -math.log(32.0)), writes=[c.CONSTF[:, 2:3]])
    p.op("gpsimd", lambda e: e.memset(c.CONSTF[:, 3:4], -0.5), writes=[c.CONSTF[:, 3:4]])
    p.op("gpsimd", lambda e: e.memset(c.ONES[:], 1.0), writes=[c.ONES[:]])
    p.op("gpsimd", lambda e: e.memset(c.ONESROW[:], 1.0), writes=[c.ONESROW[:]])
    p.op("gpsimd", lambda e: e.memset(c.ONESROWF[:], 1.0), writes=[c.ONESROWF[:]])
    p.op("gpsimd", lambda e: e.iota(c.PVI[:, 0:8], pattern=[[0, 8]], base=0,
         channel_multiplier=1), writes=[c.PVI[:, 0:8]])
    p.op("vector", lambda e: e.tensor_scalar(out=c.PVI[:, 8:16], in0=c.PVI[:, 0:8],
         scalar1=1, scalar2=None, op0=ALU.bitwise_and),
         reads=[c.PVI[:, 0:8]], writes=[c.PVI[:, 8:16]])
    p.op("vector", lambda e: e.tensor_copy(out=c.CONSTF[:, 4:12], in_=c.PVI[:, 8:16]),
         reads=[c.PVI[:, 8:16]], writes=[c.CONSTF[:, 4:12]])
    p.op("vector", lambda e: e.tensor_copy(out=c.CONSTF[:, 12:20], in_=c.PVI[:, 0:8]),
         reads=[c.PVI[:, 0:8]], writes=[c.CONSTF[:, 12:20]])
    p.op("gpsimd", lambda e: e.memset(c.SEL[:], 0.0), writes=[c.SEL[:]])
    for se in range(8):
        p.op("gpsimd", lambda e, sl=c.SEL[:, se * 128:(se + 1) * 128], b=-se: e.affine_select(
            out=sl, in_=sl, compare_op=ALU.not_equal, fill=1.0,
            base=b, pattern=[[0, 128]], channel_multiplier=1),
            reads=[c.SEL[:]], writes=[c.SEL[:]])
    for t_, nm in ((c.IDENT, "id16"), (c.IDENTF, "idf")):
        p.op("gpsimd", lambda e, t=t_: e.memset(t[:], 0.0), writes=[t_[:]])
        p.op("gpsimd", lambda e, t=t_: e.affine_select(
            out=t[:], in_=t[:], compare_op=ALU.not_equal, fill=1.0,
            base=0, pattern=[[-1, 128]], channel_multiplier=1),
            reads=[t_[:]], writes=[t_[:]])
    return c


def xt(c, i):
    return c.X[:, i * T:(i + 1) * T]


def pt(buf, i):
    return buf[:, i * T:(i + 1) * T]


def emit_ln_prefix(c, out_tiles, variant="plain", gamma=None, beta=None,
                   fold_half_y=False, rr_out=None, chunk_cb=None, tm_f32=False):
    p = c.p
    X = [xt(c, i) for i in range(NTILE)]
    onecol = c.ONES[:, 0:1]
    for tc in range(NTC):
        cs = slice(tc * TC, (tc + 1) * TC)
        for i in range(NTILE):
            src = X[i][:, cs]
            xbs = c.SC16[:, (i % 4) * TC:(i % 4 + 1) * TC]
            sqs = c.SC16b[:, (i % 4) * TC:(i % 4 + 1) * TC]
            p.op("vector", lambda e, a=xbs, b=src: e.tensor_copy(out=a, in_=b),
                 reads=[src], writes=[xbs])
            p.op("scalar", lambda e, a=sqs, b=src: e.activation(a, b, AF.Square),
                 reads=[src], writes=[sqs])
            p.op("tensor", lambda e, o=c.PS_A[0:1, :], l=onecol, r=xbs, s=(i == 0), st=(i == NTILE - 1):
                 e.matmul(o, l, r, start=s, stop=st),
                 reads=[onecol, xbs], writes=[c.PS_A[0:1, :]])
            p.op("tensor", lambda e, o=c.PS_B[0:1, :], l=onecol, r=sqs, s=(i == 0), st=(i == NTILE - 1):
                 e.matmul(o, l, r, start=s, stop=st),
                 reads=[onecol, sqs], writes=[c.PS_B[0:1, :]])
        mu, var, r1 = row(c, R_MU), row(c, R_VAR), row(c, R_R1)
        S, Bv = row(c, R_S), row(c, R_B)
        t0, t1 = row(c, R_T0), row(c, R_T1)
        p.op("vector", lambda e, m=mu: e.tensor_scalar(out=m, in0=c.PS_A[0:1, :],
             scalar1=1.0 / D, scalar2=None, op0=ALU.mult),
             reads=[c.PS_A[0:1, :]], writes=[mu])
        p.op("vector", lambda e, a=t0, m=mu: e.tensor_mul(out=a, in0=m, in1=m),
             reads=[mu], writes=[t0])
        p.op("vector", lambda e, v=var, a=t0: e.scalar_tensor_tensor(out=v,
             in0=c.PS_B[0:1, :], scalar=1.0 / D, in1=a, op0=ALU.mult, op1=ALU.subtract),
             reads=[c.PS_B[0:1, :], t0], writes=[var])
        p.op("scalar", lambda e, a=t1, v=var: e.activation(a, v, AF.Sqrt, bias=c.CONSTF[0:1, 0:1]),
             reads=[var, c.CONSTF[0:1, 0:1]], writes=[t1])
        p.op("vector", lambda e, r=r1, a=t1: e.reciprocal(out=r, in_=a),
             reads=[t1], writes=[r1])
        if variant == "plain":
            p.op("vector", lambda e, s=S, r=r1: e.tensor_copy(out=s, in_=r),
                 reads=[r1], writes=[S])
        elif variant == "l0":
            p.op("vector", lambda e, a=t0, v=var, r=r1: e.tensor_mul(out=a, in0=v, in1=r),
                 reads=[var, r1], writes=[t0])
            p.op("vector", lambda e, a=t0, r=r1: e.tensor_mul(out=a, in0=a, in1=r),
                 reads=[t0, r1], writes=[t0])
            p.op("scalar", lambda e, a=t1, b_=t0: e.activation(a, b_, AF.Sqrt, bias=c.CONSTF[0:1, 0:1]),
                 reads=[t0, c.CONSTF[0:1, 0:1]], writes=[t1])
            p.op("vector", lambda e, a=t1: e.reciprocal(out=a, in_=a),
                 reads=[t1], writes=[t1])
            p.op("vector", lambda e, s=S, r=r1, a=t1: e.tensor_mul(out=s, in0=r, in1=a),
                 reads=[r1, t1], writes=[S])
        if rr_out is not None:
            p.op("vector", lambda e, a=t0, v=var, r=r1: e.tensor_mul(out=a, in0=v, in1=r),
                 reads=[var, r1], writes=[t0])
            p.op("vector", lambda e, a=t0, r=r1: e.tensor_mul(out=a, in0=a, in1=r),
                 reads=[t0, r1], writes=[t0])
            p.op("scalar", lambda e, a=t1, b_=t0: e.activation(a, b_, AF.Sqrt, bias=c.CONSTF[0:1, 1:2]),
                 reads=[t0, c.CONSTF[0:1, 1:2]], writes=[t1])
            rr = rr_out(tc)
            p.op("vector", lambda e, a=rr, b_=t1: e.reciprocal(out=a, in_=b_),
                 reads=[t1], writes=[rr])
            p.op("vector", lambda e, a=rr: e.tensor_scalar(out=a, in0=a,
                 scalar1=1.0 / math.sqrt(D), scalar2=None, op0=ALU.mult),
                 reads=[rr], writes=[rr])
        p.op("vector", lambda e, b_=Bv, m=mu, s=S: e.scalar_tensor_tensor(out=b_,
             in0=m, scalar=-1.0, in1=s, op0=ALU.mult, op1=ALU.mult),
             reads=[mu, S], writes=[Bv])
        emit_bcast_chunk(c, S, c.SBC, c.PS_A)
        emit_bcast_chunk(c, Bv, c.BBC, c.PS_B)
        for i in range(NTILE):
            src = X[i][:, cs]
            if tm_f32:
                tmp = c.TMP32[:, 0:TC]
                p.op("vector", lambda e, a=tmp, x=src: e.tensor_mul(out=a, in0=x, in1=c.SBC[:, :]),
                     reads=[src, c.SBC[:, :]], writes=[tmp])
                p.op("gpsimd", lambda e, a=tmp: e.tensor_tensor(out=a, in0=a, in1=c.BBC[:, :], op=ALU.add),
                     reads=[tmp, c.BBC[:, :]], writes=[tmp])
                for j in range(4):
                    tt_ = tc * 4 + j
                    ps = (c.PS_A, c.PS_B)[j % 2]
                    p.op("tensor", lambda e, o=ps[:, 0:128], i_=tmp[:, j * 128:(j + 1) * 128]:
                         e.transpose(o, i_, c.IDENTF[:]),
                         reads=[tmp[:, j * 128:(j + 1) * 128], c.IDENTF[:]], writes=[ps[:, 0:128]])
                    dstj = out_tiles[tt_][:, i * 128:(i + 1) * 128]
                    p.op("vector", lambda e, a=dstj, b_=ps[:, 0:128]: e.tensor_copy(out=a, in_=b_),
                         reads=[ps[:, 0:128]], writes=[dstj])
                continue
            dst = out_tiles[i][:, cs]
            t16 = c.SC16[:, (i % 4) * TC:(i % 4 + 1) * TC]
            p.op("vector", lambda e, a=t16, x=src: e.tensor_mul(out=a, in0=x, in1=c.SBC[:, :]),
                 reads=[src, c.SBC[:, :]], writes=[t16])
            if gamma is None:
                p.op("gpsimd", lambda e, a=dst, x=t16: e.tensor_tensor(out=a, in0=x,
                     in1=c.BBC[:, :], op=ALU.add), reads=[t16, c.BBC[:, :]], writes=[dst])
            else:
                tf = c.SC16b[:, (i % 4) * TC:(i % 4 + 1) * TC]
                p.op("gpsimd", lambda e, a=tf, x=t16: e.tensor_tensor(out=a, in0=x,
                     in1=c.BBC[:, :], op=ALU.add), reads=[t16, c.BBC[:, :]], writes=[tf])
                g = gamma(i)
                bt = beta(i)
                p.op("vector", lambda e, a=dst, x=tf, gg=g, bb=bt:
                     e.tensor_scalar(out=a, in0=x, scalar1=gg, scalar2=bb,
                                     op0=ALU.mult, op1=ALU.add),
                     reads=[tf, g, bt], writes=[dst])
        if chunk_cb is not None:
            chunk_cb(tc)
        if fold_half_y:
            Sp, Bp = row(c, R_SP), row(c, R_BP)
            p.op("vector", lambda e, s=Sp, r=r1: e.tensor_scalar(out=s, in0=r,
                 scalar1=0.5, scalar2=1.0, op0=ALU.mult, op1=ALU.add),
                 reads=[r1], writes=[Sp])
            p.op("vector", lambda e, b_=Bp, m=mu, r=r1: e.scalar_tensor_tensor(out=b_,
                 in0=m, scalar=-0.5, in1=r, op0=ALU.mult, op1=ALU.mult),
                 reads=[mu, r1], writes=[Bp])
            emit_bcast_chunk(c, Sp, c.SBC, c.PS_A)
            emit_bcast_chunk(c, Bp, c.BBC, c.PS_B)
            for i in range(NTILE):
                xs = X[i][:, cs]
                p.op("vector", lambda e, x=xs: e.tensor_mul(out=x, in0=x, in1=c.SBC[:, :]),
                     reads=[xs, c.SBC[:, :]], writes=[xs])
                p.op("vector", lambda e, x=xs: e.tensor_add(out=x, in0=x, in1=c.BBC[:, :]),
                     reads=[xs, c.BBC[:, :]], writes=[xs])


def emit_bcast_chunk(c, src_row, dst, psum):
    p = c.p
    if callable(getattr(dst, "ap", None)):
        dst = dst.ap()
    p.op("tensor", lambda e, o=psum[:, :], r=src_row:
         e.matmul(o, c.ONESROWF[0:1, :], r, start=True, stop=True),
         reads=[c.ONESROWF[0:1, :], src_row], writes=[psum[:, :]])
    p.op("vector", lambda e, a=dst, b=psum[:, :]: e.tensor_copy(out=a, in_=b),
         reads=[psum[:, :]], writes=[dst])


def emit_l0_ffn(c, w, do_dequant):
    p = c.p
    X = [xt(c, i) for i in range(NTILE)]

    gam = c.PVEC[:, 0:8]
    bet = c.PVEC[:, 8:16]
    b1 = c.PVEC[:, 16:48]
    qb2 = c.PVEC[:, 48:56]
    p.dma("sync", gam, w["gamma"], stream="misc")
    p.dma("sync", bet, w["beta"], stream="misc")
    p.dma("sync", b1, w["b1"], stream="misc")
    p.dma("sync", c.PVEC[:, 56:64], w["b2"], stream="misc")
    p.op("vector", lambda e: e.tensor_scalar(out=qb2, in0=c.PVEC[:, 56:64],
         scalar1=0.25, scalar2=None, op0=ALU.mult),
         reads=[c.PVEC[:, 56:64]], writes=[qb2])

    emit_ln_prefix(c, [pt(c.A, i) for i in range(NTILE)], variant="l0",
                   gamma=lambda i: gam[:, i:i + 1], beta=lambda i: bet[:, i:i + 1],
                   fold_half_y=True)

    if do_dequant:
        _dequant_matrix(c, w["q1"], w["s1"], w["w1b"], K=1024, N=4096, tag="dq1")
        _dequant_matrix(c, w["q2"], w["s2"], w["w2b"], K=4096, N=1024, tag="dq2")

    for i in range(NTILE):
        p.op("vector", lambda e, x=X[i], bb=qb2[:, i:i + 1]:
             e.tensor_scalar(out=x, in0=x, scalar1=bb, scalar2=None, op0=ALU.add),
             reads=[X[i], qb2[:, i:i + 1]], writes=[X[i]])

    WS16 = c.WS[:].bitcast(BF16)
    for tci in range(NTC):
        ts_ = slice(tci * TC, (tci + 1) * TC)
        for m in range(32):
            buf = m % 2
            wslot = WS16[:, buf * 1024:(buf + 1) * 1024]
            p.dma("gpsimd", wslot.rearrange("p (g m) -> p g m", m=128),
                  w["w1b"].rearrange("(g p) n -> p g n", p=128)[:, :, m * 128:(m + 1) * 128],
                  stream=f"w1_{buf}")
            ps = (c.PS_A, c.PS_B)[m % 2]
            for g in range(NTILE):
                lhs = wslot[:, g * 128:(g + 1) * 128]
                rhs = pt(c.A, g)[:, ts_]
                p.op("tensor", lambda e, o=ps[:, :], l=lhs, r=rhs, s=(g == 0), st=(g == NTILE - 1):
                     e.matmul(o, l, r, start=s, stop=st),
                     reads=[lhs, rhs], writes=[ps[:, :]])
            ydst = c.B[:, m * TC:(m + 1) * TC]
            bia = b1[:, m:m + 1]
            p.op("scalar", lambda e, a=ydst, b_=ps[:, :], bb=bia:
                 e.activation(a, b_, AF.Silu, bias=bb),
                 reads=[ps[:, :], bia], writes=[ydst])
        for mo in range(NTILE):
            ps = (c.PS_A, c.PS_B)[mo % 2]
            for quar in range(4):
                buf = quar % 2
                wslot = WS16[:, 2048 + buf * 1024:2048 + (buf + 1) * 1024]
                p.dma("gpsimd", wslot.rearrange("p (g m) -> p g m", m=128),
                      w["w2b"][quar * 1024:(quar + 1) * 1024, :]
                      .rearrange("(g p) n -> p g n", p=128)[:, :, mo * 128:(mo + 1) * 128],
                      stream=f"w2_{buf}")
                for g in range(8):
                    gabs = quar * 8 + g
                    lhs = wslot[:, g * 128:(g + 1) * 128]
                    rhs = c.B[:, gabs * TC:(gabs + 1) * TC]
                    p.op("tensor", lambda e, o=ps[:, :], l=lhs, r=rhs, s=(gabs == 0), st=(gabs == 31):
                         e.matmul(o, l, r, start=s, stop=st),
                         reads=[lhs, rhs], writes=[ps[:, :]])
            xdst = X[mo][:, ts_]
            p.op("vector", lambda e, x=xdst, b_=ps[:, :]:
                 e.scalar_tensor_tensor(out=x, in0=b_, scalar=0.25, in1=x,
                                        op0=ALU.mult, op1=ALU.add),
                 reads=[ps[:, :], xdst], writes=[xdst])


def _dequant_matrix(c, q_ap, s_ap, out_ap, K, N, tag):
    p = c.p
    NG = K // 128
    NN = N // 512
    for g in range(NG):
        for j in range(NN):
            buf = (g * NN + j) % 2
            qs = c.WS[:, buf * 512:(buf + 1) * 512]
            p.dma("gpsimd", qs.bitcast(I32),
                  q_ap[g * 128:(g + 1) * 128, j * 512:(j + 1) * 512],
                  stream=f"{tag}_{buf}")
            srow = row(c, R_S if buf == 0 else R_B)
            p.dma("sync", srow, s_ap[g:g + 1, j * 512:(j + 1) * 512], stream=f"{tag}s")
            pss = (c.PS_A, c.PS_B)[buf]
            p.op("tensor", lambda e, o=pss[:, :], r=srow:
                 e.matmul(o, c.ONESROWF[0:1, :], r, start=True, stop=True),
                 reads=[c.ONESROWF[0:1, :], srow], writes=[pss[:, :]])
            tq = c.WS[:, 1024 + buf * 512:1024 + (buf + 1) * 512]
            p.op("scalar", lambda e, a=tq, b_=qs.bitcast(I32):
                 e.activation(a, b_, AF.Copy, bias=-128.0),
                 reads=[qs], writes=[tq])
            ob = c.SC16[:, (buf) * 1024:(buf) * 1024 + 512]
            p.op("vector", lambda e, a=ob, x=tq, s_=pss[:, :]:
                 e.tensor_tensor(out=a, in0=x, in1=s_, op=ALU.mult),
                 reads=[tq, pss[:, :]], writes=[ob])
            p.dma("scalar", out_ap[g * 128:(g + 1) * 128, j * 512:(j + 1) * 512], ob,
                  stream=f"{tag}o")


def emit_attn_core(c, NH, DH, causal, q_ap, k_ap, vT_ap, o_dst, score_hook=None):
    p = c.p
    HPG = 512 // DH
    NGRP = NH // HPG
    SUMS = c.PVEC[:, 64:80]
    RECS = c.PVEC[:, 80:96]
    NMAX = c.PVEC[:, 104:105]
    for qc in range(16):
        ext = (qc + 1) * 128 if causal else T
        ntk = ext // 128
        for grp in range(NGRP):
            for hh in range(HPG):
                h = grp * HPG + hh
                qh = q_ap(h, qc)
                nb = (ext + 511) // 512
                for b in range(nb):
                    w0 = b * 512
                    w1 = min(ext, w0 + 512)
                    rhs = k_ap(h, w0, w1)
                    o = c.PS_BIG[:, w0:w1]
                    p.op("tensor", lambda e, oo=o, l=qh, r=rhs:
                         e.matmul(oo, l, r, start=True, stop=True),
                         reads=[qh, rhs], writes=[o])
                if score_hook is not None:
                    score_hook(h, qc, ext)
                p.op("vector", lambda e, a=NMAX, b_=c.PS_BIG[:, 0:ext]:
                     e.tensor_reduce(out=a, in_=b_, axis=AX.X, op=ALU.max, negate=True),
                     reads=[c.PS_BIG[:, 0:ext]], writes=[NMAX])
                Pbuf = c.SC16
                if causal:
                    if qc > 0:
                        p.op("scalar", lambda e, a=Pbuf[:, 0:ext - 128], b_=c.PS_BIG[:, 0:ext - 128]:
                             e.activation(a, b_, AF.Exp, bias=NMAX),
                             reads=[c.PS_BIG[:, 0:ext - 128], NMAX], writes=[Pbuf[:, 0:ext - 128]])
                    dtile = Pbuf[:, ext - 128:ext]
                    p.op("scalar", lambda e, a=dtile, b_=c.PS_BIG[:, ext - 128:ext]:
                         e.activation(a, b_, AF.Exp, bias=NMAX),
                         reads=[c.PS_BIG[:, ext - 128:ext], NMAX], writes=[dtile])
                    p.op("gpsimd", lambda e, a=dtile: e.affine_select(
                         out=a, in_=a, compare_op=ALU.is_ge, fill=0.0,
                         base=0, pattern=[[-1, 128]], channel_multiplier=1),
                         reads=[dtile], writes=[dtile])
                else:
                    p.op("scalar", lambda e, a=Pbuf[:, 0:ext], b_=c.PS_BIG[:, 0:ext]:
                         e.activation(a, b_, AF.Exp, bias=NMAX),
                         reads=[c.PS_BIG[:, 0:ext], NMAX], writes=[Pbuf[:, 0:ext]])
                sacc = SUMS[:, h:h + 1]
                p.op("vector", lambda e, a=sacc, b_=Pbuf[:, 0:ext]:
                     e.tensor_reduce(out=a, in_=b_, axis=AX.X, op=ALU.add),
                     reads=[Pbuf[:, 0:ext]], writes=[sacc])
                for t in range(ntk):
                    ptile = Pbuf[:, t * 128:(t + 1) * 128]
                    pslot = c.PS_T[:, (t % 4) * 128:(t % 4) * 128 + 128]
                    p.op("tensor", lambda e, o=pslot, i_=ptile: e.transpose(o, i_, c.IDENT[:]),
                         reads=[ptile, c.IDENT[:]], writes=[pslot])
                    pts = c.SC16b[:, 128 + (t % 4) * 128:128 + (t % 4) * 128 + 128]
                    p.op("vector", lambda e, a=pts, b_=pslot: e.tensor_copy(out=a, in_=b_),
                         reads=[pslot], writes=[pts])
                    vts = vT_ap(h, t)
                    o = c.PS_O[:, hh * DH:(hh + 1) * DH]
                    p.op("tensor", lambda e, oo=o, l=pts, r=vts, s=(t == 0), st=(t == ntk - 1):
                         e.matmul(oo, l, r, start=s, stop=st),
                         reads=[pts, vts], writes=[o])
            p.op("vector", lambda e, a=RECS: e.reciprocal(out=a, in_=SUMS),
                 reads=[SUMS], writes=[RECS])
            o16 = c.SC16b[:, 640 + grp * 512:640 + (grp + 1) * 512]
            for hh in range(HPG):
                h = grp * HPG + hh
                p.op("vector", lambda e, a=o16[:, hh * DH:(hh + 1) * DH],
                     b_=c.PS_O[:, hh * DH:(hh + 1) * DH], r=RECS[:, h:h + 1]:
                     e.tensor_scalar(out=a, in0=b_, scalar1=r, scalar2=None, op0=ALU.mult),
                     reads=[c.PS_O[:, hh * DH:(hh + 1) * DH], RECS[:, h:h + 1]],
                     writes=[o16[:, hh * DH:(hh + 1) * DH]])
        for g in range(NTILE):
            srcb = c.SC16b[:, 640 + g * 128:640 + (g + 1) * 128]
            pslot = c.PS_T[:, (g % 8) * 128:(g % 8) * 128 + 128]
            p.op("tensor", lambda e, o=pslot, i_=srcb: e.transpose(o, i_, c.IDENT[:]),
                 reads=[srcb, c.IDENT[:]], writes=[pslot])
            dst = o_dst(g, qc)
            p.op("vector", lambda e, a=dst, b_=pslot: e.tensor_copy(out=a, in_=b_),
                 reads=[pslot], writes=[dst])


def emit_l0_attn(c, w):
    p = c.p
    X = [xt(c, i) for i in range(NTILE)]
    emit_ln_prefix(c, [pt(c.A, i) for i in range(NTILE)], variant="plain")
    WS16 = c.WS[:].bitcast(BF16)

    bv = c.PVEC[:, 0:8]
    bo = c.PVEC[:, 8:16]
    p.dma("sync", bv, w["wv_b"], stream="misc")
    p.dma("sync", bo, w["out_b"], stream="misc")

    p.dma("gpsimd", c.LAT[:, 0:1024], w["watt"], stream="watt")

    for m in range(NTILE):
        buf = m % 2
        wslot = WS16[:, buf * 1024:(buf + 1) * 1024]
        p.dma("gpsimd", wslot.rearrange("p (g m) -> p g m", m=128),
              w["wvT"].rearrange("(g p) n -> p g n", p=128)[:, :, m * 128:(m + 1) * 128],
              stream=f"wv{buf}")
        for tc in range(NTC):
            ps = (c.PS_A, c.PS_B)[tc % 2]
            for g in range(NTILE):
                lhs = wslot[:, g * 128:(g + 1) * 128]
                rhs = pt(c.A, g)[:, tc * TC:(tc + 1) * TC]
                p.op("tensor", lambda e, o=ps[:, :], l=lhs, r=rhs, s=(g == 0), st=(g == 7):
                     e.matmul(o, l, r, start=s, stop=st),
                     reads=[lhs, rhs], writes=[ps[:, :]])
            dst = c.SC16[:, tc * TC:(tc + 1) * TC]
            p.op("vector", lambda e, a=dst, b_=ps[:, :], bb=bv[:, m:m + 1]:
                 e.tensor_scalar(out=a, in0=b_, scalar1=bb, scalar2=None, op0=ALU.add),
                 reads=[ps[:, :], bv[:, m:m + 1]], writes=[dst])
        for h2 in range(2):
            h = 2 * m + h2
            for t in range(16):
                srct = c.SC16[64 * h2:64 * h2 + 64, t * 128:(t + 1) * 128]
                ident = c.IDENT[64 * h2:64 * h2 + 64, 64 * h2:64 * h2 + 64]
                dstp = c.PS_T[:, t * 64:(t + 1) * 64]
                p.op("tensor", lambda e, o=dstp, i_=srct, ii=ident: e.transpose(o, i_, ii),
                     reads=[srct, ident], writes=[dstp])
            dstv = c.B[:, h * 1024:(h + 1) * 1024]
            p.op("vector", lambda e, a=dstv, b_=c.PS_T[:, :]: e.tensor_copy(out=a, in_=b_),
                 reads=[c.PS_T[:, :]], writes=[dstv])

    for j in range(NTILE):
        wblk = c.LAT[:, j * 128:(j + 1) * 128]
        for tc in range(NTC):
            ps = (c.PS_A, c.PS_B)[tc % 2]
            rhs = pt(c.A, j)[:, tc * TC:(tc + 1) * TC]
            p.op("tensor", lambda e, o=ps[:, :], l=wblk, r=rhs:
                 e.matmul(o, l, r, start=True, stop=True),
                 reads=[wblk, rhs], writes=[ps[:, :]])
            dst = pt(c.C, j)[:, tc * TC:(tc + 1) * TC]
            p.op("scalar", lambda e, a=dst, b_=ps[:, :]: e.activation(a, b_, AF.Copy, scale=0.125),
                 reads=[ps[:, :]], writes=[dst])

    emit_attn_core(
        c, NH=16, DH=64, causal=False,
        q_ap=lambda h, qc: pt(c.C, h // 2)[64 * (h % 2):64 * (h % 2) + 64, qc * 128:(qc + 1) * 128],
        k_ap=lambda h, w0, w1: pt(c.A, h // 2)[64 * (h % 2):64 * (h % 2) + 64, w0:w1],
        vT_ap=lambda h, t: c.B[:, h * 1024 + t * 64:h * 1024 + t * 64 + 64],
        o_dst=lambda g, qc: pt(c.C, g)[:, qc * 128:(qc + 1) * 128])

    for m in range(NTILE):
        buf = m % 2
        wslot = WS16[:, buf * 1024:(buf + 1) * 1024]
        p.dma("gpsimd", wslot.rearrange("p (g m) -> p g m", m=128),
              w["outwT"].rearrange("(g p) n -> p g n", p=128)[:, :, m * 128:(m + 1) * 128],
              stream=f"wo{buf}")
        for tc in range(NTC):
            ps = (c.PS_A, c.PS_B)[tc % 2]
            for g in range(NTILE):
                lhs = wslot[:, g * 128:(g + 1) * 128]
                rhs = pt(c.C, g)[:, tc * TC:(tc + 1) * TC]
                p.op("tensor", lambda e, o=ps[:, :], l=lhs, r=rhs, s=(g == 0), st=(g == 7):
                     e.matmul(o, l, r, start=s, stop=st),
                     reads=[lhs, rhs], writes=[ps[:, :]])
            xdst = X[m][:, tc * TC:(tc + 1) * TC]
            p.op("vector", lambda e, x=xdst, b_=ps[:, :]:
                 e.scalar_tensor_tensor(out=x, in0=b_, scalar=1.0, in1=x,
                                        op0=ALU.mult, op1=ALU.add),
                 reads=[ps[:, :], xdst], writes=[xdst])
    for m in range(NTILE):
        p.op("vector", lambda e, x=X[m], bb=bo[:, m:m + 1]:
             e.tensor_scalar(out=x, in0=x, scalar1=bb, scalar2=None, op0=ALU.add),
             reads=[X[m], bo[:, m:m + 1]], writes=[X[m]])


def emit_l1_attn(c, w):
    p = c.p
    X = [xt(c, i) for i in range(NTILE)]
    emit_ln_prefix(c, [pt(c.A, i) for i in range(NTILE)], variant="plain")
    WS16 = c.WS[:].bitcast(BF16)

    def proj_stream(wap, dst_fn, tag, nk=NTILE, evac=None):
        NM = NTILE
        for m in range(NM):
            buf = m % 2
            wslot = WS16[:, buf * 1024:(buf + 1) * 1024]
            p.dma("gpsimd", wslot.rearrange("p (g m) -> p g m", m=128),
                  wap.rearrange("(g p) n -> p g n", p=128)[:, :, m * 128:(m + 1) * 128],
                  stream=f"{tag}{buf}")
            for tc in range(NTC):
                ps = (c.PS_A, c.PS_B)[tc % 2]
                for g in range(nk):
                    lhs = wslot[:, g * 128:(g + 1) * 128]
                    rhs = pt(c.A, g)[:, tc * TC:(tc + 1) * TC]
                    p.op("tensor", lambda e, o=ps[:, :], l=lhs, r=rhs, s=(g == 0), st=(g == nk - 1):
                         e.matmul(o, l, r, start=s, stop=st),
                         reads=[lhs, rhs], writes=[ps[:, :]])
                if evac is None:
                    dst = dst_fn(m, tc)
                    p.op("scalar", lambda e, a=dst, b_=ps[:, :]: e.activation(a, b_, AF.Copy),
                         reads=[ps[:, :]], writes=[dst])
                else:
                    evac(m, tc, ps)

    proj_stream(w["wk"], lambda m, tc: pt(c.B, m)[:, tc * TC:(tc + 1) * TC], "k")
    proj_stream(w["wv"], lambda m, tc: pt(c.C, m)[:, tc * TC:(tc + 1) * TC], "v")
    for m in range(NTILE):
        bm, cm = pt(c.B, m), pt(c.C, m)
        p.op("vector", lambda e, a=bm, b_=cm: e.tensor_add(out=a, in0=a, in1=b_),
             reads=[bm, cm], writes=[bm])
    p.dma("gpsimd", WS16[:, 0:2048].rearrange("p (g m) -> p g m", m=256),
          w["wdkv"].rearrange("(g p) n -> p g n", p=128), stream="wdkv")
    lat_t = [c.LAT[:, 0:2048], c.SC16[:, 0:2048]]
    for m in range(2):
        for tc in range(NTC):
            ps = (c.PS_A, c.PS_B)[tc % 2]
            for g in range(NTILE):
                lhs = WS16[:, g * 256 + m * 128:g * 256 + (m + 1) * 128]
                rhs = pt(c.B, g)[:, tc * TC:(tc + 1) * TC]
                p.op("tensor", lambda e, o=ps[:, :], l=lhs, r=rhs, s=(g == 0), st=(g == 7):
                     e.matmul(o, l, r, start=s, stop=st),
                     reads=[lhs, rhs], writes=[ps[:, :]])
            dst = lat_t[m][:, tc * TC:(tc + 1) * TC]
            p.op("scalar", lambda e, a=dst, b_=ps[:, :]: e.activation(a, b_, AF.Copy),
                 reads=[ps[:, :]], writes=[dst])
    p.dma("gpsimd", WS16[:, 2048:4096].rearrange("p (g m) -> p g m", m=1024),
          w["wupk"].rearrange("(g p) n -> p g n", p=128), stream="wupk")
    for m in range(NTILE):
        for tc in range(NTC):
            ps = (c.PS_A, c.PS_B)[tc % 2]
            for g in range(2):
                lhs = WS16[:, 2048 + g * 1024 + m * 128:2048 + g * 1024 + (m + 1) * 128]
                rhs = lat_t[g][:, tc * TC:(tc + 1) * TC]
                p.op("tensor", lambda e, o=ps[:, :], l=lhs, r=rhs, s=(g == 0), st=(g == 1):
                     e.matmul(o, l, r, start=s, stop=st),
                     reads=[lhs, rhs], writes=[ps[:, :]])
            dst = pt(c.C, m)[:, tc * TC:(tc + 1) * TC]
            p.op("scalar", lambda e, a=dst, b_=ps[:, :]: e.activation(a, b_, AF.Copy),
                 reads=[ps[:, :]], writes=[dst])
    p.dma("gpsimd", WS16[:, 2048:4096].rearrange("p (g m) -> p g m", m=1024),
          w["wupv"].rearrange("(g p) n -> p g n", p=128), stream="wupv")
    for m in range(NTILE):
        for tc in range(NTC):
            ps = (c.PS_A, c.PS_B)[tc % 2]
            for g in range(2):
                lhs = WS16[:, 2048 + g * 1024 + m * 128:2048 + g * 1024 + (m + 1) * 128]
                rhs = lat_t[g][:, tc * TC:(tc + 1) * TC]
                p.op("tensor", lambda e, o=ps[:, :], l=lhs, r=rhs, s=(g == 0), st=(g == 1):
                     e.matmul(o, l, r, start=s, stop=st),
                     reads=[lhs, rhs], writes=[ps[:, :]])
            dst = c.SC16b[:, tc * TC:(tc + 1) * TC]
            p.op("scalar", lambda e, a=dst, b_=ps[:, :]: e.activation(a, b_, AF.Copy),
                 reads=[ps[:, :]], writes=[dst])
        for h2 in range(2):
            h = 2 * m + h2
            for t in range(16):
                srct = c.SC16b[64 * h2:64 * h2 + 64, t * 128:(t + 1) * 128]
                ident = c.IDENT[64 * h2:64 * h2 + 64, 64 * h2:64 * h2 + 64]
                dstp = c.PS_T[:, t * 64:(t + 1) * 64]
                p.op("tensor", lambda e, o=dstp, i_=srct, ii=ident: e.transpose(o, i_, ii),
                     reads=[srct, ident], writes=[dstp])
            dstv = pt(c.B, h // 2)[:, (h % 2) * 1024:(h % 2) * 1024 + 1024] if False else \
                c.B[:, h * 1024:(h + 1) * 1024]
            p.op("vector", lambda e, a=dstv, b_=c.PS_T[:, :]: e.tensor_copy(out=a, in_=b_),
                 reads=[c.PS_T[:, :]], writes=[dstv])

    for tc in range(NTC):
        for m in range(NTILE):
            buf = m % 2
            wslot = WS16[:, buf * 1024:(buf + 1) * 1024]
            p.dma("gpsimd", wslot.rearrange("p (g m) -> p g m", m=128),
                  w["wq"].rearrange("(g p) n -> p g n", p=128)[:, :, m * 128:(m + 1) * 128],
                  stream=f"wq{buf}")
            ps = (c.PS_A, c.PS_B)[m % 2]
            for g in range(NTILE):
                lhs = wslot[:, g * 128:(g + 1) * 128]
                rhs = pt(c.A, g)[:, tc * TC:(tc + 1) * TC]
                p.op("tensor", lambda e, o=ps[:, :], l=lhs, r=rhs, s=(g == 0), st=(g == 7):
                     e.matmul(o, l, r, start=s, stop=st),
                     reads=[lhs, rhs], writes=[ps[:, :]])
            qtmp = (c.SC16 if m < 4 else c.SC16b)[:, (m % 4) * TC:(m % 4 + 1) * TC]
            p.op("scalar", lambda e, a=qtmp, b_=ps[:, :]: e.activation(a, b_, AF.Copy),
                 reads=[ps[:, :]], writes=[qtmp])
        for m in range(NTILE):
            qtmp = (c.SC16 if m < 4 else c.SC16b)[:, (m % 4) * TC:(m % 4 + 1) * TC]
            dst = pt(c.A, m)[:, tc * TC:(tc + 1) * TC]
            p.op("vector", lambda e, a=dst, b_=qtmp: e.tensor_copy(out=a, in_=b_),
                 reads=[qtmp], writes=[dst])

    LAM = 0.01
    last_qc = [-1]

    def hook(h, qc, ext):
        if last_qc[0] != qc:
            last_qc[0] = qc
            p.op("gpsimd", lambda e: e.iota(c.LAT[:, 0:T], pattern=[[1, T]],
                 base=-qc * 128, channel_multiplier=-1,
                 allow_small_or_imprecise_dtypes=True),
                 writes=[c.LAT[:, 0:T]])
            p.op("scalar", lambda e: e.activation(c.LAT[:, 0:T], c.LAT[:, 0:T], AF.Abs),
                 reads=[c.LAT[:, 0:T]], writes=[c.LAT[:, 0:T]])
            p.op("scalar", lambda e: e.activation(c.LAT[:, 0:T], c.LAT[:, 0:T], AF.Exp,
                 scale=-LAM, bias=c.CONSTF[:, 2:3]),
                 reads=[c.LAT[:, 0:T], c.CONSTF[:, 2:3]], writes=[c.LAT[:, 0:T]])
        p.op("vector", lambda e: e.tensor_tensor(out=c.PS_BIG[:, 0:T], in0=c.PS_BIG[:, 0:T],
             in1=c.LAT[:, 0:T], op=ALU.mult),
             reads=[c.PS_BIG[:, 0:T], c.LAT[:, 0:T]], writes=[c.PS_BIG[:, 0:T]])

    emit_attn_core(
        c, NH=16, DH=64, causal=False,
        q_ap=lambda h, qc: pt(c.A, h // 2)[64 * (h % 2):64 * (h % 2) + 64, qc * 128:(qc + 1) * 128],
        k_ap=lambda h, w0, w1: pt(c.C, h // 2)[64 * (h % 2):64 * (h % 2) + 64, w0:w1],
        vT_ap=lambda h, t: c.B[:, h * 1024 + t * 64:h * 1024 + t * 64 + 64],
        o_dst=lambda g, qc: pt(c.A, g)[:, qc * 128:(qc + 1) * 128],
        score_hook=hook)

    for m in range(NTILE):
        buf = m % 2
        wslot = WS16[:, buf * 1024:(buf + 1) * 1024]
        p.dma("gpsimd", wslot.rearrange("p (g m) -> p g m", m=128),
              w["wo"].rearrange("(g p) n -> p g n", p=128)[:, :, m * 128:(m + 1) * 128],
              stream=f"wo{buf}")
        for tc in range(NTC):
            ps = (c.PS_A, c.PS_B)[tc % 2]
            for g in range(NTILE):
                lhs = wslot[:, g * 128:(g + 1) * 128]
                rhs = pt(c.A, g)[:, tc * TC:(tc + 1) * TC]
                p.op("tensor", lambda e, o=ps[:, :], l=lhs, r=rhs, s=(g == 0), st=(g == 7):
                     e.matmul(o, l, r, start=s, stop=st),
                     reads=[lhs, rhs], writes=[ps[:, :]])
            xdst = X[m][:, tc * TC:(tc + 1) * TC]
            p.op("vector", lambda e, x=xdst, b_=ps[:, :]:
                 e.scalar_tensor_tensor(out=x, in0=b_, scalar=1.0, in1=x,
                                        op0=ALU.mult, op1=ALU.add),
                 reads=[ps[:, :], xdst], writes=[xdst])


def emit_l2_attn(c, w):
    p = c.p
    X = [xt(c, i) for i in range(NTILE)]
    emit_ln_prefix(c, [pt(c.A, i) for i in range(NTILE)], variant="plain")
    WS16 = c.WS[:].bitcast(BF16)
    SCALE = 1.0 / math.sqrt(128.0)

    iop = c.CONSTF[:, 12:13]
    par = c.CONSTF[:, 4:5]
    fv = c.PVEC[:, 2:3]
    qv = c.PVEC[:, 3:4]
    t0 = c.PVEC[:, 6:7]
    p.op("vector", lambda e: e.tensor_scalar(out=t0, in0=par, scalar1=-0.5, scalar2=None,
         op0=ALU.mult), reads=[par], writes=[t0])
    p.op("vector", lambda e: e.scalar_tensor_tensor(out=t0, in0=iop, scalar=0.5, in1=t0,
         op0=ALU.mult, op1=ALU.add), reads=[iop, t0], writes=[t0])
    p.op("scalar", lambda e: e.activation(fv, t0, AF.Exp, scale=-math.log(1e4) / 64.0),
         reads=[t0], writes=[fv])
    p.op("vector", lambda e: e.tensor_scalar(out=fv, in0=fv, scalar1=1.0 / (2 * math.pi),
         scalar2=None, op0=ALU.mult), reads=[fv], writes=[fv])
    p.op("vector", lambda e: e.tensor_scalar(out=qv, in0=par, scalar1=0.25, scalar2=None,
         op0=ALU.mult), reads=[par], writes=[qv])

    for tc in range(NTC):
        cs = slice(tc * TC, (tc + 1) * TC)
        iob = c.SBC[:, :]
        p.op("gpsimd", lambda e, a=iob, b0=tc * TC: e.iota(a, pattern=[[1, TC]], base=b0,
             channel_multiplier=0, allow_small_or_imprecise_dtypes=True), writes=[iob])
        p.op("vector", lambda e, a=c.BBC[:, :], b_=iob: e.tensor_scalar(out=a, in0=b_,
             scalar1=fv, scalar2=qv, op0=ALU.mult, op1=ALU.add),
             reads=[iob, fv, qv], writes=[c.BBC[:, :]])
        ri = c.RNDI[:, :]
        p.op("vector", lambda e, a=ri, b_=c.BBC[:, :]: e.tensor_copy(out=a, in_=b_),
             reads=[c.BBC[:, :]], writes=[ri])
        p.op("vector", lambda e, a=c.SBC[:, :], b_=ri: e.tensor_copy(out=a, in_=b_),
             reads=[ri], writes=[c.SBC[:, :]])
        p.op("vector", lambda e: e.tensor_sub(out=c.BBC[:, :], in0=c.BBC[:, :], in1=c.SBC[:, :]),
             reads=[c.BBC[:, :], c.SBC[:, :]], writes=[c.BBC[:, :]])
        dst = c.LAT[:, cs]
        p.op("scalar", lambda e, a=dst: e.activation(a, c.BBC[:, :], AF.Sin,
             scale=2 * math.pi), reads=[c.BBC[:, :]], writes=[dst])
        p.op("vector", lambda e, a=dst: e.tensor_scalar(out=a, in0=a, scalar1=SCALE,
             scalar2=None, op0=ALU.mult), reads=[dst], writes=[dst])

    p.dma("gpsimd", WS16[:, 0:1024].rearrange("p (g m) -> p g m", m=128),
          w["wkT"].rearrange("(g p) n -> p g n", p=128), stream="wk0")
    for tc in range(NTC):
        ps = (c.PS_A, c.PS_B)[tc % 2]
        for g in range(NTILE):
            lhs = WS16[:, g * 128:(g + 1) * 128]
            rhs = pt(c.A, g)[:, tc * TC:(tc + 1) * TC]
            p.op("tensor", lambda e, o=ps[:, :], l=lhs, r=rhs, s=(g == 0), st=(g == 7):
                 e.matmul(o, l, r, start=s, stop=st),
                 reads=[lhs, rhs], writes=[ps[:, :]])
        dst = c.C[:, tc * TC:(tc + 1) * TC]
        p.op("vector", lambda e, a=dst, b_=ps[:, :]:
             e.tensor_scalar(out=a, in0=b_, scalar1=par, scalar2=None, op0=ALU.add),
             reads=[ps[:, :], par], writes=[dst])
    p.dma("gpsimd", WS16[:, 1024:2048].rearrange("p (g m) -> p g m", m=128),
          w["wvT"].rearrange("(g p) n -> p g n", p=128), stream="wv0")
    for tc in range(NTC):
        ps = (c.PS_A, c.PS_B)[tc % 2]
        for g in range(NTILE):
            lhs = WS16[:, 1024 + g * 128:1024 + (g + 1) * 128]
            rhs = pt(c.A, g)[:, tc * TC:(tc + 1) * TC]
            p.op("tensor", lambda e, o=ps[:, :], l=lhs, r=rhs, s=(g == 0), st=(g == 7):
                 e.matmul(o, l, r, start=s, stop=st),
                 reads=[lhs, rhs], writes=[ps[:, :]])
        dst = c.SC16[:, tc * TC:(tc + 1) * TC]
        p.op("scalar", lambda e, a=dst, b_=ps[:, :]: e.activation(a, b_, AF.Copy),
             reads=[ps[:, :]], writes=[dst])
    for t in range(16):
        srct = c.SC16[:, t * 128:(t + 1) * 128]
        dstp = c.PS_T[:, (t % 8) * 128:(t % 8) * 128 + 128]
        p.op("tensor", lambda e, o=dstp, i_=srct: e.transpose(o, i_, c.IDENT[:]),
             reads=[srct, c.IDENT[:]], writes=[dstp])
        dstv = c.B[:, t * 128:(t + 1) * 128]
        p.op("vector", lambda e, a=dstv, b_=dstp: e.tensor_copy(out=a, in_=b_),
             reads=[dstp], writes=[dstv])

    for tc in range(NTC):
        for m in range(NTILE):
            buf = m % 2
            wslot = WS16[:, buf * 1024:(buf + 1) * 1024]
            p.dma("gpsimd", wslot.rearrange("p (g m) -> p g m", m=128),
                  w["wqT"].rearrange("(g p) n -> p g n", p=128)[:, :, m * 128:(m + 1) * 128],
                  stream=f"wq{buf}")
            ps = (c.PS_A, c.PS_B)[m % 2]
            for g in range(NTILE):
                lhs = wslot[:, g * 128:(g + 1) * 128]
                rhs = pt(c.A, g)[:, tc * TC:(tc + 1) * TC]
                p.op("tensor", lambda e, o=ps[:, :], l=lhs, r=rhs, s=(g == 0), st=(g == 7):
                     e.matmul(o, l, r, start=s, stop=st),
                     reads=[lhs, rhs], writes=[ps[:, :]])
            qtmp = (c.SC16 if m < 4 else c.SC16b)[:, (m % 4) * TC:(m % 4 + 1) * TC]
            pel = c.LAT[:, tc * TC:(tc + 1) * TC]
            p.op("vector", lambda e, a=qtmp, b_=ps[:, :], pe=pel:
                 e.scalar_tensor_tensor(out=a, in0=b_, scalar=SCALE, in1=pe,
                                        op0=ALU.mult, op1=ALU.add),
                 reads=[ps[:, :], pel], writes=[qtmp])
        for m in range(NTILE):
            qtmp = (c.SC16 if m < 4 else c.SC16b)[:, (m % 4) * TC:(m % 4 + 1) * TC]
            dst = pt(c.A, m)[:, tc * TC:(tc + 1) * TC]
            p.op("vector", lambda e, a=dst, b_=qtmp: e.tensor_copy(out=a, in_=b_),
                 reads=[qtmp], writes=[dst])

    import os as _os
    if int(_os.environ.get("L2_SKIP_CORE", "0")):
        return
    emit_attn_core(
        c, NH=8, DH=128, causal=True,
        q_ap=lambda h, qc: pt(c.A, h)[:, qc * 128:(qc + 1) * 128],
        k_ap=lambda h, w0, w1: c.C[:, w0:w1],
        vT_ap=lambda h, t: c.B[:, t * 128:(t + 1) * 128],
        o_dst=lambda g, qc: pt(c.A, g)[:, qc * 128:(qc + 1) * 128])

    for m in range(NTILE):
        buf = m % 2
        wslot = WS16[:, buf * 1024:(buf + 1) * 1024]
        p.dma("gpsimd", wslot.rearrange("p (g m) -> p g m", m=128),
              w["woT"].rearrange("(g p) n -> p g n", p=128)[:, :, m * 128:(m + 1) * 128],
              stream=f"wo{buf}")
        for tc in range(NTC):
            ps = (c.PS_A, c.PS_B)[tc % 2]
            for g in range(NTILE):
                lhs = wslot[:, g * 128:(g + 1) * 128]
                rhs = pt(c.A, g)[:, tc * TC:(tc + 1) * TC]
                p.op("tensor", lambda e, o=ps[:, :], l=lhs, r=rhs, s=(g == 0), st=(g == 7):
                     e.matmul(o, l, r, start=s, stop=st),
                     reads=[lhs, rhs], writes=[ps[:, :]])
            xdst = X[m][:, tc * TC:(tc + 1) * TC]
            p.op("vector", lambda e, x=xdst, b_=ps[:, :]:
                 e.scalar_tensor_tensor(out=x, in0=b_, scalar=1.0, in1=x,
                                        op0=ALU.mult, op1=ALU.add),
                 reads=[ps[:, :], xdst], writes=[xdst])


def emit_l2_ffn(c, w):
    p = c.p
    X = [xt(c, i) for i in range(NTILE)]
    emit_ln_prefix(c, [pt(c.A, i) for i in range(NTILE)], variant="plain")
    WS16 = c.WS[:].bitcast(BF16)

    bbase = c.PVEC[:, 0:8]
    benc = c.PVEC[:, 8:24]
    bdec = c.PVEC[:, 24:32]
    p.dma("sync", bbase, w["b_base"], stream="misc")
    p.dma("sync", benc, w["b_enc"], stream="misc2")
    bdn = c.PVEC[:, 32:40]
    bup = c.PVEC[:, 40:104]
    p.dma("sync", bdn, w["b_down"], stream="misc3")
    p.dma("sync", bup, w["b_up"], stream="misc4")
    bdecv = c.PVEC[:, 104:168] if False else None
    bdec64 = c.TMP32[:, 128:192]
    p.dma("sync", bdec64, w["b_dec"], stream="misc5")

    LOSSV = c.PVEC[:, 64:72]
    TACC = c.PVEC[:, 72:73]

    for e in range(8):
        p.dma("gpsimd", WS16[:, 0:2048].rearrange("p (g m) -> p g m", m=256),
              w["W_enc"][e].rearrange("(g p) n -> p g n", p=128), stream="enc")
        for m2 in range(2):
            for tc in range(NTC):
                ps = (c.PS_A, c.PS_B)[tc % 2]
                for g in range(NTILE):
                    lhs = WS16[:, g * 256 + m2 * 128:g * 256 + (m2 + 1) * 128]
                    rhs = pt(c.A, g)[:, tc * TC:(tc + 1) * TC]
                    p.op("tensor", lambda e_, o=ps[:, :], l=lhs, r=rhs, s=(g == 0), st=(g == 7):
                         e_.matmul(o, l, r, start=s, stop=st),
                         reads=[lhs, rhs], writes=[ps[:, :]])
                dst = pt(c.B, m2)[:, tc * TC:(tc + 1) * TC]
                bia = benc[:, e * 2 + m2:e * 2 + m2 + 1]
                p.op("scalar", lambda e_, a=dst, b_=ps[:, :], bb=bia:
                     e_.activation(a, b_, AF.Relu, bias=bb),
                     reads=[ps[:, :], bia], writes=[dst])
        p.dma("gpsimd", WS16[:, 2048:4096].rearrange("p (g m) -> p g m", m=1024),
              w["W_dec"][e].rearrange("(g p) n -> p g n", p=128), stream="dec")
        first = [True]
        for m in range(NTILE):
            for tc in range(NTC):
                ps = (c.PS_A, c.PS_B)[tc % 2]
                for g in range(2):
                    lhs = WS16[:, 2048 + g * 1024 + m * 128:2048 + g * 1024 + (m + 1) * 128]
                    rhs = pt(c.B, g)[:, tc * TC:(tc + 1) * TC]
                    p.op("tensor", lambda e_, o=ps[:, :], l=lhs, r=rhs, s=(g == 0), st=(g == 1):
                         e_.matmul(o, l, r, start=s, stop=st),
                         reads=[lhs, rhs], writes=[ps[:, :]])
                d = c.SBC[:, :]
                nref = pt(c.A, m)[:, tc * TC:(tc + 1) * TC]
                bd = bdec64[:, e * 8 + m:e * 8 + m + 1]
                p.op("vector", lambda e_, a=d, b_=ps[:, :], bb=bd:
                     e_.tensor_scalar(out=a, in0=b_, scalar1=bb, scalar2=None, op0=ALU.add),
                     reads=[ps[:, :], bd], writes=[d])
                p.op("vector", lambda e_, a=d, n_=nref: e_.tensor_sub(out=a, in0=a, in1=n_),
                     reads=[d, nref], writes=[d])
                sq = c.BBC[:, :]
                p.op("scalar", lambda e_, a=sq, b_=d: e_.activation(a, b_, AF.Square),
                     reads=[d], writes=[sq])
                tgt = LOSSV[:, e:e + 1]
                if first[0]:
                    p.op("vector", lambda e_, a=tgt, b_=sq: e_.tensor_reduce(
                         out=a, in_=b_, axis=AX.X, op=ALU.add),
                         reads=[sq], writes=[tgt])
                    first[0] = False
                else:
                    p.op("vector", lambda e_, a=TACC, b_=sq: e_.tensor_reduce(
                         out=a, in_=b_, axis=AX.X, op=ALU.add),
                         reads=[sq], writes=[TACC])
                    p.op("vector", lambda e_, a=tgt: e_.tensor_add(out=a, in0=a, in1=TACC),
                         reads=[tgt, TACC], writes=[tgt])

    onecol = c.ONES[:, 0:1]
    lv16 = c.SC16[:, 0:8]
    p.op("vector", lambda e_: e_.tensor_copy(out=lv16, in_=LOSSV), reads=[LOSSV], writes=[lv16])
    p.op("tensor", lambda e_: e_.matmul(c.PS_A[0:1, 0:8], onecol, lv16, start=True, stop=True),
         reads=[onecol, lv16], writes=[c.PS_A[0:1, 0:8]])
    LOSS = row(c, R_T0)[:, 0:8]
    p.op("vector", lambda e_: e_.tensor_copy(out=LOSS, in_=c.PS_A[0:1, 0:8]),
         reads=[c.PS_A[0:1, 0:8]], writes=[LOSS])
    MN = row(c, R_T0)[:, 8:9]
    p.op("vector", lambda e_: e_.tensor_reduce(out=MN, in_=LOSS, axis=AX.X, op=ALU.min),
         reads=[LOSS], writes=[MN])
    ISM = row(c, R_T0)[:, 16:24]
    p.op("vector", lambda e_: e_.tensor_scalar(out=ISM, in0=LOSS, scalar1=MN, scalar2=None,
         op0=ALU.subtract), reads=[LOSS, MN], writes=[ISM])
    p.op("scalar", lambda e_: e_.activation(ISM, ISM, AF.Sign), reads=[ISM], writes=[ISM])
    p.op("vector", lambda e_: e_.tensor_scalar(out=ISM, in0=ISM, scalar1=-1.0, scalar2=1.0,
         op0=ALU.mult, op1=ALU.add), reads=[ISM], writes=[ISM])
    ci = c.PVI[0:1, 0:8]
    p.dma("sync", ci, w["cidx"], stream="misc6")
    cif = row(c, R_T0)[:, 24:32]
    p.op("vector", lambda e_: e_.tensor_copy(out=cif, in_=ci), reads=[c.PVI[0:1, 0:8]], writes=[cif])
    OH = row(c, R_T1)[:, 0:8]
    tn = row(c, R_T1)[:, 8:16]
    tu = row(c, R_T1)[:, 16:24]
    for n in range(8):
        p.op("vector", lambda e_, nn=float(n): e_.tensor_scalar(out=tn, in0=cif,
             scalar1=-nn, scalar2=None, op0=ALU.add), reads=[cif], writes=[tn])
        p.op("scalar", lambda e_: e_.activation(tn, tn, AF.Abs), reads=[tn], writes=[tn])
        p.op("scalar", lambda e_: e_.activation(tn, tn, AF.Sign, bias=c.CONSTF[0:1, 3:4]),
             reads=[tn, c.CONSTF[0:1, 3:4]], writes=[tn])
        p.op("vector", lambda e_: e_.tensor_scalar(out=tn, in0=tn, scalar1=-0.5, scalar2=0.5,
             op0=ALU.mult, op1=ALU.add), reads=[tn], writes=[tn])
        p.op("vector", lambda e_: e_.tensor_mul(out=tu, in0=tn, in1=ISM),
             reads=[tn, ISM], writes=[tu])
        p.op("vector", lambda e_, oo=OH[:, n:n + 1]: e_.tensor_reduce(out=oo, in_=tu,
             axis=AX.X, op=ALU.add), reads=[tu], writes=[OH[:, n:n + 1]])
    OHB = c.PVEC[:, 72:80]
    p.op("tensor", lambda e_: e_.matmul(c.PS_B[:, 0:8], c.ONESROWF[0:1, :], OH, start=True, stop=True),
         reads=[c.ONESROWF[0:1, :], OH], writes=[c.PS_B[:, 0:8]])
    p.op("vector", lambda e_: e_.tensor_copy(out=OHB, in_=c.PS_B[:, 0:8]),
         reads=[c.PS_B[:, 0:8]], writes=[OHB])

    ohrep = c.SBC[:, :]
    p.op("vector", lambda e_: e_.tensor_copy(
         out=ohrep.rearrange("p (n w) -> p n w", w=64),
         in_=OHB.unsqueeze(2).broadcast_to([128, 8, 64])),
         reads=[OHB], writes=[ohrep])
    for g in range(NTILE):
        slot = WS16[:, 0:512]
        p.dma("gpsimd", slot.rearrange("p (n r) -> p n r", r=64),
              w["W_down"].rearrange("e (g q) r -> q e r", q=128)[:, :, :] if False else
              w["W_down"][:, g * 128:(g + 1) * 128, :].rearrange("e q r -> q e r"),
              stream="wdn")
        msk = c.BBC[:, :]
        p.op("vector", lambda e_, s_=slot, mm=msk, oo=ohrep: e_.tensor_mul(out=mm, in0=s_, in1=oo),
             reads=[slot, ohrep], writes=[msk])
        p.op("vector", lambda e_, mm=msk: e_.tensor_add(out=mm[:, 0:256], in0=mm[:, 0:256], in1=mm[:, 256:512]),
             reads=[msk], writes=[msk[:, 0:256]])
        p.op("vector", lambda e_, mm=msk: e_.tensor_add(out=mm[:, 0:128], in0=mm[:, 0:128], in1=mm[:, 128:256]),
             reads=[msk[:, 0:256]], writes=[msk[:, 0:128]])
        p.op("vector", lambda e_, mm=msk, dst=c.LAT[:, g * 64:(g + 1) * 64]:
             e_.tensor_add(out=dst, in0=mm[:, 0:64], in1=mm[:, 64:128]),
             reads=[msk[:, 0:128]], writes=[c.LAT[:, g * 64:(g + 1) * 64]])
    ohrep2 = c.SBC[0:64, :].rearrange("p (n w) -> p n w", w=64)
    for m in range(NTILE):
        slot = WS16[0:64, 512:1536]
        p.dma("gpsimd", slot.rearrange("p (n r) -> p n r", r=128),
              w["W_up"][:, :, m * 128:(m + 1) * 128].rearrange("e q r -> q e r"),
              stream="wup")
        msk = c.BBC[0:64, :].rearrange("p (n w) -> p n w", w=128) if False else c.BBC[0:64, :]
        for half in range(2):
            mh = c.BBC[0:64, :]
            src = slot[:, half * 512:(half + 1) * 512]
            ohr = OHB[0:64, half * 4:(half + 1) * 4].unsqueeze(2).broadcast_to([64, 4, 128])
            p.op("vector", lambda e_, a=mh, s_=src, o_=ohr: e_.tensor_tensor(
                 out=a.rearrange("p (n w) -> p n w", w=128), in0=s_.rearrange("p (n w) -> p n w", w=128),
                 in1=o_, op=ALU.mult),
                 reads=[src, OHB], writes=[mh])
            p.op("vector", lambda e_, a=mh: e_.tensor_add(out=a[:, 0:256], in0=a[:, 0:256], in1=a[:, 256:512]),
                 reads=[mh], writes=[mh[:, 0:256]])
            p.op("vector", lambda e_, a=mh: e_.tensor_add(out=a[:, 0:128], in0=a[:, 0:128], in1=a[:, 128:256]),
                 reads=[mh[:, 0:256]], writes=[mh[:, 0:128]])
            if half == 0:
                p.op("vector", lambda e_, a=c.TMP32[0:64, 192:320]: e_.tensor_copy(out=a, in_=mh[:, 0:128]),
                     reads=[mh[:, 0:128]], writes=[c.TMP32[0:64, 192:320]])
            else:
                p.op("vector", lambda e_, dst=c.LAT[0:64, 512 + m * 128:512 + (m + 1) * 128]:
                     e_.tensor_add(out=dst, in0=mh[:, 0:128], in1=c.TMP32[0:64, 192:320]),
                     reads=[mh[:, 0:128], c.TMP32[0:64, 192:320]], writes=[c.LAT[0:64, 512 + m * 128:512 + (m + 1) * 128]])
    bds = c.PVEC[:, 80:81]
    tt8 = c.PVEC[:, 81:89]
    p.op("vector", lambda e_: e_.tensor_mul(out=tt8, in0=bdn, in1=OHB), reads=[bdn, OHB], writes=[tt8])
    p.op("vector", lambda e_: e_.tensor_reduce(out=bds, in_=tt8, axis=AX.X, op=ALU.add),
         reads=[tt8], writes=[bds])
    bus = c.PVEC[:, 96:104]
    t64 = c.TMP32[:, 320:384]
    p.op("vector", lambda e_: e_.tensor_tensor(
         out=t64.rearrange("p (n w) -> p n w", w=8), in0=bup.rearrange("p (n w) -> p n w", w=8),
         in1=OHB.unsqueeze(2).broadcast_to([128, 8, 8]), op=ALU.mult),
         reads=[bup, OHB], writes=[t64])
    p.op("vector", lambda e_: e_.tensor_add(out=t64[:, 0:32], in0=t64[:, 0:32], in1=t64[:, 32:64]),
         reads=[t64], writes=[t64[:, 0:32]])
    p.op("vector", lambda e_: e_.tensor_add(out=t64[:, 0:16], in0=t64[:, 0:16], in1=t64[:, 16:32]),
         reads=[t64[:, 0:32]], writes=[t64[:, 0:16]])
    p.op("vector", lambda e_: e_.tensor_add(out=bus, in0=t64[:, 0:8], in1=t64[:, 8:16]),
         reads=[t64[:, 0:16]], writes=[bus])
    bbh = c.PVEC[:, 104:112]
    p.op("vector", lambda e_: e_.tensor_add(out=bbh, in0=bbase, in1=bus), reads=[bbase, bus], writes=[bbh])
    p.op("vector", lambda e_: e_.tensor_scalar(out=bbh, in0=bbh, scalar1=0.5, scalar2=None,
         op0=ALU.mult), reads=[bbh], writes=[bbh])

    for tc in range(NTC):
        ps = (c.PS_A, c.PS_B)[tc % 2]
        for g in range(NTILE):
            lhs = c.LAT[:, g * 64:(g + 1) * 64]
            rhs = pt(c.A, g)[:, tc * TC:(tc + 1) * TC]
            p.op("tensor", lambda e_, o=ps[0:64, :], l=lhs, r=rhs, s=(g == 0), st=(g == 7):
                 e_.matmul(o, l, r, start=s, stop=st),
                 reads=[lhs, rhs], writes=[ps[0:64, :]])
        dst = c.SC16[0:64, tc * TC:(tc + 1) * TC]
        p.op("scalar", lambda e_, a=dst, b_=ps[0:64, :]:
             e_.activation(a, b_, AF.Relu, bias=bds[0:64, :]),
             reads=[ps[0:64, :], bds], writes=[dst])

    for m in range(NTILE):
        buf = m % 2
        wslot = WS16[:, buf * 1024:(buf + 1) * 1024]
        p.dma("gpsimd", wslot.rearrange("p (g m) -> p g m", m=128),
              w["W_base"].rearrange("(g p) n -> p g n", p=128)[:, :, m * 128:(m + 1) * 128],
              stream=f"wb{buf}")
        for tc in range(NTC):
            ps = (c.PS_A, c.PS_B)[tc % 2]
            for g in range(NTILE):
                lhs = wslot[:, g * 128:(g + 1) * 128]
                rhs = pt(c.A, g)[:, tc * TC:(tc + 1) * TC]
                p.op("tensor", lambda e_, o=ps[:, :], l=lhs, r=rhs, s=(g == 0), st=False:
                     e_.matmul(o, l, r, start=s, stop=st),
                     reads=[lhs, rhs], writes=[ps[:, :]])
            lhs2 = c.LAT[0:64, 512 + m * 128:512 + (m + 1) * 128]
            rhs2 = c.SC16[0:64, tc * TC:(tc + 1) * TC]
            p.op("tensor", lambda e_, o=ps[:, :], l=lhs2, r=rhs2:
                 e_.matmul(o, l, r, start=False, stop=True),
                 reads=[lhs2, rhs2], writes=[ps[:, :]])
            xdst = X[m][:, tc * TC:(tc + 1) * TC]
            p.op("vector", lambda e_, x=xdst, b_=ps[:, :]:
                 e_.scalar_tensor_tensor(out=x, in0=b_, scalar=0.5, in1=x,
                                        op0=ALU.mult, op1=ALU.add),
                 reads=[ps[:, :], xdst], writes=[xdst])
        p.op("vector", lambda e_, x=X[m], bb=bbh[:, m:m + 1]:
             e_.tensor_scalar(out=x, in0=x, scalar1=bb, scalar2=None, op0=ALU.add),
             reads=[X[m], bbh[:, m:m + 1]], writes=[X[m]])


def emit_l3_attn(c, w):
    p = c.p
    X = [xt(c, i) for i in range(NTILE)]
    NH, DH = 16, 64

    emit_ln_prefix(c, [pt(c.A, i) for i in range(NTILE)], variant="plain")

    WS16 = c.WS[:].bitcast(BF16)

    for m in range(NTILE):
        buf = m % 2
        wslot = WS16[:, buf * 1024:(buf + 1) * 1024]
        p.dma("gpsimd", wslot.rearrange("p (g m) -> p g m", m=128),
              w["wkT"].rearrange("(g p) n -> p g n", p=128)[:, :, m * 128:(m + 1) * 128],
              stream=f"wk{buf}")
        for tc in range(NTC):
            ps = (c.PS_A, c.PS_B)[tc % 2]
            for g in range(NTILE):
                lhs = wslot[:, g * 128:(g + 1) * 128]
                rhs = pt(c.A, g)[:, tc * TC:(tc + 1) * TC]
                p.op("tensor", lambda e, o=ps[:, :], l=lhs, r=rhs, s=(g == 0), st=(g == 7):
                     e.matmul(o, l, r, start=s, stop=st),
                     reads=[lhs, rhs], writes=[ps[:, :]])
            dst = pt(c.C, m)[:, tc * TC:(tc + 1) * TC]
            p.op("scalar", lambda e, a=dst, b_=ps[:, :]: e.activation(a, b_, AF.Copy),
                 reads=[ps[:, :]], writes=[dst])

    for m in range(NTILE):
        buf = m % 2
        wslot = WS16[:, 2048 + buf * 1024:2048 + (buf + 1) * 1024]
        p.dma("gpsimd", wslot.rearrange("p (g m) -> p g m", m=128),
              w["wvT"].rearrange("(g p) n -> p g n", p=128)[:, :, m * 128:(m + 1) * 128],
              stream=f"wv{buf}")
        for tc in range(NTC):
            ps = (c.PS_A, c.PS_B)[tc % 2]
            for g in range(NTILE):
                lhs = wslot[:, g * 128:(g + 1) * 128]
                rhs = pt(c.A, g)[:, tc * TC:(tc + 1) * TC]
                p.op("tensor", lambda e, o=ps[:, :], l=lhs, r=rhs, s=(g == 0), st=(g == 7):
                     e.matmul(o, l, r, start=s, stop=st),
                     reads=[lhs, rhs], writes=[ps[:, :]])
            dst = c.SC16[:, tc * TC:(tc + 1) * TC]
            p.op("scalar", lambda e, a=dst, b_=ps[:, :]: e.activation(a, b_, AF.Copy),
                 reads=[ps[:, :]], writes=[dst])
        for h2 in range(2):
            h = 2 * m + h2
            for t in range(16):
                src = c.SC16[64 * h2:64 * h2 + 64, t * 128:(t + 1) * 128]
                ident = c.IDENT[64 * h2:64 * h2 + 64, 64 * h2:64 * h2 + 64]
                dstp = c.PS_T[:, t * 64:(t + 1) * 64]
                p.op("tensor", lambda e, o=dstp, i_=src, ii=ident: e.transpose(o, i_, ii),
                     reads=[src, ident], writes=[dstp])
            dstv = c.B[:, h * 1024:(h + 1) * 1024]
            p.op("vector", lambda e, a=dstv, b_=c.PS_T[:, :]: e.tensor_copy(out=a, in_=b_),
                 reads=[c.PS_T[:, :]], writes=[dstv])

    for tc in range(NTC):
        for m in range(NTILE):
            buf = m % 2
            wslot = WS16[:, buf * 1024:(buf + 1) * 1024]
            p.dma("gpsimd", wslot.rearrange("p (g m) -> p g m", m=128),
                  w["wqT"].rearrange("(g p) n -> p g n", p=128)[:, :, m * 128:(m + 1) * 128],
                  stream=f"wq{buf}")
            ps = (c.PS_A, c.PS_B)[m % 2]
            for g in range(NTILE):
                lhs = wslot[:, g * 128:(g + 1) * 128]
                rhs = pt(c.A, g)[:, tc * TC:(tc + 1) * TC]
                p.op("tensor", lambda e, o=ps[:, :], l=lhs, r=rhs, s=(g == 0), st=(g == 7):
                     e.matmul(o, l, r, start=s, stop=st),
                     reads=[lhs, rhs], writes=[ps[:, :]])
            qtmp = (c.SC16 if m < 4 else c.SC16b)[:, (m % 4) * TC:(m % 4 + 1) * TC]
            p.op("scalar", lambda e, a=qtmp, b_=ps[:, :]: e.activation(a, b_, AF.Copy, scale=0.125),
                 reads=[ps[:, :]], writes=[qtmp])
        for m in range(NTILE):
            qtmp = (c.SC16 if m < 4 else c.SC16b)[:, (m % 4) * TC:(m % 4 + 1) * TC]
            dst = pt(c.A, m)[:, tc * TC:(tc + 1) * TC]
            p.op("vector", lambda e, a=dst, b_=qtmp: e.tensor_copy(out=a, in_=b_),
                 reads=[qtmp], writes=[dst])

    emit_attn_core(
        c, NH=16, DH=64, causal=True,
        q_ap=lambda h, qc: pt(c.A, h // 2)[64 * (h % 2):64 * (h % 2) + 64, qc * 128:(qc + 1) * 128],
        k_ap=lambda h, w0, w1: pt(c.C, h // 2)[64 * (h % 2):64 * (h % 2) + 64, w0:w1],
        vT_ap=lambda h, t: c.B[:, h * 1024 + t * 64:h * 1024 + t * 64 + 64],
        o_dst=lambda g, qc: pt(c.A, g)[:, qc * 128:(qc + 1) * 128])

    for m in range(NTILE):
        buf = m % 2
        wslot = WS16[:, buf * 1024:(buf + 1) * 1024]
        p.dma("gpsimd", wslot.rearrange("p (g m) -> p g m", m=128),
              w["woT"].rearrange("(g p) n -> p g n", p=128)[:, :, m * 128:(m + 1) * 128],
              stream=f"wo{buf}")
        for tc in range(NTC):
            ps = (c.PS_A, c.PS_B)[tc % 2]
            for g in range(NTILE):
                lhs = wslot[:, g * 128:(g + 1) * 128]
                rhs = pt(c.A, g)[:, tc * TC:(tc + 1) * TC]
                p.op("tensor", lambda e, o=ps[:, :], l=lhs, r=rhs, s=(g == 0), st=(g == 7):
                     e.matmul(o, l, r, start=s, stop=st),
                     reads=[lhs, rhs], writes=[ps[:, :]])
            xdst = X[m][:, tc * TC:(tc + 1) * TC]
            p.op("vector", lambda e, x=xdst, b_=ps[:, :]:
                 e.scalar_tensor_tensor(out=x, in0=b_, scalar=1.0, in1=x,
                                        op0=ALU.mult, op1=ALU.add),
                 reads=[ps[:, :], xdst], writes=[xdst])


def emit_l3_ffn(c, w):
    p = c.p
    X = [xt(c, i) for i in range(NTILE)]
    WS16 = c.WS[:].bitcast(BF16)

    rsv = c.PVEC[:, 0:8]
    p.dma("sync", rsv, w["rs"], stream="misc")
    pesr = row(c, R_T0)[:, 40:48]
    p.dma("sync", pesr, w["pes"], stream="misc7")
    PESB = c.PVEC[:, 8:16]
    p.op("tensor", lambda e_: e_.matmul(c.PS_O[:, 0:8], c.ONESROWF[0:1, :], pesr,
         start=True, stop=True), reads=[c.ONESROWF[0:1, :], pesr], writes=[c.PS_O[:, 0:8]])
    p.op("vector", lambda e_: e_.tensor_copy(out=PESB, in_=c.PS_O[:, 0:8]),
         reads=[c.PS_O[:, 0:8]], writes=[PESB])
    rwb = WS16[:, 3968:4032]
    p.dma("gpsimd", rwb.rearrange("p (g m) -> p g m", m=8),
          w["rw"].rearrange("(g p) n -> p g n", p=128), stream="rw")
    for g in range(NTILE):
        sl = rwb[:, g * 8:(g + 1) * 8]
        p.op("vector", lambda e_, s_=sl, rc=rsv[:, g:g + 1]: e_.tensor_scalar(
             out=s_, in0=s_, scalar1=rc, scalar2=None, op0=ALU.mult),
             reads=[rwb, rsv], writes=[sl])

    LOGT = c.TMP32[:, 128:256]
    CT = c.TMP32[:, 0:128]

    def router_cb(tc):
        rr = row(c, R_VAR)
        for g in range(NTILE):
            lhs = rwb[:, g * 8:(g + 1) * 8]
            rhs = pt(c.A, g)[:, tc * TC:(tc + 1) * TC]
            p.op("tensor", lambda e_, o=c.PS_A[0:8, :], l=lhs, r=rhs, s=(g == 0), st=(g == 7):
                 e_.matmul(o, l, r, start=s, stop=st),
                 reads=[lhs, rhs], writes=[c.PS_A[0:8, :]])
        p.op("tensor", lambda e_, r=rr: e_.matmul(c.PS_B[0:8, :], c.ONESROWF[0:1, 0:8], r,
             start=True, stop=True), reads=[c.ONESROWF[0:1, 0:8], rr], writes=[c.PS_B[0:8, :]])
        lg = c.SBC[0:8, :]
        p.op("vector", lambda e_, a=lg: e_.tensor_copy(out=a, in_=c.PS_B[0:8, :]),
             reads=[c.PS_B[0:8, :]], writes=[lg])
        p.op("vector", lambda e_, a=lg: e_.tensor_tensor(out=a, in0=c.PS_A[0:8, :], in1=a,
             op=ALU.mult), reads=[c.PS_A[0:8, :], lg], writes=[lg])
        for j in range(4):
            t = tc * 4 + j
            srcl = lg[:, j * 128:(j + 1) * 128]
            ps = c.PS_O[:, (j % 2) * 128:(j % 2) * 128 + 8]
            p.op("tensor", lambda e_, o=ps, i_=srcl: e_.transpose(o, i_, c.IDENTF[0:8, 0:8]),
                 reads=[srcl, c.IDENTF[0:8, 0:8]], writes=[ps])
            dst = LOGT[:, t * 8:(t + 1) * 8]
            p.op("vector", lambda e_, a=dst, b_=ps: e_.tensor_copy(out=a, in_=b_),
                 reads=[ps], writes=[dst])

    emit_ln_prefix(c, [pt(c.A, i) for i in range(NTILE)], variant="plain",
                   rr_out=lambda tc: row(c, R_VAR), chunk_cb=router_cb)

    T8a = c.PVEC[:, 96:104]
    T8b = c.PVEC[:, 88:96]
    M1 = c.PVEC[:, 104:105]
    NM1 = c.PVEC[:, 105:106]
    M2 = c.PVEC[:, 106:107]
    SS = c.PVEC[:, 107:108]
    for t in range(16):
        lt = LOGT[:, t * 8:(t + 1) * 8]
        p.op("vector", lambda e_, l=lt: e_.tensor_reduce(out=M1, in_=l, axis=AX.X, op=ALU.max),
             reads=[lt], writes=[M1])
        p.op("vector", lambda e_, l=lt: e_.tensor_scalar(out=T8a, in0=l, scalar1=M1,
             scalar2=None, op0=ALU.subtract), reads=[lt, M1], writes=[T8a])
        p.op("scalar", lambda e_: e_.activation(T8a, T8a, AF.Sign), reads=[T8a], writes=[T8a])
        p.op("vector", lambda e_: e_.tensor_scalar(out=T8a, in0=T8a, scalar1=1.0,
             scalar2=None, op0=ALU.add), reads=[T8a], writes=[T8a])
        p.op("vector", lambda e_, l=lt: e_.scalar_tensor_tensor(out=T8a, in0=T8a,
             scalar=-1e30, in1=l, op0=ALU.mult, op1=ALU.add), reads=[T8a, lt], writes=[T8a])
        p.op("vector", lambda e_: e_.tensor_reduce(out=M2, in_=T8a, axis=AX.X, op=ALU.max),
             reads=[T8a], writes=[M2])
        p.op("vector", lambda e_, l=lt: e_.tensor_scalar(out=T8a, in0=l, scalar1=M2,
             scalar2=None, op0=ALU.subtract), reads=[lt, M2], writes=[T8a])
        p.op("scalar", lambda e_: e_.activation(T8a, T8a, AF.Sign), reads=[T8a], writes=[T8a])
        p.op("vector", lambda e_: e_.tensor_scalar(out=T8a, in0=T8a, scalar1=1.0,
             scalar2=1.0, op0=ALU.add, op1=ALU.min), reads=[T8a], writes=[T8a])
        p.op("vector", lambda e_: e_.tensor_scalar(out=NM1, in0=M1, scalar1=-1.0,
             scalar2=None, op0=ALU.mult), reads=[M1], writes=[NM1])
        p.op("scalar", lambda e_, l=lt: e_.activation(T8b, l, AF.Exp, bias=NM1),
             reads=[lt, NM1], writes=[T8b])
        p.op("vector", lambda e_: e_.tensor_mul(out=T8b, in0=T8b, in1=T8a),
             reads=[T8b, T8a], writes=[T8b])
        p.op("vector", lambda e_: e_.tensor_reduce(out=SS, in_=T8b, axis=AX.X, op=ALU.add),
             reads=[T8b], writes=[SS])
        p.op("vector", lambda e_: e_.reciprocal(out=SS, in_=SS), reads=[SS], writes=[SS])
        p.op("vector", lambda e_: e_.tensor_scalar(out=T8b, in0=T8b, scalar1=SS,
             scalar2=None, op0=ALU.mult), reads=[T8b, SS], writes=[T8b])
        p.op("vector", lambda e_, o=CT[:, t * 8:(t + 1) * 8]: e_.tensor_mul(out=o, in0=T8b,
             in1=PESB), reads=[T8b, PESB], writes=[CT[:, t * 8:(t + 1) * 8]])

    for t in range(16):
        srcc = CT[:, t * 8:(t + 1) * 8]
        ps = c.PS_O[0:8, (t % 2) * 128:(t % 2) * 128 + 128]
        p.op("tensor", lambda e_, o=ps, i_=srcc: e_.transpose(o, i_, c.IDENTF[:]),
             reads=[srcc, c.IDENTF[:]], writes=[ps])
        dst = c.LAT[0:8, t * 128:(t + 1) * 128]
        p.op("vector", lambda e_, a=dst, b_=ps: e_.tensor_copy(out=a, in_=b_),
             reads=[ps], writes=[dst])

    for e in range(8):
        for t in range(16):
            rhs = c.LAT[0:8, t * 128:(t + 1) * 128]
            lhs = c.SEL[:, e * 128:(e + 1) * 128]
            ps = (c.PS_A, c.PS_B)[t % 2]
            p.op("tensor", lambda e_, o=ps[:, 0:128], l=lhs, r=rhs:
                 e_.matmul(o, l, r, start=True, stop=True),
                 reads=[lhs, rhs], writes=[ps[:, 0:128]])
            dst = c.SC16[:, t * 128:(t + 1) * 128]
            p.op("vector", lambda e_, a=dst, b_=ps[:, 0:128]: e_.tensor_copy(out=a, in_=b_),
                 reads=[ps[:, 0:128]], writes=[dst])
        for hidm in range(16):
            buf = hidm % 2
            w0s = WS16[:, buf * 1024:(buf + 1) * 1024]
            w1s = WS16[:, 2048 + buf * 1024:2048 + (buf + 1) * 1024]
            p.dma("gpsimd", w0s.rearrange("p (g m) -> p g m", m=128),
                  w["g0T"][e].rearrange("(g p) n -> p g n", p=128)[:, :, hidm * 128:(hidm + 1) * 128],
                  stream=f"g0_{buf}")
            p.dma("gpsimd", w1s.rearrange("p (g m) -> p g m", m=128),
                  w["g1T"][e].rearrange("(g p) n -> p g n", p=128)[:, :, hidm * 128:(hidm + 1) * 128],
                  stream=f"g1_{buf}")
            pool = c.B if hidm < 8 else c.C
            for tc in range(NTC):
                for g in range(NTILE):
                    lhs = w0s[:, g * 128:(g + 1) * 128]
                    rhs = pt(c.A, g)[:, tc * TC:(tc + 1) * TC]
                    p.op("tensor", lambda e_, o=c.PS_A[:, :], l=lhs, r=rhs, s=(g == 0), st=(g == 7):
                         e_.matmul(o, l, r, start=s, stop=st),
                         reads=[lhs, rhs], writes=[c.PS_A[:, :]])
                gg = c.SC16b[:, (tc % 2) * TC:(tc % 2) * TC + TC]
                p.op("scalar", lambda e_, a=gg: e_.activation(a, c.PS_A[:, :], AF.Gelu_apprx_tanh),
                     reads=[c.PS_A[:, :]], writes=[gg])
                for g in range(NTILE):
                    lhs = w1s[:, g * 128:(g + 1) * 128]
                    rhs = pt(c.A, g)[:, tc * TC:(tc + 1) * TC]
                    p.op("tensor", lambda e_, o=c.PS_B[:, :], l=lhs, r=rhs, s=(g == 0), st=(g == 7):
                         e_.matmul(o, l, r, start=s, stop=st),
                         reads=[lhs, rhs], writes=[c.PS_B[:, :]])
                slot = pool[:, (hidm % 8) * T + tc * TC:(hidm % 8) * T + (tc + 1) * TC]
                p.op("vector", lambda e_, a=slot, g_=gg: e_.tensor_tensor(out=a, in0=g_,
                     in1=c.PS_B[:, :], op=ALU.mult), reads=[gg, c.PS_B[:, :]], writes=[slot])
                cbch = c.SC16[:, tc * TC:(tc + 1) * TC]
                p.op("vector", lambda e_, a=slot, cb=cbch: e_.tensor_tensor(out=a, in0=a,
                     in1=cb, op=ALU.mult), reads=[slot, cbch], writes=[slot])
        for dm in range(NTILE):
            buf = dm % 2
            lws = WS16[:, buf * 2048:(buf + 1) * 2048]
            p.dma("gpsimd", lws.rearrange("p (g m) -> p g m", m=128),
                  w["lw"][e].rearrange("(g p) n -> p g n", p=128)[:, :, dm * 128:(dm + 1) * 128],
                  stream=f"lw{buf}")
            for tc in range(NTC):
                ps = (c.PS_A, c.PS_B)[tc % 2]
                for k2 in range(16):
                    pool = c.B if k2 < 8 else c.C
                    lhs = lws[:, k2 * 128:(k2 + 1) * 128]
                    rhs = pool[:, (k2 % 8) * T + tc * TC:(k2 % 8) * T + (tc + 1) * TC]
                    p.op("tensor", lambda e_, o=ps[:, :], l=lhs, r=rhs, s=(k2 == 0), st=(k2 == 15):
                         e_.matmul(o, l, r, start=s, stop=st),
                         reads=[lhs, rhs], writes=[ps[:, :]])
                xdst = X[dm][:, tc * TC:(tc + 1) * TC]
                p.op("vector", lambda e_, x=xdst, b_=ps[:, :]:
                     e_.scalar_tensor_tensor(out=x, in0=b_, scalar=0.5, in1=x,
                                            op0=ALU.mult, op1=ALU.add),
                     reads=[ps[:, :], xdst], writes=[xdst])


def emit_quant(c, src, dst, W, xn=None, tmp=None):
    p = c.p
    nb = W // 16
    BM = c.QSC[:, 0:nb]
    S = c.QSC[:, 32:32 + nb]
    R = c.QSC[:, 64:64 + nb]
    XN = xn if xn is not None else c.TMP32[:, 0:W]
    UN = c.SBC[:, 0:W]
    QA = c.BBC[:, 0:W]
    TB = tmp if tmp is not None else c.PS_O[:, 0:W]
    RN = c.RNDI[:, 0:W]
    p.op("vector", lambda e: e.tensor_reduce(out=BM, in_=src.rearrange("p (n w) -> p n w", w=16),
         axis=AX.X, op=ALU.max, apply_absolute_value=True), reads=[src], writes=[BM])
    p.op("vector", lambda e: e.tensor_scalar(out=S, in0=BM, scalar1=1.0 / 6.0, scalar2=1e-12,
         op0=ALU.mult, op1=ALU.max), reads=[BM], writes=[S])
    p.op("vector", lambda e: e.reciprocal(out=R, in_=S), reads=[S], writes=[R])
    p.op("vector", lambda e: e.tensor_tensor(out=XN.rearrange("p (n w) -> p n w", w=16),
         in0=src.rearrange("p (n w) -> p n w", w=16),
         in1=R.unsqueeze(2).broadcast_to([128, nb, 16]), op=ALU.mult),
         reads=[src, R], writes=[XN])
    p.op("scalar", lambda e: e.activation(UN, XN, AF.Abs), reads=[XN], writes=[UN])
    p.op("vector", lambda e: e.tensor_scalar(out=QA, in0=UN, scalar1=2.0, scalar2=None,
         op0=ALU.mult), reads=[UN], writes=[QA])
    p.op("vector", lambda e: e.tensor_copy(out=RN, in_=QA), reads=[QA], writes=[RN])
    p.op("vector", lambda e: e.tensor_copy(out=QA, in_=RN), reads=[RN], writes=[QA])
    p.op("vector", lambda e: e.tensor_scalar(out=QA, in0=QA, scalar1=0.5, scalar2=2.0,
         op0=ALU.mult, op1=ALU.min), reads=[QA], writes=[QA])
    p.op("vector", lambda e: e.tensor_scalar(out=TB, in0=UN, scalar1=-2.0, scalar2=None,
         op0=ALU.add), reads=[UN], writes=[TB])
    p.op("vector", lambda e: e.tensor_copy(out=RN, in_=TB), reads=[TB], writes=[RN])
    p.op("vector", lambda e: e.tensor_copy(out=TB, in_=RN), reads=[RN], writes=[TB])
    p.op("vector", lambda e: e.tensor_scalar(out=TB, in0=TB, scalar1=0.0, scalar2=2.0,
         op0=ALU.max, op1=ALU.min), reads=[TB], writes=[TB])
    p.op("vector", lambda e: e.tensor_add(out=QA, in0=QA, in1=TB), reads=[QA, TB], writes=[QA])
    p.op("vector", lambda e: e.tensor_scalar(out=TB, in0=UN, scalar1=-4.5, scalar2=None,
         op0=ALU.add), reads=[UN], writes=[TB])
    p.op("vector", lambda e: e.tensor_copy(out=RN, in_=TB), reads=[TB], writes=[RN])
    p.op("vector", lambda e: e.tensor_copy(out=TB, in_=RN), reads=[RN], writes=[TB])
    p.op("vector", lambda e: e.tensor_scalar(out=TB, in0=TB, scalar1=0.0, scalar2=1.0,
         op0=ALU.max, op1=ALU.min), reads=[TB], writes=[TB])
    p.op("vector", lambda e: e.tensor_add(out=QA, in0=QA, in1=TB), reads=[QA, TB], writes=[QA])
    p.op("vector", lambda e: e.tensor_add(out=QA, in0=QA, in1=TB), reads=[QA, TB], writes=[QA])
    p.op("scalar", lambda e: e.activation(TB, XN, AF.Sign), reads=[XN], writes=[TB])
    p.op("vector", lambda e: e.tensor_mul(out=QA, in0=QA, in1=TB), reads=[QA, TB], writes=[QA])
    p.op("vector", lambda e: e.tensor_tensor(out=dst.rearrange("p (n w) -> p n w", w=16),
         in0=QA.rearrange("p (n w) -> p n w", w=16),
         in1=S.unsqueeze(2).broadcast_to([128, nb, 16]), op=ALU.mult),
         reads=[QA, S], writes=[dst])


def emit_l1_ffn(c, w, do_quant):
    p = c.p
    X = [xt(c, i) for i in range(NTILE)]
    WS16 = c.WS[:].bitcast(BF16)
    BF = c.B[:].bitcast(F32)
    CF = c.C[:].bitcast(F32)

    def xtm(tt):
        if tt < 8:
            return BF[:, tt * 1024:(tt + 1) * 1024]
        return CF[:, (tt - 8) * 1024:(tt - 7) * 1024]

    emit_ln_prefix(c, [xtm(tt) for tt in range(16)], variant="plain", tm_f32=True)

    for tt in range(16):
        for ch in range(2):
            srcq = xtm(tt)[:, ch * 512:(ch + 1) * 512]
            dstq = c.A[:, tt * 1024 + ch * 512:tt * 1024 + (ch + 1) * 512]
            emit_quant(c, srcq, dstq, 512)
    for tt in range(16):
        for dch in range(NTILE):
            srct = c.A[:, tt * 1024 + dch * 128:tt * 1024 + (dch + 1) * 128]
            pslot = c.PS_T[:, (dch % 8) * 128:(dch % 8) * 128 + 128]
            p.op("tensor", lambda e, o=pslot, i_=srct: e.transpose(o, i_, c.IDENT[:]),
                 reads=[srct, c.IDENT[:]], writes=[pslot])
            dstt = pt(c.B, dch)[:, tt * 128:(tt + 1) * 128]
            p.op("vector", lambda e, a=dstt, b_=pslot: e.tensor_copy(out=a, in_=b_),
                 reads=[pslot], writes=[dstt])

    if do_quant:
        for wap, outap, nr in ((w["wg"], w["wgT"], 2816), (w["wu"], w["wuT"], 2816)):
            for it in range(22):
                buf = it % 2
                slot = c.WS[:, buf * 1024:(buf + 1) * 1024]
                p.dma("gpsimd", slot, wap[it * 128:(it + 1) * 128, :], stream=f"qw{buf}")
                qd = c.SC16[:, 0:1024]
                for ch in range(2):
                    emit_quant(c, slot[:, ch * 512:(ch + 1) * 512],
                               qd[:, ch * 512:(ch + 1) * 512], 512)
                tout = c.SC16b[:, 0:1024]
                for dch in range(NTILE):
                    pslot = c.PS_T[:, (dch % 8) * 128:(dch % 8) * 128 + 128]
                    p.op("tensor", lambda e, o=pslot, i_=qd[:, dch * 128:(dch + 1) * 128]:
                         e.transpose(o, i_, c.IDENT[:]),
                         reads=[qd[:, dch * 128:(dch + 1) * 128], c.IDENT[:]], writes=[pslot])
                    p.op("vector", lambda e, a=tout[:, dch * 128:(dch + 1) * 128], b_=pslot:
                         e.tensor_copy(out=a, in_=b_),
                         reads=[pslot], writes=[tout[:, dch * 128:(dch + 1) * 128]])
                p.dma("scalar",
                      outap.rearrange("(dch p) n -> p dch n", p=128)[:, :, it * 128:(it + 1) * 128],
                      tout.rearrange("p (dch m) -> p dch m", m=128), stream="qwo")
        for dt in range(NTILE):
            slot = c.WS[:, 0:1408]
            for half in range(2):
                p.dma("gpsimd", slot,
                      w["wd"][dt * 128:(dt + 1) * 128, half * 1408:(half + 1) * 1408],
                      stream="qd0")
                qd = c.SC16[:, 0:1408]
                emit_quant(c, slot[:, 0:512], qd[:, 0:512], 512)
                emit_quant(c, slot[:, 512:1024], qd[:, 512:1024], 512)
                emit_quant(c, slot[:, 1024:1408], qd[:, 1024:1408], 384)
                tout = c.SC16b[:, 0:1408]
                for ich in range(11):
                    pslot = c.PS_T[:, (ich % 8) * 128:(ich % 8) * 128 + 128]
                    p.op("tensor", lambda e, o=pslot, i_=qd[:, ich * 128:(ich + 1) * 128]:
                         e.transpose(o, i_, c.IDENT[:]),
                         reads=[qd[:, ich * 128:(ich + 1) * 128], c.IDENT[:]], writes=[pslot])
                    p.op("vector", lambda e, a=tout[:, ich * 128:(ich + 1) * 128], b_=pslot:
                         e.tensor_copy(out=a, in_=b_),
                         reads=[pslot], writes=[tout[:, ich * 128:(ich + 1) * 128]])
                p.dma("scalar",
                      w["wdT"][half * 1408:(half + 1) * 1408, :]
                      .rearrange("(ich p) n -> p ich n", p=128)[:, :, dt * 128:(dt + 1) * 128],
                      tout.rearrange("p (ich m) -> p ich m", m=128), stream="qdo")

    NIC = 11
    for slab in range(4):
        ts0 = slab * 512
        for ic in range(NIC):
            bufg = ic % 2
            gslot = WS16[:, bufg * 2048:(bufg + 1) * 2048]
            p.dma("gpsimd", gslot.rearrange("p (g m) -> p g m", m=256),
                  w["wgT"].rearrange("(g p) n -> p g n", p=128)[:, :, ic * 256:(ic + 1) * 256],
                  stream=f"gt{bufg}")
            for tokm in range(4):
                tt = slab * 4 + tokm
                ps = (c.PS_A, c.PS_B)[tokm % 2]
                for g in range(NTILE):
                    lhs = pt(c.B, g)[:, tt * 128:(tt + 1) * 128]
                    rhs = gslot[:, g * 256:(g + 1) * 256]
                    p.op("tensor", lambda e, o=ps[:, 0:256], l=lhs, r=rhs, s=(g == 0), st=(g == 7):
                         e.matmul(o, l, r, start=s, stop=st),
                         reads=[lhs, rhs], writes=[ps[:, 0:256]])
                gg = c.SC16b[:, tokm * 256:(tokm + 1) * 256]
                p.op("scalar", lambda e, a=gg, b_=ps[:, 0:256]: e.activation(a, b_, AF.Silu),
                     reads=[ps[:, 0:256]], writes=[gg])
            uslot = WS16[:, 2048 + bufg * 2048:2048 + (bufg + 1) * 2048] if False else gslot
            bufu = (ic + 1) % 2
            uslot = WS16[:, bufu * 2048:(bufu + 1) * 2048]
            p.dma("gpsimd", uslot.rearrange("p (g m) -> p g m", m=256),
                  w["wuT"].rearrange("(g p) n -> p g n", p=128)[:, :, ic * 256:(ic + 1) * 256],
                  stream=f"ut{bufu}")
            for tokm in range(4):
                tt = slab * 4 + tokm
                ps = (c.PS_A, c.PS_B)[tokm % 2]
                for g in range(NTILE):
                    lhs = pt(c.B, g)[:, tt * 128:(tt + 1) * 128]
                    rhs = uslot[:, g * 256:(g + 1) * 256]
                    p.op("tensor", lambda e, o=ps[:, 0:256], l=lhs, r=rhs, s=(g == 0), st=(g == 7):
                         e.matmul(o, l, r, start=s, stop=st),
                         reads=[lhs, rhs], writes=[ps[:, 0:256]])
                hh = c.TMP32[:, 256:512]
                gg = c.SC16b[:, tokm * 256:(tokm + 1) * 256]
                p.op("vector", lambda e, a=hh, g_=gg, b_=ps[:, 0:256]:
                     e.tensor_tensor(out=a, in0=g_, in1=b_, op=ALU.mult),
                     reads=[gg, ps[:, 0:256]], writes=[hh])
                hq = c.SC16[:, tokm * 256:(tokm + 1) * 256]
                emit_quant(c, hh, hq, 256, xn=c.PS_O[:, 0:256], tmp=c.PS_O[:, 256:512])
                for j in range(2):
                    it = ic * 2 + j
                    pslot = c.PS_T[:, ((tokm * 2 + j) % 8) * 128:((tokm * 2 + j) % 8) * 128 + 128]
                    p.op("tensor", lambda e, o=pslot, i_=hq[:, j * 128:(j + 1) * 128]:
                         e.transpose(o, i_, c.IDENT[:]),
                         reads=[hq[:, j * 128:(j + 1) * 128], c.IDENT[:]], writes=[pslot])
                    dsth = c.C[:, it * 512 + tokm * 128:it * 512 + (tokm + 1) * 128]
                    p.op("vector", lambda e, a=dsth, b_=pslot: e.tensor_copy(out=a, in_=b_),
                         reads=[pslot], writes=[dsth])
        for dm in range(NTILE):
            lws = WS16[:, 0:2816]
            p.dma("gpsimd", lws.rearrange("p (g m) -> p g m", m=128),
                  w["wdT"].rearrange("(g p) n -> p g n", p=128)[:, :, dm * 128:(dm + 1) * 128],
                  stream="dnw")
            for tokm in range(4):
                ps = (c.PS_A, c.PS_B)[tokm % 2]
                for k2 in range(22):
                    lhs = lws[:, k2 * 128:(k2 + 1) * 128]
                    rhs = c.C[:, k2 * 512 + tokm * 128:k2 * 512 + (tokm + 1) * 128]
                    p.op("tensor", lambda e, o=ps[:, 0:128], l=lhs, r=rhs, s=(k2 == 0), st=(k2 == 21):
                         e.matmul(o, l, r, start=s, stop=st),
                         reads=[lhs, rhs], writes=[ps[:, 0:128]])
                xdst = X[dm][:, ts0 + tokm * 128:ts0 + (tokm + 1) * 128]
                p.op("vector", lambda e, x=xdst, b_=ps[:, 0:128]:
                     e.scalar_tensor_tensor(out=x, in0=b_, scalar=0.5, in1=x,
                                            op0=ALU.mult, op1=ALU.add),
                     reads=[ps[:, 0:128], xdst], writes=[xdst])


def _pp(v):
    a = np.asarray(v, np.float32)
    return np.ascontiguousarray(a.reshape(-1, 128).T)


def _prep_weights(inputs):
    g = {}
    f32 = np.float32
    A = lambda x: np.ascontiguousarray(np.asarray(x))
    g["l0f_q1"] = A(inputs["L0_ffn_fc1_q"]).astype(np.int32)
    g["l0f_s1"] = A(inputs["L0_ffn_fc1_scales"]).astype(f32)
    g["l0f_b1"] = _pp(inputs["L0_ffn_fc1_bias"])
    g["l0f_q2"] = A(inputs["L0_ffn_fc2_q"]).astype(np.int32)
    g["l0f_s2"] = A(inputs["L0_ffn_fc2_scales"]).astype(f32)
    g["l0f_b2"] = _pp(inputs["L0_ffn_fc2_bias"])
    g["l0f_gamma"] = _pp(inputs["L0_ffn_ln_gamma"])
    g["l0f_beta"] = _pp(inputs["L0_ffn_ln_beta"])
    wv = np.asarray(inputs["L0_attn_w_att_val"], f32)
    blk = np.zeros((8, 128, 128), f32)
    for j in range(8):
        blk[j, 0:64, 0:64] = wv[2 * j]
        blk[j, 64:128, 64:128] = wv[2 * j + 1]
    g["l0a_watt"] = A(blk.transpose(1, 0, 2).reshape(128, 1024))
    g["l0a_wvT"] = A(np.asarray(inputs["L0_attn_wv_w"], f32).T)
    g["l0a_outwT"] = A(np.asarray(inputs["L0_attn_out_w"], f32).T)
    g["l0a_wv_b"] = _pp(inputs["L0_attn_wv_b"])
    g["l0a_out_b"] = _pp(inputs["L0_attn_out_b"])
    g["l1a_wq"] = A(np.asarray(inputs["L1_attn_W_q"], f32))
    g["l1a_wk"] = A(np.asarray(inputs["L1_attn_W_k"], f32))
    g["l1a_wv"] = A(np.asarray(inputs["L1_attn_W_v"], f32))
    g["l1a_wo"] = A(np.asarray(inputs["L1_attn_W_o"], f32))
    g["l1a_wdkv"] = A(np.asarray(inputs["L1_attn_W_down_kv"], f32))
    g["l1a_wupk"] = A(np.asarray(inputs["L1_attn_W_up_k"], f32))
    g["l1a_wupv"] = A(np.asarray(inputs["L1_attn_W_up_v"], f32))
    g["l2a_wqT"] = A(np.asarray(inputs["L2_attn_Wq"], f32).T)
    g["l2a_wkT"] = A(np.asarray(inputs["L2_attn_Wk"], f32).T)
    g["l2a_wvT"] = A(np.asarray(inputs["L2_attn_Wv"], f32).T)
    g["l2a_woT"] = A(np.asarray(inputs["L2_attn_Wo"], f32).T)
    g["l3f_g0T"] = A(np.asarray(inputs["L3_ffn_gating_w"], f32)[:, 0].transpose(0, 2, 1))
    g["l3f_g1T"] = A(np.asarray(inputs["L3_ffn_gating_w"], f32)[:, 1].transpose(0, 2, 1))
    g["l3f_lw"] = A(np.asarray(inputs["L3_ffn_linear_w"], f32))
    g["l3f_rs"] = _pp(inputs["L3_ffn_router_scale"])
    g["l3f_rw"] = A(np.asarray(inputs["L3_ffn_router_w"], f32))
    g["l3f_pes"] = A(np.asarray(inputs["L3_ffn_per_expert_scale"], f32).reshape(1, 8))
    g["l1f_wg"] = A(np.asarray(inputs["L1_ffn_w_gate"], f32))
    g["l1f_wu"] = A(np.asarray(inputs["L1_ffn_w_up"], f32))
    g["l1f_wd"] = A(np.asarray(inputs["L1_ffn_w_down"], f32))
    g["l2f_W_base"] = A(np.asarray(inputs["L2_ffn_W_base"], f32))
    g["l2f_W_enc"] = A(np.asarray(inputs["L2_ffn_W_enc"], f32))
    g["l2f_W_dec"] = A(np.asarray(inputs["L2_ffn_W_dec"], f32))
    g["l2f_W_down"] = A(np.asarray(inputs["L2_ffn_W_down"], f32))
    g["l2f_W_up"] = A(np.asarray(inputs["L2_ffn_W_up"], f32))
    g["l2f_b_base"] = _pp(inputs["L2_ffn_b_base"])
    be = np.asarray(inputs["L2_ffn_b_enc"], f32).reshape(8, 2, 128)
    g["l2f_b_enc"] = A(be.transpose(2, 0, 1).reshape(128, 16))
    bd = np.asarray(inputs["L2_ffn_b_dec"], f32).reshape(8, 8, 128)
    g["l2f_b_dec"] = A(bd.transpose(2, 0, 1).reshape(128, 64))
    bdn = np.zeros((128, 8), f32)
    bdn[0:64, :] = np.asarray(inputs["L2_ffn_b_down"], f32).T
    g["l2f_b_down"] = A(bdn)
    bu = np.asarray(inputs["L2_ffn_b_up"], f32).reshape(8, 8, 128)
    g["l2f_b_up"] = A(bu.transpose(2, 0, 1).reshape(128, 64))
    g["l2f_cidx"] = A(np.asarray(inputs["L2_ffn_connected_idx"], np.int32).reshape(1, 8))
    g["l3a_wqT"] = A(np.asarray(inputs["L3_attn_WQ"], f32).T)
    g["l3a_wkT"] = A(np.asarray(inputs["L3_attn_WK"], f32).T)
    g["l3a_wvT"] = A(np.asarray(inputs["L3_attn_WV"], f32).T)
    g["l3a_woT"] = A(np.asarray(inputs["L3_attn_WO"], f32).T)
    return g


DEBUG_DUMP = []

SUBLAYERS = [
    (0, "ffn"), (0, "attn"), (0, "ffn"),
    (1, "ffn"), (1, "attn"), (1, "ffn"),
    (2, "ffn"), (2, "attn"), (2, "ffn"),
    (3, "ffn"), (3, "attn"), (3, "ffn"),
]


def build_program(nc, sublayers):
    p = Prog(nc)
    in_names = []

    def dr(name, shape, dt, kind="ExternalInput"):
        h = nc.dram_tensor(name, shape, dt, kind=kind)
        p.reg(h, "dram")
        if kind == "ExternalInput":
            in_names.append(name)
        return h.ap()

    x_in = dr("x", [1024, T], F32)
    x_out = dr("out", [1024, T], F32, kind="ExternalOutput")

    need = {SUBLAYERS[s] for s in sublayers}
    w = {}
    if (0, "ffn") in need:
        w["l0f"] = {
            "q1": dr("l0f_q1", [1024, 4096], I32),
            "s1": dr("l0f_s1", [8, 4096], F32),
            "b1": dr("l0f_b1", [128, 32], F32),
            "q2": dr("l0f_q2", [4096, 1024], I32),
            "s2": dr("l0f_s2", [32, 1024], F32),
            "b2": dr("l0f_b2", [128, 8], F32),
            "gamma": dr("l0f_gamma", [128, 8], F32),
            "beta": dr("l0f_beta", [128, 8], F32),
            "w1b": dr("l0f_w1b", [1024, 4096], BF16, kind="Internal"),
            "w2b": dr("l0f_w2b", [4096, 1024], BF16, kind="Internal"),
        }

    if (0, "attn") in need:
        w["l0a"] = {
            "watt": dr("l0a_watt", [128, 1024], F32),
            "wvT": dr("l0a_wvT", [1024, 1024], F32),
            "outwT": dr("l0a_outwT", [1024, 1024], F32),
            "wv_b": dr("l0a_wv_b", [128, 8], F32),
            "out_b": dr("l0a_out_b", [128, 8], F32),
        }
    if (1, "attn") in need:
        w["l1a"] = {
            "wq": dr("l1a_wq", [1024, 1024], F32),
            "wk": dr("l1a_wk", [1024, 1024], F32),
            "wv": dr("l1a_wv", [1024, 1024], F32),
            "wo": dr("l1a_wo", [1024, 1024], F32),
            "wdkv": dr("l1a_wdkv", [1024, 256], F32),
            "wupk": dr("l1a_wupk", [256, 1024], F32),
            "wupv": dr("l1a_wupv", [256, 1024], F32),
        }
    if (2, "attn") in need:
        w["l2a"] = {
            "wqT": dr("l2a_wqT", [1024, 1024], F32),
            "wkT": dr("l2a_wkT", [1024, 128], F32),
            "wvT": dr("l2a_wvT", [1024, 128], F32),
            "woT": dr("l2a_woT", [1024, 1024], F32),
        }
    if (1, "ffn") in need:
        w["l1f"] = {
            "wg": dr("l1f_wg", [2816, 1024], F32),
            "wu": dr("l1f_wu", [2816, 1024], F32),
            "wd": dr("l1f_wd", [1024, 2816], F32),
            "wgT": dr("l1f_wgT", [1024, 2816], BF16, kind="Internal"),
            "wuT": dr("l1f_wuT", [1024, 2816], BF16, kind="Internal"),
            "wdT": dr("l1f_wdT", [2816, 1024], BF16, kind="Internal"),
        }
    if (2, "ffn") in need:
        w["l2f"] = {
            "W_base": dr("l2f_W_base", [1024, 1024], F32),
            "W_enc": dr("l2f_W_enc", [8, 1024, 256], F32),
            "W_dec": dr("l2f_W_dec", [8, 256, 1024], F32),
            "W_down": dr("l2f_W_down", [8, 1024, 64], F32),
            "W_up": dr("l2f_W_up", [8, 64, 1024], F32),
            "b_base": dr("l2f_b_base", [128, 8], F32),
            "b_enc": dr("l2f_b_enc", [128, 16], F32),
            "b_dec": dr("l2f_b_dec", [128, 64], F32),
            "b_down": dr("l2f_b_down", [128, 8], F32),
            "b_up": dr("l2f_b_up", [128, 64], F32),
            "cidx": dr("l2f_cidx", [1, 8], I32),
        }
    if (3, "ffn") in need:
        w["l3f"] = {
            "g0T": dr("l3f_g0T", [8, 1024, 2048], F32),
            "g1T": dr("l3f_g1T", [8, 1024, 2048], F32),
            "lw": dr("l3f_lw", [8, 2048, 1024], F32),
            "rs": dr("l3f_rs", [128, 8], F32),
            "rw": dr("l3f_rw", [1024, 8], F32),
            "pes": dr("l3f_pes", [1, 8], F32),
        }
    if (3, "attn") in need:
        w["l3a"] = {
            "wqT": dr("l3a_wqT", [1024, 1024], F32),
            "wkT": dr("l3a_wkT", [1024, 1024], F32),
            "wvT": dr("l3a_wvT", [1024, 1024], F32),
            "woT": dr("l3a_woT", [1024, 1024], F32),
        }

    with contextlib.ExitStack() as stack:
        c = alloc_common(nc, p, stack)
        for i in range(NTILE):
            p.dma("sync", xt(c, i), x_in[i * 128:(i + 1) * 128, :], stream=f"xin{i % 4}")
        done = set()
        for s in sublayers:
            key = SUBLAYERS[s]
            if key == (0, "ffn"):
                emit_l0_ffn(c, w["l0f"], do_dequant=key not in done)
            elif key == (0, "attn"):
                emit_l0_attn(c, w["l0a"])
            elif key == (1, "attn"):
                emit_l1_attn(c, w["l1a"])
            elif key == (2, "attn"):
                emit_l2_attn(c, w["l2a"])
            elif key == (1, "ffn"):
                emit_l1_ffn(c, w["l1f"], do_quant=key not in done)
            elif key == (2, "ffn"):
                emit_l2_ffn(c, w["l2f"])
            elif key == (3, "ffn"):
                emit_l3_ffn(c, w["l3f"])
            elif key == (3, "attn"):
                emit_l3_attn(c, w["l3a"])
            else:
                raise NotImplementedError(str(key))
            done.add(key)
        for i in range(NTILE):
            p.dma("sync", x_out[i * 128:(i + 1) * 128, :], xt(c, i), stream=f"xout{i % 4}")
        for bn in DEBUG_DUMP:
            buf = getattr(c, bn)
            bap = buf.ap() if callable(getattr(buf, "ap", None)) else buf[:]
            sh = list(bap.shape)
            dt_ = bap.dtype
            dap = nc.dram_tensor("dbg_" + bn, sh, dt_, kind="ExternalOutput").ap()
            p.reg(nc.get_tensor("dbg_" + bn) if hasattr(nc, "get_tensor") else _DH(dap), "dram") \
                if False else p.meta.update({dap.name: (int(np.prod(sh)), 2 if dt_ == BF16 else 4, "dram")})
            p.dma("sync", dap, bap, stream="dbg")
        p.barrier_wait("sync")
        p.emit(stack)
    return in_names


def run_sublayers(inputs, sublayers, x_override=None, n_cores=8, trace=False):
    g = _prep_weights(inputs)
    x = np.asarray(x_override if x_override is not None else inputs["L0_attn_x"], np.float32)
    nc = bass.Bass("TRN2", target_bir_lowering=False, dynamic_dma_scratch_size=16384)
    in_names = build_program(nc, sublayers)
    in_maps = []
    for core in range(n_cores):
        m = {}
        for n in in_names:
            if n == "x":
                m["x"] = np.ascontiguousarray(x[core].T)
            else:
                m[n] = g[n]
        in_maps.append(m)
    res = run_bass_kernel_spmd(nc, in_maps, core_ids=list(range(n_cores)), trace=trace)
    outs = [np.asarray(r["out"], np.float32).T for r in res.results]
    return np.stack(outs, axis=0), res


def kernel(**inputs):
    out, _ = run_sublayers(inputs, list(range(12)))
    return out.astype(np.float32)


# version 02
# speedup vs baseline: 1.2045x; 1.2045x over previous
import contextlib
import math
import numpy as np

import concourse.bass as bass
import concourse.mybir as mybir
from concourse.bass_utils import run_bass_kernel_spmd

F32 = mybir.dt.float32
BF16 = mybir.dt.bfloat16
I32 = mybir.dt.int32
AF = mybir.ActivationFunctionType
ALU = mybir.AluOpType
AX = mybir.AxisListType

ENGINES = ("tensor", "vector", "scalar", "gpsimd", "sync")

T = 2048
D = 1024
NTILE = D // 128
TC = 512
NTC = T // TC


import os as _os
SEM_ROT = int(_os.environ.get("SEM_ROT", "30000"))


class Prog:
    def __init__(self, nc, n_dma_sems=24):
        self.nc = nc
        self.ops = {e: [] for e in ENGINES}
        self.cs = {e: 0 for e in ENGINES}
        self.cgen = {e: 0 for e in ENGINES}
        self.n_dma = n_dma_sems
        self.dmac = [0] * n_dma_sems
        self.acc = {}
        self.waited = {e: {} for e in ENGINES}
        self.meta = {}
        self.stream_sem = {}
        self.next_stream = 0
        self.nwaits = 0

    def reg(self, handle, space):
        ap = handle.ap() if hasattr(handle, "ap") else handle[:]
        shape = list(ap.shape)
        rowlen = 1
        for s in shape[1:]:
            rowlen *= s
        if space == "dram":
            rowlen = rowlen * max(1, shape[0] if shape else 1)
        isz = {F32: 4, BF16: 2, I32: 4}.get(ap.dtype, 4)
        self.meta[ap.name] = (rowlen, isz, space)
        return handle

    def _rect(self, ap):
        if callable(getattr(ap, "ap", None)):
            ap = ap.ap()
        name = ap.name
        rowlen, isz, space = self.meta[name]
        if space == "dram":
            lo = hi = ap.offset
            for st, ct in ap.ap:
                if st >= 0:
                    hi += st * (ct - 1)
                else:
                    lo += st * (ct - 1)
            return name, (0, 1, lo * isz, (hi + 1) * isz)
        bp = ap.base_partition
        p0 = bp() if callable(bp) else bp
        pn = ap.partition_size
        pn = pn() if callable(pn) else pn
        f0 = ap.offset - p0 * rowlen
        lo = hi = f0
        for st, ct in ap.ap[1:]:
            if st >= 0:
                hi += st * (ct - 1)
            else:
                lo += st * (ct - 1)
        b0, b1 = lo * isz, (hi + 1) * isz
        if space == "psum":
            b0 = (b0 // 2048) * 2048
            b1 = ((b1 + 2047) // 2048) * 2048
        return name, (p0, p0 + pn, b0, b1)

    @staticmethod
    def _overlap(r1, r2):
        return not (r1[1] <= r2[0] or r2[1] <= r1[0]
                    or r1[3] <= r2[2] or r2[3] <= r1[2])

    def _need(self, engine, aps_r, aps_w):
        need = {}
        for ap, as_write in [(a, False) for a in aps_r] + [(a, True) for a in aps_w]:
            name, rect = self._rect(ap)
            for (is_w, r, sem, val, eng2, isdma) in self.acc.get(name, ()):
                if not (is_w or as_write):
                    continue
                if not self._overlap(rect, r):
                    continue
                if not isdma and eng2 == engine:
                    if not is_w:
                        continue
                    if engine == "tensor":
                        continue
                    if val <= self.cs[engine] - 8:
                        continue
                if self.waited[engine].get(sem, 0) >= val:
                    continue
                if need.get(sem, 0) < val:
                    need[sem] = val
        return need

    def _record(self, aps_r, aps_w, sem, val, engine, isdma):
        for ap, is_w in [(a, False) for a in aps_r] + [(a, True) for a in aps_w]:
            name, rect = self._rect(ap)
            lst = self.acc.setdefault(name, [])
            lst.append((is_w, rect, sem, val, engine, isdma))
            if len(lst) > 192:
                old, keep = lst[:96], lst[96:]
                floors = {}
                for (w2, r2, s2, v2, e2, d2) in old:
                    k = (s2, e2, d2, w2)
                    if floors.get(k, 0) < v2:
                        floors[k] = v2
                whole = (0, 1 << 20, 0, 1 << 40)
                self.acc[name] = [(w2, whole, s2, v2, e2, d2)
                                  for (s2, e2, d2, w2), v2 in floors.items()] + keep

    def op(self, engine, fn, reads=(), writes=()):
        need = self._need(engine, reads, writes)
        for sem, val in need.items():
            self.waited[engine][sem] = val
        self.nwaits += len(need)
        self.cs[engine] += 1
        if self.cs[engine] > SEM_ROT:
            self.cgen[engine] += 1
            self.cs[engine] = 1
        sem = f"c_{engine}_g{self.cgen[engine]}"
        self._record(reads, writes, sem, self.cs[engine], engine, False)
        self.ops[engine].append((need, fn, (sem, 1)))

    def dma(self, engine, out, in_, stream="w", **kw):
        if stream not in self.stream_sem:
            self.stream_sem[stream] = self.next_stream % self.n_dma
            self.next_stream += 1
        si = self.stream_sem[stream]
        need = self._need(engine, [in_], [out])
        if self.dmac[si] > 0:
            prev = 16 * self.dmac[si]
            if self.waited[engine].get(f"d{si}", 0) < prev and need.get(f"d{si}", 0) < prev:
                need[f"d{si}"] = prev
        for sem, val in need.items():
            self.waited[engine][sem] = val
        self.nwaits += len(need)
        self.dmac[si] += 1
        sem = f"d{si}"
        self._record([in_], [out], sem, 16 * self.dmac[si], engine, True)
        fn = lambda e: e.dma_start(out=out, in_=in_, **kw)
        self.ops[engine].append((need, fn, (sem, 16)))

    def barrier_wait(self, engine):
        need = {}
        for e in ENGINES:
            sem = f"c_{e}_g{self.cgen[e]}"
            if e != engine and self.cs[e] > 0 and self.waited[engine].get(sem, 0) < self.cs[e]:
                need[sem] = self.cs[e]
        for i in range(self.n_dma):
            if self.dmac[i] > 0 and self.waited[engine].get(f"d{i}", 0) < 16 * self.dmac[i]:
                need[f"d{i}"] = 16 * self.dmac[i]
        for sem, val in need.items():
            self.waited[engine][sem] = val
        self.ops[engine].append((need, None, None))

    def emit(self, stack):
        nc = self.nc
        sems = {}
        for e in ENGINES:
            for g in range(self.cgen[e] + 1):
                sems[f"c_{e}_g{g}"] = stack.enter_context(nc.semaphore(f"cs_{e}_{g}"))
        for i in range(self.n_dma):
            sems[f"d{i}"] = stack.enter_context(nc.semaphore(f"ds_{i}"))
        block = stack.enter_context(nc.Block())
        for ename in ENGINES:
            steps = self.ops[ename]
            if not steps:
                continue

            def mk(steps):
                def body(eng):
                    for waits, fn, inc in steps:
                        for sem, val in waits.items():
                            eng.wait_ge(sems[sem], val)
                        if fn is not None:
                            ins = fn(eng)
                            if inc is not None:
                                ins.then_inc(sems[inc[0]], inc[1])
                return body
            getattr(block, ename)(mk(steps))


class Ctx:
    pass


NROW = 7
R_MU, R_VAR, R_R1, R_S, R_B, R_T0, R_T1 = range(7)
R_SP, R_BP = R_T0, R_T1
R_ACC = R_S


def row(c, i):
    return c.ROWS[0:1, i * TC:(i + 1) * TC]


def alloc_common(nc, p, stack):
    c = Ctx()
    c.nc = nc
    c.p = p
    sb = lambda name, shape, dt: p.reg(
        stack.enter_context(nc.sbuf_tensor(name, shape, dt)), "sbuf")
    ps = lambda name, shape, dt: p.reg(
        stack.enter_context(nc.psum_tensor(name, shape, dt)), "psum")

    c.X = sb("X", [128, NTILE * T], F32)
    c.A = sb("A", [128, NTILE * T], BF16)
    c.B = sb("B", [128, NTILE * T], BF16)
    c.C = sb("C", [128, NTILE * T], BF16)
    c.SBC = sb("SBC", [128, TC], F32)
    c.BBC = sb("BBC", [128, TC], F32)
    c.SC16 = sb("SC16", [128, T], BF16)
    c.SC16b = sb("SC16b", [128, T], BF16)
    c.LAT = sb("LAT", [128, 2048], BF16)
    c.ROWS = sb("ROWS", [1, NROW * TC], F32)
    c.WS = sb("WS", [128, 2048], F32)
    c.PVEC = sb("PVEC", [128, 112], F32)
    c.CONSTF = sb("CONSTF", [128, 24], F32)
    c.PVI = sb("PVI", [128, 16], I32)
    c.RNDI = sb("RNDI", [128, TC], I32)
    c.TMP32 = sb("TMP32", [128, 512], F32)
    c.QSC = sb("QSC", [128, 96], F32)
    c.SEL = sb("SEL", [8, 1024], BF16)
    c.ONES = sb("ONES", [128, 8], BF16)
    c.ONESROW = sb("ONESROW", [1, 128], BF16)
    c.ONESROWF = sb("ONESROWF", [1, 128], F32)
    c.IDENT = sb("IDENT", [128, 128], BF16)
    c.IDENTF = sb("IDENTF", [128, 128], F32)

    c.PS_BIG = ps("PS_BIG", [128, 2048], F32)
    c.PS_A = ps("PS_A", [128, 512], F32)
    c.PS_B = ps("PS_B", [128, 512], F32)
    c.PS_T = ps("PS_T", [128, 1024], BF16)
    c.PS_O = ps("PS_O", [128, 512], F32)

    p.op("gpsimd", lambda e: e.memset(c.CONSTF[:, 0:1], 1e-5), writes=[c.CONSTF[:, 0:1]])
    p.op("gpsimd", lambda e: e.memset(c.CONSTF[:, 1:2], 1e-6), writes=[c.CONSTF[:, 1:2]])
    p.op("gpsimd", lambda e: e.memset(c.CONSTF[:, 2:3], -math.log(32.0)), writes=[c.CONSTF[:, 2:3]])
    p.op("gpsimd", lambda e: e.memset(c.CONSTF[:, 3:4], -0.5), writes=[c.CONSTF[:, 3:4]])
    p.op("gpsimd", lambda e: e.memset(c.ONES[:], 1.0), writes=[c.ONES[:]])
    p.op("gpsimd", lambda e: e.memset(c.ONESROW[:], 1.0), writes=[c.ONESROW[:]])
    p.op("gpsimd", lambda e: e.memset(c.ONESROWF[:], 1.0), writes=[c.ONESROWF[:]])
    p.op("gpsimd", lambda e: e.iota(c.PVI[:, 0:8], pattern=[[0, 8]], base=0,
         channel_multiplier=1), writes=[c.PVI[:, 0:8]])
    p.op("vector", lambda e: e.tensor_scalar(out=c.PVI[:, 8:16], in0=c.PVI[:, 0:8],
         scalar1=1, scalar2=None, op0=ALU.bitwise_and),
         reads=[c.PVI[:, 0:8]], writes=[c.PVI[:, 8:16]])
    p.op("vector", lambda e: e.tensor_copy(out=c.CONSTF[:, 4:12], in_=c.PVI[:, 8:16]),
         reads=[c.PVI[:, 8:16]], writes=[c.CONSTF[:, 4:12]])
    p.op("vector", lambda e: e.tensor_copy(out=c.CONSTF[:, 12:20], in_=c.PVI[:, 0:8]),
         reads=[c.PVI[:, 0:8]], writes=[c.CONSTF[:, 12:20]])
    p.op("gpsimd", lambda e: e.memset(c.SEL[:], 0.0), writes=[c.SEL[:]])
    for se in range(8):
        p.op("gpsimd", lambda e, sl=c.SEL[:, se * 128:(se + 1) * 128], b=-se: e.affine_select(
            out=sl, in_=sl, compare_op=ALU.not_equal, fill=1.0,
            base=b, pattern=[[0, 128]], channel_multiplier=1),
            reads=[c.SEL[:]], writes=[c.SEL[:]])
    for t_, nm in ((c.IDENT, "id16"), (c.IDENTF, "idf")):
        p.op("gpsimd", lambda e, t=t_: e.memset(t[:], 0.0), writes=[t_[:]])
        p.op("gpsimd", lambda e, t=t_: e.affine_select(
            out=t[:], in_=t[:], compare_op=ALU.not_equal, fill=1.0,
            base=0, pattern=[[-1, 128]], channel_multiplier=1),
            reads=[t_[:]], writes=[t_[:]])
    return c


def xt(c, i):
    return c.X[:, i * T:(i + 1) * T]


def pt(buf, i):
    return buf[:, i * T:(i + 1) * T]


def emit_ln_prefix(c, out_tiles, variant="plain", gamma=None, beta=None,
                   fold_half_y=False, rr_out=None, chunk_cb=None, tm_f32=False):
    p = c.p
    X = [xt(c, i) for i in range(NTILE)]
    onecol = c.ONES[:, 0:1]
    for tc in range(NTC):
        cs = slice(tc * TC, (tc + 1) * TC)
        for i in range(NTILE):
            src = X[i][:, cs]
            xbs = c.SC16[:, (i % 4) * TC:(i % 4 + 1) * TC]
            sqs = c.SC16b[:, (i % 4) * TC:(i % 4 + 1) * TC]
            p.op("vector", lambda e, a=xbs, b=src: e.tensor_copy(out=a, in_=b),
                 reads=[src], writes=[xbs])
            p.op("scalar", lambda e, a=sqs, b=src: e.activation(a, b, AF.Square),
                 reads=[src], writes=[sqs])
            p.op("tensor", lambda e, o=c.PS_A[0:1, :], l=onecol, r=xbs, s=(i == 0), st=(i == NTILE - 1):
                 e.matmul(o, l, r, start=s, stop=st),
                 reads=[onecol, xbs], writes=[c.PS_A[0:1, :]])
            p.op("tensor", lambda e, o=c.PS_B[0:1, :], l=onecol, r=sqs, s=(i == 0), st=(i == NTILE - 1):
                 e.matmul(o, l, r, start=s, stop=st),
                 reads=[onecol, sqs], writes=[c.PS_B[0:1, :]])
        mu, var, r1 = row(c, R_MU), row(c, R_VAR), row(c, R_R1)
        S, Bv = row(c, R_S), row(c, R_B)
        t0, t1 = row(c, R_T0), row(c, R_T1)
        p.op("vector", lambda e, m=mu: e.tensor_scalar(out=m, in0=c.PS_A[0:1, :],
             scalar1=1.0 / D, scalar2=None, op0=ALU.mult),
             reads=[c.PS_A[0:1, :]], writes=[mu])
        p.op("vector", lambda e, a=t0, m=mu: e.tensor_mul(out=a, in0=m, in1=m),
             reads=[mu], writes=[t0])
        p.op("vector", lambda e, v=var, a=t0: e.scalar_tensor_tensor(out=v,
             in0=c.PS_B[0:1, :], scalar=1.0 / D, in1=a, op0=ALU.mult, op1=ALU.subtract),
             reads=[c.PS_B[0:1, :], t0], writes=[var])
        p.op("scalar", lambda e, a=t1, v=var: e.activation(a, v, AF.Sqrt, bias=c.CONSTF[0:1, 0:1]),
             reads=[var, c.CONSTF[0:1, 0:1]], writes=[t1])
        p.op("vector", lambda e, r=r1, a=t1: e.reciprocal(out=r, in_=a),
             reads=[t1], writes=[r1])
        if variant == "plain":
            p.op("vector", lambda e, s=S, r=r1: e.tensor_copy(out=s, in_=r),
                 reads=[r1], writes=[S])
        elif variant == "l0":
            p.op("vector", lambda e, a=t0, v=var, r=r1: e.tensor_mul(out=a, in0=v, in1=r),
                 reads=[var, r1], writes=[t0])
            p.op("vector", lambda e, a=t0, r=r1: e.tensor_mul(out=a, in0=a, in1=r),
                 reads=[t0, r1], writes=[t0])
            p.op("scalar", lambda e, a=t1, b_=t0: e.activation(a, b_, AF.Sqrt, bias=c.CONSTF[0:1, 0:1]),
                 reads=[t0, c.CONSTF[0:1, 0:1]], writes=[t1])
            p.op("vector", lambda e, a=t1: e.reciprocal(out=a, in_=a),
                 reads=[t1], writes=[t1])
            p.op("vector", lambda e, s=S, r=r1, a=t1: e.tensor_mul(out=s, in0=r, in1=a),
                 reads=[r1, t1], writes=[S])
        if rr_out is not None:
            p.op("vector", lambda e, a=t0, v=var, r=r1: e.tensor_mul(out=a, in0=v, in1=r),
                 reads=[var, r1], writes=[t0])
            p.op("vector", lambda e, a=t0, r=r1: e.tensor_mul(out=a, in0=a, in1=r),
                 reads=[t0, r1], writes=[t0])
            p.op("scalar", lambda e, a=t1, b_=t0: e.activation(a, b_, AF.Sqrt, bias=c.CONSTF[0:1, 1:2]),
                 reads=[t0, c.CONSTF[0:1, 1:2]], writes=[t1])
            rr = rr_out(tc)
            p.op("vector", lambda e, a=rr, b_=t1: e.reciprocal(out=a, in_=b_),
                 reads=[t1], writes=[rr])
            p.op("vector", lambda e, a=rr: e.tensor_scalar(out=a, in0=a,
                 scalar1=1.0 / math.sqrt(D), scalar2=None, op0=ALU.mult),
                 reads=[rr], writes=[rr])
        p.op("vector", lambda e, b_=Bv, m=mu, s=S: e.scalar_tensor_tensor(out=b_,
             in0=m, scalar=-1.0, in1=s, op0=ALU.mult, op1=ALU.mult),
             reads=[mu, S], writes=[Bv])
        emit_bcast_chunk(c, S, c.SBC, c.PS_A)
        emit_bcast_chunk(c, Bv, c.BBC, c.PS_B)
        for i in range(NTILE):
            src = X[i][:, cs]
            if tm_f32:
                tmp = c.TMP32[:, 0:TC]
                p.op("vector", lambda e, a=tmp, x=src: e.tensor_mul(out=a, in0=x, in1=c.SBC[:, :]),
                     reads=[src, c.SBC[:, :]], writes=[tmp])
                p.op("gpsimd", lambda e, a=tmp: e.tensor_tensor(out=a, in0=a, in1=c.BBC[:, :], op=ALU.add),
                     reads=[tmp, c.BBC[:, :]], writes=[tmp])
                for j in range(4):
                    tt_ = tc * 4 + j
                    ps = (c.PS_A, c.PS_B)[j % 2]
                    p.op("tensor", lambda e, o=ps[:, 0:128], i_=tmp[:, j * 128:(j + 1) * 128]:
                         e.transpose(o, i_, c.IDENTF[:]),
                         reads=[tmp[:, j * 128:(j + 1) * 128], c.IDENTF[:]], writes=[ps[:, 0:128]])
                    dstj = out_tiles[tt_][:, i * 128:(i + 1) * 128]
                    p.op("vector", lambda e, a=dstj, b_=ps[:, 0:128]: e.tensor_copy(out=a, in_=b_),
                         reads=[ps[:, 0:128]], writes=[dstj])
                continue
            dst = out_tiles[i][:, cs]
            t16 = c.SC16[:, (i % 4) * TC:(i % 4 + 1) * TC]
            p.op("vector", lambda e, a=t16, x=src: e.tensor_mul(out=a, in0=x, in1=c.SBC[:, :]),
                 reads=[src, c.SBC[:, :]], writes=[t16])
            if gamma is None:
                p.op("gpsimd", lambda e, a=dst, x=t16: e.tensor_tensor(out=a, in0=x,
                     in1=c.BBC[:, :], op=ALU.add), reads=[t16, c.BBC[:, :]], writes=[dst])
            else:
                tf = c.SC16b[:, (i % 4) * TC:(i % 4 + 1) * TC]
                p.op("gpsimd", lambda e, a=tf, x=t16: e.tensor_tensor(out=a, in0=x,
                     in1=c.BBC[:, :], op=ALU.add), reads=[t16, c.BBC[:, :]], writes=[tf])
                g = gamma(i)
                bt = beta(i)
                p.op("vector", lambda e, a=dst, x=tf, gg=g, bb=bt:
                     e.tensor_scalar(out=a, in0=x, scalar1=gg, scalar2=bb,
                                     op0=ALU.mult, op1=ALU.add),
                     reads=[tf, g, bt], writes=[dst])
        if chunk_cb is not None:
            chunk_cb(tc)
        if fold_half_y:
            Sp, Bp = row(c, R_SP), row(c, R_BP)
            p.op("vector", lambda e, s=Sp, r=r1: e.tensor_scalar(out=s, in0=r,
                 scalar1=0.5, scalar2=1.0, op0=ALU.mult, op1=ALU.add),
                 reads=[r1], writes=[Sp])
            p.op("vector", lambda e, b_=Bp, m=mu, r=r1: e.scalar_tensor_tensor(out=b_,
                 in0=m, scalar=-0.5, in1=r, op0=ALU.mult, op1=ALU.mult),
                 reads=[mu, r1], writes=[Bp])
            emit_bcast_chunk(c, Sp, c.SBC, c.PS_A)
            emit_bcast_chunk(c, Bp, c.BBC, c.PS_B)
            for i in range(NTILE):
                xs = X[i][:, cs]
                p.op("vector", lambda e, x=xs: e.tensor_mul(out=x, in0=x, in1=c.SBC[:, :]),
                     reads=[xs, c.SBC[:, :]], writes=[xs])
                p.op("vector", lambda e, x=xs: e.tensor_add(out=x, in0=x, in1=c.BBC[:, :]),
                     reads=[xs, c.BBC[:, :]], writes=[xs])


def emit_bcast_chunk(c, src_row, dst, psum):
    p = c.p
    if callable(getattr(dst, "ap", None)):
        dst = dst.ap()
    p.op("tensor", lambda e, o=psum[:, :], r=src_row:
         e.matmul(o, c.ONESROWF[0:1, :], r, start=True, stop=True),
         reads=[c.ONESROWF[0:1, :], src_row], writes=[psum[:, :]])
    p.op("vector", lambda e, a=dst, b=psum[:, :]: e.tensor_copy(out=a, in_=b),
         reads=[psum[:, :]], writes=[dst])


def emit_l0_ffn(c, w, do_dequant):
    p = c.p
    X = [xt(c, i) for i in range(NTILE)]

    gam = c.PVEC[:, 0:8]
    bet = c.PVEC[:, 8:16]
    b1 = c.PVEC[:, 16:48]
    qb2 = c.PVEC[:, 48:56]
    p.dma("sync", gam, w["gamma"], stream="misc")
    p.dma("sync", bet, w["beta"], stream="misc")
    p.dma("sync", b1, w["b1"], stream="misc")
    p.dma("sync", c.PVEC[:, 56:64], w["b2"], stream="misc")
    p.op("vector", lambda e: e.tensor_scalar(out=qb2, in0=c.PVEC[:, 56:64],
         scalar1=0.25, scalar2=None, op0=ALU.mult),
         reads=[c.PVEC[:, 56:64]], writes=[qb2])

    emit_ln_prefix(c, [pt(c.A, i) for i in range(NTILE)], variant="l0",
                   gamma=lambda i: gam[:, i:i + 1], beta=lambda i: bet[:, i:i + 1],
                   fold_half_y=True)

    if do_dequant:
        _dequant_matrix(c, w["q1"], w["s1"], w["w1b"], K=1024, N=4096, tag="dq1")
        _dequant_matrix(c, w["q2"], w["s2"], w["w2b"], K=4096, N=1024, tag="dq2")

    for i in range(NTILE):
        p.op("vector", lambda e, x=X[i], bb=qb2[:, i:i + 1]:
             e.tensor_scalar(out=x, in0=x, scalar1=bb, scalar2=None, op0=ALU.add),
             reads=[X[i], qb2[:, i:i + 1]], writes=[X[i]])

    WS16 = c.WS[:].bitcast(BF16)
    for tci in range(NTC):
        ts_ = slice(tci * TC, (tci + 1) * TC)
        for m in range(32):
            buf = m % 2
            wslot = WS16[:, buf * 1024:(buf + 1) * 1024]
            p.dma("gpsimd", wslot.rearrange("p (g m) -> p g m", m=128),
                  w["w1b"].rearrange("(g p) n -> p g n", p=128)[:, :, m * 128:(m + 1) * 128],
                  stream=f"w1_{buf}")
            ps = (c.PS_A, c.PS_B)[m % 2]
            for g in range(NTILE):
                lhs = wslot[:, g * 128:(g + 1) * 128]
                rhs = pt(c.A, g)[:, ts_]
                p.op("tensor", lambda e, o=ps[:, :], l=lhs, r=rhs, s=(g == 0), st=(g == NTILE - 1):
                     e.matmul(o, l, r, start=s, stop=st),
                     reads=[lhs, rhs], writes=[ps[:, :]])
            ydst = c.B[:, m * TC:(m + 1) * TC]
            bia = b1[:, m:m + 1]
            p.op("scalar", lambda e, a=ydst, b_=ps[:, :], bb=bia:
                 e.activation(a, b_, AF.Silu, bias=bb),
                 reads=[ps[:, :], bia], writes=[ydst])
        for mo in range(NTILE):
            ps = (c.PS_A, c.PS_B)[mo % 2]
            for quar in range(4):
                buf = quar % 2
                wslot = WS16[:, 2048 + buf * 1024:2048 + (buf + 1) * 1024]
                p.dma("gpsimd", wslot.rearrange("p (g m) -> p g m", m=128),
                      w["w2b"][quar * 1024:(quar + 1) * 1024, :]
                      .rearrange("(g p) n -> p g n", p=128)[:, :, mo * 128:(mo + 1) * 128],
                      stream=f"w2_{buf}")
                for g in range(8):
                    gabs = quar * 8 + g
                    lhs = wslot[:, g * 128:(g + 1) * 128]
                    rhs = c.B[:, gabs * TC:(gabs + 1) * TC]
                    p.op("tensor", lambda e, o=ps[:, :], l=lhs, r=rhs, s=(gabs == 0), st=(gabs == 31):
                         e.matmul(o, l, r, start=s, stop=st),
                         reads=[lhs, rhs], writes=[ps[:, :]])
            xdst = X[mo][:, ts_]
            p.op("vector", lambda e, x=xdst, b_=ps[:, :]:
                 e.scalar_tensor_tensor(out=x, in0=b_, scalar=0.25, in1=x,
                                        op0=ALU.mult, op1=ALU.add),
                 reads=[ps[:, :], xdst], writes=[xdst])


def _dequant_matrix(c, q_ap, s_ap, out_ap, K, N, tag):
    p = c.p
    NG = K // 128
    NN = N // 512
    for g in range(NG):
        for j in range(NN):
            buf = (g * NN + j) % 2
            qs = c.WS[:, buf * 512:(buf + 1) * 512]
            p.dma("gpsimd", qs.bitcast(I32),
                  q_ap[g * 128:(g + 1) * 128, j * 512:(j + 1) * 512],
                  stream=f"{tag}_{buf}")
            srow = row(c, R_S if buf == 0 else R_B)
            p.dma("sync", srow, s_ap[g:g + 1, j * 512:(j + 1) * 512], stream=f"{tag}s")
            pss = (c.PS_A, c.PS_B)[buf]
            p.op("tensor", lambda e, o=pss[:, :], r=srow:
                 e.matmul(o, c.ONESROWF[0:1, :], r, start=True, stop=True),
                 reads=[c.ONESROWF[0:1, :], srow], writes=[pss[:, :]])
            tq = c.WS[:, 1024 + buf * 512:1024 + (buf + 1) * 512]
            p.op("scalar", lambda e, a=tq, b_=qs.bitcast(I32):
                 e.activation(a, b_, AF.Copy, bias=-128.0),
                 reads=[qs], writes=[tq])
            ob = c.SC16[:, (buf) * 1024:(buf) * 1024 + 512]
            p.op("vector", lambda e, a=ob, x=tq, s_=pss[:, :]:
                 e.tensor_tensor(out=a, in0=x, in1=s_, op=ALU.mult),
                 reads=[tq, pss[:, :]], writes=[ob])
            p.dma("scalar", out_ap[g * 128:(g + 1) * 128, j * 512:(j + 1) * 512], ob,
                  stream=f"{tag}o")


def emit_attn_core(c, NH, DH, causal, q_ap, k_ap, vT_ap, o_dst, score_hook=None):
    p = c.p
    HPG = 512 // DH
    NGRP = NH // HPG
    SUMS = c.PVEC[:, 64:80]
    RECS = c.PVEC[:, 80:96]
    NMAX = c.PVEC[:, 104:105]
    for qc in range(16):
        ext = (qc + 1) * 128 if causal else T
        ntk = ext // 128
        for grp in range(NGRP):
            for hh in range(HPG):
                h = grp * HPG + hh
                qh = q_ap(h, qc)
                ext1 = min(ext, 1024) if score_hook is None else ext
                NMX2 = c.PVEC[:, 108:110]
                for b in range((ext1 + 511) // 512):
                    w0 = b * 512
                    w1 = min(ext1, w0 + 512)
                    rhs = k_ap(h, w0, w1)
                    o = c.PS_BIG[:, w0:w1]
                    p.op("tensor", lambda e, oo=o, l=qh, r=rhs:
                         e.matmul(oo, l, r, start=True, stop=True),
                         reads=[qh, rhs], writes=[o])
                if score_hook is not None:
                    score_hook(h, qc, ext)
                p.op("vector", lambda e, a=NMX2[:, 0:1], b_=c.PS_BIG[:, 0:ext1]:
                     e.tensor_reduce(out=a, in_=b_, axis=AX.X, op=ALU.max),
                     reads=[c.PS_BIG[:, 0:ext1]], writes=[NMX2[:, 0:1]])
                if ext > ext1:
                    for b in range((ext - 1024 + 511) // 512):
                        w0 = 1024 + b * 512
                        w1 = min(ext, w0 + 512)
                        rhs = k_ap(h, w0, w1)
                        o = c.PS_BIG[:, w0:w1]
                        p.op("tensor", lambda e, oo=o, l=qh, r=rhs:
                             e.matmul(oo, l, r, start=True, stop=True),
                             reads=[qh, rhs], writes=[o])
                    p.op("vector", lambda e, a=NMX2[:, 1:2], b_=c.PS_BIG[:, 1024:ext]:
                         e.tensor_reduce(out=a, in_=b_, axis=AX.X, op=ALU.max),
                         reads=[c.PS_BIG[:, 1024:ext]], writes=[NMX2[:, 1:2]])
                    p.op("vector", lambda e, a=NMAX, b_=NMX2:
                         e.tensor_reduce(out=a, in_=b_, axis=AX.X, op=ALU.max, negate=True),
                         reads=[NMX2], writes=[NMAX])
                else:
                    p.op("vector", lambda e, a=NMAX, b_=NMX2[:, 0:1]:
                         e.tensor_reduce(out=a, in_=b_, axis=AX.X, op=ALU.max, negate=True),
                         reads=[NMX2[:, 0:1]], writes=[NMAX])
                Pbuf = c.SC16
                lo_end = ext - 128 if causal else ext
                for w0 in range(0, lo_end, 1024):
                    w1 = min(lo_end, w0 + 1024)
                    p.op("scalar", lambda e, a=Pbuf[:, w0:w1], b_=c.PS_BIG[:, w0:w1]:
                         e.activation(a, b_, AF.Exp, bias=NMAX),
                         reads=[c.PS_BIG[:, w0:w1], NMAX], writes=[Pbuf[:, w0:w1]])
                if causal:
                    dtile = Pbuf[:, ext - 128:ext]
                    p.op("scalar", lambda e, a=dtile, b_=c.PS_BIG[:, ext - 128:ext]:
                         e.activation(a, b_, AF.Exp, bias=NMAX),
                         reads=[c.PS_BIG[:, ext - 128:ext], NMAX], writes=[dtile])
                    p.op("gpsimd", lambda e, a=dtile: e.affine_select(
                         out=a, in_=a, compare_op=ALU.is_ge, fill=0.0,
                         base=0, pattern=[[-1, 128]], channel_multiplier=1),
                         reads=[dtile], writes=[dtile])
                sacc = SUMS[:, h:h + 1]
                p.op("vector", lambda e, a=sacc, b_=Pbuf[:, 0:ext]:
                     e.tensor_reduce(out=a, in_=b_, axis=AX.X, op=ALU.add),
                     reads=[Pbuf[:, 0:ext]], writes=[sacc])
                for t0 in range(0, ntk, 8):
                    t1 = min(ntk, t0 + 8)
                    for t in range(t0, t1):
                        ptile = Pbuf[:, t * 128:(t + 1) * 128]
                        pslot = c.PS_T[:, (t - t0) * 128:(t - t0 + 1) * 128]
                        p.op("tensor", lambda e, o=pslot, i_=ptile: e.transpose(o, i_, c.IDENT[:]),
                             reads=[ptile, c.IDENT[:]], writes=[pslot])
                    nb8 = (t1 - t0) * 128
                    pts8 = c.SC16b[:, 0:nb8]
                    p.op("vector", lambda e, a=pts8, b_=c.PS_T[:, 0:nb8]: e.tensor_copy(out=a, in_=b_),
                         reads=[c.PS_T[:, 0:nb8]], writes=[pts8])
                    for t in range(t0, t1):
                        pts = pts8[:, (t - t0) * 128:(t - t0 + 1) * 128]
                        vts = vT_ap(h, t)
                        o = c.PS_O[:, hh * DH:(hh + 1) * DH]
                        p.op("tensor", lambda e, oo=o, l=pts, r=vts, s=(t == 0), st=(t == ntk - 1):
                             e.matmul(oo, l, r, start=s, stop=st),
                             reads=[pts, vts], writes=[o])
            p.op("vector", lambda e, a=RECS: e.reciprocal(out=a, in_=SUMS),
                 reads=[SUMS], writes=[RECS])
            o16 = c.SC16b[:, 1024 + grp * 512:1024 + (grp + 1) * 512]
            for hh in range(HPG):
                h = grp * HPG + hh
                p.op("vector", lambda e, a=o16[:, hh * DH:(hh + 1) * DH],
                     b_=c.PS_O[:, hh * DH:(hh + 1) * DH], r=RECS[:, h:h + 1]:
                     e.tensor_scalar(out=a, in0=b_, scalar1=r, scalar2=None, op0=ALU.mult),
                     reads=[c.PS_O[:, hh * DH:(hh + 1) * DH], RECS[:, h:h + 1]],
                     writes=[o16[:, hh * DH:(hh + 1) * DH]])
        for g in range(NTILE):
            srcb = c.SC16b[:, 1024 + g * 128:1024 + (g + 1) * 128]
            pslot = c.PS_T[:, (g % 8) * 128:(g % 8) * 128 + 128]
            p.op("tensor", lambda e, o=pslot, i_=srcb: e.transpose(o, i_, c.IDENT[:]),
                 reads=[srcb, c.IDENT[:]], writes=[pslot])
            dst = o_dst(g, qc)
            p.op("vector", lambda e, a=dst, b_=pslot: e.tensor_copy(out=a, in_=b_),
                 reads=[pslot], writes=[dst])


def emit_l0_attn(c, w):
    p = c.p
    X = [xt(c, i) for i in range(NTILE)]
    emit_ln_prefix(c, [pt(c.A, i) for i in range(NTILE)], variant="plain")
    WS16 = c.WS[:].bitcast(BF16)

    bv = c.PVEC[:, 0:8]
    bo = c.PVEC[:, 8:16]
    p.dma("sync", bv, w["wv_b"], stream="misc")
    p.dma("sync", bo, w["out_b"], stream="misc")

    p.dma("gpsimd", c.LAT[:, 0:1024], w["watt"], stream="watt")

    for m in range(NTILE):
        buf = m % 2
        wslot = WS16[:, buf * 1024:(buf + 1) * 1024]
        p.dma("gpsimd", wslot.rearrange("p (g m) -> p g m", m=128),
              w["wvT"].rearrange("(g p) n -> p g n", p=128)[:, :, m * 128:(m + 1) * 128],
              stream=f"wv{buf}")
        for tc in range(NTC):
            ps = (c.PS_A, c.PS_B)[tc % 2]
            for g in range(NTILE):
                lhs = wslot[:, g * 128:(g + 1) * 128]
                rhs = pt(c.A, g)[:, tc * TC:(tc + 1) * TC]
                p.op("tensor", lambda e, o=ps[:, :], l=lhs, r=rhs, s=(g == 0), st=(g == 7):
                     e.matmul(o, l, r, start=s, stop=st),
                     reads=[lhs, rhs], writes=[ps[:, :]])
            dst = c.SC16[:, tc * TC:(tc + 1) * TC]
            p.op("vector", lambda e, a=dst, b_=ps[:, :], bb=bv[:, m:m + 1]:
                 e.tensor_scalar(out=a, in0=b_, scalar1=bb, scalar2=None, op0=ALU.add),
                 reads=[ps[:, :], bv[:, m:m + 1]], writes=[dst])
        for h2 in range(2):
            h = 2 * m + h2
            for t in range(16):
                srct = c.SC16[64 * h2:64 * h2 + 64, t * 128:(t + 1) * 128]
                ident = c.IDENT[64 * h2:64 * h2 + 64, 64 * h2:64 * h2 + 64]
                dstp = c.PS_T[:, t * 64:(t + 1) * 64]
                p.op("tensor", lambda e, o=dstp, i_=srct, ii=ident: e.transpose(o, i_, ii),
                     reads=[srct, ident], writes=[dstp])
            dstv = c.B[:, h * 1024:(h + 1) * 1024]
            p.op("vector", lambda e, a=dstv, b_=c.PS_T[:, :]: e.tensor_copy(out=a, in_=b_),
                 reads=[c.PS_T[:, :]], writes=[dstv])

    for j in range(NTILE):
        wblk = c.LAT[:, j * 128:(j + 1) * 128]
        for tc in range(NTC):
            ps = (c.PS_A, c.PS_B)[tc % 2]
            rhs = pt(c.A, j)[:, tc * TC:(tc + 1) * TC]
            p.op("tensor", lambda e, o=ps[:, :], l=wblk, r=rhs:
                 e.matmul(o, l, r, start=True, stop=True),
                 reads=[wblk, rhs], writes=[ps[:, :]])
            dst = pt(c.C, j)[:, tc * TC:(tc + 1) * TC]
            p.op("scalar", lambda e, a=dst, b_=ps[:, :]: e.activation(a, b_, AF.Copy, scale=0.125),
                 reads=[ps[:, :]], writes=[dst])

    emit_attn_core(
        c, NH=16, DH=64, causal=False,
        q_ap=lambda h, qc: pt(c.C, h // 2)[64 * (h % 2):64 * (h % 2) + 64, qc * 128:(qc + 1) * 128],
        k_ap=lambda h, w0, w1: pt(c.A, h // 2)[64 * (h % 2):64 * (h % 2) + 64, w0:w1],
        vT_ap=lambda h, t: c.B[:, h * 1024 + t * 64:h * 1024 + t * 64 + 64],
        o_dst=lambda g, qc: pt(c.C, g)[:, qc * 128:(qc + 1) * 128])

    for m in range(NTILE):
        buf = m % 2
        wslot = WS16[:, buf * 1024:(buf + 1) * 1024]
        p.dma("gpsimd", wslot.rearrange("p (g m) -> p g m", m=128),
              w["outwT"].rearrange("(g p) n -> p g n", p=128)[:, :, m * 128:(m + 1) * 128],
              stream=f"wo{buf}")
        for tc in range(NTC):
            ps = (c.PS_A, c.PS_B)[tc % 2]
            for g in range(NTILE):
                lhs = wslot[:, g * 128:(g + 1) * 128]
                rhs = pt(c.C, g)[:, tc * TC:(tc + 1) * TC]
                p.op("tensor", lambda e, o=ps[:, :], l=lhs, r=rhs, s=(g == 0), st=(g == 7):
                     e.matmul(o, l, r, start=s, stop=st),
                     reads=[lhs, rhs], writes=[ps[:, :]])
            xdst = X[m][:, tc * TC:(tc + 1) * TC]
            p.op("vector", lambda e, x=xdst, b_=ps[:, :]:
                 e.scalar_tensor_tensor(out=x, in0=b_, scalar=1.0, in1=x,
                                        op0=ALU.mult, op1=ALU.add),
                 reads=[ps[:, :], xdst], writes=[xdst])
    for m in range(NTILE):
        p.op("vector", lambda e, x=X[m], bb=bo[:, m:m + 1]:
             e.tensor_scalar(out=x, in0=x, scalar1=bb, scalar2=None, op0=ALU.add),
             reads=[X[m], bo[:, m:m + 1]], writes=[X[m]])


def emit_l1_attn(c, w):
    p = c.p
    X = [xt(c, i) for i in range(NTILE)]
    emit_ln_prefix(c, [pt(c.A, i) for i in range(NTILE)], variant="plain")
    WS16 = c.WS[:].bitcast(BF16)

    def proj_stream(wap, dst_fn, tag, nk=NTILE, evac=None):
        NM = NTILE
        for m in range(NM):
            buf = m % 2
            wslot = WS16[:, buf * 1024:(buf + 1) * 1024]
            p.dma("gpsimd", wslot.rearrange("p (g m) -> p g m", m=128),
                  wap.rearrange("(g p) n -> p g n", p=128)[:, :, m * 128:(m + 1) * 128],
                  stream=f"{tag}{buf}")
            for tc in range(NTC):
                ps = (c.PS_A, c.PS_B)[tc % 2]
                for g in range(nk):
                    lhs = wslot[:, g * 128:(g + 1) * 128]
                    rhs = pt(c.A, g)[:, tc * TC:(tc + 1) * TC]
                    p.op("tensor", lambda e, o=ps[:, :], l=lhs, r=rhs, s=(g == 0), st=(g == nk - 1):
                         e.matmul(o, l, r, start=s, stop=st),
                         reads=[lhs, rhs], writes=[ps[:, :]])
                if evac is None:
                    dst = dst_fn(m, tc)
                    p.op("scalar", lambda e, a=dst, b_=ps[:, :]: e.activation(a, b_, AF.Copy),
                         reads=[ps[:, :]], writes=[dst])
                else:
                    evac(m, tc, ps)

    proj_stream(w["wk"], lambda m, tc: pt(c.B, m)[:, tc * TC:(tc + 1) * TC], "k")
    proj_stream(w["wv"], lambda m, tc: pt(c.C, m)[:, tc * TC:(tc + 1) * TC], "v")
    for m in range(NTILE):
        bm, cm = pt(c.B, m), pt(c.C, m)
        p.op("vector", lambda e, a=bm, b_=cm: e.tensor_add(out=a, in0=a, in1=b_),
             reads=[bm, cm], writes=[bm])
    p.dma("gpsimd", WS16[:, 0:2048].rearrange("p (g m) -> p g m", m=256),
          w["wdkv"].rearrange("(g p) n -> p g n", p=128), stream="wdkv")
    lat_t = [c.LAT[:, 0:2048], c.SC16[:, 0:2048]]
    for m in range(2):
        for tc in range(NTC):
            ps = (c.PS_A, c.PS_B)[tc % 2]
            for g in range(NTILE):
                lhs = WS16[:, g * 256 + m * 128:g * 256 + (m + 1) * 128]
                rhs = pt(c.B, g)[:, tc * TC:(tc + 1) * TC]
                p.op("tensor", lambda e, o=ps[:, :], l=lhs, r=rhs, s=(g == 0), st=(g == 7):
                     e.matmul(o, l, r, start=s, stop=st),
                     reads=[lhs, rhs], writes=[ps[:, :]])
            dst = lat_t[m][:, tc * TC:(tc + 1) * TC]
            p.op("scalar", lambda e, a=dst, b_=ps[:, :]: e.activation(a, b_, AF.Copy),
                 reads=[ps[:, :]], writes=[dst])
    p.dma("gpsimd", WS16[:, 2048:4096].rearrange("p (g m) -> p g m", m=1024),
          w["wupk"].rearrange("(g p) n -> p g n", p=128), stream="wupk")
    for m in range(NTILE):
        for tc in range(NTC):
            ps = (c.PS_A, c.PS_B)[tc % 2]
            for g in range(2):
                lhs = WS16[:, 2048 + g * 1024 + m * 128:2048 + g * 1024 + (m + 1) * 128]
                rhs = lat_t[g][:, tc * TC:(tc + 1) * TC]
                p.op("tensor", lambda e, o=ps[:, :], l=lhs, r=rhs, s=(g == 0), st=(g == 1):
                     e.matmul(o, l, r, start=s, stop=st),
                     reads=[lhs, rhs], writes=[ps[:, :]])
            dst = pt(c.C, m)[:, tc * TC:(tc + 1) * TC]
            p.op("scalar", lambda e, a=dst, b_=ps[:, :]: e.activation(a, b_, AF.Copy),
                 reads=[ps[:, :]], writes=[dst])
    p.dma("gpsimd", WS16[:, 2048:4096].rearrange("p (g m) -> p g m", m=1024),
          w["wupv"].rearrange("(g p) n -> p g n", p=128), stream="wupv")
    for m in range(NTILE):
        for tc in range(NTC):
            ps = (c.PS_A, c.PS_B)[tc % 2]
            for g in range(2):
                lhs = WS16[:, 2048 + g * 1024 + m * 128:2048 + g * 1024 + (m + 1) * 128]
                rhs = lat_t[g][:, tc * TC:(tc + 1) * TC]
                p.op("tensor", lambda e, o=ps[:, :], l=lhs, r=rhs, s=(g == 0), st=(g == 1):
                     e.matmul(o, l, r, start=s, stop=st),
                     reads=[lhs, rhs], writes=[ps[:, :]])
            dst = c.SC16b[:, tc * TC:(tc + 1) * TC]
            p.op("scalar", lambda e, a=dst, b_=ps[:, :]: e.activation(a, b_, AF.Copy),
                 reads=[ps[:, :]], writes=[dst])
        for h2 in range(2):
            h = 2 * m + h2
            for t in range(16):
                srct = c.SC16b[64 * h2:64 * h2 + 64, t * 128:(t + 1) * 128]
                ident = c.IDENT[64 * h2:64 * h2 + 64, 64 * h2:64 * h2 + 64]
                dstp = c.PS_T[:, t * 64:(t + 1) * 64]
                p.op("tensor", lambda e, o=dstp, i_=srct, ii=ident: e.transpose(o, i_, ii),
                     reads=[srct, ident], writes=[dstp])
            dstv = pt(c.B, h // 2)[:, (h % 2) * 1024:(h % 2) * 1024 + 1024] if False else \
                c.B[:, h * 1024:(h + 1) * 1024]
            p.op("vector", lambda e, a=dstv, b_=c.PS_T[:, :]: e.tensor_copy(out=a, in_=b_),
                 reads=[c.PS_T[:, :]], writes=[dstv])

    for tc in range(NTC):
        for m in range(NTILE):
            buf = m % 2
            wslot = WS16[:, buf * 1024:(buf + 1) * 1024]
            p.dma("gpsimd", wslot.rearrange("p (g m) -> p g m", m=128),
                  w["wq"].rearrange("(g p) n -> p g n", p=128)[:, :, m * 128:(m + 1) * 128],
                  stream=f"wq{buf}")
            ps = (c.PS_A, c.PS_B)[m % 2]
            for g in range(NTILE):
                lhs = wslot[:, g * 128:(g + 1) * 128]
                rhs = pt(c.A, g)[:, tc * TC:(tc + 1) * TC]
                p.op("tensor", lambda e, o=ps[:, :], l=lhs, r=rhs, s=(g == 0), st=(g == 7):
                     e.matmul(o, l, r, start=s, stop=st),
                     reads=[lhs, rhs], writes=[ps[:, :]])
            qtmp = (c.SC16 if m < 4 else c.SC16b)[:, (m % 4) * TC:(m % 4 + 1) * TC]
            p.op("scalar", lambda e, a=qtmp, b_=ps[:, :]: e.activation(a, b_, AF.Copy),
                 reads=[ps[:, :]], writes=[qtmp])
        for m in range(NTILE):
            qtmp = (c.SC16 if m < 4 else c.SC16b)[:, (m % 4) * TC:(m % 4 + 1) * TC]
            dst = pt(c.A, m)[:, tc * TC:(tc + 1) * TC]
            p.op("vector", lambda e, a=dst, b_=qtmp: e.tensor_copy(out=a, in_=b_),
                 reads=[qtmp], writes=[dst])

    LAM = 0.01
    last_qc = [-1]

    def hook(h, qc, ext):
        if last_qc[0] != qc:
            last_qc[0] = qc
            p.op("gpsimd", lambda e: e.iota(c.LAT[:, 0:T], pattern=[[1, T]],
                 base=-qc * 128, channel_multiplier=-1,
                 allow_small_or_imprecise_dtypes=True),
                 writes=[c.LAT[:, 0:T]])
            p.op("scalar", lambda e: e.activation(c.LAT[:, 0:T], c.LAT[:, 0:T], AF.Abs),
                 reads=[c.LAT[:, 0:T]], writes=[c.LAT[:, 0:T]])
            p.op("scalar", lambda e: e.activation(c.LAT[:, 0:T], c.LAT[:, 0:T], AF.Exp,
                 scale=-LAM, bias=c.CONSTF[:, 2:3]),
                 reads=[c.LAT[:, 0:T], c.CONSTF[:, 2:3]], writes=[c.LAT[:, 0:T]])
        p.op("vector", lambda e: e.tensor_tensor(out=c.PS_BIG[:, 0:T], in0=c.PS_BIG[:, 0:T],
             in1=c.LAT[:, 0:T], op=ALU.mult),
             reads=[c.PS_BIG[:, 0:T], c.LAT[:, 0:T]], writes=[c.PS_BIG[:, 0:T]])

    emit_attn_core(
        c, NH=16, DH=64, causal=False,
        q_ap=lambda h, qc: pt(c.A, h // 2)[64 * (h % 2):64 * (h % 2) + 64, qc * 128:(qc + 1) * 128],
        k_ap=lambda h, w0, w1: pt(c.C, h // 2)[64 * (h % 2):64 * (h % 2) + 64, w0:w1],
        vT_ap=lambda h, t: c.B[:, h * 1024 + t * 64:h * 1024 + t * 64 + 64],
        o_dst=lambda g, qc: pt(c.A, g)[:, qc * 128:(qc + 1) * 128],
        score_hook=hook)

    for m in range(NTILE):
        buf = m % 2
        wslot = WS16[:, buf * 1024:(buf + 1) * 1024]
        p.dma("gpsimd", wslot.rearrange("p (g m) -> p g m", m=128),
              w["wo"].rearrange("(g p) n -> p g n", p=128)[:, :, m * 128:(m + 1) * 128],
              stream=f"wo{buf}")
        for tc in range(NTC):
            ps = (c.PS_A, c.PS_B)[tc % 2]
            for g in range(NTILE):
                lhs = wslot[:, g * 128:(g + 1) * 128]
                rhs = pt(c.A, g)[:, tc * TC:(tc + 1) * TC]
                p.op("tensor", lambda e, o=ps[:, :], l=lhs, r=rhs, s=(g == 0), st=(g == 7):
                     e.matmul(o, l, r, start=s, stop=st),
                     reads=[lhs, rhs], writes=[ps[:, :]])
            xdst = X[m][:, tc * TC:(tc + 1) * TC]
            p.op("vector", lambda e, x=xdst, b_=ps[:, :]:
                 e.scalar_tensor_tensor(out=x, in0=b_, scalar=1.0, in1=x,
                                        op0=ALU.mult, op1=ALU.add),
                 reads=[ps[:, :], xdst], writes=[xdst])


def emit_l2_attn(c, w):
    p = c.p
    X = [xt(c, i) for i in range(NTILE)]
    emit_ln_prefix(c, [pt(c.A, i) for i in range(NTILE)], variant="plain")
    WS16 = c.WS[:].bitcast(BF16)
    SCALE = 1.0 / math.sqrt(128.0)

    iop = c.CONSTF[:, 12:13]
    par = c.CONSTF[:, 4:5]
    fv = c.PVEC[:, 2:3]
    qv = c.PVEC[:, 3:4]
    t0 = c.PVEC[:, 6:7]
    p.op("vector", lambda e: e.tensor_scalar(out=t0, in0=par, scalar1=-0.5, scalar2=None,
         op0=ALU.mult), reads=[par], writes=[t0])
    p.op("vector", lambda e: e.scalar_tensor_tensor(out=t0, in0=iop, scalar=0.5, in1=t0,
         op0=ALU.mult, op1=ALU.add), reads=[iop, t0], writes=[t0])
    p.op("scalar", lambda e: e.activation(fv, t0, AF.Exp, scale=-math.log(1e4) / 64.0),
         reads=[t0], writes=[fv])
    p.op("vector", lambda e: e.tensor_scalar(out=fv, in0=fv, scalar1=1.0 / (2 * math.pi),
         scalar2=None, op0=ALU.mult), reads=[fv], writes=[fv])
    p.op("vector", lambda e: e.tensor_scalar(out=qv, in0=par, scalar1=0.25, scalar2=None,
         op0=ALU.mult), reads=[par], writes=[qv])

    for tc in range(NTC):
        cs = slice(tc * TC, (tc + 1) * TC)
        iob = c.SBC[:, :]
        p.op("gpsimd", lambda e, a=iob, b0=tc * TC: e.iota(a, pattern=[[1, TC]], base=b0,
             channel_multiplier=0, allow_small_or_imprecise_dtypes=True), writes=[iob])
        p.op("vector", lambda e, a=c.BBC[:, :], b_=iob: e.tensor_scalar(out=a, in0=b_,
             scalar1=fv, scalar2=qv, op0=ALU.mult, op1=ALU.add),
             reads=[iob, fv, qv], writes=[c.BBC[:, :]])
        ri = c.RNDI[:, :]
        p.op("vector", lambda e, a=ri, b_=c.BBC[:, :]: e.tensor_copy(out=a, in_=b_),
             reads=[c.BBC[:, :]], writes=[ri])
        p.op("vector", lambda e, a=c.SBC[:, :], b_=ri: e.tensor_copy(out=a, in_=b_),
             reads=[ri], writes=[c.SBC[:, :]])
        p.op("vector", lambda e: e.tensor_sub(out=c.BBC[:, :], in0=c.BBC[:, :], in1=c.SBC[:, :]),
             reads=[c.BBC[:, :], c.SBC[:, :]], writes=[c.BBC[:, :]])
        dst = c.LAT[:, cs]
        p.op("scalar", lambda e, a=dst: e.activation(a, c.BBC[:, :], AF.Sin,
             scale=2 * math.pi), reads=[c.BBC[:, :]], writes=[dst])
        p.op("vector", lambda e, a=dst: e.tensor_scalar(out=a, in0=a, scalar1=SCALE,
             scalar2=None, op0=ALU.mult), reads=[dst], writes=[dst])

    p.dma("gpsimd", WS16[:, 0:1024].rearrange("p (g m) -> p g m", m=128),
          w["wkT"].rearrange("(g p) n -> p g n", p=128), stream="wk0")
    for tc in range(NTC):
        ps = (c.PS_A, c.PS_B)[tc % 2]
        for g in range(NTILE):
            lhs = WS16[:, g * 128:(g + 1) * 128]
            rhs = pt(c.A, g)[:, tc * TC:(tc + 1) * TC]
            p.op("tensor", lambda e, o=ps[:, :], l=lhs, r=rhs, s=(g == 0), st=(g == 7):
                 e.matmul(o, l, r, start=s, stop=st),
                 reads=[lhs, rhs], writes=[ps[:, :]])
        dst = c.C[:, tc * TC:(tc + 1) * TC]
        p.op("vector", lambda e, a=dst, b_=ps[:, :]:
             e.tensor_scalar(out=a, in0=b_, scalar1=par, scalar2=None, op0=ALU.add),
             reads=[ps[:, :], par], writes=[dst])
    p.dma("gpsimd", WS16[:, 1024:2048].rearrange("p (g m) -> p g m", m=128),
          w["wvT"].rearrange("(g p) n -> p g n", p=128), stream="wv0")
    for tc in range(NTC):
        ps = (c.PS_A, c.PS_B)[tc % 2]
        for g in range(NTILE):
            lhs = WS16[:, 1024 + g * 128:1024 + (g + 1) * 128]
            rhs = pt(c.A, g)[:, tc * TC:(tc + 1) * TC]
            p.op("tensor", lambda e, o=ps[:, :], l=lhs, r=rhs, s=(g == 0), st=(g == 7):
                 e.matmul(o, l, r, start=s, stop=st),
                 reads=[lhs, rhs], writes=[ps[:, :]])
        dst = c.SC16[:, tc * TC:(tc + 1) * TC]
        p.op("scalar", lambda e, a=dst, b_=ps[:, :]: e.activation(a, b_, AF.Copy),
             reads=[ps[:, :]], writes=[dst])
    for t in range(16):
        srct = c.SC16[:, t * 128:(t + 1) * 128]
        dstp = c.PS_T[:, (t % 8) * 128:(t % 8) * 128 + 128]
        p.op("tensor", lambda e, o=dstp, i_=srct: e.transpose(o, i_, c.IDENT[:]),
             reads=[srct, c.IDENT[:]], writes=[dstp])
        dstv = c.B[:, t * 128:(t + 1) * 128]
        p.op("vector", lambda e, a=dstv, b_=dstp: e.tensor_copy(out=a, in_=b_),
             reads=[dstp], writes=[dstv])

    for tc in range(NTC):
        for m in range(NTILE):
            buf = m % 2
            wslot = WS16[:, buf * 1024:(buf + 1) * 1024]
            p.dma("gpsimd", wslot.rearrange("p (g m) -> p g m", m=128),
                  w["wqT"].rearrange("(g p) n -> p g n", p=128)[:, :, m * 128:(m + 1) * 128],
                  stream=f"wq{buf}")
            ps = (c.PS_A, c.PS_B)[m % 2]
            for g in range(NTILE):
                lhs = wslot[:, g * 128:(g + 1) * 128]
                rhs = pt(c.A, g)[:, tc * TC:(tc + 1) * TC]
                p.op("tensor", lambda e, o=ps[:, :], l=lhs, r=rhs, s=(g == 0), st=(g == 7):
                     e.matmul(o, l, r, start=s, stop=st),
                     reads=[lhs, rhs], writes=[ps[:, :]])
            qtmp = (c.SC16 if m < 4 else c.SC16b)[:, (m % 4) * TC:(m % 4 + 1) * TC]
            pel = c.LAT[:, tc * TC:(tc + 1) * TC]
            p.op("vector", lambda e, a=qtmp, b_=ps[:, :], pe=pel:
                 e.scalar_tensor_tensor(out=a, in0=b_, scalar=SCALE, in1=pe,
                                        op0=ALU.mult, op1=ALU.add),
                 reads=[ps[:, :], pel], writes=[qtmp])
        for m in range(NTILE):
            qtmp = (c.SC16 if m < 4 else c.SC16b)[:, (m % 4) * TC:(m % 4 + 1) * TC]
            dst = pt(c.A, m)[:, tc * TC:(tc + 1) * TC]
            p.op("vector", lambda e, a=dst, b_=qtmp: e.tensor_copy(out=a, in_=b_),
                 reads=[qtmp], writes=[dst])

    import os as _os
    if int(_os.environ.get("L2_SKIP_CORE", "0")):
        return
    emit_attn_core(
        c, NH=8, DH=128, causal=True,
        q_ap=lambda h, qc: pt(c.A, h)[:, qc * 128:(qc + 1) * 128],
        k_ap=lambda h, w0, w1: c.C[:, w0:w1],
        vT_ap=lambda h, t: c.B[:, t * 128:(t + 1) * 128],
        o_dst=lambda g, qc: pt(c.A, g)[:, qc * 128:(qc + 1) * 128])

    for m in range(NTILE):
        buf = m % 2
        wslot = WS16[:, buf * 1024:(buf + 1) * 1024]
        p.dma("gpsimd", wslot.rearrange("p (g m) -> p g m", m=128),
              w["woT"].rearrange("(g p) n -> p g n", p=128)[:, :, m * 128:(m + 1) * 128],
              stream=f"wo{buf}")
        for tc in range(NTC):
            ps = (c.PS_A, c.PS_B)[tc % 2]
            for g in range(NTILE):
                lhs = wslot[:, g * 128:(g + 1) * 128]
                rhs = pt(c.A, g)[:, tc * TC:(tc + 1) * TC]
                p.op("tensor", lambda e, o=ps[:, :], l=lhs, r=rhs, s=(g == 0), st=(g == 7):
                     e.matmul(o, l, r, start=s, stop=st),
                     reads=[lhs, rhs], writes=[ps[:, :]])
            xdst = X[m][:, tc * TC:(tc + 1) * TC]
            p.op("vector", lambda e, x=xdst, b_=ps[:, :]:
                 e.scalar_tensor_tensor(out=x, in0=b_, scalar=1.0, in1=x,
                                        op0=ALU.mult, op1=ALU.add),
                 reads=[ps[:, :], xdst], writes=[xdst])


def emit_l2_ffn(c, w):
    p = c.p
    X = [xt(c, i) for i in range(NTILE)]
    emit_ln_prefix(c, [pt(c.A, i) for i in range(NTILE)], variant="plain")
    WS16 = c.WS[:].bitcast(BF16)

    bbase = c.PVEC[:, 0:8]
    benc = c.PVEC[:, 8:24]
    bdec = c.PVEC[:, 24:32]
    p.dma("sync", bbase, w["b_base"], stream="misc")
    p.dma("sync", benc, w["b_enc"], stream="misc2")
    bdn = c.PVEC[:, 32:40]
    bup = c.PVEC[:, 40:104]
    p.dma("sync", bdn, w["b_down"], stream="misc3")
    p.dma("sync", bup, w["b_up"], stream="misc4")
    bdecv = c.PVEC[:, 104:168] if False else None
    bdec64 = c.TMP32[:, 128:192]
    p.dma("sync", bdec64, w["b_dec"], stream="misc5")

    LOSSV = c.PVEC[:, 64:72]
    TACC = c.PVEC[:, 72:73]

    for e in range(8):
        p.dma("gpsimd", WS16[:, 0:2048].rearrange("p (g m) -> p g m", m=256),
              w["W_enc"][e].rearrange("(g p) n -> p g n", p=128), stream="enc")
        for m2 in range(2):
            for tc in range(NTC):
                ps = (c.PS_A, c.PS_B)[tc % 2]
                for g in range(NTILE):
                    lhs = WS16[:, g * 256 + m2 * 128:g * 256 + (m2 + 1) * 128]
                    rhs = pt(c.A, g)[:, tc * TC:(tc + 1) * TC]
                    p.op("tensor", lambda e_, o=ps[:, :], l=lhs, r=rhs, s=(g == 0), st=(g == 7):
                         e_.matmul(o, l, r, start=s, stop=st),
                         reads=[lhs, rhs], writes=[ps[:, :]])
                dst = pt(c.B, m2)[:, tc * TC:(tc + 1) * TC]
                bia = benc[:, e * 2 + m2:e * 2 + m2 + 1]
                p.op("scalar", lambda e_, a=dst, b_=ps[:, :], bb=bia:
                     e_.activation(a, b_, AF.Relu, bias=bb),
                     reads=[ps[:, :], bia], writes=[dst])
        p.dma("gpsimd", WS16[:, 2048:4096].rearrange("p (g m) -> p g m", m=1024),
              w["W_dec"][e].rearrange("(g p) n -> p g n", p=128), stream="dec")
        first = [True]
        for m in range(NTILE):
            for tc in range(NTC):
                ps = (c.PS_A, c.PS_B)[tc % 2]
                for g in range(2):
                    lhs = WS16[:, 2048 + g * 1024 + m * 128:2048 + g * 1024 + (m + 1) * 128]
                    rhs = pt(c.B, g)[:, tc * TC:(tc + 1) * TC]
                    p.op("tensor", lambda e_, o=ps[:, :], l=lhs, r=rhs, s=(g == 0), st=(g == 1):
                         e_.matmul(o, l, r, start=s, stop=st),
                         reads=[lhs, rhs], writes=[ps[:, :]])
                d = c.SBC[:, :]
                nref = pt(c.A, m)[:, tc * TC:(tc + 1) * TC]
                bd = bdec64[:, e * 8 + m:e * 8 + m + 1]
                p.op("vector", lambda e_, a=d, b_=ps[:, :], bb=bd:
                     e_.tensor_scalar(out=a, in0=b_, scalar1=bb, scalar2=None, op0=ALU.add),
                     reads=[ps[:, :], bd], writes=[d])
                p.op("vector", lambda e_, a=d, n_=nref: e_.tensor_sub(out=a, in0=a, in1=n_),
                     reads=[d, nref], writes=[d])
                sq = c.BBC[:, :]
                p.op("scalar", lambda e_, a=sq, b_=d: e_.activation(a, b_, AF.Square),
                     reads=[d], writes=[sq])
                tgt = LOSSV[:, e:e + 1]
                if first[0]:
                    p.op("vector", lambda e_, a=tgt, b_=sq: e_.tensor_reduce(
                         out=a, in_=b_, axis=AX.X, op=ALU.add),
                         reads=[sq], writes=[tgt])
                    first[0] = False
                else:
                    p.op("vector", lambda e_, a=TACC, b_=sq: e_.tensor_reduce(
                         out=a, in_=b_, axis=AX.X, op=ALU.add),
                         reads=[sq], writes=[TACC])
                    p.op("vector", lambda e_, a=tgt: e_.tensor_add(out=a, in0=a, in1=TACC),
                         reads=[tgt, TACC], writes=[tgt])

    onecol = c.ONES[:, 0:1]
    lv16 = c.SC16[:, 0:8]
    p.op("vector", lambda e_: e_.tensor_copy(out=lv16, in_=LOSSV), reads=[LOSSV], writes=[lv16])
    p.op("tensor", lambda e_: e_.matmul(c.PS_A[0:1, 0:8], onecol, lv16, start=True, stop=True),
         reads=[onecol, lv16], writes=[c.PS_A[0:1, 0:8]])
    LOSS = row(c, R_T0)[:, 0:8]
    p.op("vector", lambda e_: e_.tensor_copy(out=LOSS, in_=c.PS_A[0:1, 0:8]),
         reads=[c.PS_A[0:1, 0:8]], writes=[LOSS])
    MN = row(c, R_T0)[:, 8:9]
    p.op("vector", lambda e_: e_.tensor_reduce(out=MN, in_=LOSS, axis=AX.X, op=ALU.min),
         reads=[LOSS], writes=[MN])
    ISM = row(c, R_T0)[:, 16:24]
    p.op("vector", lambda e_: e_.tensor_scalar(out=ISM, in0=LOSS, scalar1=MN, scalar2=None,
         op0=ALU.subtract), reads=[LOSS, MN], writes=[ISM])
    p.op("scalar", lambda e_: e_.activation(ISM, ISM, AF.Sign), reads=[ISM], writes=[ISM])
    p.op("vector", lambda e_: e_.tensor_scalar(out=ISM, in0=ISM, scalar1=-1.0, scalar2=1.0,
         op0=ALU.mult, op1=ALU.add), reads=[ISM], writes=[ISM])
    ci = c.PVI[0:1, 0:8]
    p.dma("sync", ci, w["cidx"], stream="misc6")
    cif = row(c, R_T0)[:, 24:32]
    p.op("vector", lambda e_: e_.tensor_copy(out=cif, in_=ci), reads=[c.PVI[0:1, 0:8]], writes=[cif])
    OH = row(c, R_T1)[:, 0:8]
    tn = row(c, R_T1)[:, 8:16]
    tu = row(c, R_T1)[:, 16:24]
    for n in range(8):
        p.op("vector", lambda e_, nn=float(n): e_.tensor_scalar(out=tn, in0=cif,
             scalar1=-nn, scalar2=None, op0=ALU.add), reads=[cif], writes=[tn])
        p.op("scalar", lambda e_: e_.activation(tn, tn, AF.Abs), reads=[tn], writes=[tn])
        p.op("scalar", lambda e_: e_.activation(tn, tn, AF.Sign, bias=c.CONSTF[0:1, 3:4]),
             reads=[tn, c.CONSTF[0:1, 3:4]], writes=[tn])
        p.op("vector", lambda e_: e_.tensor_scalar(out=tn, in0=tn, scalar1=-0.5, scalar2=0.5,
             op0=ALU.mult, op1=ALU.add), reads=[tn], writes=[tn])
        p.op("vector", lambda e_: e_.tensor_mul(out=tu, in0=tn, in1=ISM),
             reads=[tn, ISM], writes=[tu])
        p.op("vector", lambda e_, oo=OH[:, n:n + 1]: e_.tensor_reduce(out=oo, in_=tu,
             axis=AX.X, op=ALU.add), reads=[tu], writes=[OH[:, n:n + 1]])
    OHB = c.PVEC[:, 72:80]
    p.op("tensor", lambda e_: e_.matmul(c.PS_B[:, 0:8], c.ONESROWF[0:1, :], OH, start=True, stop=True),
         reads=[c.ONESROWF[0:1, :], OH], writes=[c.PS_B[:, 0:8]])
    p.op("vector", lambda e_: e_.tensor_copy(out=OHB, in_=c.PS_B[:, 0:8]),
         reads=[c.PS_B[:, 0:8]], writes=[OHB])

    ohrep = c.SBC[:, :]
    p.op("vector", lambda e_: e_.tensor_copy(
         out=ohrep.rearrange("p (n w) -> p n w", w=64),
         in_=OHB.unsqueeze(2).broadcast_to([128, 8, 64])),
         reads=[OHB], writes=[ohrep])
    for g in range(NTILE):
        slot = WS16[:, 0:512]
        p.dma("gpsimd", slot.rearrange("p (n r) -> p n r", r=64),
              w["W_down"].rearrange("e (g q) r -> q e r", q=128)[:, :, :] if False else
              w["W_down"][:, g * 128:(g + 1) * 128, :].rearrange("e q r -> q e r"),
              stream="wdn")
        msk = c.BBC[:, :]
        p.op("vector", lambda e_, s_=slot, mm=msk, oo=ohrep: e_.tensor_mul(out=mm, in0=s_, in1=oo),
             reads=[slot, ohrep], writes=[msk])
        p.op("vector", lambda e_, mm=msk: e_.tensor_add(out=mm[:, 0:256], in0=mm[:, 0:256], in1=mm[:, 256:512]),
             reads=[msk], writes=[msk[:, 0:256]])
        p.op("vector", lambda e_, mm=msk: e_.tensor_add(out=mm[:, 0:128], in0=mm[:, 0:128], in1=mm[:, 128:256]),
             reads=[msk[:, 0:256]], writes=[msk[:, 0:128]])
        p.op("vector", lambda e_, mm=msk, dst=c.LAT[:, g * 64:(g + 1) * 64]:
             e_.tensor_add(out=dst, in0=mm[:, 0:64], in1=mm[:, 64:128]),
             reads=[msk[:, 0:128]], writes=[c.LAT[:, g * 64:(g + 1) * 64]])
    ohrep2 = c.SBC[0:64, :].rearrange("p (n w) -> p n w", w=64)
    for m in range(NTILE):
        slot = WS16[0:64, 512:1536]
        p.dma("gpsimd", slot.rearrange("p (n r) -> p n r", r=128),
              w["W_up"][:, :, m * 128:(m + 1) * 128].rearrange("e q r -> q e r"),
              stream="wup")
        msk = c.BBC[0:64, :].rearrange("p (n w) -> p n w", w=128) if False else c.BBC[0:64, :]
        for half in range(2):
            mh = c.BBC[0:64, :]
            src = slot[:, half * 512:(half + 1) * 512]
            ohr = OHB[0:64, half * 4:(half + 1) * 4].unsqueeze(2).broadcast_to([64, 4, 128])
            p.op("vector", lambda e_, a=mh, s_=src, o_=ohr: e_.tensor_tensor(
                 out=a.rearrange("p (n w) -> p n w", w=128), in0=s_.rearrange("p (n w) -> p n w", w=128),
                 in1=o_, op=ALU.mult),
                 reads=[src, OHB], writes=[mh])
            p.op("vector", lambda e_, a=mh: e_.tensor_add(out=a[:, 0:256], in0=a[:, 0:256], in1=a[:, 256:512]),
                 reads=[mh], writes=[mh[:, 0:256]])
            p.op("vector", lambda e_, a=mh: e_.tensor_add(out=a[:, 0:128], in0=a[:, 0:128], in1=a[:, 128:256]),
                 reads=[mh[:, 0:256]], writes=[mh[:, 0:128]])
            if half == 0:
                p.op("vector", lambda e_, a=c.TMP32[0:64, 192:320]: e_.tensor_copy(out=a, in_=mh[:, 0:128]),
                     reads=[mh[:, 0:128]], writes=[c.TMP32[0:64, 192:320]])
            else:
                p.op("vector", lambda e_, dst=c.LAT[0:64, 512 + m * 128:512 + (m + 1) * 128]:
                     e_.tensor_add(out=dst, in0=mh[:, 0:128], in1=c.TMP32[0:64, 192:320]),
                     reads=[mh[:, 0:128], c.TMP32[0:64, 192:320]], writes=[c.LAT[0:64, 512 + m * 128:512 + (m + 1) * 128]])
    bds = c.PVEC[:, 80:81]
    tt8 = c.PVEC[:, 81:89]
    p.op("vector", lambda e_: e_.tensor_mul(out=tt8, in0=bdn, in1=OHB), reads=[bdn, OHB], writes=[tt8])
    p.op("vector", lambda e_: e_.tensor_reduce(out=bds, in_=tt8, axis=AX.X, op=ALU.add),
         reads=[tt8], writes=[bds])
    bus = c.PVEC[:, 96:104]
    t64 = c.TMP32[:, 320:384]
    p.op("vector", lambda e_: e_.tensor_tensor(
         out=t64.rearrange("p (n w) -> p n w", w=8), in0=bup.rearrange("p (n w) -> p n w", w=8),
         in1=OHB.unsqueeze(2).broadcast_to([128, 8, 8]), op=ALU.mult),
         reads=[bup, OHB], writes=[t64])
    p.op("vector", lambda e_: e_.tensor_add(out=t64[:, 0:32], in0=t64[:, 0:32], in1=t64[:, 32:64]),
         reads=[t64], writes=[t64[:, 0:32]])
    p.op("vector", lambda e_: e_.tensor_add(out=t64[:, 0:16], in0=t64[:, 0:16], in1=t64[:, 16:32]),
         reads=[t64[:, 0:32]], writes=[t64[:, 0:16]])
    p.op("vector", lambda e_: e_.tensor_add(out=bus, in0=t64[:, 0:8], in1=t64[:, 8:16]),
         reads=[t64[:, 0:16]], writes=[bus])
    bbh = c.PVEC[:, 104:112]
    p.op("vector", lambda e_: e_.tensor_add(out=bbh, in0=bbase, in1=bus), reads=[bbase, bus], writes=[bbh])
    p.op("vector", lambda e_: e_.tensor_scalar(out=bbh, in0=bbh, scalar1=0.5, scalar2=None,
         op0=ALU.mult), reads=[bbh], writes=[bbh])

    for tc in range(NTC):
        ps = (c.PS_A, c.PS_B)[tc % 2]
        for g in range(NTILE):
            lhs = c.LAT[:, g * 64:(g + 1) * 64]
            rhs = pt(c.A, g)[:, tc * TC:(tc + 1) * TC]
            p.op("tensor", lambda e_, o=ps[0:64, :], l=lhs, r=rhs, s=(g == 0), st=(g == 7):
                 e_.matmul(o, l, r, start=s, stop=st),
                 reads=[lhs, rhs], writes=[ps[0:64, :]])
        dst = c.SC16[0:64, tc * TC:(tc + 1) * TC]
        p.op("scalar", lambda e_, a=dst, b_=ps[0:64, :]:
             e_.activation(a, b_, AF.Relu, bias=bds[0:64, :]),
             reads=[ps[0:64, :], bds], writes=[dst])

    for m in range(NTILE):
        buf = m % 2
        wslot = WS16[:, buf * 1024:(buf + 1) * 1024]
        p.dma("gpsimd", wslot.rearrange("p (g m) -> p g m", m=128),
              w["W_base"].rearrange("(g p) n -> p g n", p=128)[:, :, m * 128:(m + 1) * 128],
              stream=f"wb{buf}")
        for tc in range(NTC):
            ps = (c.PS_A, c.PS_B)[tc % 2]
            for g in range(NTILE):
                lhs = wslot[:, g * 128:(g + 1) * 128]
                rhs = pt(c.A, g)[:, tc * TC:(tc + 1) * TC]
                p.op("tensor", lambda e_, o=ps[:, :], l=lhs, r=rhs, s=(g == 0), st=False:
                     e_.matmul(o, l, r, start=s, stop=st),
                     reads=[lhs, rhs], writes=[ps[:, :]])
            lhs2 = c.LAT[0:64, 512 + m * 128:512 + (m + 1) * 128]
            rhs2 = c.SC16[0:64, tc * TC:(tc + 1) * TC]
            p.op("tensor", lambda e_, o=ps[:, :], l=lhs2, r=rhs2:
                 e_.matmul(o, l, r, start=False, stop=True),
                 reads=[lhs2, rhs2], writes=[ps[:, :]])
            xdst = X[m][:, tc * TC:(tc + 1) * TC]
            p.op("vector", lambda e_, x=xdst, b_=ps[:, :]:
                 e_.scalar_tensor_tensor(out=x, in0=b_, scalar=0.5, in1=x,
                                        op0=ALU.mult, op1=ALU.add),
                 reads=[ps[:, :], xdst], writes=[xdst])
        p.op("vector", lambda e_, x=X[m], bb=bbh[:, m:m + 1]:
             e_.tensor_scalar(out=x, in0=x, scalar1=bb, scalar2=None, op0=ALU.add),
             reads=[X[m], bbh[:, m:m + 1]], writes=[X[m]])


def emit_l3_attn(c, w):
    p = c.p
    X = [xt(c, i) for i in range(NTILE)]
    NH, DH = 16, 64

    emit_ln_prefix(c, [pt(c.A, i) for i in range(NTILE)], variant="plain")

    WS16 = c.WS[:].bitcast(BF16)

    for m in range(NTILE):
        buf = m % 2
        wslot = WS16[:, buf * 1024:(buf + 1) * 1024]
        p.dma("gpsimd", wslot.rearrange("p (g m) -> p g m", m=128),
              w["wkT"].rearrange("(g p) n -> p g n", p=128)[:, :, m * 128:(m + 1) * 128],
              stream=f"wk{buf}")
        for tc in range(NTC):
            ps = (c.PS_A, c.PS_B)[tc % 2]
            for g in range(NTILE):
                lhs = wslot[:, g * 128:(g + 1) * 128]
                rhs = pt(c.A, g)[:, tc * TC:(tc + 1) * TC]
                p.op("tensor", lambda e, o=ps[:, :], l=lhs, r=rhs, s=(g == 0), st=(g == 7):
                     e.matmul(o, l, r, start=s, stop=st),
                     reads=[lhs, rhs], writes=[ps[:, :]])
            dst = pt(c.C, m)[:, tc * TC:(tc + 1) * TC]
            p.op("scalar", lambda e, a=dst, b_=ps[:, :]: e.activation(a, b_, AF.Copy),
                 reads=[ps[:, :]], writes=[dst])

    for m in range(NTILE):
        buf = m % 2
        wslot = WS16[:, 2048 + buf * 1024:2048 + (buf + 1) * 1024]
        p.dma("gpsimd", wslot.rearrange("p (g m) -> p g m", m=128),
              w["wvT"].rearrange("(g p) n -> p g n", p=128)[:, :, m * 128:(m + 1) * 128],
              stream=f"wv{buf}")
        for tc in range(NTC):
            ps = (c.PS_A, c.PS_B)[tc % 2]
            for g in range(NTILE):
                lhs = wslot[:, g * 128:(g + 1) * 128]
                rhs = pt(c.A, g)[:, tc * TC:(tc + 1) * TC]
                p.op("tensor", lambda e, o=ps[:, :], l=lhs, r=rhs, s=(g == 0), st=(g == 7):
                     e.matmul(o, l, r, start=s, stop=st),
                     reads=[lhs, rhs], writes=[ps[:, :]])
            dst = c.SC16[:, tc * TC:(tc + 1) * TC]
            p.op("scalar", lambda e, a=dst, b_=ps[:, :]: e.activation(a, b_, AF.Copy),
                 reads=[ps[:, :]], writes=[dst])
        for h2 in range(2):
            h = 2 * m + h2
            for t in range(16):
                src = c.SC16[64 * h2:64 * h2 + 64, t * 128:(t + 1) * 128]
                ident = c.IDENT[64 * h2:64 * h2 + 64, 64 * h2:64 * h2 + 64]
                dstp = c.PS_T[:, t * 64:(t + 1) * 64]
                p.op("tensor", lambda e, o=dstp, i_=src, ii=ident: e.transpose(o, i_, ii),
                     reads=[src, ident], writes=[dstp])
            dstv = c.B[:, h * 1024:(h + 1) * 1024]
            p.op("vector", lambda e, a=dstv, b_=c.PS_T[:, :]: e.tensor_copy(out=a, in_=b_),
                 reads=[c.PS_T[:, :]], writes=[dstv])

    for tc in range(NTC):
        for m in range(NTILE):
            buf = m % 2
            wslot = WS16[:, buf * 1024:(buf + 1) * 1024]
            p.dma("gpsimd", wslot.rearrange("p (g m) -> p g m", m=128),
                  w["wqT"].rearrange("(g p) n -> p g n", p=128)[:, :, m * 128:(m + 1) * 128],
                  stream=f"wq{buf}")
            ps = (c.PS_A, c.PS_B)[m % 2]
            for g in range(NTILE):
                lhs = wslot[:, g * 128:(g + 1) * 128]
                rhs = pt(c.A, g)[:, tc * TC:(tc + 1) * TC]
                p.op("tensor", lambda e, o=ps[:, :], l=lhs, r=rhs, s=(g == 0), st=(g == 7):
                     e.matmul(o, l, r, start=s, stop=st),
                     reads=[lhs, rhs], writes=[ps[:, :]])
            qtmp = (c.SC16 if m < 4 else c.SC16b)[:, (m % 4) * TC:(m % 4 + 1) * TC]
            p.op("scalar", lambda e, a=qtmp, b_=ps[:, :]: e.activation(a, b_, AF.Copy, scale=0.125),
                 reads=[ps[:, :]], writes=[qtmp])
        for m in range(NTILE):
            qtmp = (c.SC16 if m < 4 else c.SC16b)[:, (m % 4) * TC:(m % 4 + 1) * TC]
            dst = pt(c.A, m)[:, tc * TC:(tc + 1) * TC]
            p.op("vector", lambda e, a=dst, b_=qtmp: e.tensor_copy(out=a, in_=b_),
                 reads=[qtmp], writes=[dst])

    emit_attn_core(
        c, NH=16, DH=64, causal=True,
        q_ap=lambda h, qc: pt(c.A, h // 2)[64 * (h % 2):64 * (h % 2) + 64, qc * 128:(qc + 1) * 128],
        k_ap=lambda h, w0, w1: pt(c.C, h // 2)[64 * (h % 2):64 * (h % 2) + 64, w0:w1],
        vT_ap=lambda h, t: c.B[:, h * 1024 + t * 64:h * 1024 + t * 64 + 64],
        o_dst=lambda g, qc: pt(c.A, g)[:, qc * 128:(qc + 1) * 128])

    for m in range(NTILE):
        buf = m % 2
        wslot = WS16[:, buf * 1024:(buf + 1) * 1024]
        p.dma("gpsimd", wslot.rearrange("p (g m) -> p g m", m=128),
              w["woT"].rearrange("(g p) n -> p g n", p=128)[:, :, m * 128:(m + 1) * 128],
              stream=f"wo{buf}")
        for tc in range(NTC):
            ps = (c.PS_A, c.PS_B)[tc % 2]
            for g in range(NTILE):
                lhs = wslot[:, g * 128:(g + 1) * 128]
                rhs = pt(c.A, g)[:, tc * TC:(tc + 1) * TC]
                p.op("tensor", lambda e, o=ps[:, :], l=lhs, r=rhs, s=(g == 0), st=(g == 7):
                     e.matmul(o, l, r, start=s, stop=st),
                     reads=[lhs, rhs], writes=[ps[:, :]])
            xdst = X[m][:, tc * TC:(tc + 1) * TC]
            p.op("vector", lambda e, x=xdst, b_=ps[:, :]:
                 e.scalar_tensor_tensor(out=x, in0=b_, scalar=1.0, in1=x,
                                        op0=ALU.mult, op1=ALU.add),
                 reads=[ps[:, :], xdst], writes=[xdst])


def emit_l3_ffn(c, w):
    p = c.p
    X = [xt(c, i) for i in range(NTILE)]
    WS16 = c.WS[:].bitcast(BF16)

    rsv = c.PVEC[:, 0:8]
    p.dma("sync", rsv, w["rs"], stream="misc")
    pesr = row(c, R_T0)[:, 40:48]
    p.dma("sync", pesr, w["pes"], stream="misc7")
    PESB = c.PVEC[:, 8:16]
    p.op("tensor", lambda e_: e_.matmul(c.PS_O[:, 0:8], c.ONESROWF[0:1, :], pesr,
         start=True, stop=True), reads=[c.ONESROWF[0:1, :], pesr], writes=[c.PS_O[:, 0:8]])
    p.op("vector", lambda e_: e_.tensor_copy(out=PESB, in_=c.PS_O[:, 0:8]),
         reads=[c.PS_O[:, 0:8]], writes=[PESB])
    rwb = WS16[:, 3968:4032]
    p.dma("gpsimd", rwb.rearrange("p (g m) -> p g m", m=8),
          w["rw"].rearrange("(g p) n -> p g n", p=128), stream="rw")
    for g in range(NTILE):
        sl = rwb[:, g * 8:(g + 1) * 8]
        p.op("vector", lambda e_, s_=sl, rc=rsv[:, g:g + 1]: e_.tensor_scalar(
             out=s_, in0=s_, scalar1=rc, scalar2=None, op0=ALU.mult),
             reads=[rwb, rsv], writes=[sl])

    LOGT = c.TMP32[:, 128:256]
    CT = c.TMP32[:, 0:128]

    def router_cb(tc):
        rr = row(c, R_VAR)
        for g in range(NTILE):
            lhs = rwb[:, g * 8:(g + 1) * 8]
            rhs = pt(c.A, g)[:, tc * TC:(tc + 1) * TC]
            p.op("tensor", lambda e_, o=c.PS_A[0:8, :], l=lhs, r=rhs, s=(g == 0), st=(g == 7):
                 e_.matmul(o, l, r, start=s, stop=st),
                 reads=[lhs, rhs], writes=[c.PS_A[0:8, :]])
        p.op("tensor", lambda e_, r=rr: e_.matmul(c.PS_B[0:8, :], c.ONESROWF[0:1, 0:8], r,
             start=True, stop=True), reads=[c.ONESROWF[0:1, 0:8], rr], writes=[c.PS_B[0:8, :]])
        lg = c.SBC[0:8, :]
        p.op("vector", lambda e_, a=lg: e_.tensor_copy(out=a, in_=c.PS_B[0:8, :]),
             reads=[c.PS_B[0:8, :]], writes=[lg])
        p.op("vector", lambda e_, a=lg: e_.tensor_tensor(out=a, in0=c.PS_A[0:8, :], in1=a,
             op=ALU.mult), reads=[c.PS_A[0:8, :], lg], writes=[lg])
        for j in range(4):
            t = tc * 4 + j
            srcl = lg[:, j * 128:(j + 1) * 128]
            ps = c.PS_O[:, (j % 2) * 128:(j % 2) * 128 + 8]
            p.op("tensor", lambda e_, o=ps, i_=srcl: e_.transpose(o, i_, c.IDENTF[0:8, 0:8]),
                 reads=[srcl, c.IDENTF[0:8, 0:8]], writes=[ps])
            dst = LOGT[:, t * 8:(t + 1) * 8]
            p.op("vector", lambda e_, a=dst, b_=ps: e_.tensor_copy(out=a, in_=b_),
                 reads=[ps], writes=[dst])

    emit_ln_prefix(c, [pt(c.A, i) for i in range(NTILE)], variant="plain",
                   rr_out=lambda tc: row(c, R_VAR), chunk_cb=router_cb)

    T8a = c.PVEC[:, 96:104]
    T8b = c.PVEC[:, 88:96]
    M1 = c.PVEC[:, 104:105]
    NM1 = c.PVEC[:, 105:106]
    M2 = c.PVEC[:, 106:107]
    SS = c.PVEC[:, 107:108]
    for t in range(16):
        lt = LOGT[:, t * 8:(t + 1) * 8]
        p.op("vector", lambda e_, l=lt: e_.tensor_reduce(out=M1, in_=l, axis=AX.X, op=ALU.max),
             reads=[lt], writes=[M1])
        p.op("vector", lambda e_, l=lt: e_.tensor_scalar(out=T8a, in0=l, scalar1=M1,
             scalar2=None, op0=ALU.subtract), reads=[lt, M1], writes=[T8a])
        p.op("scalar", lambda e_: e_.activation(T8a, T8a, AF.Sign), reads=[T8a], writes=[T8a])
        p.op("vector", lambda e_: e_.tensor_scalar(out=T8a, in0=T8a, scalar1=1.0,
             scalar2=None, op0=ALU.add), reads=[T8a], writes=[T8a])
        p.op("vector", lambda e_, l=lt: e_.scalar_tensor_tensor(out=T8a, in0=T8a,
             scalar=-1e30, in1=l, op0=ALU.mult, op1=ALU.add), reads=[T8a, lt], writes=[T8a])
        p.op("vector", lambda e_: e_.tensor_reduce(out=M2, in_=T8a, axis=AX.X, op=ALU.max),
             reads=[T8a], writes=[M2])
        p.op("vector", lambda e_, l=lt: e_.tensor_scalar(out=T8a, in0=l, scalar1=M2,
             scalar2=None, op0=ALU.subtract), reads=[lt, M2], writes=[T8a])
        p.op("scalar", lambda e_: e_.activation(T8a, T8a, AF.Sign), reads=[T8a], writes=[T8a])
        p.op("vector", lambda e_: e_.tensor_scalar(out=T8a, in0=T8a, scalar1=1.0,
             scalar2=1.0, op0=ALU.add, op1=ALU.min), reads=[T8a], writes=[T8a])
        p.op("vector", lambda e_: e_.tensor_scalar(out=NM1, in0=M1, scalar1=-1.0,
             scalar2=None, op0=ALU.mult), reads=[M1], writes=[NM1])
        p.op("scalar", lambda e_, l=lt: e_.activation(T8b, l, AF.Exp, bias=NM1),
             reads=[lt, NM1], writes=[T8b])
        p.op("vector", lambda e_: e_.tensor_mul(out=T8b, in0=T8b, in1=T8a),
             reads=[T8b, T8a], writes=[T8b])
        p.op("vector", lambda e_: e_.tensor_reduce(out=SS, in_=T8b, axis=AX.X, op=ALU.add),
             reads=[T8b], writes=[SS])
        p.op("vector", lambda e_: e_.reciprocal(out=SS, in_=SS), reads=[SS], writes=[SS])
        p.op("vector", lambda e_: e_.tensor_scalar(out=T8b, in0=T8b, scalar1=SS,
             scalar2=None, op0=ALU.mult), reads=[T8b, SS], writes=[T8b])
        p.op("vector", lambda e_, o=CT[:, t * 8:(t + 1) * 8]: e_.tensor_mul(out=o, in0=T8b,
             in1=PESB), reads=[T8b, PESB], writes=[CT[:, t * 8:(t + 1) * 8]])

    for t in range(16):
        srcc = CT[:, t * 8:(t + 1) * 8]
        ps = c.PS_O[0:8, (t % 2) * 128:(t % 2) * 128 + 128]
        p.op("tensor", lambda e_, o=ps, i_=srcc: e_.transpose(o, i_, c.IDENTF[:]),
             reads=[srcc, c.IDENTF[:]], writes=[ps])
        dst = c.LAT[0:8, t * 128:(t + 1) * 128]
        p.op("vector", lambda e_, a=dst, b_=ps: e_.tensor_copy(out=a, in_=b_),
             reads=[ps], writes=[dst])

    for e in range(8):
        for t in range(16):
            rhs = c.LAT[0:8, t * 128:(t + 1) * 128]
            lhs = c.SEL[:, e * 128:(e + 1) * 128]
            ps = (c.PS_A, c.PS_B)[t % 2]
            p.op("tensor", lambda e_, o=ps[:, 0:128], l=lhs, r=rhs:
                 e_.matmul(o, l, r, start=True, stop=True),
                 reads=[lhs, rhs], writes=[ps[:, 0:128]])
            dst = c.SC16[:, t * 128:(t + 1) * 128]
            p.op("vector", lambda e_, a=dst, b_=ps[:, 0:128]: e_.tensor_copy(out=a, in_=b_),
                 reads=[ps[:, 0:128]], writes=[dst])
        for hidm in range(16):
            buf = hidm % 2
            w0s = WS16[:, buf * 1024:(buf + 1) * 1024]
            w1s = WS16[:, 2048 + buf * 1024:2048 + (buf + 1) * 1024]
            p.dma("gpsimd", w0s.rearrange("p (g m) -> p g m", m=128),
                  w["g0T"][e].rearrange("(g p) n -> p g n", p=128)[:, :, hidm * 128:(hidm + 1) * 128],
                  stream=f"g0_{buf}")
            p.dma("gpsimd", w1s.rearrange("p (g m) -> p g m", m=128),
                  w["g1T"][e].rearrange("(g p) n -> p g n", p=128)[:, :, hidm * 128:(hidm + 1) * 128],
                  stream=f"g1_{buf}")
            pool = c.B if hidm < 8 else c.C
            for tc in range(NTC):
                for g in range(NTILE):
                    lhs = w0s[:, g * 128:(g + 1) * 128]
                    rhs = pt(c.A, g)[:, tc * TC:(tc + 1) * TC]
                    p.op("tensor", lambda e_, o=c.PS_A[:, :], l=lhs, r=rhs, s=(g == 0), st=(g == 7):
                         e_.matmul(o, l, r, start=s, stop=st),
                         reads=[lhs, rhs], writes=[c.PS_A[:, :]])
                gg = c.SC16b[:, (tc % 2) * TC:(tc % 2) * TC + TC]
                p.op("scalar", lambda e_, a=gg: e_.activation(a, c.PS_A[:, :], AF.Gelu_apprx_tanh),
                     reads=[c.PS_A[:, :]], writes=[gg])
                for g in range(NTILE):
                    lhs = w1s[:, g * 128:(g + 1) * 128]
                    rhs = pt(c.A, g)[:, tc * TC:(tc + 1) * TC]
                    p.op("tensor", lambda e_, o=c.PS_B[:, :], l=lhs, r=rhs, s=(g == 0), st=(g == 7):
                         e_.matmul(o, l, r, start=s, stop=st),
                         reads=[lhs, rhs], writes=[c.PS_B[:, :]])
                slot = pool[:, (hidm % 8) * T + tc * TC:(hidm % 8) * T + (tc + 1) * TC]
                p.op("vector", lambda e_, a=slot, g_=gg: e_.tensor_tensor(out=a, in0=g_,
                     in1=c.PS_B[:, :], op=ALU.mult), reads=[gg, c.PS_B[:, :]], writes=[slot])
                cbch = c.SC16[:, tc * TC:(tc + 1) * TC]
                p.op("vector", lambda e_, a=slot, cb=cbch: e_.tensor_tensor(out=a, in0=a,
                     in1=cb, op=ALU.mult), reads=[slot, cbch], writes=[slot])
        for dm in range(NTILE):
            buf = dm % 2
            lws = WS16[:, buf * 2048:(buf + 1) * 2048]
            p.dma("gpsimd", lws.rearrange("p (g m) -> p g m", m=128),
                  w["lw"][e].rearrange("(g p) n -> p g n", p=128)[:, :, dm * 128:(dm + 1) * 128],
                  stream=f"lw{buf}")
            for tc in range(NTC):
                ps = (c.PS_A, c.PS_B)[tc % 2]
                for k2 in range(16):
                    pool = c.B if k2 < 8 else c.C
                    lhs = lws[:, k2 * 128:(k2 + 1) * 128]
                    rhs = pool[:, (k2 % 8) * T + tc * TC:(k2 % 8) * T + (tc + 1) * TC]
                    p.op("tensor", lambda e_, o=ps[:, :], l=lhs, r=rhs, s=(k2 == 0), st=(k2 == 15):
                         e_.matmul(o, l, r, start=s, stop=st),
                         reads=[lhs, rhs], writes=[ps[:, :]])
                xdst = X[dm][:, tc * TC:(tc + 1) * TC]
                p.op("vector", lambda e_, x=xdst, b_=ps[:, :]:
                     e_.scalar_tensor_tensor(out=x, in0=b_, scalar=0.5, in1=x,
                                            op0=ALU.mult, op1=ALU.add),
                     reads=[ps[:, :], xdst], writes=[xdst])


def emit_quant(c, src, dst, W, xn=None, tmp=None):
    p = c.p
    nb = W // 16
    BM = c.QSC[:, 0:nb]
    S = c.QSC[:, 32:32 + nb]
    R = c.QSC[:, 64:64 + nb]
    XN = xn if xn is not None else c.TMP32[:, 0:W]
    UN = c.SBC[:, 0:W]
    QA = c.BBC[:, 0:W]
    TB = tmp if tmp is not None else c.PS_O[:, 0:W]
    RN = c.RNDI[:, 0:W]
    p.op("vector", lambda e: e.tensor_reduce(out=BM, in_=src.rearrange("p (n w) -> p n w", w=16),
         axis=AX.X, op=ALU.max, apply_absolute_value=True), reads=[src], writes=[BM])
    p.op("vector", lambda e: e.tensor_scalar(out=S, in0=BM, scalar1=1.0 / 6.0, scalar2=1e-12,
         op0=ALU.mult, op1=ALU.max), reads=[BM], writes=[S])
    p.op("vector", lambda e: e.reciprocal(out=R, in_=S), reads=[S], writes=[R])
    p.op("vector", lambda e: e.tensor_tensor(out=XN.rearrange("p (n w) -> p n w", w=16),
         in0=src.rearrange("p (n w) -> p n w", w=16),
         in1=R.unsqueeze(2).broadcast_to([128, nb, 16]), op=ALU.mult),
         reads=[src, R], writes=[XN])
    p.op("scalar", lambda e: e.activation(UN, XN, AF.Abs), reads=[XN], writes=[UN])
    p.op("vector", lambda e: e.tensor_scalar(out=QA, in0=UN, scalar1=2.0, scalar2=None,
         op0=ALU.mult), reads=[UN], writes=[QA])
    p.op("vector", lambda e: e.tensor_copy(out=RN, in_=QA), reads=[QA], writes=[RN])
    p.op("vector", lambda e: e.tensor_copy(out=QA, in_=RN), reads=[RN], writes=[QA])
    p.op("vector", lambda e: e.tensor_scalar(out=QA, in0=QA, scalar1=0.5, scalar2=2.0,
         op0=ALU.mult, op1=ALU.min), reads=[QA], writes=[QA])
    p.op("vector", lambda e: e.tensor_scalar(out=TB, in0=UN, scalar1=-2.0, scalar2=None,
         op0=ALU.add), reads=[UN], writes=[TB])
    p.op("vector", lambda e: e.tensor_copy(out=RN, in_=TB), reads=[TB], writes=[RN])
    p.op("vector", lambda e: e.tensor_copy(out=TB, in_=RN), reads=[RN], writes=[TB])
    p.op("vector", lambda e: e.tensor_scalar(out=TB, in0=TB, scalar1=0.0, scalar2=2.0,
         op0=ALU.max, op1=ALU.min), reads=[TB], writes=[TB])
    p.op("vector", lambda e: e.tensor_add(out=QA, in0=QA, in1=TB), reads=[QA, TB], writes=[QA])
    p.op("vector", lambda e: e.tensor_scalar(out=TB, in0=UN, scalar1=-4.5, scalar2=None,
         op0=ALU.add), reads=[UN], writes=[TB])
    p.op("vector", lambda e: e.tensor_copy(out=RN, in_=TB), reads=[TB], writes=[RN])
    p.op("vector", lambda e: e.tensor_copy(out=TB, in_=RN), reads=[RN], writes=[TB])
    p.op("vector", lambda e: e.tensor_scalar(out=TB, in0=TB, scalar1=0.0, scalar2=1.0,
         op0=ALU.max, op1=ALU.min), reads=[TB], writes=[TB])
    p.op("vector", lambda e: e.tensor_add(out=QA, in0=QA, in1=TB), reads=[QA, TB], writes=[QA])
    p.op("vector", lambda e: e.tensor_add(out=QA, in0=QA, in1=TB), reads=[QA, TB], writes=[QA])
    p.op("scalar", lambda e: e.activation(TB, XN, AF.Sign), reads=[XN], writes=[TB])
    p.op("vector", lambda e: e.tensor_mul(out=QA, in0=QA, in1=TB), reads=[QA, TB], writes=[QA])
    p.op("vector", lambda e: e.tensor_tensor(out=dst.rearrange("p (n w) -> p n w", w=16),
         in0=QA.rearrange("p (n w) -> p n w", w=16),
         in1=S.unsqueeze(2).broadcast_to([128, nb, 16]), op=ALU.mult),
         reads=[QA, S], writes=[dst])


def emit_l1_ffn(c, w, do_quant):
    p = c.p
    X = [xt(c, i) for i in range(NTILE)]
    WS16 = c.WS[:].bitcast(BF16)
    BF = c.B[:].bitcast(F32)
    CF = c.C[:].bitcast(F32)

    def xtm(tt):
        if tt < 8:
            return BF[:, tt * 1024:(tt + 1) * 1024]
        return CF[:, (tt - 8) * 1024:(tt - 7) * 1024]

    emit_ln_prefix(c, [xtm(tt) for tt in range(16)], variant="plain", tm_f32=True)

    for tt in range(16):
        for ch in range(2):
            srcq = xtm(tt)[:, ch * 512:(ch + 1) * 512]
            dstq = c.A[:, tt * 1024 + ch * 512:tt * 1024 + (ch + 1) * 512]
            emit_quant(c, srcq, dstq, 512)
    for tt in range(16):
        for dch in range(NTILE):
            srct = c.A[:, tt * 1024 + dch * 128:tt * 1024 + (dch + 1) * 128]
            pslot = c.PS_T[:, (dch % 8) * 128:(dch % 8) * 128 + 128]
            p.op("tensor", lambda e, o=pslot, i_=srct: e.transpose(o, i_, c.IDENT[:]),
                 reads=[srct, c.IDENT[:]], writes=[pslot])
            dstt = pt(c.B, dch)[:, tt * 128:(tt + 1) * 128]
            p.op("vector", lambda e, a=dstt, b_=pslot: e.tensor_copy(out=a, in_=b_),
                 reads=[pslot], writes=[dstt])

    if do_quant:
        for wap, outap, nr in ((w["wg"], w["wgT"], 2816), (w["wu"], w["wuT"], 2816)):
            for it in range(22):
                buf = it % 2
                slot = c.WS[:, buf * 1024:(buf + 1) * 1024]
                p.dma("gpsimd", slot, wap[it * 128:(it + 1) * 128, :], stream=f"qw{buf}")
                qd = c.SC16[:, 0:1024]
                for ch in range(2):
                    emit_quant(c, slot[:, ch * 512:(ch + 1) * 512],
                               qd[:, ch * 512:(ch + 1) * 512], 512)
                tout = c.SC16b[:, 0:1024]
                for dch in range(NTILE):
                    pslot = c.PS_T[:, (dch % 8) * 128:(dch % 8) * 128 + 128]
                    p.op("tensor", lambda e, o=pslot, i_=qd[:, dch * 128:(dch + 1) * 128]:
                         e.transpose(o, i_, c.IDENT[:]),
                         reads=[qd[:, dch * 128:(dch + 1) * 128], c.IDENT[:]], writes=[pslot])
                    p.op("vector", lambda e, a=tout[:, dch * 128:(dch + 1) * 128], b_=pslot:
                         e.tensor_copy(out=a, in_=b_),
                         reads=[pslot], writes=[tout[:, dch * 128:(dch + 1) * 128]])
                p.dma("scalar",
                      outap.rearrange("(dch p) n -> p dch n", p=128)[:, :, it * 128:(it + 1) * 128],
                      tout.rearrange("p (dch m) -> p dch m", m=128), stream="qwo")
        for dt in range(NTILE):
            slot = c.WS[:, 0:1408]
            for half in range(2):
                p.dma("gpsimd", slot,
                      w["wd"][dt * 128:(dt + 1) * 128, half * 1408:(half + 1) * 1408],
                      stream="qd0")
                qd = c.SC16[:, 0:1408]
                emit_quant(c, slot[:, 0:512], qd[:, 0:512], 512)
                emit_quant(c, slot[:, 512:1024], qd[:, 512:1024], 512)
                emit_quant(c, slot[:, 1024:1408], qd[:, 1024:1408], 384)
                tout = c.SC16b[:, 0:1408]
                for ich in range(11):
                    pslot = c.PS_T[:, (ich % 8) * 128:(ich % 8) * 128 + 128]
                    p.op("tensor", lambda e, o=pslot, i_=qd[:, ich * 128:(ich + 1) * 128]:
                         e.transpose(o, i_, c.IDENT[:]),
                         reads=[qd[:, ich * 128:(ich + 1) * 128], c.IDENT[:]], writes=[pslot])
                    p.op("vector", lambda e, a=tout[:, ich * 128:(ich + 1) * 128], b_=pslot:
                         e.tensor_copy(out=a, in_=b_),
                         reads=[pslot], writes=[tout[:, ich * 128:(ich + 1) * 128]])
                p.dma("scalar",
                      w["wdT"][half * 1408:(half + 1) * 1408, :]
                      .rearrange("(ich p) n -> p ich n", p=128)[:, :, dt * 128:(dt + 1) * 128],
                      tout.rearrange("p (ich m) -> p ich m", m=128), stream="qdo")

    NIC = 11
    for slab in range(4):
        ts0 = slab * 512
        for ic in range(NIC):
            bufg = ic % 2
            gslot = WS16[:, bufg * 2048:(bufg + 1) * 2048]
            p.dma("gpsimd", gslot.rearrange("p (g m) -> p g m", m=256),
                  w["wgT"].rearrange("(g p) n -> p g n", p=128)[:, :, ic * 256:(ic + 1) * 256],
                  stream=f"gt{bufg}")
            for tokm in range(4):
                tt = slab * 4 + tokm
                ps = (c.PS_A, c.PS_B)[tokm % 2]
                for g in range(NTILE):
                    lhs = pt(c.B, g)[:, tt * 128:(tt + 1) * 128]
                    rhs = gslot[:, g * 256:(g + 1) * 256]
                    p.op("tensor", lambda e, o=ps[:, 0:256], l=lhs, r=rhs, s=(g == 0), st=(g == 7):
                         e.matmul(o, l, r, start=s, stop=st),
                         reads=[lhs, rhs], writes=[ps[:, 0:256]])
                gg = c.SC16b[:, tokm * 256:(tokm + 1) * 256]
                p.op("scalar", lambda e, a=gg, b_=ps[:, 0:256]: e.activation(a, b_, AF.Silu),
                     reads=[ps[:, 0:256]], writes=[gg])
            uslot = WS16[:, 2048 + bufg * 2048:2048 + (bufg + 1) * 2048] if False else gslot
            bufu = (ic + 1) % 2
            uslot = WS16[:, bufu * 2048:(bufu + 1) * 2048]
            p.dma("gpsimd", uslot.rearrange("p (g m) -> p g m", m=256),
                  w["wuT"].rearrange("(g p) n -> p g n", p=128)[:, :, ic * 256:(ic + 1) * 256],
                  stream=f"ut{bufu}")
            for tokm in range(4):
                tt = slab * 4 + tokm
                ps = (c.PS_A, c.PS_B)[tokm % 2]
                for g in range(NTILE):
                    lhs = pt(c.B, g)[:, tt * 128:(tt + 1) * 128]
                    rhs = uslot[:, g * 256:(g + 1) * 256]
                    p.op("tensor", lambda e, o=ps[:, 0:256], l=lhs, r=rhs, s=(g == 0), st=(g == 7):
                         e.matmul(o, l, r, start=s, stop=st),
                         reads=[lhs, rhs], writes=[ps[:, 0:256]])
                hh = c.TMP32[:, 256:512]
                gg = c.SC16b[:, tokm * 256:(tokm + 1) * 256]
                p.op("vector", lambda e, a=hh, g_=gg, b_=ps[:, 0:256]:
                     e.tensor_tensor(out=a, in0=g_, in1=b_, op=ALU.mult),
                     reads=[gg, ps[:, 0:256]], writes=[hh])
                hq = c.SC16[:, tokm * 256:(tokm + 1) * 256]
                emit_quant(c, hh, hq, 256, xn=c.PS_O[:, 0:256], tmp=c.PS_O[:, 256:512])
                for j in range(2):
                    it = ic * 2 + j
                    pslot = c.PS_T[:, ((tokm * 2 + j) % 8) * 128:((tokm * 2 + j) % 8) * 128 + 128]
                    p.op("tensor", lambda e, o=pslot, i_=hq[:, j * 128:(j + 1) * 128]:
                         e.transpose(o, i_, c.IDENT[:]),
                         reads=[hq[:, j * 128:(j + 1) * 128], c.IDENT[:]], writes=[pslot])
                    dsth = c.C[:, it * 512 + tokm * 128:it * 512 + (tokm + 1) * 128]
                    p.op("vector", lambda e, a=dsth, b_=pslot: e.tensor_copy(out=a, in_=b_),
                         reads=[pslot], writes=[dsth])
        for dm in range(NTILE):
            lws = WS16[:, 0:2816]
            p.dma("gpsimd", lws.rearrange("p (g m) -> p g m", m=128),
                  w["wdT"].rearrange("(g p) n -> p g n", p=128)[:, :, dm * 128:(dm + 1) * 128],
                  stream="dnw")
            ps = (c.PS_A, c.PS_B)[dm % 2]
            for k2 in range(22):
                lhs = lws[:, k2 * 128:(k2 + 1) * 128]
                rhs = c.C[:, k2 * 512:(k2 + 1) * 512]
                p.op("tensor", lambda e, o=ps[:, :], l=lhs, r=rhs, s=(k2 == 0), st=(k2 == 21):
                     e.matmul(o, l, r, start=s, stop=st),
                     reads=[lhs, rhs], writes=[ps[:, :]])
            xdst = X[dm][:, ts0:ts0 + 512]
            p.op("vector", lambda e, x=xdst, b_=ps[:, :]:
                 e.scalar_tensor_tensor(out=x, in0=b_, scalar=0.5, in1=x,
                                        op0=ALU.mult, op1=ALU.add),
                 reads=[ps[:, :], xdst], writes=[xdst])


def _pp(v):
    a = np.asarray(v, np.float32)
    return np.ascontiguousarray(a.reshape(-1, 128).T)


def _prep_weights(inputs):
    g = {}
    f32 = np.float32
    A = lambda x: np.ascontiguousarray(np.asarray(x))
    g["l0f_q1"] = A(inputs["L0_ffn_fc1_q"]).astype(np.int32)
    g["l0f_s1"] = A(inputs["L0_ffn_fc1_scales"]).astype(f32)
    g["l0f_b1"] = _pp(inputs["L0_ffn_fc1_bias"])
    g["l0f_q2"] = A(inputs["L0_ffn_fc2_q"]).astype(np.int32)
    g["l0f_s2"] = A(inputs["L0_ffn_fc2_scales"]).astype(f32)
    g["l0f_b2"] = _pp(inputs["L0_ffn_fc2_bias"])
    g["l0f_gamma"] = _pp(inputs["L0_ffn_ln_gamma"])
    g["l0f_beta"] = _pp(inputs["L0_ffn_ln_beta"])
    wv = np.asarray(inputs["L0_attn_w_att_val"], f32)
    blk = np.zeros((8, 128, 128), f32)
    for j in range(8):
        blk[j, 0:64, 0:64] = wv[2 * j]
        blk[j, 64:128, 64:128] = wv[2 * j + 1]
    g["l0a_watt"] = A(blk.transpose(1, 0, 2).reshape(128, 1024))
    g["l0a_wvT"] = A(np.asarray(inputs["L0_attn_wv_w"], f32).T)
    g["l0a_outwT"] = A(np.asarray(inputs["L0_attn_out_w"], f32).T)
    g["l0a_wv_b"] = _pp(inputs["L0_attn_wv_b"])
    g["l0a_out_b"] = _pp(inputs["L0_attn_out_b"])
    g["l1a_wq"] = A(np.asarray(inputs["L1_attn_W_q"], f32))
    g["l1a_wk"] = A(np.asarray(inputs["L1_attn_W_k"], f32))
    g["l1a_wv"] = A(np.asarray(inputs["L1_attn_W_v"], f32))
    g["l1a_wo"] = A(np.asarray(inputs["L1_attn_W_o"], f32))
    g["l1a_wdkv"] = A(np.asarray(inputs["L1_attn_W_down_kv"], f32))
    g["l1a_wupk"] = A(np.asarray(inputs["L1_attn_W_up_k"], f32))
    g["l1a_wupv"] = A(np.asarray(inputs["L1_attn_W_up_v"], f32))
    g["l2a_wqT"] = A(np.asarray(inputs["L2_attn_Wq"], f32).T)
    g["l2a_wkT"] = A(np.asarray(inputs["L2_attn_Wk"], f32).T)
    g["l2a_wvT"] = A(np.asarray(inputs["L2_attn_Wv"], f32).T)
    g["l2a_woT"] = A(np.asarray(inputs["L2_attn_Wo"], f32).T)
    g["l3f_g0T"] = A(np.asarray(inputs["L3_ffn_gating_w"], f32)[:, 0].transpose(0, 2, 1))
    g["l3f_g1T"] = A(np.asarray(inputs["L3_ffn_gating_w"], f32)[:, 1].transpose(0, 2, 1))
    g["l3f_lw"] = A(np.asarray(inputs["L3_ffn_linear_w"], f32))
    g["l3f_rs"] = _pp(inputs["L3_ffn_router_scale"])
    g["l3f_rw"] = A(np.asarray(inputs["L3_ffn_router_w"], f32))
    g["l3f_pes"] = A(np.asarray(inputs["L3_ffn_per_expert_scale"], f32).reshape(1, 8))
    g["l1f_wg"] = A(np.asarray(inputs["L1_ffn_w_gate"], f32))
    g["l1f_wu"] = A(np.asarray(inputs["L1_ffn_w_up"], f32))
    g["l1f_wd"] = A(np.asarray(inputs["L1_ffn_w_down"], f32))
    g["l2f_W_base"] = A(np.asarray(inputs["L2_ffn_W_base"], f32))
    g["l2f_W_enc"] = A(np.asarray(inputs["L2_ffn_W_enc"], f32))
    g["l2f_W_dec"] = A(np.asarray(inputs["L2_ffn_W_dec"], f32))
    g["l2f_W_down"] = A(np.asarray(inputs["L2_ffn_W_down"], f32))
    g["l2f_W_up"] = A(np.asarray(inputs["L2_ffn_W_up"], f32))
    g["l2f_b_base"] = _pp(inputs["L2_ffn_b_base"])
    be = np.asarray(inputs["L2_ffn_b_enc"], f32).reshape(8, 2, 128)
    g["l2f_b_enc"] = A(be.transpose(2, 0, 1).reshape(128, 16))
    bd = np.asarray(inputs["L2_ffn_b_dec"], f32).reshape(8, 8, 128)
    g["l2f_b_dec"] = A(bd.transpose(2, 0, 1).reshape(128, 64))
    bdn = np.zeros((128, 8), f32)
    bdn[0:64, :] = np.asarray(inputs["L2_ffn_b_down"], f32).T
    g["l2f_b_down"] = A(bdn)
    bu = np.asarray(inputs["L2_ffn_b_up"], f32).reshape(8, 8, 128)
    g["l2f_b_up"] = A(bu.transpose(2, 0, 1).reshape(128, 64))
    g["l2f_cidx"] = A(np.asarray(inputs["L2_ffn_connected_idx"], np.int32).reshape(1, 8))
    g["l3a_wqT"] = A(np.asarray(inputs["L3_attn_WQ"], f32).T)
    g["l3a_wkT"] = A(np.asarray(inputs["L3_attn_WK"], f32).T)
    g["l3a_wvT"] = A(np.asarray(inputs["L3_attn_WV"], f32).T)
    g["l3a_woT"] = A(np.asarray(inputs["L3_attn_WO"], f32).T)
    return g


DEBUG_DUMP = []

SUBLAYERS = [
    (0, "ffn"), (0, "attn"), (0, "ffn"),
    (1, "ffn"), (1, "attn"), (1, "ffn"),
    (2, "ffn"), (2, "attn"), (2, "ffn"),
    (3, "ffn"), (3, "attn"), (3, "ffn"),
]


def build_program(nc, sublayers):
    p = Prog(nc)
    in_names = []

    def dr(name, shape, dt, kind="ExternalInput"):
        h = nc.dram_tensor(name, shape, dt, kind=kind)
        p.reg(h, "dram")
        if kind == "ExternalInput":
            in_names.append(name)
        return h.ap()

    x_in = dr("x", [1024, T], F32)
    x_out = dr("out", [1024, T], F32, kind="ExternalOutput")

    need = {SUBLAYERS[s] for s in sublayers}
    w = {}
    if (0, "ffn") in need:
        w["l0f"] = {
            "q1": dr("l0f_q1", [1024, 4096], I32),
            "s1": dr("l0f_s1", [8, 4096], F32),
            "b1": dr("l0f_b1", [128, 32], F32),
            "q2": dr("l0f_q2", [4096, 1024], I32),
            "s2": dr("l0f_s2", [32, 1024], F32),
            "b2": dr("l0f_b2", [128, 8], F32),
            "gamma": dr("l0f_gamma", [128, 8], F32),
            "beta": dr("l0f_beta", [128, 8], F32),
            "w1b": dr("l0f_w1b", [1024, 4096], BF16, kind="Internal"),
            "w2b": dr("l0f_w2b", [4096, 1024], BF16, kind="Internal"),
        }

    if (0, "attn") in need:
        w["l0a"] = {
            "watt": dr("l0a_watt", [128, 1024], F32),
            "wvT": dr("l0a_wvT", [1024, 1024], F32),
            "outwT": dr("l0a_outwT", [1024, 1024], F32),
            "wv_b": dr("l0a_wv_b", [128, 8], F32),
            "out_b": dr("l0a_out_b", [128, 8], F32),
        }
    if (1, "attn") in need:
        w["l1a"] = {
            "wq": dr("l1a_wq", [1024, 1024], F32),
            "wk": dr("l1a_wk", [1024, 1024], F32),
            "wv": dr("l1a_wv", [1024, 1024], F32),
            "wo": dr("l1a_wo", [1024, 1024], F32),
            "wdkv": dr("l1a_wdkv", [1024, 256], F32),
            "wupk": dr("l1a_wupk", [256, 1024], F32),
            "wupv": dr("l1a_wupv", [256, 1024], F32),
        }
    if (2, "attn") in need:
        w["l2a"] = {
            "wqT": dr("l2a_wqT", [1024, 1024], F32),
            "wkT": dr("l2a_wkT", [1024, 128], F32),
            "wvT": dr("l2a_wvT", [1024, 128], F32),
            "woT": dr("l2a_woT", [1024, 1024], F32),
        }
    if (1, "ffn") in need:
        w["l1f"] = {
            "wg": dr("l1f_wg", [2816, 1024], F32),
            "wu": dr("l1f_wu", [2816, 1024], F32),
            "wd": dr("l1f_wd", [1024, 2816], F32),
            "wgT": dr("l1f_wgT", [1024, 2816], BF16, kind="Internal"),
            "wuT": dr("l1f_wuT", [1024, 2816], BF16, kind="Internal"),
            "wdT": dr("l1f_wdT", [2816, 1024], BF16, kind="Internal"),
        }
    if (2, "ffn") in need:
        w["l2f"] = {
            "W_base": dr("l2f_W_base", [1024, 1024], F32),
            "W_enc": dr("l2f_W_enc", [8, 1024, 256], F32),
            "W_dec": dr("l2f_W_dec", [8, 256, 1024], F32),
            "W_down": dr("l2f_W_down", [8, 1024, 64], F32),
            "W_up": dr("l2f_W_up", [8, 64, 1024], F32),
            "b_base": dr("l2f_b_base", [128, 8], F32),
            "b_enc": dr("l2f_b_enc", [128, 16], F32),
            "b_dec": dr("l2f_b_dec", [128, 64], F32),
            "b_down": dr("l2f_b_down", [128, 8], F32),
            "b_up": dr("l2f_b_up", [128, 64], F32),
            "cidx": dr("l2f_cidx", [1, 8], I32),
        }
    if (3, "ffn") in need:
        w["l3f"] = {
            "g0T": dr("l3f_g0T", [8, 1024, 2048], F32),
            "g1T": dr("l3f_g1T", [8, 1024, 2048], F32),
            "lw": dr("l3f_lw", [8, 2048, 1024], F32),
            "rs": dr("l3f_rs", [128, 8], F32),
            "rw": dr("l3f_rw", [1024, 8], F32),
            "pes": dr("l3f_pes", [1, 8], F32),
        }
    if (3, "attn") in need:
        w["l3a"] = {
            "wqT": dr("l3a_wqT", [1024, 1024], F32),
            "wkT": dr("l3a_wkT", [1024, 1024], F32),
            "wvT": dr("l3a_wvT", [1024, 1024], F32),
            "woT": dr("l3a_woT", [1024, 1024], F32),
        }

    with contextlib.ExitStack() as stack:
        c = alloc_common(nc, p, stack)
        for i in range(NTILE):
            p.dma("sync", xt(c, i), x_in[i * 128:(i + 1) * 128, :], stream=f"xin{i % 4}")
        done = set()
        for s in sublayers:
            key = SUBLAYERS[s]
            if key == (0, "ffn"):
                emit_l0_ffn(c, w["l0f"], do_dequant=key not in done)
            elif key == (0, "attn"):
                emit_l0_attn(c, w["l0a"])
            elif key == (1, "attn"):
                emit_l1_attn(c, w["l1a"])
            elif key == (2, "attn"):
                emit_l2_attn(c, w["l2a"])
            elif key == (1, "ffn"):
                emit_l1_ffn(c, w["l1f"], do_quant=key not in done)
            elif key == (2, "ffn"):
                emit_l2_ffn(c, w["l2f"])
            elif key == (3, "ffn"):
                emit_l3_ffn(c, w["l3f"])
            elif key == (3, "attn"):
                emit_l3_attn(c, w["l3a"])
            else:
                raise NotImplementedError(str(key))
            done.add(key)
        for i in range(NTILE):
            p.dma("sync", x_out[i * 128:(i + 1) * 128, :], xt(c, i), stream=f"xout{i % 4}")
        for bn in DEBUG_DUMP:
            buf = getattr(c, bn)
            bap = buf.ap() if callable(getattr(buf, "ap", None)) else buf[:]
            sh = list(bap.shape)
            dt_ = bap.dtype
            dap = nc.dram_tensor("dbg_" + bn, sh, dt_, kind="ExternalOutput").ap()
            p.reg(nc.get_tensor("dbg_" + bn) if hasattr(nc, "get_tensor") else _DH(dap), "dram") \
                if False else p.meta.update({dap.name: (int(np.prod(sh)), 2 if dt_ == BF16 else 4, "dram")})
            p.dma("sync", dap, bap, stream="dbg")
        p.barrier_wait("sync")
        p.emit(stack)
    return in_names


def run_sublayers(inputs, sublayers, x_override=None, n_cores=8, trace=False):
    g = _prep_weights(inputs)
    x = np.asarray(x_override if x_override is not None else inputs["L0_attn_x"], np.float32)
    nc = bass.Bass("TRN2", target_bir_lowering=False, dynamic_dma_scratch_size=16384)
    in_names = build_program(nc, sublayers)
    in_maps = []
    for core in range(n_cores):
        m = {}
        for n in in_names:
            if n == "x":
                m["x"] = np.ascontiguousarray(x[core].T)
            else:
                m[n] = g[n]
        in_maps.append(m)
    res = run_bass_kernel_spmd(nc, in_maps, core_ids=list(range(n_cores)), trace=trace)
    outs = [np.asarray(r["out"], np.float32).T for r in res.results]
    return np.stack(outs, axis=0), res


def kernel(**inputs):
    out, _ = run_sublayers(inputs, list(range(12)))
    return out.astype(np.float32)


# version 03
# speedup vs baseline: 1.2317x; 1.0226x over previous
import contextlib
import math
import numpy as np

import concourse.bass as bass
import concourse.mybir as mybir
from concourse.bass_utils import run_bass_kernel_spmd

F32 = mybir.dt.float32
BF16 = mybir.dt.bfloat16
I32 = mybir.dt.int32
AF = mybir.ActivationFunctionType
ALU = mybir.AluOpType
AX = mybir.AxisListType

ENGINES = ("tensor", "vector", "scalar", "gpsimd", "sync")

T = 2048
D = 1024
NTILE = D // 128
TC = 512
NTC = T // TC


import os as _os
SEM_ROT = int(_os.environ.get("SEM_ROT", "30000"))


class Prog:
    def __init__(self, nc, n_dma_sems=24):
        self.nc = nc
        self.ops = {e: [] for e in ENGINES}
        self.cs = {e: 0 for e in ENGINES}
        self.cgen = {e: 0 for e in ENGINES}
        self.n_dma = n_dma_sems
        self.dmac = [0] * n_dma_sems
        self.acc = {}
        self.waited = {e: {} for e in ENGINES}
        self.meta = {}
        self.stream_sem = {}
        self.next_stream = 0
        self.nwaits = 0

    def reg(self, handle, space):
        ap = handle.ap() if hasattr(handle, "ap") else handle[:]
        shape = list(ap.shape)
        rowlen = 1
        for s in shape[1:]:
            rowlen *= s
        if space == "dram":
            rowlen = rowlen * max(1, shape[0] if shape else 1)
        isz = {F32: 4, BF16: 2, I32: 4}.get(ap.dtype, 4)
        self.meta[ap.name] = (rowlen, isz, space)
        return handle

    def _rect(self, ap):
        if callable(getattr(ap, "ap", None)):
            ap = ap.ap()
        name = ap.name
        rowlen, isz, space = self.meta[name]
        if space == "dram":
            lo = hi = ap.offset
            for st, ct in ap.ap:
                if st >= 0:
                    hi += st * (ct - 1)
                else:
                    lo += st * (ct - 1)
            return name, (0, 1, lo * isz, (hi + 1) * isz)
        bp = ap.base_partition
        p0 = bp() if callable(bp) else bp
        pn = ap.partition_size
        pn = pn() if callable(pn) else pn
        f0 = ap.offset - p0 * rowlen
        lo = hi = f0
        for st, ct in ap.ap[1:]:
            if st >= 0:
                hi += st * (ct - 1)
            else:
                lo += st * (ct - 1)
        b0, b1 = lo * isz, (hi + 1) * isz
        if space == "psum":
            b0 = (b0 // 2048) * 2048
            b1 = ((b1 + 2047) // 2048) * 2048
        return name, (p0, p0 + pn, b0, b1)

    @staticmethod
    def _overlap(r1, r2):
        return not (r1[1] <= r2[0] or r2[1] <= r1[0]
                    or r1[3] <= r2[2] or r2[3] <= r1[2])

    def _need(self, engine, aps_r, aps_w):
        need = {}
        for ap, as_write in [(a, False) for a in aps_r] + [(a, True) for a in aps_w]:
            name, rect = self._rect(ap)
            for (is_w, r, sem, val, eng2, isdma) in self.acc.get(name, ()):
                if not (is_w or as_write):
                    continue
                if not self._overlap(rect, r):
                    continue
                if not isdma and eng2 == engine:
                    if not is_w:
                        continue
                    if engine == "tensor":
                        continue
                    if val <= self.cs[engine] - 8:
                        continue
                if self.waited[engine].get(sem, 0) >= val:
                    continue
                if need.get(sem, 0) < val:
                    need[sem] = val
        return need

    def _record(self, aps_r, aps_w, sem, val, engine, isdma):
        for ap, is_w in [(a, False) for a in aps_r] + [(a, True) for a in aps_w]:
            name, rect = self._rect(ap)
            lst = self.acc.setdefault(name, [])
            lst.append((is_w, rect, sem, val, engine, isdma))
            if len(lst) > 192:
                old, keep = lst[:96], lst[96:]
                floors = {}
                for (w2, r2, s2, v2, e2, d2) in old:
                    k = (s2, e2, d2, w2)
                    if floors.get(k, 0) < v2:
                        floors[k] = v2
                whole = (0, 1 << 20, 0, 1 << 40)
                self.acc[name] = [(w2, whole, s2, v2, e2, d2)
                                  for (s2, e2, d2, w2), v2 in floors.items()] + keep

    def op(self, engine, fn, reads=(), writes=()):
        need = self._need(engine, reads, writes)
        for sem, val in need.items():
            self.waited[engine][sem] = val
        self.nwaits += len(need)
        self.cs[engine] += 1
        if self.cs[engine] > SEM_ROT:
            self.cgen[engine] += 1
            self.cs[engine] = 1
        sem = f"c_{engine}_g{self.cgen[engine]}"
        self._record(reads, writes, sem, self.cs[engine], engine, False)
        self.ops[engine].append((need, fn, (sem, 1)))

    def dma(self, engine, out, in_, stream="w", **kw):
        if stream not in self.stream_sem:
            self.stream_sem[stream] = self.next_stream % self.n_dma
            self.next_stream += 1
        si = self.stream_sem[stream]
        need = self._need(engine, [in_], [out])
        if self.dmac[si] > 0:
            prev = 16 * self.dmac[si]
            if self.waited[engine].get(f"d{si}", 0) < prev and need.get(f"d{si}", 0) < prev:
                need[f"d{si}"] = prev
        for sem, val in need.items():
            self.waited[engine][sem] = val
        self.nwaits += len(need)
        self.dmac[si] += 1
        sem = f"d{si}"
        self._record([in_], [out], sem, 16 * self.dmac[si], engine, True)
        fn = lambda e: e.dma_start(out=out, in_=in_, **kw)
        self.ops[engine].append((need, fn, (sem, 16)))

    def barrier_wait(self, engine):
        need = {}
        for e in ENGINES:
            sem = f"c_{e}_g{self.cgen[e]}"
            if e != engine and self.cs[e] > 0 and self.waited[engine].get(sem, 0) < self.cs[e]:
                need[sem] = self.cs[e]
        for i in range(self.n_dma):
            if self.dmac[i] > 0 and self.waited[engine].get(f"d{i}", 0) < 16 * self.dmac[i]:
                need[f"d{i}"] = 16 * self.dmac[i]
        for sem, val in need.items():
            self.waited[engine][sem] = val
        self.ops[engine].append((need, None, None))

    def emit(self, stack):
        nc = self.nc
        sems = {}
        for e in ENGINES:
            for g in range(self.cgen[e] + 1):
                sems[f"c_{e}_g{g}"] = stack.enter_context(nc.semaphore(f"cs_{e}_{g}"))
        for i in range(self.n_dma):
            sems[f"d{i}"] = stack.enter_context(nc.semaphore(f"ds_{i}"))
        block = stack.enter_context(nc.Block())
        for ename in ENGINES:
            steps = self.ops[ename]
            if not steps:
                continue

            def mk(steps):
                def body(eng):
                    for waits, fn, inc in steps:
                        for sem, val in waits.items():
                            eng.wait_ge(sems[sem], val)
                        if fn is not None:
                            ins = fn(eng)
                            if inc is not None:
                                ins.then_inc(sems[inc[0]], inc[1])
                return body
            getattr(block, ename)(mk(steps))


class Ctx:
    pass


NROW = 7
R_MU, R_VAR, R_R1, R_S, R_B, R_T0, R_T1 = range(7)
R_SP, R_BP = R_T0, R_T1
R_ACC = R_S


def row(c, i):
    return c.ROWS[0:1, i * TC:(i + 1) * TC]


def alloc_common(nc, p, stack):
    c = Ctx()
    c.nc = nc
    c.p = p
    sb = lambda name, shape, dt: p.reg(
        stack.enter_context(nc.sbuf_tensor(name, shape, dt)), "sbuf")
    ps = lambda name, shape, dt: p.reg(
        stack.enter_context(nc.psum_tensor(name, shape, dt)), "psum")

    c.X = sb("X", [128, NTILE * T], F32)
    c.A = sb("A", [128, NTILE * T], BF16)
    c.B = sb("B", [128, NTILE * T], BF16)
    c.C = sb("C", [128, NTILE * T], BF16)
    c.SBC = sb("SBC", [128, TC], F32)
    c.BBC = sb("BBC", [128, TC], F32)
    c.SC16 = sb("SC16", [128, T], BF16)
    c.SC16b = sb("SC16b", [128, T], BF16)
    c.LAT = sb("LAT", [128, 2048], BF16)
    c.ROWS = sb("ROWS", [1, NROW * TC], F32)
    c.WS = sb("WS", [128, 2048], F32)
    c.PVEC = sb("PVEC", [128, 112], F32)
    c.CONSTF = sb("CONSTF", [128, 24], F32)
    c.PVI = sb("PVI", [128, 16], I32)
    c.RNDI = sb("RNDI", [128, TC], I32)
    c.TMP32 = sb("TMP32", [128, 512], F32)
    c.QSC = sb("QSC", [128, 96], F32)
    c.SEL = sb("SEL", [8, 1024], BF16)
    c.ONES = sb("ONES", [128, 8], BF16)
    c.ONESROW = sb("ONESROW", [1, 128], BF16)
    c.ONESROWF = sb("ONESROWF", [1, 128], F32)
    c.IDENT = sb("IDENT", [128, 128], BF16)
    c.IDENTF = sb("IDENTF", [128, 128], F32)

    c.PS_BIG = ps("PS_BIG", [128, 2048], F32)
    c.PS_A = ps("PS_A", [128, 512], F32)
    c.PS_B = ps("PS_B", [128, 512], F32)
    c.PS_T = ps("PS_T", [128, 1024], BF16)
    c.PS_O = ps("PS_O", [128, 512], F32)

    p.op("gpsimd", lambda e: e.memset(c.CONSTF[:, 0:1], 1e-5), writes=[c.CONSTF[:, 0:1]])
    p.op("gpsimd", lambda e: e.memset(c.CONSTF[:, 1:2], 1e-6), writes=[c.CONSTF[:, 1:2]])
    p.op("gpsimd", lambda e: e.memset(c.CONSTF[:, 2:3], -math.log(32.0)), writes=[c.CONSTF[:, 2:3]])
    p.op("gpsimd", lambda e: e.memset(c.CONSTF[:, 3:4], -0.5), writes=[c.CONSTF[:, 3:4]])
    p.op("gpsimd", lambda e: e.memset(c.ONES[:], 1.0), writes=[c.ONES[:]])
    p.op("gpsimd", lambda e: e.memset(c.ONESROW[:], 1.0), writes=[c.ONESROW[:]])
    p.op("gpsimd", lambda e: e.memset(c.ONESROWF[:], 1.0), writes=[c.ONESROWF[:]])
    p.op("gpsimd", lambda e: e.iota(c.PVI[:, 0:8], pattern=[[0, 8]], base=0,
         channel_multiplier=1), writes=[c.PVI[:, 0:8]])
    p.op("vector", lambda e: e.tensor_scalar(out=c.PVI[:, 8:16], in0=c.PVI[:, 0:8],
         scalar1=1, scalar2=None, op0=ALU.bitwise_and),
         reads=[c.PVI[:, 0:8]], writes=[c.PVI[:, 8:16]])
    p.op("vector", lambda e: e.tensor_copy(out=c.CONSTF[:, 4:12], in_=c.PVI[:, 8:16]),
         reads=[c.PVI[:, 8:16]], writes=[c.CONSTF[:, 4:12]])
    p.op("vector", lambda e: e.tensor_copy(out=c.CONSTF[:, 12:20], in_=c.PVI[:, 0:8]),
         reads=[c.PVI[:, 0:8]], writes=[c.CONSTF[:, 12:20]])
    p.op("gpsimd", lambda e: e.memset(c.SEL[:], 0.0), writes=[c.SEL[:]])
    for se in range(8):
        p.op("gpsimd", lambda e, sl=c.SEL[:, se * 128:(se + 1) * 128], b=-se: e.affine_select(
            out=sl, in_=sl, compare_op=ALU.not_equal, fill=1.0,
            base=b, pattern=[[0, 128]], channel_multiplier=1),
            reads=[c.SEL[:]], writes=[c.SEL[:]])
    for t_, nm in ((c.IDENT, "id16"), (c.IDENTF, "idf")):
        p.op("gpsimd", lambda e, t=t_: e.memset(t[:], 0.0), writes=[t_[:]])
        p.op("gpsimd", lambda e, t=t_: e.affine_select(
            out=t[:], in_=t[:], compare_op=ALU.not_equal, fill=1.0,
            base=0, pattern=[[-1, 128]], channel_multiplier=1),
            reads=[t_[:]], writes=[t_[:]])
    return c


def xt(c, i):
    return c.X[:, i * T:(i + 1) * T]


def pt(buf, i):
    return buf[:, i * T:(i + 1) * T]


def emit_ln_prefix(c, out_tiles, variant="plain", gamma=None, beta=None,
                   fold_half_y=False, rr_out=None, chunk_cb=None, tm_f32=False):
    p = c.p
    X = [xt(c, i) for i in range(NTILE)]
    onecol = c.ONES[:, 0:1]
    for tc in range(NTC):
        cs = slice(tc * TC, (tc + 1) * TC)
        for i in range(NTILE):
            src = X[i][:, cs]
            xbs = c.SC16[:, (i % 4) * TC:(i % 4 + 1) * TC]
            sqs = c.SC16b[:, (i % 4) * TC:(i % 4 + 1) * TC]
            p.op("vector", lambda e, a=xbs, b=src: e.tensor_copy(out=a, in_=b),
                 reads=[src], writes=[xbs])
            p.op("scalar", lambda e, a=sqs, b=src: e.activation(a, b, AF.Square),
                 reads=[src], writes=[sqs])
            p.op("tensor", lambda e, o=c.PS_A[0:1, :], l=onecol, r=xbs, s=(i == 0), st=(i == NTILE - 1):
                 e.matmul(o, l, r, start=s, stop=st),
                 reads=[onecol, xbs], writes=[c.PS_A[0:1, :]])
            p.op("tensor", lambda e, o=c.PS_B[0:1, :], l=onecol, r=sqs, s=(i == 0), st=(i == NTILE - 1):
                 e.matmul(o, l, r, start=s, stop=st),
                 reads=[onecol, sqs], writes=[c.PS_B[0:1, :]])
        mu, var, r1 = row(c, R_MU), row(c, R_VAR), row(c, R_R1)
        S, Bv = row(c, R_S), row(c, R_B)
        t0, t1 = row(c, R_T0), row(c, R_T1)
        p.op("vector", lambda e, m=mu: e.tensor_scalar(out=m, in0=c.PS_A[0:1, :],
             scalar1=1.0 / D, scalar2=None, op0=ALU.mult),
             reads=[c.PS_A[0:1, :]], writes=[mu])
        p.op("vector", lambda e, a=t0, m=mu: e.tensor_mul(out=a, in0=m, in1=m),
             reads=[mu], writes=[t0])
        p.op("vector", lambda e, v=var, a=t0: e.scalar_tensor_tensor(out=v,
             in0=c.PS_B[0:1, :], scalar=1.0 / D, in1=a, op0=ALU.mult, op1=ALU.subtract),
             reads=[c.PS_B[0:1, :], t0], writes=[var])
        p.op("scalar", lambda e, a=t1, v=var: e.activation(a, v, AF.Sqrt, bias=c.CONSTF[0:1, 0:1]),
             reads=[var, c.CONSTF[0:1, 0:1]], writes=[t1])
        p.op("vector", lambda e, r=r1, a=t1: e.reciprocal(out=r, in_=a),
             reads=[t1], writes=[r1])
        if variant == "plain":
            p.op("vector", lambda e, s=S, r=r1: e.tensor_copy(out=s, in_=r),
                 reads=[r1], writes=[S])
        elif variant == "l0":
            p.op("vector", lambda e, a=t0, v=var, r=r1: e.tensor_mul(out=a, in0=v, in1=r),
                 reads=[var, r1], writes=[t0])
            p.op("vector", lambda e, a=t0, r=r1: e.tensor_mul(out=a, in0=a, in1=r),
                 reads=[t0, r1], writes=[t0])
            p.op("scalar", lambda e, a=t1, b_=t0: e.activation(a, b_, AF.Sqrt, bias=c.CONSTF[0:1, 0:1]),
                 reads=[t0, c.CONSTF[0:1, 0:1]], writes=[t1])
            p.op("vector", lambda e, a=t1: e.reciprocal(out=a, in_=a),
                 reads=[t1], writes=[t1])
            p.op("vector", lambda e, s=S, r=r1, a=t1: e.tensor_mul(out=s, in0=r, in1=a),
                 reads=[r1, t1], writes=[S])
        if rr_out is not None:
            p.op("vector", lambda e, a=t0, v=var, r=r1: e.tensor_mul(out=a, in0=v, in1=r),
                 reads=[var, r1], writes=[t0])
            p.op("vector", lambda e, a=t0, r=r1: e.tensor_mul(out=a, in0=a, in1=r),
                 reads=[t0, r1], writes=[t0])
            p.op("scalar", lambda e, a=t1, b_=t0: e.activation(a, b_, AF.Sqrt, bias=c.CONSTF[0:1, 1:2]),
                 reads=[t0, c.CONSTF[0:1, 1:2]], writes=[t1])
            rr = rr_out(tc)
            p.op("vector", lambda e, a=rr, b_=t1: e.reciprocal(out=a, in_=b_),
                 reads=[t1], writes=[rr])
            p.op("vector", lambda e, a=rr: e.tensor_scalar(out=a, in0=a,
                 scalar1=1.0 / math.sqrt(D), scalar2=None, op0=ALU.mult),
                 reads=[rr], writes=[rr])
        p.op("vector", lambda e, b_=Bv, m=mu, s=S: e.scalar_tensor_tensor(out=b_,
             in0=m, scalar=-1.0, in1=s, op0=ALU.mult, op1=ALU.mult),
             reads=[mu, S], writes=[Bv])
        emit_bcast_chunk(c, S, c.SBC, c.PS_A)
        emit_bcast_chunk(c, Bv, c.BBC, c.PS_B)
        for i in range(NTILE):
            src = X[i][:, cs]
            if tm_f32:
                tmp = c.TMP32[:, 0:TC]
                p.op("vector", lambda e, a=tmp, x=src: e.tensor_mul(out=a, in0=x, in1=c.SBC[:, :]),
                     reads=[src, c.SBC[:, :]], writes=[tmp])
                p.op("gpsimd", lambda e, a=tmp: e.tensor_tensor(out=a, in0=a, in1=c.BBC[:, :], op=ALU.add),
                     reads=[tmp, c.BBC[:, :]], writes=[tmp])
                for j in range(4):
                    tt_ = tc * 4 + j
                    ps = (c.PS_A, c.PS_B)[j % 2]
                    p.op("tensor", lambda e, o=ps[:, 0:128], i_=tmp[:, j * 128:(j + 1) * 128]:
                         e.transpose(o, i_, c.IDENTF[:]),
                         reads=[tmp[:, j * 128:(j + 1) * 128], c.IDENTF[:]], writes=[ps[:, 0:128]])
                    dstj = out_tiles[tt_][:, i * 128:(i + 1) * 128]
                    p.op("vector", lambda e, a=dstj, b_=ps[:, 0:128]: e.tensor_copy(out=a, in_=b_),
                         reads=[ps[:, 0:128]], writes=[dstj])
                continue
            dst = out_tiles[i][:, cs]
            t16 = c.SC16[:, (i % 4) * TC:(i % 4 + 1) * TC]
            p.op("vector", lambda e, a=t16, x=src: e.tensor_mul(out=a, in0=x, in1=c.SBC[:, :]),
                 reads=[src, c.SBC[:, :]], writes=[t16])
            if gamma is None:
                p.op("gpsimd", lambda e, a=dst, x=t16: e.tensor_tensor(out=a, in0=x,
                     in1=c.BBC[:, :], op=ALU.add), reads=[t16, c.BBC[:, :]], writes=[dst])
            else:
                tf = c.SC16b[:, (i % 4) * TC:(i % 4 + 1) * TC]
                p.op("gpsimd", lambda e, a=tf, x=t16: e.tensor_tensor(out=a, in0=x,
                     in1=c.BBC[:, :], op=ALU.add), reads=[t16, c.BBC[:, :]], writes=[tf])
                g = gamma(i)
                bt = beta(i)
                p.op("vector", lambda e, a=dst, x=tf, gg=g, bb=bt:
                     e.tensor_scalar(out=a, in0=x, scalar1=gg, scalar2=bb,
                                     op0=ALU.mult, op1=ALU.add),
                     reads=[tf, g, bt], writes=[dst])
        if chunk_cb is not None:
            chunk_cb(tc)
        if fold_half_y:
            Sp, Bp = row(c, R_SP), row(c, R_BP)
            p.op("vector", lambda e, s=Sp, r=r1: e.tensor_scalar(out=s, in0=r,
                 scalar1=0.5, scalar2=1.0, op0=ALU.mult, op1=ALU.add),
                 reads=[r1], writes=[Sp])
            p.op("vector", lambda e, b_=Bp, m=mu, r=r1: e.scalar_tensor_tensor(out=b_,
                 in0=m, scalar=-0.5, in1=r, op0=ALU.mult, op1=ALU.mult),
                 reads=[mu, r1], writes=[Bp])
            emit_bcast_chunk(c, Sp, c.SBC, c.PS_A)
            emit_bcast_chunk(c, Bp, c.BBC, c.PS_B)
            for i in range(NTILE):
                xs = X[i][:, cs]
                p.op("vector", lambda e, x=xs: e.tensor_mul(out=x, in0=x, in1=c.SBC[:, :]),
                     reads=[xs, c.SBC[:, :]], writes=[xs])
                p.op("vector", lambda e, x=xs: e.tensor_add(out=x, in0=x, in1=c.BBC[:, :]),
                     reads=[xs, c.BBC[:, :]], writes=[xs])


def emit_bcast_chunk(c, src_row, dst, psum):
    p = c.p
    if callable(getattr(dst, "ap", None)):
        dst = dst.ap()
    p.op("tensor", lambda e, o=psum[:, :], r=src_row:
         e.matmul(o, c.ONESROWF[0:1, :], r, start=True, stop=True),
         reads=[c.ONESROWF[0:1, :], src_row], writes=[psum[:, :]])
    p.op("vector", lambda e, a=dst, b=psum[:, :]: e.tensor_copy(out=a, in_=b),
         reads=[psum[:, :]], writes=[dst])


def emit_l0_ffn(c, w, do_dequant):
    p = c.p
    X = [xt(c, i) for i in range(NTILE)]

    gam = c.PVEC[:, 0:8]
    bet = c.PVEC[:, 8:16]
    b1 = c.PVEC[:, 16:48]
    qb2 = c.PVEC[:, 48:56]
    p.dma("sync", gam, w["gamma"], stream="misc")
    p.dma("sync", bet, w["beta"], stream="misc")
    p.dma("sync", b1, w["b1"], stream="misc")
    p.dma("sync", c.PVEC[:, 56:64], w["b2"], stream="misc")
    p.op("vector", lambda e: e.tensor_scalar(out=qb2, in0=c.PVEC[:, 56:64],
         scalar1=0.25, scalar2=None, op0=ALU.mult),
         reads=[c.PVEC[:, 56:64]], writes=[qb2])

    emit_ln_prefix(c, [pt(c.A, i) for i in range(NTILE)], variant="l0",
                   gamma=lambda i: gam[:, i:i + 1], beta=lambda i: bet[:, i:i + 1],
                   fold_half_y=True)

    if do_dequant:
        _dequant_matrix(c, w["q1"], w["s1"], w["w1b"], K=1024, N=4096, tag="dq1")
        _dequant_matrix(c, w["q2"], w["s2"], w["w2b"], K=4096, N=1024, tag="dq2")

    for i in range(NTILE):
        p.op("vector", lambda e, x=X[i], bb=qb2[:, i:i + 1]:
             e.tensor_scalar(out=x, in0=x, scalar1=bb, scalar2=None, op0=ALU.add),
             reads=[X[i], qb2[:, i:i + 1]], writes=[X[i]])

    WS16 = c.WS[:].bitcast(BF16)
    for tci in range(NTC):
        ts_ = slice(tci * TC, (tci + 1) * TC)
        for m in range(32):
            buf = m % 2
            wslot = WS16[:, buf * 1024:(buf + 1) * 1024]
            p.dma("gpsimd", wslot.rearrange("p (g m) -> p g m", m=128),
                  w["w1b"].rearrange("(g p) n -> p g n", p=128)[:, :, m * 128:(m + 1) * 128],
                  stream=f"w1_{buf}")
            ps = (c.PS_A, c.PS_B)[m % 2]
            for g in range(NTILE):
                lhs = wslot[:, g * 128:(g + 1) * 128]
                rhs = pt(c.A, g)[:, ts_]
                p.op("tensor", lambda e, o=ps[:, :], l=lhs, r=rhs, s=(g == 0), st=(g == NTILE - 1):
                     e.matmul(o, l, r, start=s, stop=st),
                     reads=[lhs, rhs], writes=[ps[:, :]])
            ydst = c.B[:, m * TC:(m + 1) * TC]
            bia = b1[:, m:m + 1]
            p.op("scalar", lambda e, a=ydst, b_=ps[:, :], bb=bia:
                 e.activation(a, b_, AF.Silu, bias=bb),
                 reads=[ps[:, :], bia], writes=[ydst])
        for mo in range(NTILE):
            ps = (c.PS_A, c.PS_B)[mo % 2]
            for quar in range(4):
                buf = quar % 2
                wslot = WS16[:, 2048 + buf * 1024:2048 + (buf + 1) * 1024]
                p.dma("gpsimd", wslot.rearrange("p (g m) -> p g m", m=128),
                      w["w2b"][quar * 1024:(quar + 1) * 1024, :]
                      .rearrange("(g p) n -> p g n", p=128)[:, :, mo * 128:(mo + 1) * 128],
                      stream=f"w2_{buf}")
                for g in range(8):
                    gabs = quar * 8 + g
                    lhs = wslot[:, g * 128:(g + 1) * 128]
                    rhs = c.B[:, gabs * TC:(gabs + 1) * TC]
                    p.op("tensor", lambda e, o=ps[:, :], l=lhs, r=rhs, s=(gabs == 0), st=(gabs == 31):
                         e.matmul(o, l, r, start=s, stop=st),
                         reads=[lhs, rhs], writes=[ps[:, :]])
            xdst = X[mo][:, ts_]
            p.op("vector", lambda e, x=xdst, b_=ps[:, :]:
                 e.scalar_tensor_tensor(out=x, in0=b_, scalar=0.25, in1=x,
                                        op0=ALU.mult, op1=ALU.add),
                 reads=[ps[:, :], xdst], writes=[xdst])


def _dequant_matrix(c, q_ap, s_ap, out_ap, K, N, tag):
    p = c.p
    NG = K // 128
    NN = N // 512
    for g in range(NG):
        for j in range(NN):
            buf = (g * NN + j) % 2
            qs = c.WS[:, buf * 512:(buf + 1) * 512]
            p.dma("gpsimd", qs.bitcast(I32),
                  q_ap[g * 128:(g + 1) * 128, j * 512:(j + 1) * 512],
                  stream=f"{tag}_{buf}")
            srow = row(c, R_S if buf == 0 else R_B)
            p.dma("sync", srow, s_ap[g:g + 1, j * 512:(j + 1) * 512], stream=f"{tag}s")
            pss = (c.PS_A, c.PS_B)[buf]
            p.op("tensor", lambda e, o=pss[:, :], r=srow:
                 e.matmul(o, c.ONESROWF[0:1, :], r, start=True, stop=True),
                 reads=[c.ONESROWF[0:1, :], srow], writes=[pss[:, :]])
            tq = c.WS[:, 1024 + buf * 512:1024 + (buf + 1) * 512]
            p.op("scalar", lambda e, a=tq, b_=qs.bitcast(I32):
                 e.activation(a, b_, AF.Copy, bias=-128.0),
                 reads=[qs], writes=[tq])
            ob = c.SC16[:, (buf) * 1024:(buf) * 1024 + 512]
            p.op("vector", lambda e, a=ob, x=tq, s_=pss[:, :]:
                 e.tensor_tensor(out=a, in0=x, in1=s_, op=ALU.mult),
                 reads=[tq, pss[:, :]], writes=[ob])
            p.dma("scalar", out_ap[g * 128:(g + 1) * 128, j * 512:(j + 1) * 512], ob,
                  stream=f"{tag}o")


def emit_attn_core(c, NH, DH, causal, q_ap, k_ap, vT_ap, o_dst, score_hook=None):
    p = c.p
    HPG = 512 // DH
    NGRP = NH // HPG
    SUMS = c.PVEC[:, 64:80]
    RECS = c.PVEC[:, 80:96]
    NMAX = c.PVEC[:, 104:105]
    for qc in range(16):
        ext = (qc + 1) * 128 if causal else T
        ntk = ext // 128
        for grp in range(NGRP):
            for hh in range(HPG):
                h = grp * HPG + hh
                qh = q_ap(h, qc)
                ext1 = min(ext, 1024) if score_hook is None else ext
                NMX2 = c.PVEC[:, 108:110]
                for b in range((ext1 + 511) // 512):
                    w0 = b * 512
                    w1 = min(ext1, w0 + 512)
                    rhs = k_ap(h, w0, w1)
                    o = c.PS_BIG[:, w0:w1]
                    p.op("tensor", lambda e, oo=o, l=qh, r=rhs:
                         e.matmul(oo, l, r, start=True, stop=True),
                         reads=[qh, rhs], writes=[o])
                if score_hook is not None:
                    score_hook(h, qc, ext)
                p.op("vector", lambda e, a=NMX2[:, 0:1], b_=c.PS_BIG[:, 0:ext1]:
                     e.tensor_reduce(out=a, in_=b_, axis=AX.X, op=ALU.max),
                     reads=[c.PS_BIG[:, 0:ext1]], writes=[NMX2[:, 0:1]])
                if ext > ext1:
                    for b in range((ext - 1024 + 511) // 512):
                        w0 = 1024 + b * 512
                        w1 = min(ext, w0 + 512)
                        rhs = k_ap(h, w0, w1)
                        o = c.PS_BIG[:, w0:w1]
                        p.op("tensor", lambda e, oo=o, l=qh, r=rhs:
                             e.matmul(oo, l, r, start=True, stop=True),
                             reads=[qh, rhs], writes=[o])
                    p.op("vector", lambda e, a=NMX2[:, 1:2], b_=c.PS_BIG[:, 1024:ext]:
                         e.tensor_reduce(out=a, in_=b_, axis=AX.X, op=ALU.max),
                         reads=[c.PS_BIG[:, 1024:ext]], writes=[NMX2[:, 1:2]])
                    p.op("vector", lambda e, a=NMAX, b_=NMX2:
                         e.tensor_reduce(out=a, in_=b_, axis=AX.X, op=ALU.max, negate=True),
                         reads=[NMX2], writes=[NMAX])
                else:
                    p.op("vector", lambda e, a=NMAX, b_=NMX2[:, 0:1]:
                         e.tensor_reduce(out=a, in_=b_, axis=AX.X, op=ALU.max, negate=True),
                         reads=[NMX2[:, 0:1]], writes=[NMAX])
                Pbuf = c.SC16
                ACCS = c.PVEC[:, 110:112]
                lo_end = ext - 128 if causal else ext
                na = 0
                for w0 in range(0, lo_end, 1024):
                    w1 = min(lo_end, w0 + 1024)
                    acc = ACCS[:, na:na + 1]
                    na += 1
                    p.op("scalar", lambda e, a=Pbuf[:, w0:w1], b_=c.PS_BIG[:, w0:w1], ac=acc:
                         e.activation(a, b_, AF.Exp, bias=NMAX, accum_out=ac),
                         reads=[c.PS_BIG[:, w0:w1], NMAX], writes=[Pbuf[:, w0:w1], acc])
                sacc = SUMS[:, h:h + 1]
                if causal:
                    dtile = Pbuf[:, ext - 128:ext]
                    p.op("scalar", lambda e, a=dtile, b_=c.PS_BIG[:, ext - 128:ext]:
                         e.activation(a, b_, AF.Exp, bias=NMAX),
                         reads=[c.PS_BIG[:, ext - 128:ext], NMAX], writes=[dtile])
                    p.op("gpsimd", lambda e, a=dtile: e.affine_select(
                         out=a, in_=a, compare_op=ALU.is_ge, fill=0.0,
                         base=0, pattern=[[-1, 128]], channel_multiplier=1),
                         reads=[dtile], writes=[dtile])
                    p.op("vector", lambda e, a=sacc, b_=dtile:
                         e.tensor_reduce(out=a, in_=b_, axis=AX.X, op=ALU.add),
                         reads=[dtile], writes=[sacc])
                    for j in range(na):
                        p.op("vector", lambda e, a=sacc, b_=ACCS[:, j:j + 1]:
                             e.tensor_add(out=a, in0=a, in1=b_),
                             reads=[sacc, ACCS[:, j:j + 1]], writes=[sacc])
                else:
                    p.op("vector", lambda e, a=sacc, b_=ACCS[:, 0:1]: e.tensor_copy(out=a, in_=b_),
                         reads=[ACCS[:, 0:1]], writes=[sacc])
                    for j in range(1, na):
                        p.op("vector", lambda e, a=sacc, b_=ACCS[:, j:j + 1]:
                             e.tensor_add(out=a, in0=a, in1=b_),
                             reads=[sacc, ACCS[:, j:j + 1]], writes=[sacc])
                for t0 in range(0, ntk, 8):
                    t1 = min(ntk, t0 + 8)
                    for t in range(t0, t1):
                        ptile = Pbuf[:, t * 128:(t + 1) * 128]
                        pslot = c.PS_T[:, (t - t0) * 128:(t - t0 + 1) * 128]
                        p.op("tensor", lambda e, o=pslot, i_=ptile: e.transpose(o, i_, c.IDENT[:]),
                             reads=[ptile, c.IDENT[:]], writes=[pslot])
                    nb8 = (t1 - t0) * 128
                    pts8 = c.SC16b[:, 0:nb8]
                    p.op("vector", lambda e, a=pts8, b_=c.PS_T[:, 0:nb8]: e.tensor_copy(out=a, in_=b_),
                         reads=[c.PS_T[:, 0:nb8]], writes=[pts8])
                    for t in range(t0, t1):
                        pts = pts8[:, (t - t0) * 128:(t - t0 + 1) * 128]
                        vts = vT_ap(h, t)
                        o = c.PS_O[:, hh * DH:(hh + 1) * DH]
                        p.op("tensor", lambda e, oo=o, l=pts, r=vts, s=(t == 0), st=(t == ntk - 1):
                             e.matmul(oo, l, r, start=s, stop=st),
                             reads=[pts, vts], writes=[o])
            p.op("vector", lambda e, a=RECS: e.reciprocal(out=a, in_=SUMS),
                 reads=[SUMS], writes=[RECS])
            o16 = c.SC16b[:, 1024 + grp * 512:1024 + (grp + 1) * 512]
            for hh in range(HPG):
                h = grp * HPG + hh
                p.op("vector", lambda e, a=o16[:, hh * DH:(hh + 1) * DH],
                     b_=c.PS_O[:, hh * DH:(hh + 1) * DH], r=RECS[:, h:h + 1]:
                     e.tensor_scalar(out=a, in0=b_, scalar1=r, scalar2=None, op0=ALU.mult),
                     reads=[c.PS_O[:, hh * DH:(hh + 1) * DH], RECS[:, h:h + 1]],
                     writes=[o16[:, hh * DH:(hh + 1) * DH]])
        for g in range(NTILE):
            srcb = c.SC16b[:, 1024 + g * 128:1024 + (g + 1) * 128]
            pslot = c.PS_T[:, (g % 8) * 128:(g % 8) * 128 + 128]
            p.op("tensor", lambda e, o=pslot, i_=srcb: e.transpose(o, i_, c.IDENT[:]),
                 reads=[srcb, c.IDENT[:]], writes=[pslot])
            dst = o_dst(g, qc)
            p.op("vector", lambda e, a=dst, b_=pslot: e.tensor_copy(out=a, in_=b_),
                 reads=[pslot], writes=[dst])


def emit_l0_attn(c, w):
    p = c.p
    X = [xt(c, i) for i in range(NTILE)]
    emit_ln_prefix(c, [pt(c.A, i) for i in range(NTILE)], variant="plain")
    WS16 = c.WS[:].bitcast(BF16)

    bv = c.PVEC[:, 0:8]
    bo = c.PVEC[:, 8:16]
    p.dma("sync", bv, w["wv_b"], stream="misc")
    p.dma("sync", bo, w["out_b"], stream="misc")

    p.dma("gpsimd", c.LAT[:, 0:1024], w["watt"], stream="watt")

    for m in range(NTILE):
        buf = m % 2
        wslot = WS16[:, buf * 1024:(buf + 1) * 1024]
        p.dma("gpsimd", wslot.rearrange("p (g m) -> p g m", m=128),
              w["wvT"].rearrange("(g p) n -> p g n", p=128)[:, :, m * 128:(m + 1) * 128],
              stream=f"wv{buf}")
        for tc in range(NTC):
            ps = (c.PS_A, c.PS_B)[tc % 2]
            for g in range(NTILE):
                lhs = wslot[:, g * 128:(g + 1) * 128]
                rhs = pt(c.A, g)[:, tc * TC:(tc + 1) * TC]
                p.op("tensor", lambda e, o=ps[:, :], l=lhs, r=rhs, s=(g == 0), st=(g == 7):
                     e.matmul(o, l, r, start=s, stop=st),
                     reads=[lhs, rhs], writes=[ps[:, :]])
            dst = c.SC16[:, tc * TC:(tc + 1) * TC]
            p.op("vector", lambda e, a=dst, b_=ps[:, :], bb=bv[:, m:m + 1]:
                 e.tensor_scalar(out=a, in0=b_, scalar1=bb, scalar2=None, op0=ALU.add),
                 reads=[ps[:, :], bv[:, m:m + 1]], writes=[dst])
        for h2 in range(2):
            h = 2 * m + h2
            for t in range(16):
                srct = c.SC16[64 * h2:64 * h2 + 64, t * 128:(t + 1) * 128]
                ident = c.IDENT[64 * h2:64 * h2 + 64, 64 * h2:64 * h2 + 64]
                dstp = c.PS_T[:, t * 64:(t + 1) * 64]
                p.op("tensor", lambda e, o=dstp, i_=srct, ii=ident: e.transpose(o, i_, ii),
                     reads=[srct, ident], writes=[dstp])
            dstv = c.B[:, h * 1024:(h + 1) * 1024]
            p.op("vector", lambda e, a=dstv, b_=c.PS_T[:, :]: e.tensor_copy(out=a, in_=b_),
                 reads=[c.PS_T[:, :]], writes=[dstv])

    for j in range(NTILE):
        wblk = c.LAT[:, j * 128:(j + 1) * 128]
        for tc in range(NTC):
            ps = (c.PS_A, c.PS_B)[tc % 2]
            rhs = pt(c.A, j)[:, tc * TC:(tc + 1) * TC]
            p.op("tensor", lambda e, o=ps[:, :], l=wblk, r=rhs:
                 e.matmul(o, l, r, start=True, stop=True),
                 reads=[wblk, rhs], writes=[ps[:, :]])
            dst = pt(c.C, j)[:, tc * TC:(tc + 1) * TC]
            p.op("scalar", lambda e, a=dst, b_=ps[:, :]: e.activation(a, b_, AF.Copy, scale=0.125),
                 reads=[ps[:, :]], writes=[dst])

    emit_attn_core(
        c, NH=16, DH=64, causal=False,
        q_ap=lambda h, qc: pt(c.C, h // 2)[64 * (h % 2):64 * (h % 2) + 64, qc * 128:(qc + 1) * 128],
        k_ap=lambda h, w0, w1: pt(c.A, h // 2)[64 * (h % 2):64 * (h % 2) + 64, w0:w1],
        vT_ap=lambda h, t: c.B[:, h * 1024 + t * 64:h * 1024 + t * 64 + 64],
        o_dst=lambda g, qc: pt(c.C, g)[:, qc * 128:(qc + 1) * 128])

    for m in range(NTILE):
        buf = m % 2
        wslot = WS16[:, buf * 1024:(buf + 1) * 1024]
        p.dma("gpsimd", wslot.rearrange("p (g m) -> p g m", m=128),
              w["outwT"].rearrange("(g p) n -> p g n", p=128)[:, :, m * 128:(m + 1) * 128],
              stream=f"wo{buf}")
        for tc in range(NTC):
            ps = (c.PS_A, c.PS_B)[tc % 2]
            for g in range(NTILE):
                lhs = wslot[:, g * 128:(g + 1) * 128]
                rhs = pt(c.C, g)[:, tc * TC:(tc + 1) * TC]
                p.op("tensor", lambda e, o=ps[:, :], l=lhs, r=rhs, s=(g == 0), st=(g == 7):
                     e.matmul(o, l, r, start=s, stop=st),
                     reads=[lhs, rhs], writes=[ps[:, :]])
            xdst = X[m][:, tc * TC:(tc + 1) * TC]
            p.op("vector", lambda e, x=xdst, b_=ps[:, :]:
                 e.scalar_tensor_tensor(out=x, in0=b_, scalar=1.0, in1=x,
                                        op0=ALU.mult, op1=ALU.add),
                 reads=[ps[:, :], xdst], writes=[xdst])
    for m in range(NTILE):
        p.op("vector", lambda e, x=X[m], bb=bo[:, m:m + 1]:
             e.tensor_scalar(out=x, in0=x, scalar1=bb, scalar2=None, op0=ALU.add),
             reads=[X[m], bo[:, m:m + 1]], writes=[X[m]])


def emit_l1_attn(c, w):
    p = c.p
    X = [xt(c, i) for i in range(NTILE)]
    emit_ln_prefix(c, [pt(c.A, i) for i in range(NTILE)], variant="plain")
    WS16 = c.WS[:].bitcast(BF16)

    def proj_stream(wap, dst_fn, tag, nk=NTILE, evac=None):
        NM = NTILE
        for m in range(NM):
            buf = m % 2
            wslot = WS16[:, buf * 1024:(buf + 1) * 1024]
            p.dma("gpsimd", wslot.rearrange("p (g m) -> p g m", m=128),
                  wap.rearrange("(g p) n -> p g n", p=128)[:, :, m * 128:(m + 1) * 128],
                  stream=f"{tag}{buf}")
            for tc in range(NTC):
                ps = (c.PS_A, c.PS_B)[tc % 2]
                for g in range(nk):
                    lhs = wslot[:, g * 128:(g + 1) * 128]
                    rhs = pt(c.A, g)[:, tc * TC:(tc + 1) * TC]
                    p.op("tensor", lambda e, o=ps[:, :], l=lhs, r=rhs, s=(g == 0), st=(g == nk - 1):
                         e.matmul(o, l, r, start=s, stop=st),
                         reads=[lhs, rhs], writes=[ps[:, :]])
                if evac is None:
                    dst = dst_fn(m, tc)
                    p.op("scalar", lambda e, a=dst, b_=ps[:, :]: e.activation(a, b_, AF.Copy),
                         reads=[ps[:, :]], writes=[dst])
                else:
                    evac(m, tc, ps)

    proj_stream(w["wk"], lambda m, tc: pt(c.B, m)[:, tc * TC:(tc + 1) * TC], "k")
    proj_stream(w["wv"], lambda m, tc: pt(c.C, m)[:, tc * TC:(tc + 1) * TC], "v")
    for m in range(NTILE):
        bm, cm = pt(c.B, m), pt(c.C, m)
        p.op("vector", lambda e, a=bm, b_=cm: e.tensor_add(out=a, in0=a, in1=b_),
             reads=[bm, cm], writes=[bm])
    p.dma("gpsimd", WS16[:, 0:2048].rearrange("p (g m) -> p g m", m=256),
          w["wdkv"].rearrange("(g p) n -> p g n", p=128), stream="wdkv")
    lat_t = [c.LAT[:, 0:2048], c.SC16[:, 0:2048]]
    for m in range(2):
        for tc in range(NTC):
            ps = (c.PS_A, c.PS_B)[tc % 2]
            for g in range(NTILE):
                lhs = WS16[:, g * 256 + m * 128:g * 256 + (m + 1) * 128]
                rhs = pt(c.B, g)[:, tc * TC:(tc + 1) * TC]
                p.op("tensor", lambda e, o=ps[:, :], l=lhs, r=rhs, s=(g == 0), st=(g == 7):
                     e.matmul(o, l, r, start=s, stop=st),
                     reads=[lhs, rhs], writes=[ps[:, :]])
            dst = lat_t[m][:, tc * TC:(tc + 1) * TC]
            p.op("scalar", lambda e, a=dst, b_=ps[:, :]: e.activation(a, b_, AF.Copy),
                 reads=[ps[:, :]], writes=[dst])
    p.dma("gpsimd", WS16[:, 2048:4096].rearrange("p (g m) -> p g m", m=1024),
          w["wupk"].rearrange("(g p) n -> p g n", p=128), stream="wupk")
    for m in range(NTILE):
        for tc in range(NTC):
            ps = (c.PS_A, c.PS_B)[tc % 2]
            for g in range(2):
                lhs = WS16[:, 2048 + g * 1024 + m * 128:2048 + g * 1024 + (m + 1) * 128]
                rhs = lat_t[g][:, tc * TC:(tc + 1) * TC]
                p.op("tensor", lambda e, o=ps[:, :], l=lhs, r=rhs, s=(g == 0), st=(g == 1):
                     e.matmul(o, l, r, start=s, stop=st),
                     reads=[lhs, rhs], writes=[ps[:, :]])
            dst = pt(c.C, m)[:, tc * TC:(tc + 1) * TC]
            p.op("scalar", lambda e, a=dst, b_=ps[:, :]: e.activation(a, b_, AF.Copy),
                 reads=[ps[:, :]], writes=[dst])
    p.dma("gpsimd", WS16[:, 2048:4096].rearrange("p (g m) -> p g m", m=1024),
          w["wupv"].rearrange("(g p) n -> p g n", p=128), stream="wupv")
    for m in range(NTILE):
        for tc in range(NTC):
            ps = (c.PS_A, c.PS_B)[tc % 2]
            for g in range(2):
                lhs = WS16[:, 2048 + g * 1024 + m * 128:2048 + g * 1024 + (m + 1) * 128]
                rhs = lat_t[g][:, tc * TC:(tc + 1) * TC]
                p.op("tensor", lambda e, o=ps[:, :], l=lhs, r=rhs, s=(g == 0), st=(g == 1):
                     e.matmul(o, l, r, start=s, stop=st),
                     reads=[lhs, rhs], writes=[ps[:, :]])
            dst = c.SC16b[:, tc * TC:(tc + 1) * TC]
            p.op("scalar", lambda e, a=dst, b_=ps[:, :]: e.activation(a, b_, AF.Copy),
                 reads=[ps[:, :]], writes=[dst])
        for h2 in range(2):
            h = 2 * m + h2
            for t in range(16):
                srct = c.SC16b[64 * h2:64 * h2 + 64, t * 128:(t + 1) * 128]
                ident = c.IDENT[64 * h2:64 * h2 + 64, 64 * h2:64 * h2 + 64]
                dstp = c.PS_T[:, t * 64:(t + 1) * 64]
                p.op("tensor", lambda e, o=dstp, i_=srct, ii=ident: e.transpose(o, i_, ii),
                     reads=[srct, ident], writes=[dstp])
            dstv = pt(c.B, h // 2)[:, (h % 2) * 1024:(h % 2) * 1024 + 1024] if False else \
                c.B[:, h * 1024:(h + 1) * 1024]
            p.op("vector", lambda e, a=dstv, b_=c.PS_T[:, :]: e.tensor_copy(out=a, in_=b_),
                 reads=[c.PS_T[:, :]], writes=[dstv])

    for tc in range(NTC):
        for m in range(NTILE):
            buf = m % 2
            wslot = WS16[:, buf * 1024:(buf + 1) * 1024]
            p.dma("gpsimd", wslot.rearrange("p (g m) -> p g m", m=128),
                  w["wq"].rearrange("(g p) n -> p g n", p=128)[:, :, m * 128:(m + 1) * 128],
                  stream=f"wq{buf}")
            ps = (c.PS_A, c.PS_B)[m % 2]
            for g in range(NTILE):
                lhs = wslot[:, g * 128:(g + 1) * 128]
                rhs = pt(c.A, g)[:, tc * TC:(tc + 1) * TC]
                p.op("tensor", lambda e, o=ps[:, :], l=lhs, r=rhs, s=(g == 0), st=(g == 7):
                     e.matmul(o, l, r, start=s, stop=st),
                     reads=[lhs, rhs], writes=[ps[:, :]])
            qtmp = (c.SC16 if m < 4 else c.SC16b)[:, (m % 4) * TC:(m % 4 + 1) * TC]
            p.op("scalar", lambda e, a=qtmp, b_=ps[:, :]: e.activation(a, b_, AF.Copy),
                 reads=[ps[:, :]], writes=[qtmp])
        for m in range(NTILE):
            qtmp = (c.SC16 if m < 4 else c.SC16b)[:, (m % 4) * TC:(m % 4 + 1) * TC]
            dst = pt(c.A, m)[:, tc * TC:(tc + 1) * TC]
            p.op("vector", lambda e, a=dst, b_=qtmp: e.tensor_copy(out=a, in_=b_),
                 reads=[qtmp], writes=[dst])

    LAM = 0.01
    last_qc = [-1]

    def hook(h, qc, ext):
        if last_qc[0] != qc:
            last_qc[0] = qc
            p.op("gpsimd", lambda e: e.iota(c.LAT[:, 0:T], pattern=[[1, T]],
                 base=-qc * 128, channel_multiplier=-1,
                 allow_small_or_imprecise_dtypes=True),
                 writes=[c.LAT[:, 0:T]])
            p.op("scalar", lambda e: e.activation(c.LAT[:, 0:T], c.LAT[:, 0:T], AF.Abs),
                 reads=[c.LAT[:, 0:T]], writes=[c.LAT[:, 0:T]])
            p.op("scalar", lambda e: e.activation(c.LAT[:, 0:T], c.LAT[:, 0:T], AF.Exp,
                 scale=-LAM, bias=c.CONSTF[:, 2:3]),
                 reads=[c.LAT[:, 0:T], c.CONSTF[:, 2:3]], writes=[c.LAT[:, 0:T]])
        p.op("vector", lambda e: e.tensor_tensor(out=c.PS_BIG[:, 0:T], in0=c.PS_BIG[:, 0:T],
             in1=c.LAT[:, 0:T], op=ALU.mult),
             reads=[c.PS_BIG[:, 0:T], c.LAT[:, 0:T]], writes=[c.PS_BIG[:, 0:T]])

    emit_attn_core(
        c, NH=16, DH=64, causal=False,
        q_ap=lambda h, qc: pt(c.A, h // 2)[64 * (h % 2):64 * (h % 2) + 64, qc * 128:(qc + 1) * 128],
        k_ap=lambda h, w0, w1: pt(c.C, h // 2)[64 * (h % 2):64 * (h % 2) + 64, w0:w1],
        vT_ap=lambda h, t: c.B[:, h * 1024 + t * 64:h * 1024 + t * 64 + 64],
        o_dst=lambda g, qc: pt(c.A, g)[:, qc * 128:(qc + 1) * 128],
        score_hook=hook)

    for m in range(NTILE):
        buf = m % 2
        wslot = WS16[:, buf * 1024:(buf + 1) * 1024]
        p.dma("gpsimd", wslot.rearrange("p (g m) -> p g m", m=128),
              w["wo"].rearrange("(g p) n -> p g n", p=128)[:, :, m * 128:(m + 1) * 128],
              stream=f"wo{buf}")
        for tc in range(NTC):
            ps = (c.PS_A, c.PS_B)[tc % 2]
            for g in range(NTILE):
                lhs = wslot[:, g * 128:(g + 1) * 128]
                rhs = pt(c.A, g)[:, tc * TC:(tc + 1) * TC]
                p.op("tensor", lambda e, o=ps[:, :], l=lhs, r=rhs, s=(g == 0), st=(g == 7):
                     e.matmul(o, l, r, start=s, stop=st),
                     reads=[lhs, rhs], writes=[ps[:, :]])
            xdst = X[m][:, tc * TC:(tc + 1) * TC]
            p.op("vector", lambda e, x=xdst, b_=ps[:, :]:
                 e.scalar_tensor_tensor(out=x, in0=b_, scalar=1.0, in1=x,
                                        op0=ALU.mult, op1=ALU.add),
                 reads=[ps[:, :], xdst], writes=[xdst])


def emit_l2_attn(c, w):
    p = c.p
    X = [xt(c, i) for i in range(NTILE)]
    emit_ln_prefix(c, [pt(c.A, i) for i in range(NTILE)], variant="plain")
    WS16 = c.WS[:].bitcast(BF16)
    SCALE = 1.0 / math.sqrt(128.0)

    iop = c.CONSTF[:, 12:13]
    par = c.CONSTF[:, 4:5]
    fv = c.PVEC[:, 2:3]
    qv = c.PVEC[:, 3:4]
    t0 = c.PVEC[:, 6:7]
    p.op("vector", lambda e: e.tensor_scalar(out=t0, in0=par, scalar1=-0.5, scalar2=None,
         op0=ALU.mult), reads=[par], writes=[t0])
    p.op("vector", lambda e: e.scalar_tensor_tensor(out=t0, in0=iop, scalar=0.5, in1=t0,
         op0=ALU.mult, op1=ALU.add), reads=[iop, t0], writes=[t0])
    p.op("scalar", lambda e: e.activation(fv, t0, AF.Exp, scale=-math.log(1e4) / 64.0),
         reads=[t0], writes=[fv])
    p.op("vector", lambda e: e.tensor_scalar(out=fv, in0=fv, scalar1=1.0 / (2 * math.pi),
         scalar2=None, op0=ALU.mult), reads=[fv], writes=[fv])
    p.op("vector", lambda e: e.tensor_scalar(out=qv, in0=par, scalar1=0.25, scalar2=None,
         op0=ALU.mult), reads=[par], writes=[qv])

    for tc in range(NTC):
        cs = slice(tc * TC, (tc + 1) * TC)
        iob = c.SBC[:, :]
        p.op("gpsimd", lambda e, a=iob, b0=tc * TC: e.iota(a, pattern=[[1, TC]], base=b0,
             channel_multiplier=0, allow_small_or_imprecise_dtypes=True), writes=[iob])
        p.op("vector", lambda e, a=c.BBC[:, :], b_=iob: e.tensor_scalar(out=a, in0=b_,
             scalar1=fv, scalar2=qv, op0=ALU.mult, op1=ALU.add),
             reads=[iob, fv, qv], writes=[c.BBC[:, :]])
        ri = c.RNDI[:, :]
        p.op("vector", lambda e, a=ri, b_=c.BBC[:, :]: e.tensor_copy(out=a, in_=b_),
             reads=[c.BBC[:, :]], writes=[ri])
        p.op("vector", lambda e, a=c.SBC[:, :], b_=ri: e.tensor_copy(out=a, in_=b_),
             reads=[ri], writes=[c.SBC[:, :]])
        p.op("vector", lambda e: e.tensor_sub(out=c.BBC[:, :], in0=c.BBC[:, :], in1=c.SBC[:, :]),
             reads=[c.BBC[:, :], c.SBC[:, :]], writes=[c.BBC[:, :]])
        dst = c.LAT[:, cs]
        p.op("scalar", lambda e, a=dst: e.activation(a, c.BBC[:, :], AF.Sin,
             scale=2 * math.pi), reads=[c.BBC[:, :]], writes=[dst])
        p.op("vector", lambda e, a=dst: e.tensor_scalar(out=a, in0=a, scalar1=SCALE,
             scalar2=None, op0=ALU.mult), reads=[dst], writes=[dst])

    p.dma("gpsimd", WS16[:, 0:1024].rearrange("p (g m) -> p g m", m=128),
          w["wkT"].rearrange("(g p) n -> p g n", p=128), stream="wk0")
    for tc in range(NTC):
        ps = (c.PS_A, c.PS_B)[tc % 2]
        for g in range(NTILE):
            lhs = WS16[:, g * 128:(g + 1) * 128]
            rhs = pt(c.A, g)[:, tc * TC:(tc + 1) * TC]
            p.op("tensor", lambda e, o=ps[:, :], l=lhs, r=rhs, s=(g == 0), st=(g == 7):
                 e.matmul(o, l, r, start=s, stop=st),
                 reads=[lhs, rhs], writes=[ps[:, :]])
        dst = c.C[:, tc * TC:(tc + 1) * TC]
        p.op("vector", lambda e, a=dst, b_=ps[:, :]:
             e.tensor_scalar(out=a, in0=b_, scalar1=par, scalar2=None, op0=ALU.add),
             reads=[ps[:, :], par], writes=[dst])
    p.dma("gpsimd", WS16[:, 1024:2048].rearrange("p (g m) -> p g m", m=128),
          w["wvT"].rearrange("(g p) n -> p g n", p=128), stream="wv0")
    for tc in range(NTC):
        ps = (c.PS_A, c.PS_B)[tc % 2]
        for g in range(NTILE):
            lhs = WS16[:, 1024 + g * 128:1024 + (g + 1) * 128]
            rhs = pt(c.A, g)[:, tc * TC:(tc + 1) * TC]
            p.op("tensor", lambda e, o=ps[:, :], l=lhs, r=rhs, s=(g == 0), st=(g == 7):
                 e.matmul(o, l, r, start=s, stop=st),
                 reads=[lhs, rhs], writes=[ps[:, :]])
        dst = c.SC16[:, tc * TC:(tc + 1) * TC]
        p.op("scalar", lambda e, a=dst, b_=ps[:, :]: e.activation(a, b_, AF.Copy),
             reads=[ps[:, :]], writes=[dst])
    for t in range(16):
        srct = c.SC16[:, t * 128:(t + 1) * 128]
        dstp = c.PS_T[:, (t % 8) * 128:(t % 8) * 128 + 128]
        p.op("tensor", lambda e, o=dstp, i_=srct: e.transpose(o, i_, c.IDENT[:]),
             reads=[srct, c.IDENT[:]], writes=[dstp])
        dstv = c.B[:, t * 128:(t + 1) * 128]
        p.op("vector", lambda e, a=dstv, b_=dstp: e.tensor_copy(out=a, in_=b_),
             reads=[dstp], writes=[dstv])

    for tc in range(NTC):
        for m in range(NTILE):
            buf = m % 2
            wslot = WS16[:, buf * 1024:(buf + 1) * 1024]
            p.dma("gpsimd", wslot.rearrange("p (g m) -> p g m", m=128),
                  w["wqT"].rearrange("(g p) n -> p g n", p=128)[:, :, m * 128:(m + 1) * 128],
                  stream=f"wq{buf}")
            ps = (c.PS_A, c.PS_B)[m % 2]
            for g in range(NTILE):
                lhs = wslot[:, g * 128:(g + 1) * 128]
                rhs = pt(c.A, g)[:, tc * TC:(tc + 1) * TC]
                p.op("tensor", lambda e, o=ps[:, :], l=lhs, r=rhs, s=(g == 0), st=(g == 7):
                     e.matmul(o, l, r, start=s, stop=st),
                     reads=[lhs, rhs], writes=[ps[:, :]])
            qtmp = (c.SC16 if m < 4 else c.SC16b)[:, (m % 4) * TC:(m % 4 + 1) * TC]
            pel = c.LAT[:, tc * TC:(tc + 1) * TC]
            p.op("vector", lambda e, a=qtmp, b_=ps[:, :], pe=pel:
                 e.scalar_tensor_tensor(out=a, in0=b_, scalar=SCALE, in1=pe,
                                        op0=ALU.mult, op1=ALU.add),
                 reads=[ps[:, :], pel], writes=[qtmp])
        for m in range(NTILE):
            qtmp = (c.SC16 if m < 4 else c.SC16b)[:, (m % 4) * TC:(m % 4 + 1) * TC]
            dst = pt(c.A, m)[:, tc * TC:(tc + 1) * TC]
            p.op("vector", lambda e, a=dst, b_=qtmp: e.tensor_copy(out=a, in_=b_),
                 reads=[qtmp], writes=[dst])

    import os as _os
    if int(_os.environ.get("L2_SKIP_CORE", "0")):
        return
    emit_attn_core(
        c, NH=8, DH=128, causal=True,
        q_ap=lambda h, qc: pt(c.A, h)[:, qc * 128:(qc + 1) * 128],
        k_ap=lambda h, w0, w1: c.C[:, w0:w1],
        vT_ap=lambda h, t: c.B[:, t * 128:(t + 1) * 128],
        o_dst=lambda g, qc: pt(c.A, g)[:, qc * 128:(qc + 1) * 128])

    for m in range(NTILE):
        buf = m % 2
        wslot = WS16[:, buf * 1024:(buf + 1) * 1024]
        p.dma("gpsimd", wslot.rearrange("p (g m) -> p g m", m=128),
              w["woT"].rearrange("(g p) n -> p g n", p=128)[:, :, m * 128:(m + 1) * 128],
              stream=f"wo{buf}")
        for tc in range(NTC):
            ps = (c.PS_A, c.PS_B)[tc % 2]
            for g in range(NTILE):
                lhs = wslot[:, g * 128:(g + 1) * 128]
                rhs = pt(c.A, g)[:, tc * TC:(tc + 1) * TC]
                p.op("tensor", lambda e, o=ps[:, :], l=lhs, r=rhs, s=(g == 0), st=(g == 7):
                     e.matmul(o, l, r, start=s, stop=st),
                     reads=[lhs, rhs], writes=[ps[:, :]])
            xdst = X[m][:, tc * TC:(tc + 1) * TC]
            p.op("vector", lambda e, x=xdst, b_=ps[:, :]:
                 e.scalar_tensor_tensor(out=x, in0=b_, scalar=1.0, in1=x,
                                        op0=ALU.mult, op1=ALU.add),
                 reads=[ps[:, :], xdst], writes=[xdst])


def emit_l2_ffn(c, w):
    p = c.p
    X = [xt(c, i) for i in range(NTILE)]
    emit_ln_prefix(c, [pt(c.A, i) for i in range(NTILE)], variant="plain")
    WS16 = c.WS[:].bitcast(BF16)

    bbase = c.PVEC[:, 0:8]
    benc = c.PVEC[:, 8:24]
    bdec = c.PVEC[:, 24:32]
    p.dma("sync", bbase, w["b_base"], stream="misc")
    p.dma("sync", benc, w["b_enc"], stream="misc2")
    bdn = c.PVEC[:, 32:40]
    bup = c.PVEC[:, 40:104]
    p.dma("sync", bdn, w["b_down"], stream="misc3")
    p.dma("sync", bup, w["b_up"], stream="misc4")
    bdecv = c.PVEC[:, 104:168] if False else None
    bdec64 = c.TMP32[:, 128:192]
    p.dma("sync", bdec64, w["b_dec"], stream="misc5")

    LOSSV = c.PVEC[:, 64:72]
    TACC = c.PVEC[:, 72:73]

    for e in range(8):
        p.dma("gpsimd", WS16[:, 0:2048].rearrange("p (g m) -> p g m", m=256),
              w["W_enc"][e].rearrange("(g p) n -> p g n", p=128), stream="enc")
        for m2 in range(2):
            for tc in range(NTC):
                ps = (c.PS_A, c.PS_B)[tc % 2]
                for g in range(NTILE):
                    lhs = WS16[:, g * 256 + m2 * 128:g * 256 + (m2 + 1) * 128]
                    rhs = pt(c.A, g)[:, tc * TC:(tc + 1) * TC]
                    p.op("tensor", lambda e_, o=ps[:, :], l=lhs, r=rhs, s=(g == 0), st=(g == 7):
                         e_.matmul(o, l, r, start=s, stop=st),
                         reads=[lhs, rhs], writes=[ps[:, :]])
                dst = pt(c.B, m2)[:, tc * TC:(tc + 1) * TC]
                bia = benc[:, e * 2 + m2:e * 2 + m2 + 1]
                p.op("scalar", lambda e_, a=dst, b_=ps[:, :], bb=bia:
                     e_.activation(a, b_, AF.Relu, bias=bb),
                     reads=[ps[:, :], bia], writes=[dst])
        p.dma("gpsimd", WS16[:, 2048:4096].rearrange("p (g m) -> p g m", m=1024),
              w["W_dec"][e].rearrange("(g p) n -> p g n", p=128), stream="dec")
        first = [True]
        for m in range(NTILE):
            for tc in range(NTC):
                ps = (c.PS_A, c.PS_B)[tc % 2]
                for g in range(2):
                    lhs = WS16[:, 2048 + g * 1024 + m * 128:2048 + g * 1024 + (m + 1) * 128]
                    rhs = pt(c.B, g)[:, tc * TC:(tc + 1) * TC]
                    p.op("tensor", lambda e_, o=ps[:, :], l=lhs, r=rhs, s=(g == 0), st=(g == 1):
                         e_.matmul(o, l, r, start=s, stop=st),
                         reads=[lhs, rhs], writes=[ps[:, :]])
                d = c.SBC[:, :]
                nref = pt(c.A, m)[:, tc * TC:(tc + 1) * TC]
                bd = bdec64[:, e * 8 + m:e * 8 + m + 1]
                p.op("vector", lambda e_, a=d, b_=ps[:, :], bb=bd:
                     e_.tensor_scalar(out=a, in0=b_, scalar1=bb, scalar2=None, op0=ALU.add),
                     reads=[ps[:, :], bd], writes=[d])
                p.op("vector", lambda e_, a=d, n_=nref: e_.tensor_sub(out=a, in0=a, in1=n_),
                     reads=[d, nref], writes=[d])
                sq = c.BBC[:, :]
                p.op("scalar", lambda e_, a=sq, b_=d: e_.activation(a, b_, AF.Square),
                     reads=[d], writes=[sq])
                tgt = LOSSV[:, e:e + 1]
                if first[0]:
                    p.op("vector", lambda e_, a=tgt, b_=sq: e_.tensor_reduce(
                         out=a, in_=b_, axis=AX.X, op=ALU.add),
                         reads=[sq], writes=[tgt])
                    first[0] = False
                else:
                    p.op("vector", lambda e_, a=TACC, b_=sq: e_.tensor_reduce(
                         out=a, in_=b_, axis=AX.X, op=ALU.add),
                         reads=[sq], writes=[TACC])
                    p.op("vector", lambda e_, a=tgt: e_.tensor_add(out=a, in0=a, in1=TACC),
                         reads=[tgt, TACC], writes=[tgt])

    onecol = c.ONES[:, 0:1]
    lv16 = c.SC16[:, 0:8]
    p.op("vector", lambda e_: e_.tensor_copy(out=lv16, in_=LOSSV), reads=[LOSSV], writes=[lv16])
    p.op("tensor", lambda e_: e_.matmul(c.PS_A[0:1, 0:8], onecol, lv16, start=True, stop=True),
         reads=[onecol, lv16], writes=[c.PS_A[0:1, 0:8]])
    LOSS = row(c, R_T0)[:, 0:8]
    p.op("vector", lambda e_: e_.tensor_copy(out=LOSS, in_=c.PS_A[0:1, 0:8]),
         reads=[c.PS_A[0:1, 0:8]], writes=[LOSS])
    MN = row(c, R_T0)[:, 8:9]
    p.op("vector", lambda e_: e_.tensor_reduce(out=MN, in_=LOSS, axis=AX.X, op=ALU.min),
         reads=[LOSS], writes=[MN])
    ISM = row(c, R_T0)[:, 16:24]
    p.op("vector", lambda e_: e_.tensor_scalar(out=ISM, in0=LOSS, scalar1=MN, scalar2=None,
         op0=ALU.subtract), reads=[LOSS, MN], writes=[ISM])
    p.op("scalar", lambda e_: e_.activation(ISM, ISM, AF.Sign), reads=[ISM], writes=[ISM])
    p.op("vector", lambda e_: e_.tensor_scalar(out=ISM, in0=ISM, scalar1=-1.0, scalar2=1.0,
         op0=ALU.mult, op1=ALU.add), reads=[ISM], writes=[ISM])
    ci = c.PVI[0:1, 0:8]
    p.dma("sync", ci, w["cidx"], stream="misc6")
    cif = row(c, R_T0)[:, 24:32]
    p.op("vector", lambda e_: e_.tensor_copy(out=cif, in_=ci), reads=[c.PVI[0:1, 0:8]], writes=[cif])
    OH = row(c, R_T1)[:, 0:8]
    tn = row(c, R_T1)[:, 8:16]
    tu = row(c, R_T1)[:, 16:24]
    for n in range(8):
        p.op("vector", lambda e_, nn=float(n): e_.tensor_scalar(out=tn, in0=cif,
             scalar1=-nn, scalar2=None, op0=ALU.add), reads=[cif], writes=[tn])
        p.op("scalar", lambda e_: e_.activation(tn, tn, AF.Abs), reads=[tn], writes=[tn])
        p.op("scalar", lambda e_: e_.activation(tn, tn, AF.Sign, bias=c.CONSTF[0:1, 3:4]),
             reads=[tn, c.CONSTF[0:1, 3:4]], writes=[tn])
        p.op("vector", lambda e_: e_.tensor_scalar(out=tn, in0=tn, scalar1=-0.5, scalar2=0.5,
             op0=ALU.mult, op1=ALU.add), reads=[tn], writes=[tn])
        p.op("vector", lambda e_: e_.tensor_mul(out=tu, in0=tn, in1=ISM),
             reads=[tn, ISM], writes=[tu])
        p.op("vector", lambda e_, oo=OH[:, n:n + 1]: e_.tensor_reduce(out=oo, in_=tu,
             axis=AX.X, op=ALU.add), reads=[tu], writes=[OH[:, n:n + 1]])
    OHB = c.PVEC[:, 72:80]
    p.op("tensor", lambda e_: e_.matmul(c.PS_B[:, 0:8], c.ONESROWF[0:1, :], OH, start=True, stop=True),
         reads=[c.ONESROWF[0:1, :], OH], writes=[c.PS_B[:, 0:8]])
    p.op("vector", lambda e_: e_.tensor_copy(out=OHB, in_=c.PS_B[:, 0:8]),
         reads=[c.PS_B[:, 0:8]], writes=[OHB])

    ohrep = c.SBC[:, :]
    p.op("vector", lambda e_: e_.tensor_copy(
         out=ohrep.rearrange("p (n w) -> p n w", w=64),
         in_=OHB.unsqueeze(2).broadcast_to([128, 8, 64])),
         reads=[OHB], writes=[ohrep])
    for g in range(NTILE):
        slot = WS16[:, 0:512]
        p.dma("gpsimd", slot.rearrange("p (n r) -> p n r", r=64),
              w["W_down"].rearrange("e (g q) r -> q e r", q=128)[:, :, :] if False else
              w["W_down"][:, g * 128:(g + 1) * 128, :].rearrange("e q r -> q e r"),
              stream="wdn")
        msk = c.BBC[:, :]
        p.op("vector", lambda e_, s_=slot, mm=msk, oo=ohrep: e_.tensor_mul(out=mm, in0=s_, in1=oo),
             reads=[slot, ohrep], writes=[msk])
        p.op("vector", lambda e_, mm=msk: e_.tensor_add(out=mm[:, 0:256], in0=mm[:, 0:256], in1=mm[:, 256:512]),
             reads=[msk], writes=[msk[:, 0:256]])
        p.op("vector", lambda e_, mm=msk: e_.tensor_add(out=mm[:, 0:128], in0=mm[:, 0:128], in1=mm[:, 128:256]),
             reads=[msk[:, 0:256]], writes=[msk[:, 0:128]])
        p.op("vector", lambda e_, mm=msk, dst=c.LAT[:, g * 64:(g + 1) * 64]:
             e_.tensor_add(out=dst, in0=mm[:, 0:64], in1=mm[:, 64:128]),
             reads=[msk[:, 0:128]], writes=[c.LAT[:, g * 64:(g + 1) * 64]])
    ohrep2 = c.SBC[0:64, :].rearrange("p (n w) -> p n w", w=64)
    for m in range(NTILE):
        slot = WS16[0:64, 512:1536]
        p.dma("gpsimd", slot.rearrange("p (n r) -> p n r", r=128),
              w["W_up"][:, :, m * 128:(m + 1) * 128].rearrange("e q r -> q e r"),
              stream="wup")
        msk = c.BBC[0:64, :].rearrange("p (n w) -> p n w", w=128) if False else c.BBC[0:64, :]
        for half in range(2):
            mh = c.BBC[0:64, :]
            src = slot[:, half * 512:(half + 1) * 512]
            ohr = OHB[0:64, half * 4:(half + 1) * 4].unsqueeze(2).broadcast_to([64, 4, 128])
            p.op("vector", lambda e_, a=mh, s_=src, o_=ohr: e_.tensor_tensor(
                 out=a.rearrange("p (n w) -> p n w", w=128), in0=s_.rearrange("p (n w) -> p n w", w=128),
                 in1=o_, op=ALU.mult),
                 reads=[src, OHB], writes=[mh])
            p.op("vector", lambda e_, a=mh: e_.tensor_add(out=a[:, 0:256], in0=a[:, 0:256], in1=a[:, 256:512]),
                 reads=[mh], writes=[mh[:, 0:256]])
            p.op("vector", lambda e_, a=mh: e_.tensor_add(out=a[:, 0:128], in0=a[:, 0:128], in1=a[:, 128:256]),
                 reads=[mh[:, 0:256]], writes=[mh[:, 0:128]])
            if half == 0:
                p.op("vector", lambda e_, a=c.TMP32[0:64, 192:320]: e_.tensor_copy(out=a, in_=mh[:, 0:128]),
                     reads=[mh[:, 0:128]], writes=[c.TMP32[0:64, 192:320]])
            else:
                p.op("vector", lambda e_, dst=c.LAT[0:64, 512 + m * 128:512 + (m + 1) * 128]:
                     e_.tensor_add(out=dst, in0=mh[:, 0:128], in1=c.TMP32[0:64, 192:320]),
                     reads=[mh[:, 0:128], c.TMP32[0:64, 192:320]], writes=[c.LAT[0:64, 512 + m * 128:512 + (m + 1) * 128]])
    bds = c.PVEC[:, 80:81]
    tt8 = c.PVEC[:, 81:89]
    p.op("vector", lambda e_: e_.tensor_mul(out=tt8, in0=bdn, in1=OHB), reads=[bdn, OHB], writes=[tt8])
    p.op("vector", lambda e_: e_.tensor_reduce(out=bds, in_=tt8, axis=AX.X, op=ALU.add),
         reads=[tt8], writes=[bds])
    bus = c.PVEC[:, 96:104]
    t64 = c.TMP32[:, 320:384]
    p.op("vector", lambda e_: e_.tensor_tensor(
         out=t64.rearrange("p (n w) -> p n w", w=8), in0=bup.rearrange("p (n w) -> p n w", w=8),
         in1=OHB.unsqueeze(2).broadcast_to([128, 8, 8]), op=ALU.mult),
         reads=[bup, OHB], writes=[t64])
    p.op("vector", lambda e_: e_.tensor_add(out=t64[:, 0:32], in0=t64[:, 0:32], in1=t64[:, 32:64]),
         reads=[t64], writes=[t64[:, 0:32]])
    p.op("vector", lambda e_: e_.tensor_add(out=t64[:, 0:16], in0=t64[:, 0:16], in1=t64[:, 16:32]),
         reads=[t64[:, 0:32]], writes=[t64[:, 0:16]])
    p.op("vector", lambda e_: e_.tensor_add(out=bus, in0=t64[:, 0:8], in1=t64[:, 8:16]),
         reads=[t64[:, 0:16]], writes=[bus])
    bbh = c.PVEC[:, 104:112]
    p.op("vector", lambda e_: e_.tensor_add(out=bbh, in0=bbase, in1=bus), reads=[bbase, bus], writes=[bbh])
    p.op("vector", lambda e_: e_.tensor_scalar(out=bbh, in0=bbh, scalar1=0.5, scalar2=None,
         op0=ALU.mult), reads=[bbh], writes=[bbh])

    for tc in range(NTC):
        ps = (c.PS_A, c.PS_B)[tc % 2]
        for g in range(NTILE):
            lhs = c.LAT[:, g * 64:(g + 1) * 64]
            rhs = pt(c.A, g)[:, tc * TC:(tc + 1) * TC]
            p.op("tensor", lambda e_, o=ps[0:64, :], l=lhs, r=rhs, s=(g == 0), st=(g == 7):
                 e_.matmul(o, l, r, start=s, stop=st),
                 reads=[lhs, rhs], writes=[ps[0:64, :]])
        dst = c.SC16[0:64, tc * TC:(tc + 1) * TC]
        p.op("scalar", lambda e_, a=dst, b_=ps[0:64, :]:
             e_.activation(a, b_, AF.Relu, bias=bds[0:64, :]),
             reads=[ps[0:64, :], bds], writes=[dst])

    for m in range(NTILE):
        buf = m % 2
        wslot = WS16[:, buf * 1024:(buf + 1) * 1024]
        p.dma("gpsimd", wslot.rearrange("p (g m) -> p g m", m=128),
              w["W_base"].rearrange("(g p) n -> p g n", p=128)[:, :, m * 128:(m + 1) * 128],
              stream=f"wb{buf}")
        for tc in range(NTC):
            ps = (c.PS_A, c.PS_B)[tc % 2]
            for g in range(NTILE):
                lhs = wslot[:, g * 128:(g + 1) * 128]
                rhs = pt(c.A, g)[:, tc * TC:(tc + 1) * TC]
                p.op("tensor", lambda e_, o=ps[:, :], l=lhs, r=rhs, s=(g == 0), st=False:
                     e_.matmul(o, l, r, start=s, stop=st),
                     reads=[lhs, rhs], writes=[ps[:, :]])
            lhs2 = c.LAT[0:64, 512 + m * 128:512 + (m + 1) * 128]
            rhs2 = c.SC16[0:64, tc * TC:(tc + 1) * TC]
            p.op("tensor", lambda e_, o=ps[:, :], l=lhs2, r=rhs2:
                 e_.matmul(o, l, r, start=False, stop=True),
                 reads=[lhs2, rhs2], writes=[ps[:, :]])
            xdst = X[m][:, tc * TC:(tc + 1) * TC]
            p.op("vector", lambda e_, x=xdst, b_=ps[:, :]:
                 e_.scalar_tensor_tensor(out=x, in0=b_, scalar=0.5, in1=x,
                                        op0=ALU.mult, op1=ALU.add),
                 reads=[ps[:, :], xdst], writes=[xdst])
        p.op("vector", lambda e_, x=X[m], bb=bbh[:, m:m + 1]:
             e_.tensor_scalar(out=x, in0=x, scalar1=bb, scalar2=None, op0=ALU.add),
             reads=[X[m], bbh[:, m:m + 1]], writes=[X[m]])


def emit_l3_attn(c, w):
    p = c.p
    X = [xt(c, i) for i in range(NTILE)]
    NH, DH = 16, 64

    emit_ln_prefix(c, [pt(c.A, i) for i in range(NTILE)], variant="plain")

    WS16 = c.WS[:].bitcast(BF16)

    for m in range(NTILE):
        buf = m % 2
        wslot = WS16[:, buf * 1024:(buf + 1) * 1024]
        p.dma("gpsimd", wslot.rearrange("p (g m) -> p g m", m=128),
              w["wkT"].rearrange("(g p) n -> p g n", p=128)[:, :, m * 128:(m + 1) * 128],
              stream=f"wk{buf}")
        for tc in range(NTC):
            ps = (c.PS_A, c.PS_B)[tc % 2]
            for g in range(NTILE):
                lhs = wslot[:, g * 128:(g + 1) * 128]
                rhs = pt(c.A, g)[:, tc * TC:(tc + 1) * TC]
                p.op("tensor", lambda e, o=ps[:, :], l=lhs, r=rhs, s=(g == 0), st=(g == 7):
                     e.matmul(o, l, r, start=s, stop=st),
                     reads=[lhs, rhs], writes=[ps[:, :]])
            dst = pt(c.C, m)[:, tc * TC:(tc + 1) * TC]
            p.op("scalar", lambda e, a=dst, b_=ps[:, :]: e.activation(a, b_, AF.Copy),
                 reads=[ps[:, :]], writes=[dst])

    for m in range(NTILE):
        buf = m % 2
        wslot = WS16[:, 2048 + buf * 1024:2048 + (buf + 1) * 1024]
        p.dma("gpsimd", wslot.rearrange("p (g m) -> p g m", m=128),
              w["wvT"].rearrange("(g p) n -> p g n", p=128)[:, :, m * 128:(m + 1) * 128],
              stream=f"wv{buf}")
        for tc in range(NTC):
            ps = (c.PS_A, c.PS_B)[tc % 2]
            for g in range(NTILE):
                lhs = wslot[:, g * 128:(g + 1) * 128]
                rhs = pt(c.A, g)[:, tc * TC:(tc + 1) * TC]
                p.op("tensor", lambda e, o=ps[:, :], l=lhs, r=rhs, s=(g == 0), st=(g == 7):
                     e.matmul(o, l, r, start=s, stop=st),
                     reads=[lhs, rhs], writes=[ps[:, :]])
            dst = c.SC16[:, tc * TC:(tc + 1) * TC]
            p.op("scalar", lambda e, a=dst, b_=ps[:, :]: e.activation(a, b_, AF.Copy),
                 reads=[ps[:, :]], writes=[dst])
        for h2 in range(2):
            h = 2 * m + h2
            for t in range(16):
                src = c.SC16[64 * h2:64 * h2 + 64, t * 128:(t + 1) * 128]
                ident = c.IDENT[64 * h2:64 * h2 + 64, 64 * h2:64 * h2 + 64]
                dstp = c.PS_T[:, t * 64:(t + 1) * 64]
                p.op("tensor", lambda e, o=dstp, i_=src, ii=ident: e.transpose(o, i_, ii),
                     reads=[src, ident], writes=[dstp])
            dstv = c.B[:, h * 1024:(h + 1) * 1024]
            p.op("vector", lambda e, a=dstv, b_=c.PS_T[:, :]: e.tensor_copy(out=a, in_=b_),
                 reads=[c.PS_T[:, :]], writes=[dstv])

    for tc in range(NTC):
        for m in range(NTILE):
            buf = m % 2
            wslot = WS16[:, buf * 1024:(buf + 1) * 1024]
            p.dma("gpsimd", wslot.rearrange("p (g m) -> p g m", m=128),
                  w["wqT"].rearrange("(g p) n -> p g n", p=128)[:, :, m * 128:(m + 1) * 128],
                  stream=f"wq{buf}")
            ps = (c.PS_A, c.PS_B)[m % 2]
            for g in range(NTILE):
                lhs = wslot[:, g * 128:(g + 1) * 128]
                rhs = pt(c.A, g)[:, tc * TC:(tc + 1) * TC]
                p.op("tensor", lambda e, o=ps[:, :], l=lhs, r=rhs, s=(g == 0), st=(g == 7):
                     e.matmul(o, l, r, start=s, stop=st),
                     reads=[lhs, rhs], writes=[ps[:, :]])
            qtmp = (c.SC16 if m < 4 else c.SC16b)[:, (m % 4) * TC:(m % 4 + 1) * TC]
            p.op("scalar", lambda e, a=qtmp, b_=ps[:, :]: e.activation(a, b_, AF.Copy, scale=0.125),
                 reads=[ps[:, :]], writes=[qtmp])
        for m in range(NTILE):
            qtmp = (c.SC16 if m < 4 else c.SC16b)[:, (m % 4) * TC:(m % 4 + 1) * TC]
            dst = pt(c.A, m)[:, tc * TC:(tc + 1) * TC]
            p.op("vector", lambda e, a=dst, b_=qtmp: e.tensor_copy(out=a, in_=b_),
                 reads=[qtmp], writes=[dst])

    emit_attn_core(
        c, NH=16, DH=64, causal=True,
        q_ap=lambda h, qc: pt(c.A, h // 2)[64 * (h % 2):64 * (h % 2) + 64, qc * 128:(qc + 1) * 128],
        k_ap=lambda h, w0, w1: pt(c.C, h // 2)[64 * (h % 2):64 * (h % 2) + 64, w0:w1],
        vT_ap=lambda h, t: c.B[:, h * 1024 + t * 64:h * 1024 + t * 64 + 64],
        o_dst=lambda g, qc: pt(c.A, g)[:, qc * 128:(qc + 1) * 128])

    for m in range(NTILE):
        buf = m % 2
        wslot = WS16[:, buf * 1024:(buf + 1) * 1024]
        p.dma("gpsimd", wslot.rearrange("p (g m) -> p g m", m=128),
              w["woT"].rearrange("(g p) n -> p g n", p=128)[:, :, m * 128:(m + 1) * 128],
              stream=f"wo{buf}")
        for tc in range(NTC):
            ps = (c.PS_A, c.PS_B)[tc % 2]
            for g in range(NTILE):
                lhs = wslot[:, g * 128:(g + 1) * 128]
                rhs = pt(c.A, g)[:, tc * TC:(tc + 1) * TC]
                p.op("tensor", lambda e, o=ps[:, :], l=lhs, r=rhs, s=(g == 0), st=(g == 7):
                     e.matmul(o, l, r, start=s, stop=st),
                     reads=[lhs, rhs], writes=[ps[:, :]])
            xdst = X[m][:, tc * TC:(tc + 1) * TC]
            p.op("vector", lambda e, x=xdst, b_=ps[:, :]:
                 e.scalar_tensor_tensor(out=x, in0=b_, scalar=1.0, in1=x,
                                        op0=ALU.mult, op1=ALU.add),
                 reads=[ps[:, :], xdst], writes=[xdst])


def emit_l3_ffn(c, w):
    p = c.p
    X = [xt(c, i) for i in range(NTILE)]
    WS16 = c.WS[:].bitcast(BF16)

    rsv = c.PVEC[:, 0:8]
    p.dma("sync", rsv, w["rs"], stream="misc")
    pesr = row(c, R_T0)[:, 40:48]
    p.dma("sync", pesr, w["pes"], stream="misc7")
    PESB = c.PVEC[:, 8:16]
    p.op("tensor", lambda e_: e_.matmul(c.PS_O[:, 0:8], c.ONESROWF[0:1, :], pesr,
         start=True, stop=True), reads=[c.ONESROWF[0:1, :], pesr], writes=[c.PS_O[:, 0:8]])
    p.op("vector", lambda e_: e_.tensor_copy(out=PESB, in_=c.PS_O[:, 0:8]),
         reads=[c.PS_O[:, 0:8]], writes=[PESB])
    rwb = WS16[:, 3968:4032]
    p.dma("gpsimd", rwb.rearrange("p (g m) -> p g m", m=8),
          w["rw"].rearrange("(g p) n -> p g n", p=128), stream="rw")
    for g in range(NTILE):
        sl = rwb[:, g * 8:(g + 1) * 8]
        p.op("vector", lambda e_, s_=sl, rc=rsv[:, g:g + 1]: e_.tensor_scalar(
             out=s_, in0=s_, scalar1=rc, scalar2=None, op0=ALU.mult),
             reads=[rwb, rsv], writes=[sl])

    LOGT = c.TMP32[:, 128:256]
    CT = c.TMP32[:, 0:128]

    def router_cb(tc):
        rr = row(c, R_VAR)
        for g in range(NTILE):
            lhs = rwb[:, g * 8:(g + 1) * 8]
            rhs = pt(c.A, g)[:, tc * TC:(tc + 1) * TC]
            p.op("tensor", lambda e_, o=c.PS_A[0:8, :], l=lhs, r=rhs, s=(g == 0), st=(g == 7):
                 e_.matmul(o, l, r, start=s, stop=st),
                 reads=[lhs, rhs], writes=[c.PS_A[0:8, :]])
        p.op("tensor", lambda e_, r=rr: e_.matmul(c.PS_B[0:8, :], c.ONESROWF[0:1, 0:8], r,
             start=True, stop=True), reads=[c.ONESROWF[0:1, 0:8], rr], writes=[c.PS_B[0:8, :]])
        lg = c.SBC[0:8, :]
        p.op("vector", lambda e_, a=lg: e_.tensor_copy(out=a, in_=c.PS_B[0:8, :]),
             reads=[c.PS_B[0:8, :]], writes=[lg])
        p.op("vector", lambda e_, a=lg: e_.tensor_tensor(out=a, in0=c.PS_A[0:8, :], in1=a,
             op=ALU.mult), reads=[c.PS_A[0:8, :], lg], writes=[lg])
        for j in range(4):
            t = tc * 4 + j
            srcl = lg[:, j * 128:(j + 1) * 128]
            ps = c.PS_O[:, (j % 2) * 128:(j % 2) * 128 + 8]
            p.op("tensor", lambda e_, o=ps, i_=srcl: e_.transpose(o, i_, c.IDENTF[0:8, 0:8]),
                 reads=[srcl, c.IDENTF[0:8, 0:8]], writes=[ps])
            dst = LOGT[:, t * 8:(t + 1) * 8]
            p.op("vector", lambda e_, a=dst, b_=ps: e_.tensor_copy(out=a, in_=b_),
                 reads=[ps], writes=[dst])

    emit_ln_prefix(c, [pt(c.A, i) for i in range(NTILE)], variant="plain",
                   rr_out=lambda tc: row(c, R_VAR), chunk_cb=router_cb)

    T8a = c.PVEC[:, 96:104]
    T8b = c.PVEC[:, 88:96]
    M1 = c.PVEC[:, 104:105]
    NM1 = c.PVEC[:, 105:106]
    M2 = c.PVEC[:, 106:107]
    SS = c.PVEC[:, 107:108]
    for t in range(16):
        lt = LOGT[:, t * 8:(t + 1) * 8]
        p.op("vector", lambda e_, l=lt: e_.tensor_reduce(out=M1, in_=l, axis=AX.X, op=ALU.max),
             reads=[lt], writes=[M1])
        p.op("vector", lambda e_, l=lt: e_.tensor_scalar(out=T8a, in0=l, scalar1=M1,
             scalar2=None, op0=ALU.subtract), reads=[lt, M1], writes=[T8a])
        p.op("scalar", lambda e_: e_.activation(T8a, T8a, AF.Sign), reads=[T8a], writes=[T8a])
        p.op("vector", lambda e_: e_.tensor_scalar(out=T8a, in0=T8a, scalar1=1.0,
             scalar2=None, op0=ALU.add), reads=[T8a], writes=[T8a])
        p.op("vector", lambda e_, l=lt: e_.scalar_tensor_tensor(out=T8a, in0=T8a,
             scalar=-1e30, in1=l, op0=ALU.mult, op1=ALU.add), reads=[T8a, lt], writes=[T8a])
        p.op("vector", lambda e_: e_.tensor_reduce(out=M2, in_=T8a, axis=AX.X, op=ALU.max),
             reads=[T8a], writes=[M2])
        p.op("vector", lambda e_, l=lt: e_.tensor_scalar(out=T8a, in0=l, scalar1=M2,
             scalar2=None, op0=ALU.subtract), reads=[lt, M2], writes=[T8a])
        p.op("scalar", lambda e_: e_.activation(T8a, T8a, AF.Sign), reads=[T8a], writes=[T8a])
        p.op("vector", lambda e_: e_.tensor_scalar(out=T8a, in0=T8a, scalar1=1.0,
             scalar2=1.0, op0=ALU.add, op1=ALU.min), reads=[T8a], writes=[T8a])
        p.op("vector", lambda e_: e_.tensor_scalar(out=NM1, in0=M1, scalar1=-1.0,
             scalar2=None, op0=ALU.mult), reads=[M1], writes=[NM1])
        p.op("scalar", lambda e_, l=lt: e_.activation(T8b, l, AF.Exp, bias=NM1),
             reads=[lt, NM1], writes=[T8b])
        p.op("vector", lambda e_: e_.tensor_mul(out=T8b, in0=T8b, in1=T8a),
             reads=[T8b, T8a], writes=[T8b])
        p.op("vector", lambda e_: e_.tensor_reduce(out=SS, in_=T8b, axis=AX.X, op=ALU.add),
             reads=[T8b], writes=[SS])
        p.op("vector", lambda e_: e_.reciprocal(out=SS, in_=SS), reads=[SS], writes=[SS])
        p.op("vector", lambda e_: e_.tensor_scalar(out=T8b, in0=T8b, scalar1=SS,
             scalar2=None, op0=ALU.mult), reads=[T8b, SS], writes=[T8b])
        p.op("vector", lambda e_, o=CT[:, t * 8:(t + 1) * 8]: e_.tensor_mul(out=o, in0=T8b,
             in1=PESB), reads=[T8b, PESB], writes=[CT[:, t * 8:(t + 1) * 8]])

    for t in range(16):
        srcc = CT[:, t * 8:(t + 1) * 8]
        ps = c.PS_O[0:8, (t % 2) * 128:(t % 2) * 128 + 128]
        p.op("tensor", lambda e_, o=ps, i_=srcc: e_.transpose(o, i_, c.IDENTF[:]),
             reads=[srcc, c.IDENTF[:]], writes=[ps])
        dst = c.LAT[0:8, t * 128:(t + 1) * 128]
        p.op("vector", lambda e_, a=dst, b_=ps: e_.tensor_copy(out=a, in_=b_),
             reads=[ps], writes=[dst])

    for e in range(8):
        for t in range(16):
            rhs = c.LAT[0:8, t * 128:(t + 1) * 128]
            lhs = c.SEL[:, e * 128:(e + 1) * 128]
            ps = (c.PS_A, c.PS_B)[t % 2]
            p.op("tensor", lambda e_, o=ps[:, 0:128], l=lhs, r=rhs:
                 e_.matmul(o, l, r, start=True, stop=True),
                 reads=[lhs, rhs], writes=[ps[:, 0:128]])
            dst = c.SC16[:, t * 128:(t + 1) * 128]
            p.op("vector", lambda e_, a=dst, b_=ps[:, 0:128]: e_.tensor_copy(out=a, in_=b_),
                 reads=[ps[:, 0:128]], writes=[dst])
        for hidm in range(16):
            buf = hidm % 2
            w0s = WS16[:, buf * 1024:(buf + 1) * 1024]
            w1s = WS16[:, 2048 + buf * 1024:2048 + (buf + 1) * 1024]
            p.dma("gpsimd", w0s.rearrange("p (g m) -> p g m", m=128),
                  w["g0T"][e].rearrange("(g p) n -> p g n", p=128)[:, :, hidm * 128:(hidm + 1) * 128],
                  stream=f"g0_{buf}")
            p.dma("gpsimd", w1s.rearrange("p (g m) -> p g m", m=128),
                  w["g1T"][e].rearrange("(g p) n -> p g n", p=128)[:, :, hidm * 128:(hidm + 1) * 128],
                  stream=f"g1_{buf}")
            pool = c.B if hidm < 8 else c.C
            for tc in range(NTC):
                for g in range(NTILE):
                    lhs = w0s[:, g * 128:(g + 1) * 128]
                    rhs = pt(c.A, g)[:, tc * TC:(tc + 1) * TC]
                    p.op("tensor", lambda e_, o=c.PS_A[:, :], l=lhs, r=rhs, s=(g == 0), st=(g == 7):
                         e_.matmul(o, l, r, start=s, stop=st),
                         reads=[lhs, rhs], writes=[c.PS_A[:, :]])
                gg = c.SC16b[:, (tc % 2) * TC:(tc % 2) * TC + TC]
                p.op("scalar", lambda e_, a=gg: e_.activation(a, c.PS_A[:, :], AF.Gelu_apprx_tanh),
                     reads=[c.PS_A[:, :]], writes=[gg])
                for g in range(NTILE):
                    lhs = w1s[:, g * 128:(g + 1) * 128]
                    rhs = pt(c.A, g)[:, tc * TC:(tc + 1) * TC]
                    p.op("tensor", lambda e_, o=c.PS_B[:, :], l=lhs, r=rhs, s=(g == 0), st=(g == 7):
                         e_.matmul(o, l, r, start=s, stop=st),
                         reads=[lhs, rhs], writes=[c.PS_B[:, :]])
                slot = pool[:, (hidm % 8) * T + tc * TC:(hidm % 8) * T + (tc + 1) * TC]
                p.op("vector", lambda e_, a=slot, g_=gg: e_.tensor_tensor(out=a, in0=g_,
                     in1=c.PS_B[:, :], op=ALU.mult), reads=[gg, c.PS_B[:, :]], writes=[slot])
                cbch = c.SC16[:, tc * TC:(tc + 1) * TC]
                p.op("vector", lambda e_, a=slot, cb=cbch: e_.tensor_tensor(out=a, in0=a,
                     in1=cb, op=ALU.mult), reads=[slot, cbch], writes=[slot])
        for dm in range(NTILE):
            buf = dm % 2
            lws = WS16[:, buf * 2048:(buf + 1) * 2048]
            p.dma("gpsimd", lws.rearrange("p (g m) -> p g m", m=128),
                  w["lw"][e].rearrange("(g p) n -> p g n", p=128)[:, :, dm * 128:(dm + 1) * 128],
                  stream=f"lw{buf}")
            for tc in range(NTC):
                ps = (c.PS_A, c.PS_B)[tc % 2]
                for k2 in range(16):
                    pool = c.B if k2 < 8 else c.C
                    lhs = lws[:, k2 * 128:(k2 + 1) * 128]
                    rhs = pool[:, (k2 % 8) * T + tc * TC:(k2 % 8) * T + (tc + 1) * TC]
                    p.op("tensor", lambda e_, o=ps[:, :], l=lhs, r=rhs, s=(k2 == 0), st=(k2 == 15):
                         e_.matmul(o, l, r, start=s, stop=st),
                         reads=[lhs, rhs], writes=[ps[:, :]])
                xdst = X[dm][:, tc * TC:(tc + 1) * TC]
                p.op("vector", lambda e_, x=xdst, b_=ps[:, :]:
                     e_.scalar_tensor_tensor(out=x, in0=b_, scalar=0.5, in1=x,
                                            op0=ALU.mult, op1=ALU.add),
                     reads=[ps[:, :], xdst], writes=[xdst])


def emit_quant(c, src, dst, W, xn=None, tmp=None):
    p = c.p
    nb = W // 16
    BM = c.QSC[:, 0:nb]
    S = c.QSC[:, 32:32 + nb]
    R = c.QSC[:, 64:64 + nb]
    XN = xn if xn is not None else c.TMP32[:, 0:W]
    UN = c.SBC[:, 0:W]
    QA = c.BBC[:, 0:W]
    TB = tmp if tmp is not None else c.PS_O[:, 0:W]
    RN = c.RNDI[:, 0:W]
    p.op("vector", lambda e: e.tensor_reduce(out=BM, in_=src.rearrange("p (n w) -> p n w", w=16),
         axis=AX.X, op=ALU.max, apply_absolute_value=True), reads=[src], writes=[BM])
    p.op("vector", lambda e: e.tensor_scalar(out=S, in0=BM, scalar1=1.0 / 6.0, scalar2=1e-12,
         op0=ALU.mult, op1=ALU.max), reads=[BM], writes=[S])
    p.op("vector", lambda e: e.reciprocal(out=R, in_=S), reads=[S], writes=[R])
    p.op("vector", lambda e: e.tensor_tensor(out=XN.rearrange("p (n w) -> p n w", w=16),
         in0=src.rearrange("p (n w) -> p n w", w=16),
         in1=R.unsqueeze(2).broadcast_to([128, nb, 16]), op=ALU.mult),
         reads=[src, R], writes=[XN])
    p.op("scalar", lambda e: e.activation(UN, XN, AF.Abs), reads=[XN], writes=[UN])
    p.op("vector", lambda e: e.tensor_scalar(out=QA, in0=UN, scalar1=2.0, scalar2=None,
         op0=ALU.mult), reads=[UN], writes=[QA])
    p.op("vector", lambda e: e.tensor_copy(out=RN, in_=QA), reads=[QA], writes=[RN])
    p.op("scalar", lambda e: e.copy(out=QA, in_=RN), reads=[RN], writes=[QA])
    p.op("vector", lambda e: e.tensor_scalar(out=QA, in0=QA, scalar1=0.5, scalar2=2.0,
         op0=ALU.mult, op1=ALU.min), reads=[QA], writes=[QA])
    p.op("vector", lambda e: e.tensor_scalar(out=TB, in0=UN, scalar1=-2.0, scalar2=None,
         op0=ALU.add), reads=[UN], writes=[TB])
    p.op("vector", lambda e: e.tensor_copy(out=RN, in_=TB), reads=[TB], writes=[RN])
    p.op("scalar", lambda e: e.copy(out=TB, in_=RN), reads=[RN], writes=[TB])
    p.op("vector", lambda e: e.tensor_scalar(out=TB, in0=TB, scalar1=0.0, scalar2=2.0,
         op0=ALU.max, op1=ALU.min), reads=[TB], writes=[TB])
    p.op("vector", lambda e: e.tensor_add(out=QA, in0=QA, in1=TB), reads=[QA, TB], writes=[QA])
    p.op("vector", lambda e: e.tensor_scalar(out=TB, in0=UN, scalar1=-4.5, scalar2=None,
         op0=ALU.add), reads=[UN], writes=[TB])
    p.op("vector", lambda e: e.tensor_copy(out=RN, in_=TB), reads=[TB], writes=[RN])
    p.op("scalar", lambda e: e.copy(out=TB, in_=RN), reads=[RN], writes=[TB])
    p.op("vector", lambda e: e.tensor_scalar(out=TB, in0=TB, scalar1=0.0, scalar2=1.0,
         op0=ALU.max, op1=ALU.min), reads=[TB], writes=[TB])
    p.op("vector", lambda e: e.tensor_add(out=QA, in0=QA, in1=TB), reads=[QA, TB], writes=[QA])
    p.op("vector", lambda e: e.tensor_add(out=QA, in0=QA, in1=TB), reads=[QA, TB], writes=[QA])
    p.op("scalar", lambda e: e.activation(TB, XN, AF.Sign), reads=[XN], writes=[TB])
    p.op("vector", lambda e: e.tensor_mul(out=QA, in0=QA, in1=TB), reads=[QA, TB], writes=[QA])
    p.op("vector", lambda e: e.tensor_tensor(out=dst.rearrange("p (n w) -> p n w", w=16),
         in0=QA.rearrange("p (n w) -> p n w", w=16),
         in1=S.unsqueeze(2).broadcast_to([128, nb, 16]), op=ALU.mult),
         reads=[QA, S], writes=[dst])


def emit_l1_ffn(c, w, do_quant):
    p = c.p
    X = [xt(c, i) for i in range(NTILE)]
    WS16 = c.WS[:].bitcast(BF16)
    BF = c.B[:].bitcast(F32)
    CF = c.C[:].bitcast(F32)

    def xtm(tt):
        if tt < 8:
            return BF[:, tt * 1024:(tt + 1) * 1024]
        return CF[:, (tt - 8) * 1024:(tt - 7) * 1024]

    emit_ln_prefix(c, [xtm(tt) for tt in range(16)], variant="plain", tm_f32=True)

    for tt in range(16):
        for ch in range(2):
            srcq = xtm(tt)[:, ch * 512:(ch + 1) * 512]
            dstq = c.A[:, tt * 1024 + ch * 512:tt * 1024 + (ch + 1) * 512]
            emit_quant(c, srcq, dstq, 512)
    for tt in range(16):
        for dch in range(NTILE):
            srct = c.A[:, tt * 1024 + dch * 128:tt * 1024 + (dch + 1) * 128]
            pslot = c.PS_T[:, (dch % 8) * 128:(dch % 8) * 128 + 128]
            p.op("tensor", lambda e, o=pslot, i_=srct: e.transpose(o, i_, c.IDENT[:]),
                 reads=[srct, c.IDENT[:]], writes=[pslot])
            dstt = pt(c.B, dch)[:, tt * 128:(tt + 1) * 128]
            p.op("vector", lambda e, a=dstt, b_=pslot: e.tensor_copy(out=a, in_=b_),
                 reads=[pslot], writes=[dstt])

    if do_quant:
        for wap, outap, nr in ((w["wg"], w["wgT"], 2816), (w["wu"], w["wuT"], 2816)):
            for it in range(22):
                buf = it % 2
                slot = c.WS[:, buf * 1024:(buf + 1) * 1024]
                p.dma("gpsimd", slot, wap[it * 128:(it + 1) * 128, :], stream=f"qw{buf}")
                qd = c.SC16[:, 0:1024]
                for ch in range(2):
                    emit_quant(c, slot[:, ch * 512:(ch + 1) * 512],
                               qd[:, ch * 512:(ch + 1) * 512], 512)
                tout = c.SC16b[:, 0:1024]
                for dch in range(NTILE):
                    pslot = c.PS_T[:, (dch % 8) * 128:(dch % 8) * 128 + 128]
                    p.op("tensor", lambda e, o=pslot, i_=qd[:, dch * 128:(dch + 1) * 128]:
                         e.transpose(o, i_, c.IDENT[:]),
                         reads=[qd[:, dch * 128:(dch + 1) * 128], c.IDENT[:]], writes=[pslot])
                    p.op("vector", lambda e, a=tout[:, dch * 128:(dch + 1) * 128], b_=pslot:
                         e.tensor_copy(out=a, in_=b_),
                         reads=[pslot], writes=[tout[:, dch * 128:(dch + 1) * 128]])
                p.dma("scalar",
                      outap.rearrange("(dch p) n -> p dch n", p=128)[:, :, it * 128:(it + 1) * 128],
                      tout.rearrange("p (dch m) -> p dch m", m=128), stream="qwo")
        for dt in range(NTILE):
            slot = c.WS[:, 0:1408]
            for half in range(2):
                p.dma("gpsimd", slot,
                      w["wd"][dt * 128:(dt + 1) * 128, half * 1408:(half + 1) * 1408],
                      stream="qd0")
                qd = c.SC16[:, 0:1408]
                emit_quant(c, slot[:, 0:512], qd[:, 0:512], 512)
                emit_quant(c, slot[:, 512:1024], qd[:, 512:1024], 512)
                emit_quant(c, slot[:, 1024:1408], qd[:, 1024:1408], 384)
                tout = c.SC16b[:, 0:1408]
                for ich in range(11):
                    pslot = c.PS_T[:, (ich % 8) * 128:(ich % 8) * 128 + 128]
                    p.op("tensor", lambda e, o=pslot, i_=qd[:, ich * 128:(ich + 1) * 128]:
                         e.transpose(o, i_, c.IDENT[:]),
                         reads=[qd[:, ich * 128:(ich + 1) * 128], c.IDENT[:]], writes=[pslot])
                    p.op("vector", lambda e, a=tout[:, ich * 128:(ich + 1) * 128], b_=pslot:
                         e.tensor_copy(out=a, in_=b_),
                         reads=[pslot], writes=[tout[:, ich * 128:(ich + 1) * 128]])
                p.dma("scalar",
                      w["wdT"][half * 1408:(half + 1) * 1408, :]
                      .rearrange("(ich p) n -> p ich n", p=128)[:, :, dt * 128:(dt + 1) * 128],
                      tout.rearrange("p (ich m) -> p ich m", m=128), stream="qdo")

    NIC = 11
    for slab in range(4):
        ts0 = slab * 512
        for ic in range(NIC):
            bufg = ic % 2
            gslot = WS16[:, bufg * 2048:(bufg + 1) * 2048]
            p.dma("gpsimd", gslot.rearrange("p (g m) -> p g m", m=256),
                  w["wgT"].rearrange("(g p) n -> p g n", p=128)[:, :, ic * 256:(ic + 1) * 256],
                  stream=f"gt{bufg}")
            for tokm in range(4):
                tt = slab * 4 + tokm
                ps = (c.PS_A, c.PS_B)[tokm % 2]
                for g in range(NTILE):
                    lhs = pt(c.B, g)[:, tt * 128:(tt + 1) * 128]
                    rhs = gslot[:, g * 256:(g + 1) * 256]
                    p.op("tensor", lambda e, o=ps[:, 0:256], l=lhs, r=rhs, s=(g == 0), st=(g == 7):
                         e.matmul(o, l, r, start=s, stop=st),
                         reads=[lhs, rhs], writes=[ps[:, 0:256]])
                gg = c.SC16b[:, tokm * 256:(tokm + 1) * 256]
                p.op("scalar", lambda e, a=gg, b_=ps[:, 0:256]: e.activation(a, b_, AF.Silu),
                     reads=[ps[:, 0:256]], writes=[gg])
            uslot = WS16[:, 2048 + bufg * 2048:2048 + (bufg + 1) * 2048] if False else gslot
            bufu = (ic + 1) % 2
            uslot = WS16[:, bufu * 2048:(bufu + 1) * 2048]
            p.dma("gpsimd", uslot.rearrange("p (g m) -> p g m", m=256),
                  w["wuT"].rearrange("(g p) n -> p g n", p=128)[:, :, ic * 256:(ic + 1) * 256],
                  stream=f"ut{bufu}")
            for tokm in range(4):
                tt = slab * 4 + tokm
                ps = (c.PS_A, c.PS_B)[tokm % 2]
                for g in range(NTILE):
                    lhs = pt(c.B, g)[:, tt * 128:(tt + 1) * 128]
                    rhs = uslot[:, g * 256:(g + 1) * 256]
                    p.op("tensor", lambda e, o=ps[:, 0:256], l=lhs, r=rhs, s=(g == 0), st=(g == 7):
                         e.matmul(o, l, r, start=s, stop=st),
                         reads=[lhs, rhs], writes=[ps[:, 0:256]])
                hh = c.TMP32[:, 256:512]
                gg = c.SC16b[:, tokm * 256:(tokm + 1) * 256]
                p.op("vector", lambda e, a=hh, g_=gg, b_=ps[:, 0:256]:
                     e.tensor_tensor(out=a, in0=g_, in1=b_, op=ALU.mult),
                     reads=[gg, ps[:, 0:256]], writes=[hh])
                hq = c.SC16[:, tokm * 256:(tokm + 1) * 256]
                emit_quant(c, hh, hq, 256, xn=c.PS_O[:, 0:256], tmp=c.PS_O[:, 256:512])
                for j in range(2):
                    it = ic * 2 + j
                    pslot = c.PS_T[:, ((tokm * 2 + j) % 8) * 128:((tokm * 2 + j) % 8) * 128 + 128]
                    p.op("tensor", lambda e, o=pslot, i_=hq[:, j * 128:(j + 1) * 128]:
                         e.transpose(o, i_, c.IDENT[:]),
                         reads=[hq[:, j * 128:(j + 1) * 128], c.IDENT[:]], writes=[pslot])
                    dsth = c.C[:, it * 512 + tokm * 128:it * 512 + (tokm + 1) * 128]
                    p.op("vector", lambda e, a=dsth, b_=pslot: e.tensor_copy(out=a, in_=b_),
                         reads=[pslot], writes=[dsth])
        for dm in range(NTILE):
            lws = WS16[:, 0:2816]
            p.dma("gpsimd", lws.rearrange("p (g m) -> p g m", m=128),
                  w["wdT"].rearrange("(g p) n -> p g n", p=128)[:, :, dm * 128:(dm + 1) * 128],
                  stream="dnw")
            ps = (c.PS_A, c.PS_B)[dm % 2]
            for k2 in range(22):
                lhs = lws[:, k2 * 128:(k2 + 1) * 128]
                rhs = c.C[:, k2 * 512:(k2 + 1) * 512]
                p.op("tensor", lambda e, o=ps[:, :], l=lhs, r=rhs, s=(k2 == 0), st=(k2 == 21):
                     e.matmul(o, l, r, start=s, stop=st),
                     reads=[lhs, rhs], writes=[ps[:, :]])
            xdst = X[dm][:, ts0:ts0 + 512]
            p.op("vector", lambda e, x=xdst, b_=ps[:, :]:
                 e.scalar_tensor_tensor(out=x, in0=b_, scalar=0.5, in1=x,
                                        op0=ALU.mult, op1=ALU.add),
                 reads=[ps[:, :], xdst], writes=[xdst])


def _pp(v):
    a = np.asarray(v, np.float32)
    return np.ascontiguousarray(a.reshape(-1, 128).T)


def _prep_weights(inputs):
    g = {}
    f32 = np.float32
    A = lambda x: np.ascontiguousarray(np.asarray(x))
    g["l0f_q1"] = A(inputs["L0_ffn_fc1_q"]).astype(np.int32)
    g["l0f_s1"] = A(inputs["L0_ffn_fc1_scales"]).astype(f32)
    g["l0f_b1"] = _pp(inputs["L0_ffn_fc1_bias"])
    g["l0f_q2"] = A(inputs["L0_ffn_fc2_q"]).astype(np.int32)
    g["l0f_s2"] = A(inputs["L0_ffn_fc2_scales"]).astype(f32)
    g["l0f_b2"] = _pp(inputs["L0_ffn_fc2_bias"])
    g["l0f_gamma"] = _pp(inputs["L0_ffn_ln_gamma"])
    g["l0f_beta"] = _pp(inputs["L0_ffn_ln_beta"])
    wv = np.asarray(inputs["L0_attn_w_att_val"], f32)
    blk = np.zeros((8, 128, 128), f32)
    for j in range(8):
        blk[j, 0:64, 0:64] = wv[2 * j]
        blk[j, 64:128, 64:128] = wv[2 * j + 1]
    g["l0a_watt"] = A(blk.transpose(1, 0, 2).reshape(128, 1024))
    g["l0a_wvT"] = A(np.asarray(inputs["L0_attn_wv_w"], f32).T)
    g["l0a_outwT"] = A(np.asarray(inputs["L0_attn_out_w"], f32).T)
    g["l0a_wv_b"] = _pp(inputs["L0_attn_wv_b"])
    g["l0a_out_b"] = _pp(inputs["L0_attn_out_b"])
    g["l1a_wq"] = A(np.asarray(inputs["L1_attn_W_q"], f32))
    g["l1a_wk"] = A(np.asarray(inputs["L1_attn_W_k"], f32))
    g["l1a_wv"] = A(np.asarray(inputs["L1_attn_W_v"], f32))
    g["l1a_wo"] = A(np.asarray(inputs["L1_attn_W_o"], f32))
    g["l1a_wdkv"] = A(np.asarray(inputs["L1_attn_W_down_kv"], f32))
    g["l1a_wupk"] = A(np.asarray(inputs["L1_attn_W_up_k"], f32))
    g["l1a_wupv"] = A(np.asarray(inputs["L1_attn_W_up_v"], f32))
    g["l2a_wqT"] = A(np.asarray(inputs["L2_attn_Wq"], f32).T)
    g["l2a_wkT"] = A(np.asarray(inputs["L2_attn_Wk"], f32).T)
    g["l2a_wvT"] = A(np.asarray(inputs["L2_attn_Wv"], f32).T)
    g["l2a_woT"] = A(np.asarray(inputs["L2_attn_Wo"], f32).T)
    g["l3f_g0T"] = A(np.asarray(inputs["L3_ffn_gating_w"], f32)[:, 0].transpose(0, 2, 1))
    g["l3f_g1T"] = A(np.asarray(inputs["L3_ffn_gating_w"], f32)[:, 1].transpose(0, 2, 1))
    g["l3f_lw"] = A(np.asarray(inputs["L3_ffn_linear_w"], f32))
    g["l3f_rs"] = _pp(inputs["L3_ffn_router_scale"])
    g["l3f_rw"] = A(np.asarray(inputs["L3_ffn_router_w"], f32))
    g["l3f_pes"] = A(np.asarray(inputs["L3_ffn_per_expert_scale"], f32).reshape(1, 8))
    g["l1f_wg"] = A(np.asarray(inputs["L1_ffn_w_gate"], f32))
    g["l1f_wu"] = A(np.asarray(inputs["L1_ffn_w_up"], f32))
    g["l1f_wd"] = A(np.asarray(inputs["L1_ffn_w_down"], f32))
    g["l2f_W_base"] = A(np.asarray(inputs["L2_ffn_W_base"], f32))
    g["l2f_W_enc"] = A(np.asarray(inputs["L2_ffn_W_enc"], f32))
    g["l2f_W_dec"] = A(np.asarray(inputs["L2_ffn_W_dec"], f32))
    g["l2f_W_down"] = A(np.asarray(inputs["L2_ffn_W_down"], f32))
    g["l2f_W_up"] = A(np.asarray(inputs["L2_ffn_W_up"], f32))
    g["l2f_b_base"] = _pp(inputs["L2_ffn_b_base"])
    be = np.asarray(inputs["L2_ffn_b_enc"], f32).reshape(8, 2, 128)
    g["l2f_b_enc"] = A(be.transpose(2, 0, 1).reshape(128, 16))
    bd = np.asarray(inputs["L2_ffn_b_dec"], f32).reshape(8, 8, 128)
    g["l2f_b_dec"] = A(bd.transpose(2, 0, 1).reshape(128, 64))
    bdn = np.zeros((128, 8), f32)
    bdn[0:64, :] = np.asarray(inputs["L2_ffn_b_down"], f32).T
    g["l2f_b_down"] = A(bdn)
    bu = np.asarray(inputs["L2_ffn_b_up"], f32).reshape(8, 8, 128)
    g["l2f_b_up"] = A(bu.transpose(2, 0, 1).reshape(128, 64))
    g["l2f_cidx"] = A(np.asarray(inputs["L2_ffn_connected_idx"], np.int32).reshape(1, 8))
    g["l3a_wqT"] = A(np.asarray(inputs["L3_attn_WQ"], f32).T)
    g["l3a_wkT"] = A(np.asarray(inputs["L3_attn_WK"], f32).T)
    g["l3a_wvT"] = A(np.asarray(inputs["L3_attn_WV"], f32).T)
    g["l3a_woT"] = A(np.asarray(inputs["L3_attn_WO"], f32).T)
    return g


DEBUG_DUMP = []

SUBLAYERS = [
    (0, "ffn"), (0, "attn"), (0, "ffn"),
    (1, "ffn"), (1, "attn"), (1, "ffn"),
    (2, "ffn"), (2, "attn"), (2, "ffn"),
    (3, "ffn"), (3, "attn"), (3, "ffn"),
]


def build_program(nc, sublayers):
    p = Prog(nc)
    in_names = []

    def dr(name, shape, dt, kind="ExternalInput"):
        h = nc.dram_tensor(name, shape, dt, kind=kind)
        p.reg(h, "dram")
        if kind == "ExternalInput":
            in_names.append(name)
        return h.ap()

    x_in = dr("x", [1024, T], F32)
    x_out = dr("out", [1024, T], F32, kind="ExternalOutput")

    need = {SUBLAYERS[s] for s in sublayers}
    w = {}
    if (0, "ffn") in need:
        w["l0f"] = {
            "q1": dr("l0f_q1", [1024, 4096], I32),
            "s1": dr("l0f_s1", [8, 4096], F32),
            "b1": dr("l0f_b1", [128, 32], F32),
            "q2": dr("l0f_q2", [4096, 1024], I32),
            "s2": dr("l0f_s2", [32, 1024], F32),
            "b2": dr("l0f_b2", [128, 8], F32),
            "gamma": dr("l0f_gamma", [128, 8], F32),
            "beta": dr("l0f_beta", [128, 8], F32),
            "w1b": dr("l0f_w1b", [1024, 4096], BF16, kind="Internal"),
            "w2b": dr("l0f_w2b", [4096, 1024], BF16, kind="Internal"),
        }

    if (0, "attn") in need:
        w["l0a"] = {
            "watt": dr("l0a_watt", [128, 1024], F32),
            "wvT": dr("l0a_wvT", [1024, 1024], F32),
            "outwT": dr("l0a_outwT", [1024, 1024], F32),
            "wv_b": dr("l0a_wv_b", [128, 8], F32),
            "out_b": dr("l0a_out_b", [128, 8], F32),
        }
    if (1, "attn") in need:
        w["l1a"] = {
            "wq": dr("l1a_wq", [1024, 1024], F32),
            "wk": dr("l1a_wk", [1024, 1024], F32),
            "wv": dr("l1a_wv", [1024, 1024], F32),
            "wo": dr("l1a_wo", [1024, 1024], F32),
            "wdkv": dr("l1a_wdkv", [1024, 256], F32),
            "wupk": dr("l1a_wupk", [256, 1024], F32),
            "wupv": dr("l1a_wupv", [256, 1024], F32),
        }
    if (2, "attn") in need:
        w["l2a"] = {
            "wqT": dr("l2a_wqT", [1024, 1024], F32),
            "wkT": dr("l2a_wkT", [1024, 128], F32),
            "wvT": dr("l2a_wvT", [1024, 128], F32),
            "woT": dr("l2a_woT", [1024, 1024], F32),
        }
    if (1, "ffn") in need:
        w["l1f"] = {
            "wg": dr("l1f_wg", [2816, 1024], F32),
            "wu": dr("l1f_wu", [2816, 1024], F32),
            "wd": dr("l1f_wd", [1024, 2816], F32),
            "wgT": dr("l1f_wgT", [1024, 2816], BF16, kind="Internal"),
            "wuT": dr("l1f_wuT", [1024, 2816], BF16, kind="Internal"),
            "wdT": dr("l1f_wdT", [2816, 1024], BF16, kind="Internal"),
        }
    if (2, "ffn") in need:
        w["l2f"] = {
            "W_base": dr("l2f_W_base", [1024, 1024], F32),
            "W_enc": dr("l2f_W_enc", [8, 1024, 256], F32),
            "W_dec": dr("l2f_W_dec", [8, 256, 1024], F32),
            "W_down": dr("l2f_W_down", [8, 1024, 64], F32),
            "W_up": dr("l2f_W_up", [8, 64, 1024], F32),
            "b_base": dr("l2f_b_base", [128, 8], F32),
            "b_enc": dr("l2f_b_enc", [128, 16], F32),
            "b_dec": dr("l2f_b_dec", [128, 64], F32),
            "b_down": dr("l2f_b_down", [128, 8], F32),
            "b_up": dr("l2f_b_up", [128, 64], F32),
            "cidx": dr("l2f_cidx", [1, 8], I32),
        }
    if (3, "ffn") in need:
        w["l3f"] = {
            "g0T": dr("l3f_g0T", [8, 1024, 2048], F32),
            "g1T": dr("l3f_g1T", [8, 1024, 2048], F32),
            "lw": dr("l3f_lw", [8, 2048, 1024], F32),
            "rs": dr("l3f_rs", [128, 8], F32),
            "rw": dr("l3f_rw", [1024, 8], F32),
            "pes": dr("l3f_pes", [1, 8], F32),
        }
    if (3, "attn") in need:
        w["l3a"] = {
            "wqT": dr("l3a_wqT", [1024, 1024], F32),
            "wkT": dr("l3a_wkT", [1024, 1024], F32),
            "wvT": dr("l3a_wvT", [1024, 1024], F32),
            "woT": dr("l3a_woT", [1024, 1024], F32),
        }

    with contextlib.ExitStack() as stack:
        c = alloc_common(nc, p, stack)
        for i in range(NTILE):
            p.dma("sync", xt(c, i), x_in[i * 128:(i + 1) * 128, :], stream=f"xin{i % 4}")
        done = set()
        for s in sublayers:
            key = SUBLAYERS[s]
            if key == (0, "ffn"):
                emit_l0_ffn(c, w["l0f"], do_dequant=key not in done)
            elif key == (0, "attn"):
                emit_l0_attn(c, w["l0a"])
            elif key == (1, "attn"):
                emit_l1_attn(c, w["l1a"])
            elif key == (2, "attn"):
                emit_l2_attn(c, w["l2a"])
            elif key == (1, "ffn"):
                emit_l1_ffn(c, w["l1f"], do_quant=key not in done)
            elif key == (2, "ffn"):
                emit_l2_ffn(c, w["l2f"])
            elif key == (3, "ffn"):
                emit_l3_ffn(c, w["l3f"])
            elif key == (3, "attn"):
                emit_l3_attn(c, w["l3a"])
            else:
                raise NotImplementedError(str(key))
            done.add(key)
        for i in range(NTILE):
            p.dma("sync", x_out[i * 128:(i + 1) * 128, :], xt(c, i), stream=f"xout{i % 4}")
        for bn in DEBUG_DUMP:
            buf = getattr(c, bn)
            bap = buf.ap() if callable(getattr(buf, "ap", None)) else buf[:]
            sh = list(bap.shape)
            dt_ = bap.dtype
            dap = nc.dram_tensor("dbg_" + bn, sh, dt_, kind="ExternalOutput").ap()
            p.reg(nc.get_tensor("dbg_" + bn) if hasattr(nc, "get_tensor") else _DH(dap), "dram") \
                if False else p.meta.update({dap.name: (int(np.prod(sh)), 2 if dt_ == BF16 else 4, "dram")})
            p.dma("sync", dap, bap, stream="dbg")
        p.barrier_wait("sync")
        p.emit(stack)
    return in_names


def run_sublayers(inputs, sublayers, x_override=None, n_cores=8, trace=False):
    g = _prep_weights(inputs)
    x = np.asarray(x_override if x_override is not None else inputs["L0_attn_x"], np.float32)
    nc = bass.Bass("TRN2", target_bir_lowering=False, dynamic_dma_scratch_size=16384)
    in_names = build_program(nc, sublayers)
    in_maps = []
    for core in range(n_cores):
        m = {}
        for n in in_names:
            if n == "x":
                m["x"] = np.ascontiguousarray(x[core].T)
            else:
                m[n] = g[n]
        in_maps.append(m)
    res = run_bass_kernel_spmd(nc, in_maps, core_ids=list(range(n_cores)), trace=trace)
    outs = [np.asarray(r["out"], np.float32).T for r in res.results]
    return np.stack(outs, axis=0), res


def kernel(**inputs):
    out, _ = run_sublayers(inputs, list(range(12)))
    return out.astype(np.float32)


# version 04
# speedup vs baseline: 1.3274x; 1.0776x over previous
import contextlib
import math
import numpy as np

import concourse.bass as bass
import concourse.mybir as mybir
from concourse.bass_utils import run_bass_kernel_spmd

F32 = mybir.dt.float32
BF16 = mybir.dt.bfloat16
I32 = mybir.dt.int32
AF = mybir.ActivationFunctionType
ALU = mybir.AluOpType
AX = mybir.AxisListType

ENGINES = ("tensor", "vector", "scalar", "gpsimd", "sync")

T = 2048
D = 1024
NTILE = D // 128
TC = 512
NTC = T // TC


import os as _os
SEM_ROT = int(_os.environ.get("SEM_ROT", "30000"))


class Prog:
    def __init__(self, nc, n_dma_sems=24):
        self.nc = nc
        self.ops = {e: [] for e in ENGINES}
        self.cs = {e: 0 for e in ENGINES}
        self.cgen = {e: 0 for e in ENGINES}
        self.n_dma = n_dma_sems
        self.dmac = [0] * n_dma_sems
        self.acc = {}
        self.waited = {e: {} for e in ENGINES}
        self.meta = {}
        self.rowcols = {}
        self.stream_sem = {}
        self.next_stream = 0
        self.nwaits = 0

    def reg(self, handle, space):
        ap = handle.ap() if hasattr(handle, "ap") else handle[:]
        shape = list(ap.shape)
        rowlen = 1
        for s in shape[1:]:
            rowlen *= s
        if space == "dram":
            if len(shape) >= 2:
                rc = 1
                for s in shape[1:]:
                    rc *= s
                self.rowcols[ap.name] = rc
            rowlen = rowlen * max(1, shape[0] if shape else 1)
        isz = {F32: 4, BF16: 2, I32: 4}.get(ap.dtype, 4)
        self.meta[ap.name] = (rowlen, isz, space)
        return handle

    def _rect(self, ap):
        if callable(getattr(ap, "ap", None)):
            ap = ap.ap()
        name = ap.name
        rowlen, isz, space = self.meta[name]
        if space == "dram":
            C = self.rowcols.get(name)
            ok = C is not None
            r0 = c0 = r1 = c1 = 0
            if ok:
                r0 = ap.offset // C
                c0 = ap.offset % C
                r1, c1 = r0, c0
                for st, ct in ap.ap:
                    if st == 0 or ct == 1:
                        continue
                    if st % C == 0 and st >= 0:
                        r1 += (st // C) * (ct - 1)
                    elif 0 < st < C:
                        c1 += st * (ct - 1)
                        if c1 >= C:
                            ok = False
                            break
                    else:
                        ok = False
                        break
            if ok:
                return name, (r0, r1 + 1, c0 * isz, (c1 + 1) * isz)
            lo = hi = ap.offset
            for st, ct in ap.ap:
                if st >= 0:
                    hi += st * (ct - 1)
                else:
                    lo += st * (ct - 1)
            return name, (0, 1 << 20, lo * isz, (hi + 1) * isz)
        bp = ap.base_partition
        p0 = bp() if callable(bp) else bp
        pn = ap.partition_size
        pn = pn() if callable(pn) else pn
        f0 = ap.offset - p0 * rowlen
        lo = hi = f0
        for st, ct in ap.ap[1:]:
            if st >= 0:
                hi += st * (ct - 1)
            else:
                lo += st * (ct - 1)
        b0, b1 = lo * isz, (hi + 1) * isz
        if space == "psum":
            b0 = (b0 // 2048) * 2048
            b1 = ((b1 + 2047) // 2048) * 2048
        return name, (p0, p0 + pn, b0, b1)

    @staticmethod
    def _overlap(r1, r2):
        return not (r1[1] <= r2[0] or r2[1] <= r1[0]
                    or r1[3] <= r2[2] or r2[3] <= r1[2])

    def _need(self, engine, aps_r, aps_w):
        need = {}
        for ap, as_write in [(a, False) for a in aps_r] + [(a, True) for a in aps_w]:
            name, rect = self._rect(ap)
            for (is_w, r, sem, val, eng2, isdma) in self.acc.get(name, ()):
                if not (is_w or as_write):
                    continue
                if not self._overlap(rect, r):
                    continue
                if not isdma and eng2 == engine:
                    if not is_w:
                        continue
                    if engine == "tensor":
                        continue
                    if val <= self.cs[engine] - 8:
                        continue
                if self.waited[engine].get(sem, 0) >= val:
                    continue
                if need.get(sem, 0) < val:
                    need[sem] = val
        return need

    def _record(self, aps_r, aps_w, sem, val, engine, isdma):
        for ap, is_w in [(a, False) for a in aps_r] + [(a, True) for a in aps_w]:
            name, rect = self._rect(ap)
            lst = self.acc.setdefault(name, [])
            lst.append((is_w, rect, sem, val, engine, isdma))
            if len(lst) > 192:
                old, keep = lst[:96], lst[96:]
                floors = {}
                for (w2, r2, s2, v2, e2, d2) in old:
                    k = (s2, e2, d2, w2)
                    if floors.get(k, 0) < v2:
                        floors[k] = v2
                whole = (0, 1 << 20, 0, 1 << 40)
                self.acc[name] = [(w2, whole, s2, v2, e2, d2)
                                  for (s2, e2, d2, w2), v2 in floors.items()] + keep

    def op(self, engine, fn, reads=(), writes=()):
        need = self._need(engine, reads, writes)
        for sem, val in need.items():
            self.waited[engine][sem] = val
        self.nwaits += len(need)
        self.cs[engine] += 1
        if self.cs[engine] > SEM_ROT:
            self.cgen[engine] += 1
            self.cs[engine] = 1
        sem = f"c_{engine}_g{self.cgen[engine]}"
        self._record(reads, writes, sem, self.cs[engine], engine, False)
        self.ops[engine].append((need, fn, (sem, 1)))

    def dma(self, engine, out, in_, stream="w", **kw):
        if stream not in self.stream_sem:
            self.stream_sem[stream] = self.next_stream % self.n_dma
            self.next_stream += 1
        si = self.stream_sem[stream]
        need = self._need(engine, [in_], [out])
        if self.dmac[si] > 0:
            prev = 16 * self.dmac[si]
            if self.waited[engine].get(f"d{si}", 0) < prev and need.get(f"d{si}", 0) < prev:
                need[f"d{si}"] = prev
        for sem, val in need.items():
            self.waited[engine][sem] = val
        self.nwaits += len(need)
        self.dmac[si] += 1
        sem = f"d{si}"
        self._record([in_], [out], sem, 16 * self.dmac[si], engine, True)
        fn = lambda e: e.dma_start(out=out, in_=in_, **kw)
        self.ops[engine].append((need, fn, (sem, 16)))

    def barrier_wait(self, engine):
        need = {}
        for e in ENGINES:
            sem = f"c_{e}_g{self.cgen[e]}"
            if e != engine and self.cs[e] > 0 and self.waited[engine].get(sem, 0) < self.cs[e]:
                need[sem] = self.cs[e]
        for i in range(self.n_dma):
            if self.dmac[i] > 0 and self.waited[engine].get(f"d{i}", 0) < 16 * self.dmac[i]:
                need[f"d{i}"] = 16 * self.dmac[i]
        for sem, val in need.items():
            self.waited[engine][sem] = val
        self.ops[engine].append((need, None, None))

    def emit(self, stack):
        nc = self.nc
        sems = {}
        for e in ENGINES:
            for g in range(self.cgen[e] + 1):
                sems[f"c_{e}_g{g}"] = stack.enter_context(nc.semaphore(f"cs_{e}_{g}"))
        for i in range(self.n_dma):
            sems[f"d{i}"] = stack.enter_context(nc.semaphore(f"ds_{i}"))
        block = stack.enter_context(nc.Block())
        for ename in ENGINES:
            steps = self.ops[ename]
            if not steps:
                continue

            def mk(steps):
                def body(eng):
                    for waits, fn, inc in steps:
                        for sem, val in waits.items():
                            eng.wait_ge(sems[sem], val)
                        if fn is not None:
                            ins = fn(eng)
                            if inc is not None:
                                ins.then_inc(sems[inc[0]], inc[1])
                return body
            getattr(block, ename)(mk(steps))


class Ctx:
    pass


NROW = 7
R_MU, R_VAR, R_R1, R_S, R_B, R_T0, R_T1 = range(7)
R_SP, R_BP = R_T0, R_T1
R_ACC = R_S


def row(c, i):
    return c.ROWS[0:1, i * TC:(i + 1) * TC]


def alloc_common(nc, p, stack):
    c = Ctx()
    c.nc = nc
    c.p = p
    sb = lambda name, shape, dt: p.reg(
        stack.enter_context(nc.sbuf_tensor(name, shape, dt)), "sbuf")
    ps = lambda name, shape, dt: p.reg(
        stack.enter_context(nc.psum_tensor(name, shape, dt)), "psum")

    c.X = sb("X", [128, NTILE * T], F32)
    c.A = sb("A", [128, NTILE * T], BF16)
    c.B = sb("B", [128, NTILE * T], BF16)
    c.C = sb("C", [128, NTILE * T], BF16)
    c.SBC = sb("SBC", [128, TC], F32)
    c.BBC = sb("BBC", [128, TC], F32)
    c.SC16 = sb("SC16", [128, T], BF16)
    c.SC16b = sb("SC16b", [128, T], BF16)
    c.LAT = sb("LAT", [128, 2048], BF16)
    c.ROWS = sb("ROWS", [1, NROW * TC], F32)
    c.WS = sb("WS", [128, 2048], F32)
    c.PVEC = sb("PVEC", [128, 112], F32)
    c.CONSTF = sb("CONSTF", [128, 24], F32)
    c.PVI = sb("PVI", [128, 16], I32)
    c.RNDI = sb("RNDI", [128, TC], I32)
    c.TMP32 = sb("TMP32", [128, 512], F32)
    c.QSC = sb("QSC", [128, 96], F32)
    c.SEL = sb("SEL", [8, 1024], BF16)
    c.ONES = sb("ONES", [128, 8], BF16)
    c.ONESROW = sb("ONESROW", [1, 128], BF16)
    c.ONESROWF = sb("ONESROWF", [1, 128], F32)
    c.IDENT = sb("IDENT", [128, 128], BF16)
    c.NIDENT = sb("NIDENT", [128, 128], BF16)
    c.IDENTF = sb("IDENTF", [128, 128], F32)

    c.PS_BIG = ps("PS_BIG", [128, 2048], F32)
    c.PS_A = ps("PS_A", [128, 512], F32)
    c.PS_B = ps("PS_B", [128, 512], F32)
    c.PS_T = ps("PS_T", [128, 1024], BF16)
    c.PS_O = ps("PS_O", [128, 512], F32)

    p.op("gpsimd", lambda e: e.memset(c.CONSTF[:, 0:1], 1e-5), writes=[c.CONSTF[:, 0:1]])
    p.op("gpsimd", lambda e: e.memset(c.CONSTF[:, 1:2], 1e-6), writes=[c.CONSTF[:, 1:2]])
    p.op("gpsimd", lambda e: e.memset(c.CONSTF[:, 2:3], -math.log(32.0)), writes=[c.CONSTF[:, 2:3]])
    p.op("gpsimd", lambda e: e.memset(c.CONSTF[:, 3:4], -0.5), writes=[c.CONSTF[:, 3:4]])
    p.op("gpsimd", lambda e: e.memset(c.ONES[:], 1.0), writes=[c.ONES[:]])
    p.op("gpsimd", lambda e: e.memset(c.ONESROW[:], 1.0), writes=[c.ONESROW[:]])
    p.op("gpsimd", lambda e: e.memset(c.ONESROWF[:], 1.0), writes=[c.ONESROWF[:]])
    p.op("gpsimd", lambda e: e.iota(c.PVI[:, 0:8], pattern=[[0, 8]], base=0,
         channel_multiplier=1), writes=[c.PVI[:, 0:8]])
    p.op("vector", lambda e: e.tensor_scalar(out=c.PVI[:, 8:16], in0=c.PVI[:, 0:8],
         scalar1=1, scalar2=None, op0=ALU.bitwise_and),
         reads=[c.PVI[:, 0:8]], writes=[c.PVI[:, 8:16]])
    p.op("vector", lambda e: e.tensor_copy(out=c.CONSTF[:, 4:12], in_=c.PVI[:, 8:16]),
         reads=[c.PVI[:, 8:16]], writes=[c.CONSTF[:, 4:12]])
    p.op("vector", lambda e: e.tensor_copy(out=c.CONSTF[:, 12:20], in_=c.PVI[:, 0:8]),
         reads=[c.PVI[:, 0:8]], writes=[c.CONSTF[:, 12:20]])
    p.op("gpsimd", lambda e: e.memset(c.SEL[:], 0.0), writes=[c.SEL[:]])
    for se in range(8):
        p.op("gpsimd", lambda e, sl=c.SEL[:, se * 128:(se + 1) * 128], b=-se: e.affine_select(
            out=sl, in_=sl, compare_op=ALU.not_equal, fill=1.0,
            base=b, pattern=[[0, 128]], channel_multiplier=1),
            reads=[c.SEL[:]], writes=[c.SEL[:]])
    for t_, fill in ((c.IDENT, 1.0), (c.IDENTF, 1.0), (c.NIDENT, -1.0)):
        p.op("gpsimd", lambda e, t=t_: e.memset(t[:], 0.0), writes=[t_[:]])
        p.op("gpsimd", lambda e, t=t_, f=fill: e.affine_select(
            out=t[:], in_=t[:], compare_op=ALU.not_equal, fill=f,
            base=0, pattern=[[-1, 128]], channel_multiplier=1),
            reads=[t_[:]], writes=[t_[:]])
    return c


def xt(c, i):
    return c.X[:, i * T:(i + 1) * T]


def pt(buf, i):
    return buf[:, i * T:(i + 1) * T]


def emit_ln_prefix(c, out_tiles, variant="plain", gamma=None, beta=None,
                   fold_half_y=False, rr_out=None, chunk_cb=None, tm_f32=False):
    p = c.p
    X = [xt(c, i) for i in range(NTILE)]
    onecol = c.ONES[:, 0:1]
    for tc in range(NTC):
        cs = slice(tc * TC, (tc + 1) * TC)
        for i in range(NTILE):
            src = X[i][:, cs]
            xbs = c.SC16[:, (i % 4) * TC:(i % 4 + 1) * TC]
            sqs = c.SC16b[:, (i % 4) * TC:(i % 4 + 1) * TC]
            p.op("vector", lambda e, a=xbs, b=src: e.tensor_copy(out=a, in_=b),
                 reads=[src], writes=[xbs])
            p.op("scalar", lambda e, a=sqs, b=src: e.activation(a, b, AF.Square),
                 reads=[src], writes=[sqs])
            p.op("tensor", lambda e, o=c.PS_A[0:1, :], l=onecol, r=xbs, s=(i == 0), st=(i == NTILE - 1):
                 e.matmul(o, l, r, start=s, stop=st),
                 reads=[onecol, xbs], writes=[c.PS_A[0:1, :]])
            p.op("tensor", lambda e, o=c.PS_B[0:1, :], l=onecol, r=sqs, s=(i == 0), st=(i == NTILE - 1):
                 e.matmul(o, l, r, start=s, stop=st),
                 reads=[onecol, sqs], writes=[c.PS_B[0:1, :]])
        mu, var, r1 = row(c, R_MU), row(c, R_VAR), row(c, R_R1)
        S, Bv = row(c, R_S), row(c, R_B)
        t0, t1 = row(c, R_T0), row(c, R_T1)
        p.op("vector", lambda e, m=mu: e.tensor_scalar(out=m, in0=c.PS_A[0:1, :],
             scalar1=1.0 / D, scalar2=None, op0=ALU.mult),
             reads=[c.PS_A[0:1, :]], writes=[mu])
        p.op("vector", lambda e, a=t0, m=mu: e.tensor_mul(out=a, in0=m, in1=m),
             reads=[mu], writes=[t0])
        p.op("vector", lambda e, v=var, a=t0: e.scalar_tensor_tensor(out=v,
             in0=c.PS_B[0:1, :], scalar=1.0 / D, in1=a, op0=ALU.mult, op1=ALU.subtract),
             reads=[c.PS_B[0:1, :], t0], writes=[var])
        p.op("scalar", lambda e, a=t1, v=var: e.activation(a, v, AF.Sqrt, bias=c.CONSTF[0:1, 0:1]),
             reads=[var, c.CONSTF[0:1, 0:1]], writes=[t1])
        p.op("vector", lambda e, r=r1, a=t1: e.reciprocal(out=r, in_=a),
             reads=[t1], writes=[r1])
        if variant == "plain":
            p.op("vector", lambda e, s=S, r=r1: e.tensor_copy(out=s, in_=r),
                 reads=[r1], writes=[S])
        elif variant == "l0":
            p.op("vector", lambda e, a=t0, v=var, r=r1: e.tensor_mul(out=a, in0=v, in1=r),
                 reads=[var, r1], writes=[t0])
            p.op("vector", lambda e, a=t0, r=r1: e.tensor_mul(out=a, in0=a, in1=r),
                 reads=[t0, r1], writes=[t0])
            p.op("scalar", lambda e, a=t1, b_=t0: e.activation(a, b_, AF.Sqrt, bias=c.CONSTF[0:1, 0:1]),
                 reads=[t0, c.CONSTF[0:1, 0:1]], writes=[t1])
            p.op("vector", lambda e, a=t1: e.reciprocal(out=a, in_=a),
                 reads=[t1], writes=[t1])
            p.op("vector", lambda e, s=S, r=r1, a=t1: e.tensor_mul(out=s, in0=r, in1=a),
                 reads=[r1, t1], writes=[S])
        if rr_out is not None:
            p.op("vector", lambda e, a=t0, v=var, r=r1: e.tensor_mul(out=a, in0=v, in1=r),
                 reads=[var, r1], writes=[t0])
            p.op("vector", lambda e, a=t0, r=r1: e.tensor_mul(out=a, in0=a, in1=r),
                 reads=[t0, r1], writes=[t0])
            p.op("scalar", lambda e, a=t1, b_=t0: e.activation(a, b_, AF.Sqrt, bias=c.CONSTF[0:1, 1:2]),
                 reads=[t0, c.CONSTF[0:1, 1:2]], writes=[t1])
            rr = rr_out(tc)
            p.op("vector", lambda e, a=rr, b_=t1: e.reciprocal(out=a, in_=b_),
                 reads=[t1], writes=[rr])
            p.op("vector", lambda e, a=rr: e.tensor_scalar(out=a, in0=a,
                 scalar1=1.0 / math.sqrt(D), scalar2=None, op0=ALU.mult),
                 reads=[rr], writes=[rr])
        p.op("vector", lambda e, b_=Bv, m=mu, s=S: e.scalar_tensor_tensor(out=b_,
             in0=m, scalar=-1.0, in1=s, op0=ALU.mult, op1=ALU.mult),
             reads=[mu, S], writes=[Bv])
        emit_bcast_chunk(c, S, c.SBC, c.PS_A)
        emit_bcast_chunk(c, Bv, c.BBC, c.PS_B)
        for i in range(NTILE):
            src = X[i][:, cs]
            if tm_f32:
                tmp = c.TMP32[:, 0:TC]
                p.op("vector", lambda e, a=tmp, x=src: e.tensor_mul(out=a, in0=x, in1=c.SBC[:, :]),
                     reads=[src, c.SBC[:, :]], writes=[tmp])
                p.op("gpsimd", lambda e, a=tmp: e.tensor_tensor(out=a, in0=a, in1=c.BBC[:, :], op=ALU.add),
                     reads=[tmp, c.BBC[:, :]], writes=[tmp])
                for j in range(4):
                    tt_ = tc * 4 + j
                    ps = (c.PS_A, c.PS_B)[j % 2]
                    p.op("tensor", lambda e, o=ps[:, 0:128], i_=tmp[:, j * 128:(j + 1) * 128]:
                         e.transpose(o, i_, c.IDENTF[:]),
                         reads=[tmp[:, j * 128:(j + 1) * 128], c.IDENTF[:]], writes=[ps[:, 0:128]])
                    dstj = out_tiles[tt_][:, i * 128:(i + 1) * 128]
                    p.op("vector", lambda e, a=dstj, b_=ps[:, 0:128]: e.tensor_copy(out=a, in_=b_),
                         reads=[ps[:, 0:128]], writes=[dstj])
                continue
            dst = out_tiles[i][:, cs]
            t16 = c.SC16[:, (i % 4) * TC:(i % 4 + 1) * TC]
            p.op("vector", lambda e, a=t16, x=src: e.tensor_mul(out=a, in0=x, in1=c.SBC[:, :]),
                 reads=[src, c.SBC[:, :]], writes=[t16])
            if gamma is None:
                p.op("gpsimd", lambda e, a=dst, x=t16: e.tensor_tensor(out=a, in0=x,
                     in1=c.BBC[:, :], op=ALU.add), reads=[t16, c.BBC[:, :]], writes=[dst])
            else:
                tf = c.SC16b[:, (i % 4) * TC:(i % 4 + 1) * TC]
                p.op("gpsimd", lambda e, a=tf, x=t16: e.tensor_tensor(out=a, in0=x,
                     in1=c.BBC[:, :], op=ALU.add), reads=[t16, c.BBC[:, :]], writes=[tf])
                g = gamma(i)
                bt = beta(i)
                p.op("vector", lambda e, a=dst, x=tf, gg=g, bb=bt:
                     e.tensor_scalar(out=a, in0=x, scalar1=gg, scalar2=bb,
                                     op0=ALU.mult, op1=ALU.add),
                     reads=[tf, g, bt], writes=[dst])
        if chunk_cb is not None:
            chunk_cb(tc)
        if fold_half_y:
            Sp, Bp = row(c, R_SP), row(c, R_BP)
            p.op("vector", lambda e, s=Sp, r=r1: e.tensor_scalar(out=s, in0=r,
                 scalar1=0.5, scalar2=1.0, op0=ALU.mult, op1=ALU.add),
                 reads=[r1], writes=[Sp])
            p.op("vector", lambda e, b_=Bp, m=mu, r=r1: e.scalar_tensor_tensor(out=b_,
                 in0=m, scalar=-0.5, in1=r, op0=ALU.mult, op1=ALU.mult),
                 reads=[mu, r1], writes=[Bp])
            emit_bcast_chunk(c, Sp, c.SBC, c.PS_A)
            emit_bcast_chunk(c, Bp, c.BBC, c.PS_B)
            for i in range(NTILE):
                xs = X[i][:, cs]
                p.op("vector", lambda e, x=xs: e.tensor_mul(out=x, in0=x, in1=c.SBC[:, :]),
                     reads=[xs, c.SBC[:, :]], writes=[xs])
                p.op("vector", lambda e, x=xs: e.tensor_add(out=x, in0=x, in1=c.BBC[:, :]),
                     reads=[xs, c.BBC[:, :]], writes=[xs])


def emit_bcast_chunk(c, src_row, dst, psum):
    p = c.p
    if callable(getattr(dst, "ap", None)):
        dst = dst.ap()
    p.op("tensor", lambda e, o=psum[:, :], r=src_row:
         e.matmul(o, c.ONESROWF[0:1, :], r, start=True, stop=True),
         reads=[c.ONESROWF[0:1, :], src_row], writes=[psum[:, :]])
    p.op("vector", lambda e, a=dst, b=psum[:, :]: e.tensor_copy(out=a, in_=b),
         reads=[psum[:, :]], writes=[dst])


def emit_l0_ffn(c, w, do_dequant):
    p = c.p
    X = [xt(c, i) for i in range(NTILE)]

    gam = c.PVEC[:, 0:8]
    bet = c.PVEC[:, 8:16]
    b1 = c.PVEC[:, 16:48]
    qb2 = c.PVEC[:, 48:56]
    p.dma("sync", gam, w["gamma"], stream="misc")
    p.dma("sync", bet, w["beta"], stream="misc")
    p.dma("sync", b1, w["b1"], stream="misc")
    p.dma("sync", c.PVEC[:, 56:64], w["b2"], stream="misc")
    p.op("vector", lambda e: e.tensor_scalar(out=qb2, in0=c.PVEC[:, 56:64],
         scalar1=0.25, scalar2=None, op0=ALU.mult),
         reads=[c.PVEC[:, 56:64]], writes=[qb2])

    emit_ln_prefix(c, [pt(c.A, i) for i in range(NTILE)], variant="l0",
                   gamma=lambda i: gam[:, i:i + 1], beta=lambda i: bet[:, i:i + 1],
                   fold_half_y=True)

    if do_dequant:
        _dequant_matrix(c, w["q1"], w["s1"], w["w1b"], K=1024, N=4096, tag="dq1")
        _dequant_matrix(c, w["q2"], w["s2"], w["w2b"], K=4096, N=1024, tag="dq2")

    for i in range(NTILE):
        p.op("vector", lambda e, x=X[i], bb=qb2[:, i:i + 1]:
             e.tensor_scalar(out=x, in0=x, scalar1=bb, scalar2=None, op0=ALU.add),
             reads=[X[i], qb2[:, i:i + 1]], writes=[X[i]])

    WS16 = c.WS[:].bitcast(BF16)
    for tci in range(NTC):
        ts_ = slice(tci * TC, (tci + 1) * TC)
        for m in range(32):
            buf = m % 2
            wslot = WS16[:, buf * 1024:(buf + 1) * 1024]
            p.dma("gpsimd", wslot.rearrange("p (g m) -> p g m", m=128),
                  w["w1b"].rearrange("(g p) n -> p g n", p=128)[:, :, m * 128:(m + 1) * 128],
                  stream=f"w1_{buf}")
            ps = (c.PS_A, c.PS_B)[m % 2]
            for g in range(NTILE):
                lhs = wslot[:, g * 128:(g + 1) * 128]
                rhs = pt(c.A, g)[:, ts_]
                p.op("tensor", lambda e, o=ps[:, :], l=lhs, r=rhs, s=(g == 0), st=(g == NTILE - 1):
                     e.matmul(o, l, r, start=s, stop=st),
                     reads=[lhs, rhs], writes=[ps[:, :]])
            ydst = c.B[:, m * TC:(m + 1) * TC]
            bia = b1[:, m:m + 1]
            p.op("scalar", lambda e, a=ydst, b_=ps[:, :], bb=bia:
                 e.activation(a, b_, AF.Silu, bias=bb),
                 reads=[ps[:, :], bia], writes=[ydst])
        for mo in range(NTILE):
            ps = (c.PS_A, c.PS_B)[mo % 2]
            for quar in range(4):
                buf = quar % 2
                wslot = WS16[:, 2048 + buf * 1024:2048 + (buf + 1) * 1024]
                p.dma("gpsimd", wslot.rearrange("p (g m) -> p g m", m=128),
                      w["w2b"][quar * 1024:(quar + 1) * 1024, :]
                      .rearrange("(g p) n -> p g n", p=128)[:, :, mo * 128:(mo + 1) * 128],
                      stream=f"w2_{buf}")
                for g in range(8):
                    gabs = quar * 8 + g
                    lhs = wslot[:, g * 128:(g + 1) * 128]
                    rhs = c.B[:, gabs * TC:(gabs + 1) * TC]
                    p.op("tensor", lambda e, o=ps[:, :], l=lhs, r=rhs, s=(gabs == 0), st=(gabs == 31):
                         e.matmul(o, l, r, start=s, stop=st),
                         reads=[lhs, rhs], writes=[ps[:, :]])
            xdst = X[mo][:, ts_]
            p.op("vector", lambda e, x=xdst, b_=ps[:, :]:
                 e.scalar_tensor_tensor(out=x, in0=b_, scalar=0.25, in1=x,
                                        op0=ALU.mult, op1=ALU.add),
                 reads=[ps[:, :], xdst], writes=[xdst])


def _dequant_matrix(c, q_ap, s_ap, out_ap, K, N, tag):
    p = c.p
    NG = K // 128
    NN = N // 512
    for g in range(NG):
        for j in range(NN):
            buf = (g * NN + j) % 2
            qs = c.WS[:, buf * 512:(buf + 1) * 512]
            p.dma("gpsimd", qs.bitcast(I32),
                  q_ap[g * 128:(g + 1) * 128, j * 512:(j + 1) * 512],
                  stream=f"{tag}_{buf}")
            srow = row(c, R_S if buf == 0 else R_B)
            p.dma("sync", srow, s_ap[g:g + 1, j * 512:(j + 1) * 512], stream=f"{tag}s")
            pss = (c.PS_A, c.PS_B)[buf]
            p.op("tensor", lambda e, o=pss[:, :], r=srow:
                 e.matmul(o, c.ONESROWF[0:1, :], r, start=True, stop=True),
                 reads=[c.ONESROWF[0:1, :], srow], writes=[pss[:, :]])
            tq = c.WS[:, 1024 + buf * 512:1024 + (buf + 1) * 512]
            p.op("scalar", lambda e, a=tq, b_=qs.bitcast(I32):
                 e.activation(a, b_, AF.Copy, bias=-128.0),
                 reads=[qs], writes=[tq])
            ob = c.SC16[:, (buf) * 1024:(buf) * 1024 + 512]
            p.op("vector", lambda e, a=ob, x=tq, s_=pss[:, :]:
                 e.tensor_tensor(out=a, in0=x, in1=s_, op=ALU.mult),
                 reads=[tq, pss[:, :]], writes=[ob])
            p.dma("scalar", out_ap[g * 128:(g + 1) * 128, j * 512:(j + 1) * 512], ob,
                  stream=f"{tag}o")


def emit_attn_core(c, NH, DH, causal, q_ap, k_ap, vT_ap, o_dst, score_hook=None):
    p = c.p
    HPG = 512 // DH
    NGRP = NH // HPG
    SUMS = c.PVEC[:, 64:80]
    RECS = c.PVEC[:, 80:96]
    NMAX = c.PVEC[:, 104:105]
    for qc in range(16):
        ext = (qc + 1) * 128 if causal else T
        ntk = ext // 128
        for grp in range(NGRP):
            for hh in range(HPG):
                h = grp * HPG + hh
                qh = q_ap(h, qc)
                ext1 = min(ext, 1024) if score_hook is None else ext
                NMX2 = c.PVEC[:, 108:110]
                for b in range((ext1 + 511) // 512):
                    w0 = b * 512
                    w1 = min(ext1, w0 + 512)
                    rhs = k_ap(h, w0, w1)
                    o = c.PS_BIG[:, w0:w1]
                    p.op("tensor", lambda e, oo=o, l=qh, r=rhs:
                         e.matmul(oo, l, r, start=True, stop=True),
                         reads=[qh, rhs], writes=[o])
                if score_hook is not None:
                    score_hook(h, qc, ext)
                p.op("vector", lambda e, a=NMX2[:, 0:1], b_=c.PS_BIG[:, 0:ext1]:
                     e.tensor_reduce(out=a, in_=b_, axis=AX.X, op=ALU.max),
                     reads=[c.PS_BIG[:, 0:ext1]], writes=[NMX2[:, 0:1]])
                if ext > ext1:
                    for b in range((ext - 1024 + 511) // 512):
                        w0 = 1024 + b * 512
                        w1 = min(ext, w0 + 512)
                        rhs = k_ap(h, w0, w1)
                        o = c.PS_BIG[:, w0:w1]
                        p.op("tensor", lambda e, oo=o, l=qh, r=rhs:
                             e.matmul(oo, l, r, start=True, stop=True),
                             reads=[qh, rhs], writes=[o])
                    p.op("vector", lambda e, a=NMX2[:, 1:2], b_=c.PS_BIG[:, 1024:ext]:
                         e.tensor_reduce(out=a, in_=b_, axis=AX.X, op=ALU.max),
                         reads=[c.PS_BIG[:, 1024:ext]], writes=[NMX2[:, 1:2]])
                    p.op("vector", lambda e, a=NMAX, b_=NMX2:
                         e.tensor_reduce(out=a, in_=b_, axis=AX.X, op=ALU.max, negate=True),
                         reads=[NMX2], writes=[NMAX])
                else:
                    p.op("vector", lambda e, a=NMAX, b_=NMX2[:, 0:1]:
                         e.tensor_reduce(out=a, in_=b_, axis=AX.X, op=ALU.max, negate=True),
                         reads=[NMX2[:, 0:1]], writes=[NMAX])
                Pbuf = c.SC16
                ACCS = c.PVEC[:, 110:112]
                lo_end = ext - 128 if causal else ext
                na = 0
                for w0 in range(0, lo_end, 1024):
                    w1 = min(lo_end, w0 + 1024)
                    acc = ACCS[:, na:na + 1]
                    na += 1
                    p.op("scalar", lambda e, a=Pbuf[:, w0:w1], b_=c.PS_BIG[:, w0:w1], ac=acc:
                         e.activation(a, b_, AF.Exp, bias=NMAX, accum_out=ac),
                         reads=[c.PS_BIG[:, w0:w1], NMAX], writes=[Pbuf[:, w0:w1], acc])
                sacc = SUMS[:, h:h + 1]
                if causal:
                    dtile = Pbuf[:, ext - 128:ext]
                    p.op("scalar", lambda e, a=dtile, b_=c.PS_BIG[:, ext - 128:ext]:
                         e.activation(a, b_, AF.Exp, bias=NMAX),
                         reads=[c.PS_BIG[:, ext - 128:ext], NMAX], writes=[dtile])
                    p.op("gpsimd", lambda e, a=dtile: e.affine_select(
                         out=a, in_=a, compare_op=ALU.is_ge, fill=0.0,
                         base=0, pattern=[[-1, 128]], channel_multiplier=1),
                         reads=[dtile], writes=[dtile])
                    p.op("vector", lambda e, a=sacc, b_=dtile:
                         e.tensor_reduce(out=a, in_=b_, axis=AX.X, op=ALU.add),
                         reads=[dtile], writes=[sacc])
                    for j in range(na):
                        p.op("vector", lambda e, a=sacc, b_=ACCS[:, j:j + 1]:
                             e.tensor_add(out=a, in0=a, in1=b_),
                             reads=[sacc, ACCS[:, j:j + 1]], writes=[sacc])
                else:
                    p.op("vector", lambda e, a=sacc, b_=ACCS[:, 0:1]: e.tensor_copy(out=a, in_=b_),
                         reads=[ACCS[:, 0:1]], writes=[sacc])
                    for j in range(1, na):
                        p.op("vector", lambda e, a=sacc, b_=ACCS[:, j:j + 1]:
                             e.tensor_add(out=a, in0=a, in1=b_),
                             reads=[sacc, ACCS[:, j:j + 1]], writes=[sacc])
                for t0 in range(0, ntk, 8):
                    t1 = min(ntk, t0 + 8)
                    for t in range(t0, t1):
                        ptile = Pbuf[:, t * 128:(t + 1) * 128]
                        pslot = c.PS_T[:, (t - t0) * 128:(t - t0 + 1) * 128]
                        p.op("tensor", lambda e, o=pslot, i_=ptile: e.transpose(o, i_, c.IDENT[:]),
                             reads=[ptile, c.IDENT[:]], writes=[pslot])
                    nb8 = (t1 - t0) * 128
                    pts8 = c.SC16b[:, 0:nb8]
                    p.op("vector", lambda e, a=pts8, b_=c.PS_T[:, 0:nb8]: e.tensor_copy(out=a, in_=b_),
                         reads=[c.PS_T[:, 0:nb8]], writes=[pts8])
                    for t in range(t0, t1):
                        pts = pts8[:, (t - t0) * 128:(t - t0 + 1) * 128]
                        vts = vT_ap(h, t)
                        o = c.PS_O[:, hh * DH:(hh + 1) * DH]
                        p.op("tensor", lambda e, oo=o, l=pts, r=vts, s=(t == 0), st=(t == ntk - 1):
                             e.matmul(oo, l, r, start=s, stop=st),
                             reads=[pts, vts], writes=[o])
            p.op("vector", lambda e, a=RECS: e.reciprocal(out=a, in_=SUMS),
                 reads=[SUMS], writes=[RECS])
            o16 = c.SC16b[:, 1024 + grp * 512:1024 + (grp + 1) * 512]
            for hh in range(HPG):
                h = grp * HPG + hh
                p.op("vector", lambda e, a=o16[:, hh * DH:(hh + 1) * DH],
                     b_=c.PS_O[:, hh * DH:(hh + 1) * DH], r=RECS[:, h:h + 1]:
                     e.tensor_scalar(out=a, in0=b_, scalar1=r, scalar2=None, op0=ALU.mult),
                     reads=[c.PS_O[:, hh * DH:(hh + 1) * DH], RECS[:, h:h + 1]],
                     writes=[o16[:, hh * DH:(hh + 1) * DH]])
        for g in range(NTILE):
            srcb = c.SC16b[:, 1024 + g * 128:1024 + (g + 1) * 128]
            pslot = c.PS_T[:, (g % 8) * 128:(g % 8) * 128 + 128]
            p.op("tensor", lambda e, o=pslot, i_=srcb: e.transpose(o, i_, c.IDENT[:]),
                 reads=[srcb, c.IDENT[:]], writes=[pslot])
            dst = o_dst(g, qc)
            p.op("vector", lambda e, a=dst, b_=pslot: e.tensor_copy(out=a, in_=b_),
                 reads=[pslot], writes=[dst])


def emit_l0_attn(c, w):
    p = c.p
    X = [xt(c, i) for i in range(NTILE)]
    emit_ln_prefix(c, [pt(c.A, i) for i in range(NTILE)], variant="plain")
    WS16 = c.WS[:].bitcast(BF16)

    bv = c.PVEC[:, 0:8]
    bo = c.PVEC[:, 8:16]
    p.dma("sync", bv, w["wv_b"], stream="misc")
    p.dma("sync", bo, w["out_b"], stream="misc")

    p.dma("gpsimd", c.LAT[:, 0:1024], w["watt"], stream="watt")

    for m in range(NTILE):
        buf = m % 2
        wslot = WS16[:, buf * 1024:(buf + 1) * 1024]
        p.dma("gpsimd", wslot.rearrange("p (g m) -> p g m", m=128),
              w["wvT"].rearrange("(g p) n -> p g n", p=128)[:, :, m * 128:(m + 1) * 128],
              stream=f"wv{buf}")
        for tc in range(NTC):
            ps = (c.PS_A, c.PS_B)[tc % 2]
            for g in range(NTILE):
                lhs = wslot[:, g * 128:(g + 1) * 128]
                rhs = pt(c.A, g)[:, tc * TC:(tc + 1) * TC]
                p.op("tensor", lambda e, o=ps[:, :], l=lhs, r=rhs, s=(g == 0), st=(g == 7):
                     e.matmul(o, l, r, start=s, stop=st),
                     reads=[lhs, rhs], writes=[ps[:, :]])
            dst = c.SC16[:, tc * TC:(tc + 1) * TC]
            p.op("vector", lambda e, a=dst, b_=ps[:, :], bb=bv[:, m:m + 1]:
                 e.tensor_scalar(out=a, in0=b_, scalar1=bb, scalar2=None, op0=ALU.add),
                 reads=[ps[:, :], bv[:, m:m + 1]], writes=[dst])
        for h2 in range(2):
            h = 2 * m + h2
            for t in range(16):
                srct = c.SC16[64 * h2:64 * h2 + 64, t * 128:(t + 1) * 128]
                ident = c.IDENT[64 * h2:64 * h2 + 64, 64 * h2:64 * h2 + 64]
                dstp = c.PS_T[:, t * 64:(t + 1) * 64]
                p.op("tensor", lambda e, o=dstp, i_=srct, ii=ident: e.transpose(o, i_, ii),
                     reads=[srct, ident], writes=[dstp])
            dstv = c.B[:, h * 1024:(h + 1) * 1024]
            p.op("vector", lambda e, a=dstv, b_=c.PS_T[:, :]: e.tensor_copy(out=a, in_=b_),
                 reads=[c.PS_T[:, :]], writes=[dstv])

    for j in range(NTILE):
        wblk = c.LAT[:, j * 128:(j + 1) * 128]
        for tc in range(NTC):
            ps = (c.PS_A, c.PS_B)[tc % 2]
            rhs = pt(c.A, j)[:, tc * TC:(tc + 1) * TC]
            p.op("tensor", lambda e, o=ps[:, :], l=wblk, r=rhs:
                 e.matmul(o, l, r, start=True, stop=True),
                 reads=[wblk, rhs], writes=[ps[:, :]])
            dst = pt(c.C, j)[:, tc * TC:(tc + 1) * TC]
            p.op("scalar", lambda e, a=dst, b_=ps[:, :]: e.activation(a, b_, AF.Copy, scale=0.125),
                 reads=[ps[:, :]], writes=[dst])

    emit_attn_core(
        c, NH=16, DH=64, causal=False,
        q_ap=lambda h, qc: pt(c.C, h // 2)[64 * (h % 2):64 * (h % 2) + 64, qc * 128:(qc + 1) * 128],
        k_ap=lambda h, w0, w1: pt(c.A, h // 2)[64 * (h % 2):64 * (h % 2) + 64, w0:w1],
        vT_ap=lambda h, t: c.B[:, h * 1024 + t * 64:h * 1024 + t * 64 + 64],
        o_dst=lambda g, qc: pt(c.C, g)[:, qc * 128:(qc + 1) * 128])

    for m in range(NTILE):
        buf = m % 2
        wslot = WS16[:, buf * 1024:(buf + 1) * 1024]
        p.dma("gpsimd", wslot.rearrange("p (g m) -> p g m", m=128),
              w["outwT"].rearrange("(g p) n -> p g n", p=128)[:, :, m * 128:(m + 1) * 128],
              stream=f"wo{buf}")
        for tc in range(NTC):
            ps = (c.PS_A, c.PS_B)[tc % 2]
            for g in range(NTILE):
                lhs = wslot[:, g * 128:(g + 1) * 128]
                rhs = pt(c.C, g)[:, tc * TC:(tc + 1) * TC]
                p.op("tensor", lambda e, o=ps[:, :], l=lhs, r=rhs, s=(g == 0), st=(g == 7):
                     e.matmul(o, l, r, start=s, stop=st),
                     reads=[lhs, rhs], writes=[ps[:, :]])
            xdst = X[m][:, tc * TC:(tc + 1) * TC]
            p.op("vector", lambda e, x=xdst, b_=ps[:, :]:
                 e.scalar_tensor_tensor(out=x, in0=b_, scalar=1.0, in1=x,
                                        op0=ALU.mult, op1=ALU.add),
                 reads=[ps[:, :], xdst], writes=[xdst])
    for m in range(NTILE):
        p.op("vector", lambda e, x=X[m], bb=bo[:, m:m + 1]:
             e.tensor_scalar(out=x, in0=x, scalar1=bb, scalar2=None, op0=ALU.add),
             reads=[X[m], bo[:, m:m + 1]], writes=[X[m]])


def emit_l1_attn(c, w):
    p = c.p
    X = [xt(c, i) for i in range(NTILE)]
    emit_ln_prefix(c, [pt(c.A, i) for i in range(NTILE)], variant="plain")
    WS16 = c.WS[:].bitcast(BF16)

    def proj_stream(wap, dst_fn, tag, nk=NTILE, evac=None):
        NM = NTILE
        for m in range(NM):
            buf = m % 2
            wslot = WS16[:, buf * 1024:(buf + 1) * 1024]
            p.dma("gpsimd", wslot.rearrange("p (g m) -> p g m", m=128),
                  wap.rearrange("(g p) n -> p g n", p=128)[:, :, m * 128:(m + 1) * 128],
                  stream=f"{tag}{buf}")
            for tc in range(NTC):
                ps = (c.PS_A, c.PS_B)[tc % 2]
                for g in range(nk):
                    lhs = wslot[:, g * 128:(g + 1) * 128]
                    rhs = pt(c.A, g)[:, tc * TC:(tc + 1) * TC]
                    p.op("tensor", lambda e, o=ps[:, :], l=lhs, r=rhs, s=(g == 0), st=(g == nk - 1):
                         e.matmul(o, l, r, start=s, stop=st),
                         reads=[lhs, rhs], writes=[ps[:, :]])
                if evac is None:
                    dst = dst_fn(m, tc)
                    p.op("scalar", lambda e, a=dst, b_=ps[:, :]: e.activation(a, b_, AF.Copy),
                         reads=[ps[:, :]], writes=[dst])
                else:
                    evac(m, tc, ps)

    proj_stream(w["wk"], lambda m, tc: pt(c.B, m)[:, tc * TC:(tc + 1) * TC], "k")
    proj_stream(w["wv"], lambda m, tc: pt(c.C, m)[:, tc * TC:(tc + 1) * TC], "v")
    for m in range(NTILE):
        bm, cm = pt(c.B, m), pt(c.C, m)
        p.op("vector", lambda e, a=bm, b_=cm: e.tensor_add(out=a, in0=a, in1=b_),
             reads=[bm, cm], writes=[bm])
    p.dma("gpsimd", WS16[:, 0:2048].rearrange("p (g m) -> p g m", m=256),
          w["wdkv"].rearrange("(g p) n -> p g n", p=128), stream="wdkv")
    lat_t = [c.LAT[:, 0:2048], c.SC16[:, 0:2048]]
    for m in range(2):
        for tc in range(NTC):
            ps = (c.PS_A, c.PS_B)[tc % 2]
            for g in range(NTILE):
                lhs = WS16[:, g * 256 + m * 128:g * 256 + (m + 1) * 128]
                rhs = pt(c.B, g)[:, tc * TC:(tc + 1) * TC]
                p.op("tensor", lambda e, o=ps[:, :], l=lhs, r=rhs, s=(g == 0), st=(g == 7):
                     e.matmul(o, l, r, start=s, stop=st),
                     reads=[lhs, rhs], writes=[ps[:, :]])
            dst = lat_t[m][:, tc * TC:(tc + 1) * TC]
            p.op("scalar", lambda e, a=dst, b_=ps[:, :]: e.activation(a, b_, AF.Copy),
                 reads=[ps[:, :]], writes=[dst])
    p.dma("gpsimd", WS16[:, 2048:4096].rearrange("p (g m) -> p g m", m=1024),
          w["wupk"].rearrange("(g p) n -> p g n", p=128), stream="wupk")
    for m in range(NTILE):
        for tc in range(NTC):
            ps = (c.PS_A, c.PS_B)[tc % 2]
            for g in range(2):
                lhs = WS16[:, 2048 + g * 1024 + m * 128:2048 + g * 1024 + (m + 1) * 128]
                rhs = lat_t[g][:, tc * TC:(tc + 1) * TC]
                p.op("tensor", lambda e, o=ps[:, :], l=lhs, r=rhs, s=(g == 0), st=(g == 1):
                     e.matmul(o, l, r, start=s, stop=st),
                     reads=[lhs, rhs], writes=[ps[:, :]])
            dst = pt(c.C, m)[:, tc * TC:(tc + 1) * TC]
            p.op("scalar", lambda e, a=dst, b_=ps[:, :]: e.activation(a, b_, AF.Copy),
                 reads=[ps[:, :]], writes=[dst])
    p.dma("gpsimd", WS16[:, 2048:4096].rearrange("p (g m) -> p g m", m=1024),
          w["wupv"].rearrange("(g p) n -> p g n", p=128), stream="wupv")
    for m in range(NTILE):
        for tc in range(NTC):
            ps = (c.PS_A, c.PS_B)[tc % 2]
            for g in range(2):
                lhs = WS16[:, 2048 + g * 1024 + m * 128:2048 + g * 1024 + (m + 1) * 128]
                rhs = lat_t[g][:, tc * TC:(tc + 1) * TC]
                p.op("tensor", lambda e, o=ps[:, :], l=lhs, r=rhs, s=(g == 0), st=(g == 1):
                     e.matmul(o, l, r, start=s, stop=st),
                     reads=[lhs, rhs], writes=[ps[:, :]])
            dst = c.SC16b[:, tc * TC:(tc + 1) * TC]
            p.op("scalar", lambda e, a=dst, b_=ps[:, :]: e.activation(a, b_, AF.Copy),
                 reads=[ps[:, :]], writes=[dst])
        for h2 in range(2):
            h = 2 * m + h2
            for t in range(16):
                srct = c.SC16b[64 * h2:64 * h2 + 64, t * 128:(t + 1) * 128]
                ident = c.IDENT[64 * h2:64 * h2 + 64, 64 * h2:64 * h2 + 64]
                dstp = c.PS_T[:, t * 64:(t + 1) * 64]
                p.op("tensor", lambda e, o=dstp, i_=srct, ii=ident: e.transpose(o, i_, ii),
                     reads=[srct, ident], writes=[dstp])
            dstv = pt(c.B, h // 2)[:, (h % 2) * 1024:(h % 2) * 1024 + 1024] if False else \
                c.B[:, h * 1024:(h + 1) * 1024]
            p.op("vector", lambda e, a=dstv, b_=c.PS_T[:, :]: e.tensor_copy(out=a, in_=b_),
                 reads=[c.PS_T[:, :]], writes=[dstv])

    for tc in range(NTC):
        for m in range(NTILE):
            buf = m % 2
            wslot = WS16[:, buf * 1024:(buf + 1) * 1024]
            p.dma("gpsimd", wslot.rearrange("p (g m) -> p g m", m=128),
                  w["wq"].rearrange("(g p) n -> p g n", p=128)[:, :, m * 128:(m + 1) * 128],
                  stream=f"wq{buf}")
            ps = (c.PS_A, c.PS_B)[m % 2]
            for g in range(NTILE):
                lhs = wslot[:, g * 128:(g + 1) * 128]
                rhs = pt(c.A, g)[:, tc * TC:(tc + 1) * TC]
                p.op("tensor", lambda e, o=ps[:, :], l=lhs, r=rhs, s=(g == 0), st=(g == 7):
                     e.matmul(o, l, r, start=s, stop=st),
                     reads=[lhs, rhs], writes=[ps[:, :]])
            qtmp = (c.SC16 if m < 4 else c.SC16b)[:, (m % 4) * TC:(m % 4 + 1) * TC]
            p.op("scalar", lambda e, a=qtmp, b_=ps[:, :]: e.activation(a, b_, AF.Copy),
                 reads=[ps[:, :]], writes=[qtmp])
        for m in range(NTILE):
            qtmp = (c.SC16 if m < 4 else c.SC16b)[:, (m % 4) * TC:(m % 4 + 1) * TC]
            dst = pt(c.A, m)[:, tc * TC:(tc + 1) * TC]
            p.op("vector", lambda e, a=dst, b_=qtmp: e.tensor_copy(out=a, in_=b_),
                 reads=[qtmp], writes=[dst])

    LAM = 0.01
    last_qc = [-1]

    def hook(h, qc, ext):
        if last_qc[0] != qc:
            last_qc[0] = qc
            p.op("gpsimd", lambda e: e.iota(c.LAT[:, 0:T], pattern=[[1, T]],
                 base=-qc * 128, channel_multiplier=-1,
                 allow_small_or_imprecise_dtypes=True),
                 writes=[c.LAT[:, 0:T]])
            p.op("scalar", lambda e: e.activation(c.LAT[:, 0:T], c.LAT[:, 0:T], AF.Abs),
                 reads=[c.LAT[:, 0:T]], writes=[c.LAT[:, 0:T]])
            p.op("scalar", lambda e: e.activation(c.LAT[:, 0:T], c.LAT[:, 0:T], AF.Exp,
                 scale=-LAM, bias=c.CONSTF[:, 2:3]),
                 reads=[c.LAT[:, 0:T], c.CONSTF[:, 2:3]], writes=[c.LAT[:, 0:T]])
        p.op("vector", lambda e: e.tensor_tensor(out=c.PS_BIG[:, 0:T], in0=c.PS_BIG[:, 0:T],
             in1=c.LAT[:, 0:T], op=ALU.mult),
             reads=[c.PS_BIG[:, 0:T], c.LAT[:, 0:T]], writes=[c.PS_BIG[:, 0:T]])

    emit_attn_core(
        c, NH=16, DH=64, causal=False,
        q_ap=lambda h, qc: pt(c.A, h // 2)[64 * (h % 2):64 * (h % 2) + 64, qc * 128:(qc + 1) * 128],
        k_ap=lambda h, w0, w1: pt(c.C, h // 2)[64 * (h % 2):64 * (h % 2) + 64, w0:w1],
        vT_ap=lambda h, t: c.B[:, h * 1024 + t * 64:h * 1024 + t * 64 + 64],
        o_dst=lambda g, qc: pt(c.A, g)[:, qc * 128:(qc + 1) * 128],
        score_hook=hook)

    for m in range(NTILE):
        buf = m % 2
        wslot = WS16[:, buf * 1024:(buf + 1) * 1024]
        p.dma("gpsimd", wslot.rearrange("p (g m) -> p g m", m=128),
              w["wo"].rearrange("(g p) n -> p g n", p=128)[:, :, m * 128:(m + 1) * 128],
              stream=f"wo{buf}")
        for tc in range(NTC):
            ps = (c.PS_A, c.PS_B)[tc % 2]
            for g in range(NTILE):
                lhs = wslot[:, g * 128:(g + 1) * 128]
                rhs = pt(c.A, g)[:, tc * TC:(tc + 1) * TC]
                p.op("tensor", lambda e, o=ps[:, :], l=lhs, r=rhs, s=(g == 0), st=(g == 7):
                     e.matmul(o, l, r, start=s, stop=st),
                     reads=[lhs, rhs], writes=[ps[:, :]])
            xdst = X[m][:, tc * TC:(tc + 1) * TC]
            p.op("vector", lambda e, x=xdst, b_=ps[:, :]:
                 e.scalar_tensor_tensor(out=x, in0=b_, scalar=1.0, in1=x,
                                        op0=ALU.mult, op1=ALU.add),
                 reads=[ps[:, :], xdst], writes=[xdst])


def emit_l2_attn(c, w):
    p = c.p
    X = [xt(c, i) for i in range(NTILE)]
    emit_ln_prefix(c, [pt(c.A, i) for i in range(NTILE)], variant="plain")
    WS16 = c.WS[:].bitcast(BF16)
    SCALE = 1.0 / math.sqrt(128.0)

    iop = c.CONSTF[:, 12:13]
    par = c.CONSTF[:, 4:5]
    fv = c.PVEC[:, 2:3]
    qv = c.PVEC[:, 3:4]
    t0 = c.PVEC[:, 6:7]
    p.op("vector", lambda e: e.tensor_scalar(out=t0, in0=par, scalar1=-0.5, scalar2=None,
         op0=ALU.mult), reads=[par], writes=[t0])
    p.op("vector", lambda e: e.scalar_tensor_tensor(out=t0, in0=iop, scalar=0.5, in1=t0,
         op0=ALU.mult, op1=ALU.add), reads=[iop, t0], writes=[t0])
    p.op("scalar", lambda e: e.activation(fv, t0, AF.Exp, scale=-math.log(1e4) / 64.0),
         reads=[t0], writes=[fv])
    p.op("vector", lambda e: e.tensor_scalar(out=fv, in0=fv, scalar1=1.0 / (2 * math.pi),
         scalar2=None, op0=ALU.mult), reads=[fv], writes=[fv])
    p.op("vector", lambda e: e.tensor_scalar(out=qv, in0=par, scalar1=0.25, scalar2=None,
         op0=ALU.mult), reads=[par], writes=[qv])

    for tc in range(NTC):
        cs = slice(tc * TC, (tc + 1) * TC)
        iob = c.SBC[:, :]
        p.op("gpsimd", lambda e, a=iob, b0=tc * TC: e.iota(a, pattern=[[1, TC]], base=b0,
             channel_multiplier=0, allow_small_or_imprecise_dtypes=True), writes=[iob])
        p.op("vector", lambda e, a=c.BBC[:, :], b_=iob: e.tensor_scalar(out=a, in0=b_,
             scalar1=fv, scalar2=qv, op0=ALU.mult, op1=ALU.add),
             reads=[iob, fv, qv], writes=[c.BBC[:, :]])
        ri = c.RNDI[:, :]
        p.op("vector", lambda e, a=ri, b_=c.BBC[:, :]: e.tensor_copy(out=a, in_=b_),
             reads=[c.BBC[:, :]], writes=[ri])
        p.op("vector", lambda e, a=c.SBC[:, :], b_=ri: e.tensor_copy(out=a, in_=b_),
             reads=[ri], writes=[c.SBC[:, :]])
        p.op("vector", lambda e: e.tensor_sub(out=c.BBC[:, :], in0=c.BBC[:, :], in1=c.SBC[:, :]),
             reads=[c.BBC[:, :], c.SBC[:, :]], writes=[c.BBC[:, :]])
        dst = c.LAT[:, cs]
        p.op("scalar", lambda e, a=dst: e.activation(a, c.BBC[:, :], AF.Sin,
             scale=2 * math.pi), reads=[c.BBC[:, :]], writes=[dst])
        p.op("vector", lambda e, a=dst: e.tensor_scalar(out=a, in0=a, scalar1=SCALE,
             scalar2=None, op0=ALU.mult), reads=[dst], writes=[dst])

    p.dma("gpsimd", WS16[:, 0:1024].rearrange("p (g m) -> p g m", m=128),
          w["wkT"].rearrange("(g p) n -> p g n", p=128), stream="wk0")
    for tc in range(NTC):
        ps = (c.PS_A, c.PS_B)[tc % 2]
        for g in range(NTILE):
            lhs = WS16[:, g * 128:(g + 1) * 128]
            rhs = pt(c.A, g)[:, tc * TC:(tc + 1) * TC]
            p.op("tensor", lambda e, o=ps[:, :], l=lhs, r=rhs, s=(g == 0), st=(g == 7):
                 e.matmul(o, l, r, start=s, stop=st),
                 reads=[lhs, rhs], writes=[ps[:, :]])
        dst = c.C[:, tc * TC:(tc + 1) * TC]
        p.op("vector", lambda e, a=dst, b_=ps[:, :]:
             e.tensor_scalar(out=a, in0=b_, scalar1=par, scalar2=None, op0=ALU.add),
             reads=[ps[:, :], par], writes=[dst])
    p.dma("gpsimd", WS16[:, 1024:2048].rearrange("p (g m) -> p g m", m=128),
          w["wvT"].rearrange("(g p) n -> p g n", p=128), stream="wv0")
    for tc in range(NTC):
        ps = (c.PS_A, c.PS_B)[tc % 2]
        for g in range(NTILE):
            lhs = WS16[:, 1024 + g * 128:1024 + (g + 1) * 128]
            rhs = pt(c.A, g)[:, tc * TC:(tc + 1) * TC]
            p.op("tensor", lambda e, o=ps[:, :], l=lhs, r=rhs, s=(g == 0), st=(g == 7):
                 e.matmul(o, l, r, start=s, stop=st),
                 reads=[lhs, rhs], writes=[ps[:, :]])
        dst = c.SC16[:, tc * TC:(tc + 1) * TC]
        p.op("scalar", lambda e, a=dst, b_=ps[:, :]: e.activation(a, b_, AF.Copy),
             reads=[ps[:, :]], writes=[dst])
    for t in range(16):
        srct = c.SC16[:, t * 128:(t + 1) * 128]
        dstp = c.PS_T[:, (t % 8) * 128:(t % 8) * 128 + 128]
        p.op("tensor", lambda e, o=dstp, i_=srct: e.transpose(o, i_, c.IDENT[:]),
             reads=[srct, c.IDENT[:]], writes=[dstp])
        dstv = c.B[:, t * 128:(t + 1) * 128]
        p.op("vector", lambda e, a=dstv, b_=dstp: e.tensor_copy(out=a, in_=b_),
             reads=[dstp], writes=[dstv])

    for tc in range(NTC):
        for m in range(NTILE):
            buf = m % 2
            wslot = WS16[:, buf * 1024:(buf + 1) * 1024]
            p.dma("gpsimd", wslot.rearrange("p (g m) -> p g m", m=128),
                  w["wqT"].rearrange("(g p) n -> p g n", p=128)[:, :, m * 128:(m + 1) * 128],
                  stream=f"wq{buf}")
            ps = (c.PS_A, c.PS_B)[m % 2]
            for g in range(NTILE):
                lhs = wslot[:, g * 128:(g + 1) * 128]
                rhs = pt(c.A, g)[:, tc * TC:(tc + 1) * TC]
                p.op("tensor", lambda e, o=ps[:, :], l=lhs, r=rhs, s=(g == 0), st=(g == 7):
                     e.matmul(o, l, r, start=s, stop=st),
                     reads=[lhs, rhs], writes=[ps[:, :]])
            qtmp = (c.SC16 if m < 4 else c.SC16b)[:, (m % 4) * TC:(m % 4 + 1) * TC]
            pel = c.LAT[:, tc * TC:(tc + 1) * TC]
            p.op("vector", lambda e, a=qtmp, b_=ps[:, :], pe=pel:
                 e.scalar_tensor_tensor(out=a, in0=b_, scalar=SCALE, in1=pe,
                                        op0=ALU.mult, op1=ALU.add),
                 reads=[ps[:, :], pel], writes=[qtmp])
        for m in range(NTILE):
            qtmp = (c.SC16 if m < 4 else c.SC16b)[:, (m % 4) * TC:(m % 4 + 1) * TC]
            dst = pt(c.A, m)[:, tc * TC:(tc + 1) * TC]
            p.op("vector", lambda e, a=dst, b_=qtmp: e.tensor_copy(out=a, in_=b_),
                 reads=[qtmp], writes=[dst])

    import os as _os
    if int(_os.environ.get("L2_SKIP_CORE", "0")):
        return
    emit_attn_core(
        c, NH=8, DH=128, causal=True,
        q_ap=lambda h, qc: pt(c.A, h)[:, qc * 128:(qc + 1) * 128],
        k_ap=lambda h, w0, w1: c.C[:, w0:w1],
        vT_ap=lambda h, t: c.B[:, t * 128:(t + 1) * 128],
        o_dst=lambda g, qc: pt(c.A, g)[:, qc * 128:(qc + 1) * 128])

    for m in range(NTILE):
        buf = m % 2
        wslot = WS16[:, buf * 1024:(buf + 1) * 1024]
        p.dma("gpsimd", wslot.rearrange("p (g m) -> p g m", m=128),
              w["woT"].rearrange("(g p) n -> p g n", p=128)[:, :, m * 128:(m + 1) * 128],
              stream=f"wo{buf}")
        for tc in range(NTC):
            ps = (c.PS_A, c.PS_B)[tc % 2]
            for g in range(NTILE):
                lhs = wslot[:, g * 128:(g + 1) * 128]
                rhs = pt(c.A, g)[:, tc * TC:(tc + 1) * TC]
                p.op("tensor", lambda e, o=ps[:, :], l=lhs, r=rhs, s=(g == 0), st=(g == 7):
                     e.matmul(o, l, r, start=s, stop=st),
                     reads=[lhs, rhs], writes=[ps[:, :]])
            xdst = X[m][:, tc * TC:(tc + 1) * TC]
            p.op("vector", lambda e, x=xdst, b_=ps[:, :]:
                 e.scalar_tensor_tensor(out=x, in0=b_, scalar=1.0, in1=x,
                                        op0=ALU.mult, op1=ALU.add),
                 reads=[ps[:, :], xdst], writes=[xdst])


def emit_l2_ffn(c, w):
    p = c.p
    X = [xt(c, i) for i in range(NTILE)]
    emit_ln_prefix(c, [pt(c.A, i) for i in range(NTILE)], variant="plain")
    WS16 = c.WS[:].bitcast(BF16)

    bbase = c.PVEC[:, 0:8]
    benc = c.PVEC[:, 8:24]
    bdec = c.PVEC[:, 24:32]
    p.dma("sync", bbase, w["b_base"], stream="misc")
    p.dma("sync", benc, w["b_enc"], stream="misc2")
    bdn = c.PVEC[:, 32:40]
    bup = c.PVEC[:, 40:104]
    p.dma("sync", bdn, w["b_down"], stream="misc3")
    p.dma("sync", bup, w["b_up"], stream="misc4")
    bdecv = c.PVEC[:, 104:168] if False else None
    bdec64 = c.TMP32[:, 128:192]
    p.dma("sync", bdec64, w["b_dec"], stream="misc5")

    LOSSV = c.PVEC[:, 64:72]
    TACC = c.PVEC[:, 72:73]

    for e in range(8):
        p.dma("gpsimd", WS16[:, 0:2048].rearrange("p (g m) -> p g m", m=256),
              w["W_enc"][e].rearrange("(g p) n -> p g n", p=128), stream="enc")
        for m2 in range(2):
            for tc in range(NTC):
                ps = (c.PS_A, c.PS_B)[tc % 2]
                for g in range(NTILE):
                    lhs = WS16[:, g * 256 + m2 * 128:g * 256 + (m2 + 1) * 128]
                    rhs = pt(c.A, g)[:, tc * TC:(tc + 1) * TC]
                    p.op("tensor", lambda e_, o=ps[:, :], l=lhs, r=rhs, s=(g == 0), st=(g == 7):
                         e_.matmul(o, l, r, start=s, stop=st),
                         reads=[lhs, rhs], writes=[ps[:, :]])
                dst = pt(c.B, m2)[:, tc * TC:(tc + 1) * TC]
                bia = benc[:, e * 2 + m2:e * 2 + m2 + 1]
                p.op("scalar", lambda e_, a=dst, b_=ps[:, :], bb=bia:
                     e_.activation(a, b_, AF.Relu, bias=bb),
                     reads=[ps[:, :], bia], writes=[dst])
        p.dma("gpsimd", WS16[:, 2048:4096].rearrange("p (g m) -> p g m", m=1024),
              w["W_dec"][e].rearrange("(g p) n -> p g n", p=128), stream="dec")
        first = [True]
        for m in range(NTILE):
            for tc in range(NTC):
                ps = (c.PS_A, c.PS_B)[tc % 2]
                for g in range(2):
                    lhs = WS16[:, 2048 + g * 1024 + m * 128:2048 + g * 1024 + (m + 1) * 128]
                    rhs = pt(c.B, g)[:, tc * TC:(tc + 1) * TC]
                    p.op("tensor", lambda e_, o=ps[:, :], l=lhs, r=rhs, s=(g == 0), st=False:
                         e_.matmul(o, l, r, start=s, stop=st),
                         reads=[lhs, rhs], writes=[ps[:, :]])
                nref = pt(c.A, m)[:, tc * TC:(tc + 1) * TC]
                p.op("tensor", lambda e_, o=ps[:, :], r=nref:
                     e_.matmul(o, c.NIDENT[:], r, start=False, stop=True),
                     reads=[c.NIDENT[:], nref], writes=[ps[:, :]])
                bd = bdec64[:, e * 8 + m:e * 8 + m + 1]
                sq = c.SC16b[:, 0:TC]
                tgt = LOSSV[:, e:e + 1]
                if first[0]:
                    p.op("scalar", lambda e_, a=sq, b_=ps[:, :], bb=bd, ac=tgt:
                         e_.activation(a, b_, AF.Square, bias=bb, accum_out=ac),
                         reads=[ps[:, :], bd], writes=[sq, tgt])
                    first[0] = False
                else:
                    p.op("scalar", lambda e_, a=sq, b_=ps[:, :], bb=bd, ac=TACC:
                         e_.activation(a, b_, AF.Square, bias=bb, accum_out=ac),
                         reads=[ps[:, :], bd], writes=[sq, TACC])
                    p.op("vector", lambda e_, a=tgt: e_.tensor_add(out=a, in0=a, in1=TACC),
                         reads=[tgt, TACC], writes=[tgt])

    onecol = c.ONES[:, 0:1]
    lv16 = c.SC16[:, 0:8]
    p.op("vector", lambda e_: e_.tensor_copy(out=lv16, in_=LOSSV), reads=[LOSSV], writes=[lv16])
    p.op("tensor", lambda e_: e_.matmul(c.PS_A[0:1, 0:8], onecol, lv16, start=True, stop=True),
         reads=[onecol, lv16], writes=[c.PS_A[0:1, 0:8]])
    LOSS = row(c, R_T0)[:, 0:8]
    p.op("vector", lambda e_: e_.tensor_copy(out=LOSS, in_=c.PS_A[0:1, 0:8]),
         reads=[c.PS_A[0:1, 0:8]], writes=[LOSS])
    MN = row(c, R_T0)[:, 8:9]
    p.op("vector", lambda e_: e_.tensor_reduce(out=MN, in_=LOSS, axis=AX.X, op=ALU.min),
         reads=[LOSS], writes=[MN])
    ISM = row(c, R_T0)[:, 16:24]
    p.op("vector", lambda e_: e_.tensor_scalar(out=ISM, in0=LOSS, scalar1=MN, scalar2=None,
         op0=ALU.subtract), reads=[LOSS, MN], writes=[ISM])
    p.op("scalar", lambda e_: e_.activation(ISM, ISM, AF.Sign), reads=[ISM], writes=[ISM])
    p.op("vector", lambda e_: e_.tensor_scalar(out=ISM, in0=ISM, scalar1=-1.0, scalar2=1.0,
         op0=ALU.mult, op1=ALU.add), reads=[ISM], writes=[ISM])
    ci = c.PVI[0:1, 0:8]
    p.dma("sync", ci, w["cidx"], stream="misc6")
    cif = row(c, R_T0)[:, 24:32]
    p.op("vector", lambda e_: e_.tensor_copy(out=cif, in_=ci), reads=[c.PVI[0:1, 0:8]], writes=[cif])
    OH = row(c, R_T1)[:, 0:8]
    tn = row(c, R_T1)[:, 8:16]
    tu = row(c, R_T1)[:, 16:24]
    for n in range(8):
        p.op("vector", lambda e_, nn=float(n): e_.tensor_scalar(out=tn, in0=cif,
             scalar1=-nn, scalar2=None, op0=ALU.add), reads=[cif], writes=[tn])
        p.op("scalar", lambda e_: e_.activation(tn, tn, AF.Abs), reads=[tn], writes=[tn])
        p.op("scalar", lambda e_: e_.activation(tn, tn, AF.Sign, bias=c.CONSTF[0:1, 3:4]),
             reads=[tn, c.CONSTF[0:1, 3:4]], writes=[tn])
        p.op("vector", lambda e_: e_.tensor_scalar(out=tn, in0=tn, scalar1=-0.5, scalar2=0.5,
             op0=ALU.mult, op1=ALU.add), reads=[tn], writes=[tn])
        p.op("vector", lambda e_: e_.tensor_mul(out=tu, in0=tn, in1=ISM),
             reads=[tn, ISM], writes=[tu])
        p.op("vector", lambda e_, oo=OH[:, n:n + 1]: e_.tensor_reduce(out=oo, in_=tu,
             axis=AX.X, op=ALU.add), reads=[tu], writes=[OH[:, n:n + 1]])
    OHB = c.PVEC[:, 72:80]
    p.op("tensor", lambda e_: e_.matmul(c.PS_B[:, 0:8], c.ONESROWF[0:1, :], OH, start=True, stop=True),
         reads=[c.ONESROWF[0:1, :], OH], writes=[c.PS_B[:, 0:8]])
    p.op("vector", lambda e_: e_.tensor_copy(out=OHB, in_=c.PS_B[:, 0:8]),
         reads=[c.PS_B[:, 0:8]], writes=[OHB])

    ohrep = c.SBC[:, :]
    p.op("vector", lambda e_: e_.tensor_copy(
         out=ohrep.rearrange("p (n w) -> p n w", w=64),
         in_=OHB.unsqueeze(2).broadcast_to([128, 8, 64])),
         reads=[OHB], writes=[ohrep])
    for g in range(NTILE):
        slot = WS16[:, 0:512]
        p.dma("gpsimd", slot.rearrange("p (n r) -> p n r", r=64),
              w["W_down"].rearrange("e (g q) r -> q e r", q=128)[:, :, :] if False else
              w["W_down"][:, g * 128:(g + 1) * 128, :].rearrange("e q r -> q e r"),
              stream="wdn")
        msk = c.BBC[:, :]
        p.op("vector", lambda e_, s_=slot, mm=msk, oo=ohrep: e_.tensor_mul(out=mm, in0=s_, in1=oo),
             reads=[slot, ohrep], writes=[msk])
        p.op("vector", lambda e_, mm=msk: e_.tensor_add(out=mm[:, 0:256], in0=mm[:, 0:256], in1=mm[:, 256:512]),
             reads=[msk], writes=[msk[:, 0:256]])
        p.op("vector", lambda e_, mm=msk: e_.tensor_add(out=mm[:, 0:128], in0=mm[:, 0:128], in1=mm[:, 128:256]),
             reads=[msk[:, 0:256]], writes=[msk[:, 0:128]])
        p.op("vector", lambda e_, mm=msk, dst=c.LAT[:, g * 64:(g + 1) * 64]:
             e_.tensor_add(out=dst, in0=mm[:, 0:64], in1=mm[:, 64:128]),
             reads=[msk[:, 0:128]], writes=[c.LAT[:, g * 64:(g + 1) * 64]])
    ohrep2 = c.SBC[0:64, :].rearrange("p (n w) -> p n w", w=64)
    for m in range(NTILE):
        slot = WS16[0:64, 512:1536]
        p.dma("gpsimd", slot.rearrange("p (n r) -> p n r", r=128),
              w["W_up"][:, :, m * 128:(m + 1) * 128].rearrange("e q r -> q e r"),
              stream="wup")
        msk = c.BBC[0:64, :].rearrange("p (n w) -> p n w", w=128) if False else c.BBC[0:64, :]
        for half in range(2):
            mh = c.BBC[0:64, :]
            src = slot[:, half * 512:(half + 1) * 512]
            ohr = OHB[0:64, half * 4:(half + 1) * 4].unsqueeze(2).broadcast_to([64, 4, 128])
            p.op("vector", lambda e_, a=mh, s_=src, o_=ohr: e_.tensor_tensor(
                 out=a.rearrange("p (n w) -> p n w", w=128), in0=s_.rearrange("p (n w) -> p n w", w=128),
                 in1=o_, op=ALU.mult),
                 reads=[src, OHB], writes=[mh])
            p.op("vector", lambda e_, a=mh: e_.tensor_add(out=a[:, 0:256], in0=a[:, 0:256], in1=a[:, 256:512]),
                 reads=[mh], writes=[mh[:, 0:256]])
            p.op("vector", lambda e_, a=mh: e_.tensor_add(out=a[:, 0:128], in0=a[:, 0:128], in1=a[:, 128:256]),
                 reads=[mh[:, 0:256]], writes=[mh[:, 0:128]])
            if half == 0:
                p.op("vector", lambda e_, a=c.TMP32[0:64, 192:320]: e_.tensor_copy(out=a, in_=mh[:, 0:128]),
                     reads=[mh[:, 0:128]], writes=[c.TMP32[0:64, 192:320]])
            else:
                p.op("vector", lambda e_, dst=c.LAT[0:64, 512 + m * 128:512 + (m + 1) * 128]:
                     e_.tensor_add(out=dst, in0=mh[:, 0:128], in1=c.TMP32[0:64, 192:320]),
                     reads=[mh[:, 0:128], c.TMP32[0:64, 192:320]], writes=[c.LAT[0:64, 512 + m * 128:512 + (m + 1) * 128]])
    bds = c.PVEC[:, 80:81]
    tt8 = c.PVEC[:, 81:89]
    p.op("vector", lambda e_: e_.tensor_mul(out=tt8, in0=bdn, in1=OHB), reads=[bdn, OHB], writes=[tt8])
    p.op("vector", lambda e_: e_.tensor_reduce(out=bds, in_=tt8, axis=AX.X, op=ALU.add),
         reads=[tt8], writes=[bds])
    bus = c.PVEC[:, 96:104]
    t64 = c.TMP32[:, 320:384]
    p.op("vector", lambda e_: e_.tensor_tensor(
         out=t64.rearrange("p (n w) -> p n w", w=8), in0=bup.rearrange("p (n w) -> p n w", w=8),
         in1=OHB.unsqueeze(2).broadcast_to([128, 8, 8]), op=ALU.mult),
         reads=[bup, OHB], writes=[t64])
    p.op("vector", lambda e_: e_.tensor_add(out=t64[:, 0:32], in0=t64[:, 0:32], in1=t64[:, 32:64]),
         reads=[t64], writes=[t64[:, 0:32]])
    p.op("vector", lambda e_: e_.tensor_add(out=t64[:, 0:16], in0=t64[:, 0:16], in1=t64[:, 16:32]),
         reads=[t64[:, 0:32]], writes=[t64[:, 0:16]])
    p.op("vector", lambda e_: e_.tensor_add(out=bus, in0=t64[:, 0:8], in1=t64[:, 8:16]),
         reads=[t64[:, 0:16]], writes=[bus])
    bbh = c.PVEC[:, 104:112]
    p.op("vector", lambda e_: e_.tensor_add(out=bbh, in0=bbase, in1=bus), reads=[bbase, bus], writes=[bbh])
    p.op("vector", lambda e_: e_.tensor_scalar(out=bbh, in0=bbh, scalar1=0.5, scalar2=None,
         op0=ALU.mult), reads=[bbh], writes=[bbh])

    for tc in range(NTC):
        ps = (c.PS_A, c.PS_B)[tc % 2]
        for g in range(NTILE):
            lhs = c.LAT[:, g * 64:(g + 1) * 64]
            rhs = pt(c.A, g)[:, tc * TC:(tc + 1) * TC]
            p.op("tensor", lambda e_, o=ps[0:64, :], l=lhs, r=rhs, s=(g == 0), st=(g == 7):
                 e_.matmul(o, l, r, start=s, stop=st),
                 reads=[lhs, rhs], writes=[ps[0:64, :]])
        dst = c.SC16[0:64, tc * TC:(tc + 1) * TC]
        p.op("scalar", lambda e_, a=dst, b_=ps[0:64, :]:
             e_.activation(a, b_, AF.Relu, bias=bds[0:64, :]),
             reads=[ps[0:64, :], bds], writes=[dst])

    for m in range(NTILE):
        buf = m % 2
        wslot = WS16[:, buf * 1024:(buf + 1) * 1024]
        p.dma("gpsimd", wslot.rearrange("p (g m) -> p g m", m=128),
              w["W_base"].rearrange("(g p) n -> p g n", p=128)[:, :, m * 128:(m + 1) * 128],
              stream=f"wb{buf}")
        for tc in range(NTC):
            ps = (c.PS_A, c.PS_B)[tc % 2]
            for g in range(NTILE):
                lhs = wslot[:, g * 128:(g + 1) * 128]
                rhs = pt(c.A, g)[:, tc * TC:(tc + 1) * TC]
                p.op("tensor", lambda e_, o=ps[:, :], l=lhs, r=rhs, s=(g == 0), st=False:
                     e_.matmul(o, l, r, start=s, stop=st),
                     reads=[lhs, rhs], writes=[ps[:, :]])
            lhs2 = c.LAT[0:64, 512 + m * 128:512 + (m + 1) * 128]
            rhs2 = c.SC16[0:64, tc * TC:(tc + 1) * TC]
            p.op("tensor", lambda e_, o=ps[:, :], l=lhs2, r=rhs2:
                 e_.matmul(o, l, r, start=False, stop=True),
                 reads=[lhs2, rhs2], writes=[ps[:, :]])
            xdst = X[m][:, tc * TC:(tc + 1) * TC]
            p.op("vector", lambda e_, x=xdst, b_=ps[:, :]:
                 e_.scalar_tensor_tensor(out=x, in0=b_, scalar=0.5, in1=x,
                                        op0=ALU.mult, op1=ALU.add),
                 reads=[ps[:, :], xdst], writes=[xdst])
        p.op("vector", lambda e_, x=X[m], bb=bbh[:, m:m + 1]:
             e_.tensor_scalar(out=x, in0=x, scalar1=bb, scalar2=None, op0=ALU.add),
             reads=[X[m], bbh[:, m:m + 1]], writes=[X[m]])


def emit_l3_attn(c, w):
    p = c.p
    X = [xt(c, i) for i in range(NTILE)]
    NH, DH = 16, 64

    emit_ln_prefix(c, [pt(c.A, i) for i in range(NTILE)], variant="plain")

    WS16 = c.WS[:].bitcast(BF16)

    for m in range(NTILE):
        buf = m % 2
        wslot = WS16[:, buf * 1024:(buf + 1) * 1024]
        p.dma("gpsimd", wslot.rearrange("p (g m) -> p g m", m=128),
              w["wkT"].rearrange("(g p) n -> p g n", p=128)[:, :, m * 128:(m + 1) * 128],
              stream=f"wk{buf}")
        for tc in range(NTC):
            ps = (c.PS_A, c.PS_B)[tc % 2]
            for g in range(NTILE):
                lhs = wslot[:, g * 128:(g + 1) * 128]
                rhs = pt(c.A, g)[:, tc * TC:(tc + 1) * TC]
                p.op("tensor", lambda e, o=ps[:, :], l=lhs, r=rhs, s=(g == 0), st=(g == 7):
                     e.matmul(o, l, r, start=s, stop=st),
                     reads=[lhs, rhs], writes=[ps[:, :]])
            dst = pt(c.C, m)[:, tc * TC:(tc + 1) * TC]
            p.op("scalar", lambda e, a=dst, b_=ps[:, :]: e.activation(a, b_, AF.Copy),
                 reads=[ps[:, :]], writes=[dst])

    for m in range(NTILE):
        buf = m % 2
        wslot = WS16[:, 2048 + buf * 1024:2048 + (buf + 1) * 1024]
        p.dma("gpsimd", wslot.rearrange("p (g m) -> p g m", m=128),
              w["wvT"].rearrange("(g p) n -> p g n", p=128)[:, :, m * 128:(m + 1) * 128],
              stream=f"wv{buf}")
        for tc in range(NTC):
            ps = (c.PS_A, c.PS_B)[tc % 2]
            for g in range(NTILE):
                lhs = wslot[:, g * 128:(g + 1) * 128]
                rhs = pt(c.A, g)[:, tc * TC:(tc + 1) * TC]
                p.op("tensor", lambda e, o=ps[:, :], l=lhs, r=rhs, s=(g == 0), st=(g == 7):
                     e.matmul(o, l, r, start=s, stop=st),
                     reads=[lhs, rhs], writes=[ps[:, :]])
            dst = c.SC16[:, tc * TC:(tc + 1) * TC]
            p.op("scalar", lambda e, a=dst, b_=ps[:, :]: e.activation(a, b_, AF.Copy),
                 reads=[ps[:, :]], writes=[dst])
        for h2 in range(2):
            h = 2 * m + h2
            for t in range(16):
                src = c.SC16[64 * h2:64 * h2 + 64, t * 128:(t + 1) * 128]
                ident = c.IDENT[64 * h2:64 * h2 + 64, 64 * h2:64 * h2 + 64]
                dstp = c.PS_T[:, t * 64:(t + 1) * 64]
                p.op("tensor", lambda e, o=dstp, i_=src, ii=ident: e.transpose(o, i_, ii),
                     reads=[src, ident], writes=[dstp])
            dstv = c.B[:, h * 1024:(h + 1) * 1024]
            p.op("vector", lambda e, a=dstv, b_=c.PS_T[:, :]: e.tensor_copy(out=a, in_=b_),
                 reads=[c.PS_T[:, :]], writes=[dstv])

    for tc in range(NTC):
        for m in range(NTILE):
            buf = m % 2
            wslot = WS16[:, buf * 1024:(buf + 1) * 1024]
            p.dma("gpsimd", wslot.rearrange("p (g m) -> p g m", m=128),
                  w["wqT"].rearrange("(g p) n -> p g n", p=128)[:, :, m * 128:(m + 1) * 128],
                  stream=f"wq{buf}")
            ps = (c.PS_A, c.PS_B)[m % 2]
            for g in range(NTILE):
                lhs = wslot[:, g * 128:(g + 1) * 128]
                rhs = pt(c.A, g)[:, tc * TC:(tc + 1) * TC]
                p.op("tensor", lambda e, o=ps[:, :], l=lhs, r=rhs, s=(g == 0), st=(g == 7):
                     e.matmul(o, l, r, start=s, stop=st),
                     reads=[lhs, rhs], writes=[ps[:, :]])
            qtmp = (c.SC16 if m < 4 else c.SC16b)[:, (m % 4) * TC:(m % 4 + 1) * TC]
            p.op("scalar", lambda e, a=qtmp, b_=ps[:, :]: e.activation(a, b_, AF.Copy, scale=0.125),
                 reads=[ps[:, :]], writes=[qtmp])
        for m in range(NTILE):
            qtmp = (c.SC16 if m < 4 else c.SC16b)[:, (m % 4) * TC:(m % 4 + 1) * TC]
            dst = pt(c.A, m)[:, tc * TC:(tc + 1) * TC]
            p.op("vector", lambda e, a=dst, b_=qtmp: e.tensor_copy(out=a, in_=b_),
                 reads=[qtmp], writes=[dst])

    emit_attn_core(
        c, NH=16, DH=64, causal=True,
        q_ap=lambda h, qc: pt(c.A, h // 2)[64 * (h % 2):64 * (h % 2) + 64, qc * 128:(qc + 1) * 128],
        k_ap=lambda h, w0, w1: pt(c.C, h // 2)[64 * (h % 2):64 * (h % 2) + 64, w0:w1],
        vT_ap=lambda h, t: c.B[:, h * 1024 + t * 64:h * 1024 + t * 64 + 64],
        o_dst=lambda g, qc: pt(c.A, g)[:, qc * 128:(qc + 1) * 128])

    for m in range(NTILE):
        buf = m % 2
        wslot = WS16[:, buf * 1024:(buf + 1) * 1024]
        p.dma("gpsimd", wslot.rearrange("p (g m) -> p g m", m=128),
              w["woT"].rearrange("(g p) n -> p g n", p=128)[:, :, m * 128:(m + 1) * 128],
              stream=f"wo{buf}")
        for tc in range(NTC):
            ps = (c.PS_A, c.PS_B)[tc % 2]
            for g in range(NTILE):
                lhs = wslot[:, g * 128:(g + 1) * 128]
                rhs = pt(c.A, g)[:, tc * TC:(tc + 1) * TC]
                p.op("tensor", lambda e, o=ps[:, :], l=lhs, r=rhs, s=(g == 0), st=(g == 7):
                     e.matmul(o, l, r, start=s, stop=st),
                     reads=[lhs, rhs], writes=[ps[:, :]])
            xdst = X[m][:, tc * TC:(tc + 1) * TC]
            p.op("vector", lambda e, x=xdst, b_=ps[:, :]:
                 e.scalar_tensor_tensor(out=x, in0=b_, scalar=1.0, in1=x,
                                        op0=ALU.mult, op1=ALU.add),
                 reads=[ps[:, :], xdst], writes=[xdst])


def emit_l3_ffn(c, w):
    p = c.p
    X = [xt(c, i) for i in range(NTILE)]
    WS16 = c.WS[:].bitcast(BF16)

    rsv = c.PVEC[:, 0:8]
    p.dma("sync", rsv, w["rs"], stream="misc")
    pesr = row(c, R_T0)[:, 40:48]
    p.dma("sync", pesr, w["pes"], stream="misc7")
    PESB = c.PVEC[:, 8:16]
    p.op("tensor", lambda e_: e_.matmul(c.PS_O[:, 0:8], c.ONESROWF[0:1, :], pesr,
         start=True, stop=True), reads=[c.ONESROWF[0:1, :], pesr], writes=[c.PS_O[:, 0:8]])
    p.op("vector", lambda e_: e_.tensor_copy(out=PESB, in_=c.PS_O[:, 0:8]),
         reads=[c.PS_O[:, 0:8]], writes=[PESB])
    rwb = WS16[:, 3968:4032]
    p.dma("gpsimd", rwb.rearrange("p (g m) -> p g m", m=8),
          w["rw"].rearrange("(g p) n -> p g n", p=128), stream="rw")
    for g in range(NTILE):
        sl = rwb[:, g * 8:(g + 1) * 8]
        p.op("vector", lambda e_, s_=sl, rc=rsv[:, g:g + 1]: e_.tensor_scalar(
             out=s_, in0=s_, scalar1=rc, scalar2=None, op0=ALU.mult),
             reads=[rwb, rsv], writes=[sl])

    LOGT = c.TMP32[:, 128:256]
    CT = c.TMP32[:, 0:128]

    def router_cb(tc):
        rr = row(c, R_VAR)
        for g in range(NTILE):
            lhs = rwb[:, g * 8:(g + 1) * 8]
            rhs = pt(c.A, g)[:, tc * TC:(tc + 1) * TC]
            p.op("tensor", lambda e_, o=c.PS_A[0:8, :], l=lhs, r=rhs, s=(g == 0), st=(g == 7):
                 e_.matmul(o, l, r, start=s, stop=st),
                 reads=[lhs, rhs], writes=[c.PS_A[0:8, :]])
        p.op("tensor", lambda e_, r=rr: e_.matmul(c.PS_B[0:8, :], c.ONESROWF[0:1, 0:8], r,
             start=True, stop=True), reads=[c.ONESROWF[0:1, 0:8], rr], writes=[c.PS_B[0:8, :]])
        lg = c.SBC[0:8, :]
        p.op("vector", lambda e_, a=lg: e_.tensor_copy(out=a, in_=c.PS_B[0:8, :]),
             reads=[c.PS_B[0:8, :]], writes=[lg])
        p.op("vector", lambda e_, a=lg: e_.tensor_tensor(out=a, in0=c.PS_A[0:8, :], in1=a,
             op=ALU.mult), reads=[c.PS_A[0:8, :], lg], writes=[lg])
        for j in range(4):
            t = tc * 4 + j
            srcl = lg[:, j * 128:(j + 1) * 128]
            ps = c.PS_O[:, (j % 2) * 128:(j % 2) * 128 + 8]
            p.op("tensor", lambda e_, o=ps, i_=srcl: e_.transpose(o, i_, c.IDENTF[0:8, 0:8]),
                 reads=[srcl, c.IDENTF[0:8, 0:8]], writes=[ps])
            dst = LOGT[:, t * 8:(t + 1) * 8]
            p.op("vector", lambda e_, a=dst, b_=ps: e_.tensor_copy(out=a, in_=b_),
                 reads=[ps], writes=[dst])

    emit_ln_prefix(c, [pt(c.A, i) for i in range(NTILE)], variant="plain",
                   rr_out=lambda tc: row(c, R_VAR), chunk_cb=router_cb)

    T8a = c.PVEC[:, 96:104]
    T8b = c.PVEC[:, 88:96]
    M1 = c.PVEC[:, 104:105]
    NM1 = c.PVEC[:, 105:106]
    M2 = c.PVEC[:, 106:107]
    SS = c.PVEC[:, 107:108]
    for t in range(16):
        lt = LOGT[:, t * 8:(t + 1) * 8]
        p.op("vector", lambda e_, l=lt: e_.tensor_reduce(out=M1, in_=l, axis=AX.X, op=ALU.max),
             reads=[lt], writes=[M1])
        p.op("vector", lambda e_, l=lt: e_.tensor_scalar(out=T8a, in0=l, scalar1=M1,
             scalar2=None, op0=ALU.subtract), reads=[lt, M1], writes=[T8a])
        p.op("scalar", lambda e_: e_.activation(T8a, T8a, AF.Sign), reads=[T8a], writes=[T8a])
        p.op("vector", lambda e_: e_.tensor_scalar(out=T8a, in0=T8a, scalar1=1.0,
             scalar2=None, op0=ALU.add), reads=[T8a], writes=[T8a])
        p.op("vector", lambda e_, l=lt: e_.scalar_tensor_tensor(out=T8a, in0=T8a,
             scalar=-1e30, in1=l, op0=ALU.mult, op1=ALU.add), reads=[T8a, lt], writes=[T8a])
        p.op("vector", lambda e_: e_.tensor_reduce(out=M2, in_=T8a, axis=AX.X, op=ALU.max),
             reads=[T8a], writes=[M2])
        p.op("vector", lambda e_, l=lt: e_.tensor_scalar(out=T8a, in0=l, scalar1=M2,
             scalar2=None, op0=ALU.subtract), reads=[lt, M2], writes=[T8a])
        p.op("scalar", lambda e_: e_.activation(T8a, T8a, AF.Sign), reads=[T8a], writes=[T8a])
        p.op("vector", lambda e_: e_.tensor_scalar(out=T8a, in0=T8a, scalar1=1.0,
             scalar2=1.0, op0=ALU.add, op1=ALU.min), reads=[T8a], writes=[T8a])
        p.op("vector", lambda e_: e_.tensor_scalar(out=NM1, in0=M1, scalar1=-1.0,
             scalar2=None, op0=ALU.mult), reads=[M1], writes=[NM1])
        p.op("scalar", lambda e_, l=lt: e_.activation(T8b, l, AF.Exp, bias=NM1),
             reads=[lt, NM1], writes=[T8b])
        p.op("vector", lambda e_: e_.tensor_mul(out=T8b, in0=T8b, in1=T8a),
             reads=[T8b, T8a], writes=[T8b])
        p.op("vector", lambda e_: e_.tensor_reduce(out=SS, in_=T8b, axis=AX.X, op=ALU.add),
             reads=[T8b], writes=[SS])
        p.op("vector", lambda e_: e_.reciprocal(out=SS, in_=SS), reads=[SS], writes=[SS])
        p.op("vector", lambda e_: e_.tensor_scalar(out=T8b, in0=T8b, scalar1=SS,
             scalar2=None, op0=ALU.mult), reads=[T8b, SS], writes=[T8b])
        p.op("vector", lambda e_, o=CT[:, t * 8:(t + 1) * 8]: e_.tensor_mul(out=o, in0=T8b,
             in1=PESB), reads=[T8b, PESB], writes=[CT[:, t * 8:(t + 1) * 8]])

    for t in range(16):
        srcc = CT[:, t * 8:(t + 1) * 8]
        ps = c.PS_O[0:8, (t % 2) * 128:(t % 2) * 128 + 128]
        p.op("tensor", lambda e_, o=ps, i_=srcc: e_.transpose(o, i_, c.IDENTF[:]),
             reads=[srcc, c.IDENTF[:]], writes=[ps])
        dst = c.LAT[0:8, t * 128:(t + 1) * 128]
        p.op("vector", lambda e_, a=dst, b_=ps: e_.tensor_copy(out=a, in_=b_),
             reads=[ps], writes=[dst])

    for e in range(8):
        for t in range(16):
            rhs = c.LAT[0:8, t * 128:(t + 1) * 128]
            lhs = c.SEL[:, e * 128:(e + 1) * 128]
            ps = (c.PS_A, c.PS_B)[t % 2]
            p.op("tensor", lambda e_, o=ps[:, 0:128], l=lhs, r=rhs:
                 e_.matmul(o, l, r, start=True, stop=True),
                 reads=[lhs, rhs], writes=[ps[:, 0:128]])
            dst = c.SC16[:, t * 128:(t + 1) * 128]
            p.op("vector", lambda e_, a=dst, b_=ps[:, 0:128]: e_.tensor_copy(out=a, in_=b_),
                 reads=[ps[:, 0:128]], writes=[dst])
        for hidm in range(16):
            buf = hidm % 2
            w0s = WS16[:, buf * 1024:(buf + 1) * 1024]
            w1s = WS16[:, 2048 + buf * 1024:2048 + (buf + 1) * 1024]
            p.dma("gpsimd", w0s.rearrange("p (g m) -> p g m", m=128),
                  w["g0T"][e].rearrange("(g p) n -> p g n", p=128)[:, :, hidm * 128:(hidm + 1) * 128],
                  stream=f"g0_{buf}")
            p.dma("gpsimd", w1s.rearrange("p (g m) -> p g m", m=128),
                  w["g1T"][e].rearrange("(g p) n -> p g n", p=128)[:, :, hidm * 128:(hidm + 1) * 128],
                  stream=f"g1_{buf}")
            pool = c.B if hidm < 8 else c.C
            for tc in range(NTC):
                for g in range(NTILE):
                    lhs = w0s[:, g * 128:(g + 1) * 128]
                    rhs = pt(c.A, g)[:, tc * TC:(tc + 1) * TC]
                    p.op("tensor", lambda e_, o=c.PS_A[:, :], l=lhs, r=rhs, s=(g == 0), st=(g == 7):
                         e_.matmul(o, l, r, start=s, stop=st),
                         reads=[lhs, rhs], writes=[c.PS_A[:, :]])
                gg = c.SC16b[:, (tc % 2) * TC:(tc % 2) * TC + TC]
                p.op("scalar", lambda e_, a=gg: e_.activation(a, c.PS_A[:, :], AF.Gelu_apprx_tanh),
                     reads=[c.PS_A[:, :]], writes=[gg])
                for g in range(NTILE):
                    lhs = w1s[:, g * 128:(g + 1) * 128]
                    rhs = pt(c.A, g)[:, tc * TC:(tc + 1) * TC]
                    p.op("tensor", lambda e_, o=c.PS_B[:, :], l=lhs, r=rhs, s=(g == 0), st=(g == 7):
                         e_.matmul(o, l, r, start=s, stop=st),
                         reads=[lhs, rhs], writes=[c.PS_B[:, :]])
                slot = pool[:, (hidm % 8) * T + tc * TC:(hidm % 8) * T + (tc + 1) * TC]
                p.op("vector", lambda e_, a=slot, g_=gg: e_.tensor_tensor(out=a, in0=g_,
                     in1=c.PS_B[:, :], op=ALU.mult), reads=[gg, c.PS_B[:, :]], writes=[slot])
                cbch = c.SC16[:, tc * TC:(tc + 1) * TC]
                p.op("vector", lambda e_, a=slot, cb=cbch: e_.tensor_tensor(out=a, in0=a,
                     in1=cb, op=ALU.mult), reads=[slot, cbch], writes=[slot])
        for dm in range(NTILE):
            buf = dm % 2
            lws = WS16[:, buf * 2048:(buf + 1) * 2048]
            p.dma("gpsimd", lws.rearrange("p (g m) -> p g m", m=128),
                  w["lw"][e].rearrange("(g p) n -> p g n", p=128)[:, :, dm * 128:(dm + 1) * 128],
                  stream=f"lw{buf}")
            for tc in range(NTC):
                ps = (c.PS_A, c.PS_B)[tc % 2]
                for k2 in range(16):
                    pool = c.B if k2 < 8 else c.C
                    lhs = lws[:, k2 * 128:(k2 + 1) * 128]
                    rhs = pool[:, (k2 % 8) * T + tc * TC:(k2 % 8) * T + (tc + 1) * TC]
                    p.op("tensor", lambda e_, o=ps[:, :], l=lhs, r=rhs, s=(k2 == 0), st=(k2 == 15):
                         e_.matmul(o, l, r, start=s, stop=st),
                         reads=[lhs, rhs], writes=[ps[:, :]])
                xdst = X[dm][:, tc * TC:(tc + 1) * TC]
                p.op("vector", lambda e_, x=xdst, b_=ps[:, :]:
                     e_.scalar_tensor_tensor(out=x, in0=b_, scalar=0.5, in1=x,
                                            op0=ALU.mult, op1=ALU.add),
                     reads=[ps[:, :], xdst], writes=[xdst])


def emit_quant(c, src, dst, W, xn=None, tmp=None):
    p = c.p
    nb = W // 16
    BM = c.QSC[:, 0:nb]
    S = c.QSC[:, 32:32 + nb]
    R = c.QSC[:, 64:64 + nb]
    XN = xn if xn is not None else c.TMP32[:, 0:W]
    UN = c.SBC[:, 0:W]
    QA = c.BBC[:, 0:W]
    TB = tmp if tmp is not None else c.PS_O[:, 0:W]
    RN = c.RNDI[:, 0:W]
    p.op("vector", lambda e: e.tensor_reduce(out=BM, in_=src.rearrange("p (n w) -> p n w", w=16),
         axis=AX.X, op=ALU.max, apply_absolute_value=True), reads=[src], writes=[BM])
    p.op("vector", lambda e: e.tensor_scalar(out=S, in0=BM, scalar1=1.0 / 6.0, scalar2=1e-12,
         op0=ALU.mult, op1=ALU.max), reads=[BM], writes=[S])
    p.op("vector", lambda e: e.reciprocal(out=R, in_=S), reads=[S], writes=[R])
    p.op("vector", lambda e: e.tensor_tensor(out=XN.rearrange("p (n w) -> p n w", w=16),
         in0=src.rearrange("p (n w) -> p n w", w=16),
         in1=R.unsqueeze(2).broadcast_to([128, nb, 16]), op=ALU.mult),
         reads=[src, R], writes=[XN])
    p.op("scalar", lambda e: e.activation(UN, XN, AF.Abs), reads=[XN], writes=[UN])
    p.op("vector", lambda e: e.tensor_scalar(out=QA, in0=UN, scalar1=2.0, scalar2=None,
         op0=ALU.mult), reads=[UN], writes=[QA])
    p.op("vector", lambda e: e.tensor_copy(out=RN, in_=QA), reads=[QA], writes=[RN])
    p.op("scalar", lambda e: e.copy(out=QA, in_=RN), reads=[RN], writes=[QA])
    p.op("vector", lambda e: e.tensor_scalar(out=QA, in0=QA, scalar1=0.5, scalar2=2.0,
         op0=ALU.mult, op1=ALU.min), reads=[QA], writes=[QA])
    p.op("vector", lambda e: e.tensor_scalar(out=TB, in0=UN, scalar1=-2.0, scalar2=None,
         op0=ALU.add), reads=[UN], writes=[TB])
    p.op("vector", lambda e: e.tensor_copy(out=RN, in_=TB), reads=[TB], writes=[RN])
    p.op("scalar", lambda e: e.copy(out=TB, in_=RN), reads=[RN], writes=[TB])
    p.op("vector", lambda e: e.tensor_scalar(out=TB, in0=TB, scalar1=0.0, scalar2=2.0,
         op0=ALU.max, op1=ALU.min), reads=[TB], writes=[TB])
    p.op("vector", lambda e: e.tensor_add(out=QA, in0=QA, in1=TB), reads=[QA, TB], writes=[QA])
    p.op("vector", lambda e: e.tensor_scalar(out=TB, in0=UN, scalar1=-4.5, scalar2=None,
         op0=ALU.add), reads=[UN], writes=[TB])
    p.op("vector", lambda e: e.tensor_copy(out=RN, in_=TB), reads=[TB], writes=[RN])
    p.op("scalar", lambda e: e.copy(out=TB, in_=RN), reads=[RN], writes=[TB])
    p.op("vector", lambda e: e.tensor_scalar(out=TB, in0=TB, scalar1=0.0, scalar2=1.0,
         op0=ALU.max, op1=ALU.min), reads=[TB], writes=[TB])
    p.op("vector", lambda e: e.tensor_add(out=QA, in0=QA, in1=TB), reads=[QA, TB], writes=[QA])
    p.op("vector", lambda e: e.tensor_add(out=QA, in0=QA, in1=TB), reads=[QA, TB], writes=[QA])
    p.op("scalar", lambda e: e.activation(TB, XN, AF.Sign), reads=[XN], writes=[TB])
    p.op("vector", lambda e: e.tensor_mul(out=QA, in0=QA, in1=TB), reads=[QA, TB], writes=[QA])
    p.op("vector", lambda e: e.tensor_tensor(out=dst.rearrange("p (n w) -> p n w", w=16),
         in0=QA.rearrange("p (n w) -> p n w", w=16),
         in1=S.unsqueeze(2).broadcast_to([128, nb, 16]), op=ALU.mult),
         reads=[QA, S], writes=[dst])


def emit_l1_ffn(c, w, do_quant):
    p = c.p
    X = [xt(c, i) for i in range(NTILE)]
    WS16 = c.WS[:].bitcast(BF16)
    BF = c.B[:].bitcast(F32)
    CF = c.C[:].bitcast(F32)

    def xtm(tt):
        if tt < 8:
            return BF[:, tt * 1024:(tt + 1) * 1024]
        return CF[:, (tt - 8) * 1024:(tt - 7) * 1024]

    emit_ln_prefix(c, [xtm(tt) for tt in range(16)], variant="plain", tm_f32=True)

    for tt in range(16):
        for ch in range(2):
            srcq = xtm(tt)[:, ch * 512:(ch + 1) * 512]
            dstq = c.A[:, tt * 1024 + ch * 512:tt * 1024 + (ch + 1) * 512]
            emit_quant(c, srcq, dstq, 512)
    for tt in range(16):
        for dch in range(NTILE):
            srct = c.A[:, tt * 1024 + dch * 128:tt * 1024 + (dch + 1) * 128]
            pslot = c.PS_T[:, (dch % 8) * 128:(dch % 8) * 128 + 128]
            p.op("tensor", lambda e, o=pslot, i_=srct: e.transpose(o, i_, c.IDENT[:]),
                 reads=[srct, c.IDENT[:]], writes=[pslot])
            dstt = pt(c.B, dch)[:, tt * 128:(tt + 1) * 128]
            p.op("vector", lambda e, a=dstt, b_=pslot: e.tensor_copy(out=a, in_=b_),
                 reads=[pslot], writes=[dstt])

    if do_quant:
        for wap, outap, nr in ((w["wg"], w["wgT"], 2816), (w["wu"], w["wuT"], 2816)):
            for it in range(22):
                buf = it % 2
                slot = c.WS[:, buf * 1024:(buf + 1) * 1024]
                p.dma("gpsimd", slot, wap[it * 128:(it + 1) * 128, :], stream=f"qw{buf}")
                qd = c.SC16[:, 0:1024]
                for ch in range(2):
                    emit_quant(c, slot[:, ch * 512:(ch + 1) * 512],
                               qd[:, ch * 512:(ch + 1) * 512], 512)
                tout = c.SC16b[:, 0:1024]
                for dch in range(NTILE):
                    pslot = c.PS_T[:, (dch % 8) * 128:(dch % 8) * 128 + 128]
                    p.op("tensor", lambda e, o=pslot, i_=qd[:, dch * 128:(dch + 1) * 128]:
                         e.transpose(o, i_, c.IDENT[:]),
                         reads=[qd[:, dch * 128:(dch + 1) * 128], c.IDENT[:]], writes=[pslot])
                    p.op("vector", lambda e, a=tout[:, dch * 128:(dch + 1) * 128], b_=pslot:
                         e.tensor_copy(out=a, in_=b_),
                         reads=[pslot], writes=[tout[:, dch * 128:(dch + 1) * 128]])
                p.dma("scalar",
                      outap.rearrange("(dch p) n -> p dch n", p=128)[:, :, it * 128:(it + 1) * 128],
                      tout.rearrange("p (dch m) -> p dch m", m=128), stream="qwo")
        for dt in range(NTILE):
            slot = c.WS[:, 0:1408]
            for half in range(2):
                p.dma("gpsimd", slot,
                      w["wd"][dt * 128:(dt + 1) * 128, half * 1408:(half + 1) * 1408],
                      stream="qd0")
                qd = c.SC16[:, 0:1408]
                emit_quant(c, slot[:, 0:512], qd[:, 0:512], 512)
                emit_quant(c, slot[:, 512:1024], qd[:, 512:1024], 512)
                emit_quant(c, slot[:, 1024:1408], qd[:, 1024:1408], 384)
                tout = c.SC16b[:, 0:1408]
                for ich in range(11):
                    pslot = c.PS_T[:, (ich % 8) * 128:(ich % 8) * 128 + 128]
                    p.op("tensor", lambda e, o=pslot, i_=qd[:, ich * 128:(ich + 1) * 128]:
                         e.transpose(o, i_, c.IDENT[:]),
                         reads=[qd[:, ich * 128:(ich + 1) * 128], c.IDENT[:]], writes=[pslot])
                    p.op("vector", lambda e, a=tout[:, ich * 128:(ich + 1) * 128], b_=pslot:
                         e.tensor_copy(out=a, in_=b_),
                         reads=[pslot], writes=[tout[:, ich * 128:(ich + 1) * 128]])
                p.dma("scalar",
                      w["wdT"][half * 1408:(half + 1) * 1408, :]
                      .rearrange("(ich p) n -> p ich n", p=128)[:, :, dt * 128:(dt + 1) * 128],
                      tout.rearrange("p (ich m) -> p ich m", m=128), stream="qdo")

    NIC = 11
    for slab in range(4):
        ts0 = slab * 512
        for ic in range(NIC):
            bufg = ic % 2
            gslot = WS16[:, bufg * 2048:(bufg + 1) * 2048]
            p.dma("gpsimd", gslot.rearrange("p (g m) -> p g m", m=256),
                  w["wgT"].rearrange("(g p) n -> p g n", p=128)[:, :, ic * 256:(ic + 1) * 256],
                  stream=f"gt{bufg}")
            for tokm in range(4):
                tt = slab * 4 + tokm
                ps = (c.PS_A, c.PS_B)[tokm % 2]
                for g in range(NTILE):
                    lhs = pt(c.B, g)[:, tt * 128:(tt + 1) * 128]
                    rhs = gslot[:, g * 256:(g + 1) * 256]
                    p.op("tensor", lambda e, o=ps[:, 0:256], l=lhs, r=rhs, s=(g == 0), st=(g == 7):
                         e.matmul(o, l, r, start=s, stop=st),
                         reads=[lhs, rhs], writes=[ps[:, 0:256]])
                gg = c.SC16b[:, tokm * 256:(tokm + 1) * 256]
                p.op("scalar", lambda e, a=gg, b_=ps[:, 0:256]: e.activation(a, b_, AF.Silu),
                     reads=[ps[:, 0:256]], writes=[gg])
            uslot = WS16[:, 2048 + bufg * 2048:2048 + (bufg + 1) * 2048] if False else gslot
            bufu = (ic + 1) % 2
            uslot = WS16[:, bufu * 2048:(bufu + 1) * 2048]
            p.dma("gpsimd", uslot.rearrange("p (g m) -> p g m", m=256),
                  w["wuT"].rearrange("(g p) n -> p g n", p=128)[:, :, ic * 256:(ic + 1) * 256],
                  stream=f"ut{bufu}")
            for pair in range(2):
                for tk2 in range(2):
                    tokm = pair * 2 + tk2
                    tt = slab * 4 + tokm
                    ps = (c.PS_A, c.PS_B)[tokm % 2]
                    for g in range(NTILE):
                        lhs = pt(c.B, g)[:, tt * 128:(tt + 1) * 128]
                        rhs = uslot[:, g * 256:(g + 1) * 256]
                        p.op("tensor", lambda e, o=ps[:, 0:256], l=lhs, r=rhs, s=(g == 0), st=(g == 7):
                             e.matmul(o, l, r, start=s, stop=st),
                             reads=[lhs, rhs], writes=[ps[:, 0:256]])
                    hh = c.TMP32[:, tk2 * 256:(tk2 + 1) * 256]
                    gg = c.SC16b[:, tokm * 256:(tokm + 1) * 256]
                    p.op("vector", lambda e, a=hh, g_=gg, b_=ps[:, 0:256]:
                         e.tensor_tensor(out=a, in0=g_, in1=b_, op=ALU.mult),
                         reads=[gg, ps[:, 0:256]], writes=[hh])
                hq = c.SC16[:, pair * 512:(pair + 1) * 512]
                emit_quant(c, c.TMP32[:, 0:512], hq, 512,
                           xn=c.PS_O[:, 0:512], tmp=c.PS_BIG[:, 0:512])
                for tk2 in range(2):
                    tokm = pair * 2 + tk2
                    for j in range(2):
                        it = ic * 2 + j
                        sl8 = ((tokm * 2 + j) % 8)
                        pslot = c.PS_T[:, sl8 * 128:sl8 * 128 + 128]
                        hsrc = hq[:, tk2 * 256 + j * 128:tk2 * 256 + (j + 1) * 128]
                        p.op("tensor", lambda e, o=pslot, i_=hsrc: e.transpose(o, i_, c.IDENT[:]),
                             reads=[hsrc, c.IDENT[:]], writes=[pslot])
                        dsth = c.C[:, it * 512 + tokm * 128:it * 512 + (tokm + 1) * 128]
                        p.op("vector", lambda e, a=dsth, b_=pslot: e.tensor_copy(out=a, in_=b_),
                             reads=[pslot], writes=[dsth])
        for dm in range(NTILE):
            lws = WS16[:, 0:2816]
            p.dma("gpsimd", lws.rearrange("p (g m) -> p g m", m=128),
                  w["wdT"].rearrange("(g p) n -> p g n", p=128)[:, :, dm * 128:(dm + 1) * 128],
                  stream="dnw")
            ps = (c.PS_A, c.PS_B)[dm % 2]
            for k2 in range(22):
                lhs = lws[:, k2 * 128:(k2 + 1) * 128]
                rhs = c.C[:, k2 * 512:(k2 + 1) * 512]
                p.op("tensor", lambda e, o=ps[:, :], l=lhs, r=rhs, s=(k2 == 0), st=(k2 == 21):
                     e.matmul(o, l, r, start=s, stop=st),
                     reads=[lhs, rhs], writes=[ps[:, :]])
            xdst = X[dm][:, ts0:ts0 + 512]
            p.op("vector", lambda e, x=xdst, b_=ps[:, :]:
                 e.scalar_tensor_tensor(out=x, in0=b_, scalar=0.5, in1=x,
                                        op0=ALU.mult, op1=ALU.add),
                 reads=[ps[:, :], xdst], writes=[xdst])


def _pp(v):
    a = np.asarray(v, np.float32)
    return np.ascontiguousarray(a.reshape(-1, 128).T)


def _prep_weights(inputs):
    g = {}
    f32 = np.float32
    A = lambda x: np.ascontiguousarray(np.asarray(x))
    g["l0f_q1"] = A(inputs["L0_ffn_fc1_q"]).astype(np.int32)
    g["l0f_s1"] = A(inputs["L0_ffn_fc1_scales"]).astype(f32)
    g["l0f_b1"] = _pp(inputs["L0_ffn_fc1_bias"])
    g["l0f_q2"] = A(inputs["L0_ffn_fc2_q"]).astype(np.int32)
    g["l0f_s2"] = A(inputs["L0_ffn_fc2_scales"]).astype(f32)
    g["l0f_b2"] = _pp(inputs["L0_ffn_fc2_bias"])
    g["l0f_gamma"] = _pp(inputs["L0_ffn_ln_gamma"])
    g["l0f_beta"] = _pp(inputs["L0_ffn_ln_beta"])
    wv = np.asarray(inputs["L0_attn_w_att_val"], f32)
    blk = np.zeros((8, 128, 128), f32)
    for j in range(8):
        blk[j, 0:64, 0:64] = wv[2 * j]
        blk[j, 64:128, 64:128] = wv[2 * j + 1]
    g["l0a_watt"] = A(blk.transpose(1, 0, 2).reshape(128, 1024))
    g["l0a_wvT"] = A(np.asarray(inputs["L0_attn_wv_w"], f32).T)
    g["l0a_outwT"] = A(np.asarray(inputs["L0_attn_out_w"], f32).T)
    g["l0a_wv_b"] = _pp(inputs["L0_attn_wv_b"])
    g["l0a_out_b"] = _pp(inputs["L0_attn_out_b"])
    g["l1a_wq"] = A(np.asarray(inputs["L1_attn_W_q"], f32))
    g["l1a_wk"] = A(np.asarray(inputs["L1_attn_W_k"], f32))
    g["l1a_wv"] = A(np.asarray(inputs["L1_attn_W_v"], f32))
    g["l1a_wo"] = A(np.asarray(inputs["L1_attn_W_o"], f32))
    g["l1a_wdkv"] = A(np.asarray(inputs["L1_attn_W_down_kv"], f32))
    g["l1a_wupk"] = A(np.asarray(inputs["L1_attn_W_up_k"], f32))
    g["l1a_wupv"] = A(np.asarray(inputs["L1_attn_W_up_v"], f32))
    g["l2a_wqT"] = A(np.asarray(inputs["L2_attn_Wq"], f32).T)
    g["l2a_wkT"] = A(np.asarray(inputs["L2_attn_Wk"], f32).T)
    g["l2a_wvT"] = A(np.asarray(inputs["L2_attn_Wv"], f32).T)
    g["l2a_woT"] = A(np.asarray(inputs["L2_attn_Wo"], f32).T)
    g["l3f_g0T"] = A(np.asarray(inputs["L3_ffn_gating_w"], f32)[:, 0].transpose(0, 2, 1))
    g["l3f_g1T"] = A(np.asarray(inputs["L3_ffn_gating_w"], f32)[:, 1].transpose(0, 2, 1))
    g["l3f_lw"] = A(np.asarray(inputs["L3_ffn_linear_w"], f32))
    g["l3f_rs"] = _pp(inputs["L3_ffn_router_scale"])
    g["l3f_rw"] = A(np.asarray(inputs["L3_ffn_router_w"], f32))
    g["l3f_pes"] = A(np.asarray(inputs["L3_ffn_per_expert_scale"], f32).reshape(1, 8))
    g["l1f_wg"] = A(np.asarray(inputs["L1_ffn_w_gate"], f32))
    g["l1f_wu"] = A(np.asarray(inputs["L1_ffn_w_up"], f32))
    g["l1f_wd"] = A(np.asarray(inputs["L1_ffn_w_down"], f32))
    g["l2f_W_base"] = A(np.asarray(inputs["L2_ffn_W_base"], f32))
    g["l2f_W_enc"] = A(np.asarray(inputs["L2_ffn_W_enc"], f32))
    g["l2f_W_dec"] = A(np.asarray(inputs["L2_ffn_W_dec"], f32))
    g["l2f_W_down"] = A(np.asarray(inputs["L2_ffn_W_down"], f32))
    g["l2f_W_up"] = A(np.asarray(inputs["L2_ffn_W_up"], f32))
    g["l2f_b_base"] = _pp(inputs["L2_ffn_b_base"])
    be = np.asarray(inputs["L2_ffn_b_enc"], f32).reshape(8, 2, 128)
    g["l2f_b_enc"] = A(be.transpose(2, 0, 1).reshape(128, 16))
    bd = np.asarray(inputs["L2_ffn_b_dec"], f32).reshape(8, 8, 128)
    g["l2f_b_dec"] = A(bd.transpose(2, 0, 1).reshape(128, 64))
    bdn = np.zeros((128, 8), f32)
    bdn[0:64, :] = np.asarray(inputs["L2_ffn_b_down"], f32).T
    g["l2f_b_down"] = A(bdn)
    bu = np.asarray(inputs["L2_ffn_b_up"], f32).reshape(8, 8, 128)
    g["l2f_b_up"] = A(bu.transpose(2, 0, 1).reshape(128, 64))
    g["l2f_cidx"] = A(np.asarray(inputs["L2_ffn_connected_idx"], np.int32).reshape(1, 8))
    g["l3a_wqT"] = A(np.asarray(inputs["L3_attn_WQ"], f32).T)
    g["l3a_wkT"] = A(np.asarray(inputs["L3_attn_WK"], f32).T)
    g["l3a_wvT"] = A(np.asarray(inputs["L3_attn_WV"], f32).T)
    g["l3a_woT"] = A(np.asarray(inputs["L3_attn_WO"], f32).T)
    return g


DEBUG_DUMP = []

SUBLAYERS = [
    (0, "ffn"), (0, "attn"), (0, "ffn"),
    (1, "ffn"), (1, "attn"), (1, "ffn"),
    (2, "ffn"), (2, "attn"), (2, "ffn"),
    (3, "ffn"), (3, "attn"), (3, "ffn"),
]


def build_program(nc, sublayers):
    p = Prog(nc)
    in_names = []

    def dr(name, shape, dt, kind="ExternalInput"):
        h = nc.dram_tensor(name, shape, dt, kind=kind)
        p.reg(h, "dram")
        if kind == "ExternalInput":
            in_names.append(name)
        return h.ap()

    x_in = dr("x", [1024, T], F32)
    x_out = dr("out", [1024, T], F32, kind="ExternalOutput")

    need = {SUBLAYERS[s] for s in sublayers}
    w = {}
    if (0, "ffn") in need:
        w["l0f"] = {
            "q1": dr("l0f_q1", [1024, 4096], I32),
            "s1": dr("l0f_s1", [8, 4096], F32),
            "b1": dr("l0f_b1", [128, 32], F32),
            "q2": dr("l0f_q2", [4096, 1024], I32),
            "s2": dr("l0f_s2", [32, 1024], F32),
            "b2": dr("l0f_b2", [128, 8], F32),
            "gamma": dr("l0f_gamma", [128, 8], F32),
            "beta": dr("l0f_beta", [128, 8], F32),
            "w1b": dr("l0f_w1b", [1024, 4096], BF16, kind="Internal"),
            "w2b": dr("l0f_w2b", [4096, 1024], BF16, kind="Internal"),
        }

    if (0, "attn") in need:
        w["l0a"] = {
            "watt": dr("l0a_watt", [128, 1024], F32),
            "wvT": dr("l0a_wvT", [1024, 1024], F32),
            "outwT": dr("l0a_outwT", [1024, 1024], F32),
            "wv_b": dr("l0a_wv_b", [128, 8], F32),
            "out_b": dr("l0a_out_b", [128, 8], F32),
        }
    if (1, "attn") in need:
        w["l1a"] = {
            "wq": dr("l1a_wq", [1024, 1024], F32),
            "wk": dr("l1a_wk", [1024, 1024], F32),
            "wv": dr("l1a_wv", [1024, 1024], F32),
            "wo": dr("l1a_wo", [1024, 1024], F32),
            "wdkv": dr("l1a_wdkv", [1024, 256], F32),
            "wupk": dr("l1a_wupk", [256, 1024], F32),
            "wupv": dr("l1a_wupv", [256, 1024], F32),
        }
    if (2, "attn") in need:
        w["l2a"] = {
            "wqT": dr("l2a_wqT", [1024, 1024], F32),
            "wkT": dr("l2a_wkT", [1024, 128], F32),
            "wvT": dr("l2a_wvT", [1024, 128], F32),
            "woT": dr("l2a_woT", [1024, 1024], F32),
        }
    if (1, "ffn") in need:
        w["l1f"] = {
            "wg": dr("l1f_wg", [2816, 1024], F32),
            "wu": dr("l1f_wu", [2816, 1024], F32),
            "wd": dr("l1f_wd", [1024, 2816], F32),
            "wgT": dr("l1f_wgT", [1024, 2816], BF16, kind="Internal"),
            "wuT": dr("l1f_wuT", [1024, 2816], BF16, kind="Internal"),
            "wdT": dr("l1f_wdT", [2816, 1024], BF16, kind="Internal"),
        }
    if (2, "ffn") in need:
        w["l2f"] = {
            "W_base": dr("l2f_W_base", [1024, 1024], F32),
            "W_enc": dr("l2f_W_enc", [8, 1024, 256], F32),
            "W_dec": dr("l2f_W_dec", [8, 256, 1024], F32),
            "W_down": dr("l2f_W_down", [8, 1024, 64], F32),
            "W_up": dr("l2f_W_up", [8, 64, 1024], F32),
            "b_base": dr("l2f_b_base", [128, 8], F32),
            "b_enc": dr("l2f_b_enc", [128, 16], F32),
            "b_dec": dr("l2f_b_dec", [128, 64], F32),
            "b_down": dr("l2f_b_down", [128, 8], F32),
            "b_up": dr("l2f_b_up", [128, 64], F32),
            "cidx": dr("l2f_cidx", [1, 8], I32),
        }
    if (3, "ffn") in need:
        w["l3f"] = {
            "g0T": dr("l3f_g0T", [8, 1024, 2048], F32),
            "g1T": dr("l3f_g1T", [8, 1024, 2048], F32),
            "lw": dr("l3f_lw", [8, 2048, 1024], F32),
            "rs": dr("l3f_rs", [128, 8], F32),
            "rw": dr("l3f_rw", [1024, 8], F32),
            "pes": dr("l3f_pes", [1, 8], F32),
        }
    if (3, "attn") in need:
        w["l3a"] = {
            "wqT": dr("l3a_wqT", [1024, 1024], F32),
            "wkT": dr("l3a_wkT", [1024, 1024], F32),
            "wvT": dr("l3a_wvT", [1024, 1024], F32),
            "woT": dr("l3a_woT", [1024, 1024], F32),
        }

    with contextlib.ExitStack() as stack:
        c = alloc_common(nc, p, stack)
        for i in range(NTILE):
            p.dma("sync", xt(c, i), x_in[i * 128:(i + 1) * 128, :], stream=f"xin{i % 4}")
        done = set()
        for s in sublayers:
            key = SUBLAYERS[s]
            if key == (0, "ffn"):
                emit_l0_ffn(c, w["l0f"], do_dequant=key not in done)
            elif key == (0, "attn"):
                emit_l0_attn(c, w["l0a"])
            elif key == (1, "attn"):
                emit_l1_attn(c, w["l1a"])
            elif key == (2, "attn"):
                emit_l2_attn(c, w["l2a"])
            elif key == (1, "ffn"):
                emit_l1_ffn(c, w["l1f"], do_quant=key not in done)
            elif key == (2, "ffn"):
                emit_l2_ffn(c, w["l2f"])
            elif key == (3, "ffn"):
                emit_l3_ffn(c, w["l3f"])
            elif key == (3, "attn"):
                emit_l3_attn(c, w["l3a"])
            else:
                raise NotImplementedError(str(key))
            done.add(key)
        for i in range(NTILE):
            p.dma("sync", x_out[i * 128:(i + 1) * 128, :], xt(c, i), stream=f"xout{i % 4}")
        for bn in DEBUG_DUMP:
            buf = getattr(c, bn)
            bap = buf.ap() if callable(getattr(buf, "ap", None)) else buf[:]
            sh = list(bap.shape)
            dt_ = bap.dtype
            dap = nc.dram_tensor("dbg_" + bn, sh, dt_, kind="ExternalOutput").ap()
            p.reg(nc.get_tensor("dbg_" + bn) if hasattr(nc, "get_tensor") else _DH(dap), "dram") \
                if False else p.meta.update({dap.name: (int(np.prod(sh)), 2 if dt_ == BF16 else 4, "dram")})
            p.dma("sync", dap, bap, stream="dbg")
        p.barrier_wait("sync")
        p.emit(stack)
    return in_names


def run_sublayers(inputs, sublayers, x_override=None, n_cores=8, trace=False):
    g = _prep_weights(inputs)
    x = np.asarray(x_override if x_override is not None else inputs["L0_attn_x"], np.float32)
    nc = bass.Bass("TRN2", target_bir_lowering=False, dynamic_dma_scratch_size=16384)
    in_names = build_program(nc, sublayers)
    in_maps = []
    for core in range(n_cores):
        m = {}
        for n in in_names:
            if n == "x":
                m["x"] = np.ascontiguousarray(x[core].T)
            else:
                m[n] = g[n]
        in_maps.append(m)
    res = run_bass_kernel_spmd(nc, in_maps, core_ids=list(range(n_cores)), trace=trace)
    outs = [np.asarray(r["out"], np.float32).T for r in res.results]
    return np.stack(outs, axis=0), res


def kernel(**inputs):
    out, _ = run_sublayers(inputs, list(range(12)))
    return out.astype(np.float32)
